# version 01
# baseline (speedup 1.0000x reference)
import contextlib
import types
import numpy as np
import ml_dtypes
import concourse.bass as bass
import concourse.mybir as mybir
from concourse.bass_utils import run_bass_kernel_spmd

F32 = mybir.dt.float32
BF16 = mybir.dt.bfloat16
AX = mybir.AxisListType
ALU = mybir.AluOpType
ACT = mybir.ActivationFunctionType

S = 8192
TOWN = 2048
DM = 1024
H = 16
D = 64
FI = 4096
NCH = 8
NB = 16
EPS = 1e-5


_COUNTING = [False]
_CAPTURE = [None]
_SEP_AP = [None]
_SEP_COUNT = [0]
_HZW = 128


class _MockInst:
    def then_inc(self, h, k=1):
        return self

    def __getattr__(self, name):
        return lambda *a, **kw: self


class _Cap:
    __slots__ = ("reads", "writes", "reduce", "touched")

    def __init__(self):
        self.reads = []
        self.writes = []
        self.reduce = False
        self.touched = False


def _is_ap(x):
    return hasattr(x, "offset") and hasattr(x, "free_size") and hasattr(x, "ap")


class _MockEng:
    def __getattr__(self, name):
        def call(*a, **kw):
            cap = _CAPTURE[0]
            if cap is not None and name not in ("wait_ge", "engine_nop"):
                cap.touched = True
                if "reduce" in name:
                    cap.reduce = True
                aps = [x for x in a if _is_ap(x)]
                if aps:
                    cap.writes.append(aps[0])
                    cap.reads += aps[1:]
                cap.reads += [v for v in kw.values() if _is_ap(v)]
            return _MockInst()
        return call


_MOCK = _MockEng()


def _ap_rng(ap):
    try:
        return (ap.name, int(ap.offset), int(ap.free_size()))
    except Exception:
        return None


def _hazard(prev, cap):
    if prev is None:
        return False
    for w in prev.writes:
        wi = _ap_rng(w)
        if wi is None:
            continue
        wt, wo, ww = wi
        for rap in cap.reads:
            ri = _ap_rng(rap)
            if ri is None:
                continue
            rt, ro, rw = ri
            if rt != wt or ro + rw <= wo or wo + ww <= ro:
                continue
            if prev.reduce or ww < _HZW or (wo + ww) - ro < _HZW:
                return True
    return False


def _sep_op(e):
    s = _SEP_AP[0]
    e.tensor_copy(s[:, 1:2], s[:, 0:1])


class Sem:

    def __init__(self, st, nc, name):
        self.h = st.enter_context(nc.semaphore(name))
        self.n = 0

    def inc(self, inst, k=1):
        if _COUNTING[0]:
            self.n += k
        inst.then_inc(self.h, k)
        return self.n

    def dma(self, inst):
        return self.inc(inst, 16)


def mk_adder(lst, track=False):
    state = {"last": None}

    def add(fn):
        cap = _Cap() if track else None
        _CAPTURE[0] = cap
        _COUNTING[0] = True
        try:
            fn(_MOCK)
        finally:
            _COUNTING[0] = False
            _CAPTURE[0] = None
        if track and cap.touched:
            if _hazard(state["last"], cap):
                lst.append(_sep_op)
                _SEP_COUNT[0] += 1
            state["last"] = cap
        lst.append(fn)
    return add


class Sched:
    def __init__(self):
        self.q = {"t": [], "v": [], "s": [], "g": [], "y": []}

    def run(self, nc):
        with nc.Block() as blk:
            if self.q["y"]:
                @blk.sync
                def _(e):
                    for f in self.q["y"]:
                        f(e)
            if self.q["t"]:
                @blk.tensor
                def _(e):
                    for f in self.q["t"]:
                        f(e)
            if self.q["v"]:
                @blk.vector
                def _(e):
                    for f in self.q["v"]:
                        f(e)
            if self.q["s"]:
                @blk.scalar
                def _(e):
                    for f in self.q["s"]:
                        f(e)
            if self.q["g"]:
                @blk.gpsimd
                def _(e):
                    for f in self.q["g"]:
                        f(e)


def emit_nvfp4(E, x_ap, out_ap, sA, sB, sC, sT_, tA, tB, gs_ap, eng="v"):
    W = x_ap.shape[-1]
    nsc = W // 16
    P = x_ap.shape[0]
    E(lambda e: e.tensor_scalar(tA, x_ap, -1.0, None, ALU.mult))
    E(lambda e: e.tensor_tensor(tA, tA, x_ap, ALU.max))
    tA3 = tA.rearrange("p (a b) -> p a b", b=16)
    if eng == "v":
        E(lambda e: e.tensor_reduce(sA, tA3, AX.X, ALU.max))
    else:
        tB3 = tB.rearrange("p (a b) -> p a b", b=16)
        E(lambda e: e.tensor_tensor(tB3[:, :, 0:8], tA3[:, :, 0:8],
                                    tA3[:, :, 8:16], ALU.max))
        E(lambda e: e.tensor_tensor(tB3[:, :, 0:4], tB3[:, :, 0:4],
                                    tB3[:, :, 4:8], ALU.max))
        E(lambda e: e.tensor_tensor(tB3[:, :, 0:2], tB3[:, :, 0:2],
                                    tB3[:, :, 2:4], ALU.max))
        E(lambda e: e.tensor_tensor(tB3[:, :, 0:1], tB3[:, :, 0:1],
                                    tB3[:, :, 1:2], ALU.max))
        E(lambda e: e.tensor_copy(sA, tB3[:, :, 0:1].squeeze(2)))
    E(lambda e: e.tensor_scalar(sA, sA, float(np.float32(1.0 / 6.0)), None, ALU.mult))
    E(lambda e: e.tensor_scalar(sA, sA, gs_ap, 448.0, ALU.mult, ALU.min))
    E(lambda e: e.memset(sB, 2.0 ** 9))
    E(lambda e: e.memset(sC, 2.0 ** -9))
    for p in range(-5, 9):
        cinv = 2.0 ** (3 - p) - 2.0 ** (4 - p)
        cs = 2.0 ** (p - 3) - 2.0 ** (p - 4)
        E(lambda e, p=p, cv=cinv: e.tensor_scalar(sT_, sA, 2.0 ** p, cv, ALU.is_ge, ALU.mult))
        E(lambda e: e.tensor_tensor(sB, sB, sT_, ALU.add))
        E(lambda e, p=p, cv=cs: e.tensor_scalar(sT_, sA, 2.0 ** p, cv, ALU.is_ge, ALU.mult))
        E(lambda e: e.tensor_tensor(sC, sC, sT_, ALU.add))
    E(lambda e: e.tensor_tensor(sT_, sA, sB, ALU.mult))
    E(lambda e: e.memset(sA, 0.0))
    for k in range(16):
        op = ALU.is_gt if (k % 2 == 0) else ALU.is_ge
        E(lambda e, k=k, op=op: e.tensor_scalar(tB[:, 0:nsc], sT_, k + 0.5, None, op))
        E(lambda e: e.tensor_tensor(sA, sA, tB[:, 0:nsc], ALU.add))
    E(lambda e: e.tensor_tensor(sT_, sA, sC, ALU.mult))
    if eng == "v":
        E(lambda e: e.reciprocal(sC, sT_))
        E(lambda e: e.tensor_scalar(sC, sC, gs_ap, None, ALU.mult))
        rdq_bc = sC.unsqueeze(2).broadcast_to([P, nsc, 16])
        E(lambda e: e.tensor_tensor(tA3, tA3, rdq_bc, ALU.mult))
        E(lambda e: e.tensor_scalar(tB, tA, 0.25, 0.5, ALU.is_ge, ALU.mult))
        for m, cadd in ((0.75, 0.5), (1.25, 0.5), (1.75, 0.5),
                        (2.5, 1.0), (3.5, 1.0), (5.0, 2.0)):
            E(lambda e, m=m, cv=cadd: e.tensor_scalar(out_ap, tA, m, cv, ALU.is_ge, ALU.mult))
            E(lambda e: e.tensor_tensor(tB, tB, out_ap, ALU.add))
    else:
        E(lambda e: e.tensor_scalar(sB, sB, gs_ap, None, ALU.mult))
        gi_bc = sB.unsqueeze(2).broadcast_to([P, nsc, 16])
        E(lambda e: e.tensor_tensor(tA3, tA3, gi_bc, ALU.mult))
        first = True
        for m, cadd in ((0.25, 0.5), (0.75, 0.5), (1.25, 0.5), (1.75, 0.5),
                        (2.5, 1.0), (3.5, 1.0), (5.0, 2.0)):
            E(lambda e, m=m: e.tensor_scalar(sC, sA, m, None, ALU.mult))
            thr_bc = sC.unsqueeze(2).broadcast_to([P, nsc, 16])
            o3_ = out_ap.rearrange("p (a b) -> p a b", b=16)
            E(lambda e, o3_=o3_, thr_bc=thr_bc: e.tensor_tensor(
                o3_, tA3, thr_bc, ALU.is_ge))
            if first:
                E(lambda e, cv=cadd: e.tensor_scalar(tB, out_ap, cv, None, ALU.mult))
                first = False
            else:
                E(lambda e, cv=cadd: e.tensor_scalar(out_ap, out_ap, cv, None, ALU.mult))
                E(lambda e: e.tensor_tensor(tB, tB, out_ap, ALU.add))
    E(lambda e: e.tensor_scalar(tA, x_ap, 0.0, 2.0, ALU.is_ge, ALU.mult))
    E(lambda e: e.tensor_scalar(tA, tA, -1.0, None, ALU.add))
    E(lambda e: e.tensor_tensor(tB, tB, tA, ALU.mult))
    sc8_bc = sT_.unsqueeze(2).broadcast_to([P, nsc, 16])
    tB3f = tB.rearrange("p (a b) -> p a b", b=16)
    o3 = out_ap.rearrange("p (a b) -> p a b", b=16)
    return lambda e: e.tensor_tensor(o3, tB3f, sc8_bc, ALU.mult)


def _declare_io(nc):
    dt = nc.dram_tensor
    c = types.SimpleNamespace()
    c.x_in = dt("x_roll", [S, DM], F32, kind="ExternalInput").ap()
    for nm in ("wq", "wk", "wv", "wo", "wfq", "wfg"):
        setattr(c, nm + "_in", dt(nm, [DM, DM], F32, kind="ExternalInput").ap())
    for nm in ("bq", "bk", "bfq", "bfg"):
        setattr(c, nm + "_in", dt(nm, [128, NCH], F32, kind="ExternalInput").ap())
    c.bv_bc_in = dt("bv_bc", [128, DM], F32, kind="ExternalInput").ap()
    c.bo_bc_in = dt("bo_bc", [128, DM], F32, kind="ExternalInput").ap()
    c.upz_in = dt("upz", [128, 2 * H], F32, kind="ExternalInput").ap()
    c.jc_in = dt("jconst", [4, S], F32, kind="ExternalInput").ap()
    c.ident_in = dt("ident_in", [128, 128], F32, kind="ExternalInput").ap()
    c.gw_in = dt("gate_w", [FI, DM], F32, kind="ExternalInput").ap()
    c.uw_in = dt("up_w", [FI, DM], F32, kind="ExternalInput").ap()
    c.dw_in = dt("down_w", [DM, FI], F32, kind="ExternalInput").ap()
    c.sin_in = dt("s_in", [1, 1], F32, kind="ExternalInput").ap()
    c.sind_in = dt("s_in_down", [1, 1], F32, kind="ExternalInput").ap()
    c.out_hbm = dt("out", [TOWN, DM], F32, kind="ExternalOutput").ap()

    c.xnh_s = dt("xnh_s", [NCH, 128, S], BF16).ap()
    c.xnl_s = dt("xnl_s", [NCH, 128, S], BF16).ap()
    c.karr_s = dt("karr_s", [H, 128, S], BF16).ap()
    c.varr_s = dt("varr_s", [H, S, 128], BF16).ap()
    c.qarr2_s = dt("qarr2_s", [H, 8, TOWN], BF16).ap()
    c.jarr_s = dt("jarr_s", [8, S], BF16).ap()
    c.woar1_s = dt("woar1_s", [H, 128, DM], BF16).ap()
    c.woar2_s = dt("woar2_s", [H, 128, DM], BF16).ap()
    c.wgq_s = dt("wgq_s", [FI, DM], BF16).ap()
    c.wuq_s = dt("wuq_s", [FI, DM], BF16).ap()
    c.wdq_s = dt("wdq_s", [DM, FI], BF16).ap()
    c.wgT_s = dt("wgT_s", [DM, FI], BF16).ap()
    c.wuT_s = dt("wuT_s", [DM, FI], BF16).ap()
    c.wdT_s = dt("wdT_s", [FI, DM], BF16).ap()
    c.sacc_s = dt("sacc_s", [TOWN, DM], F32).ap()
    c.scal_s = dt("scal_s", [16, 128], F32).ap()
    c.den_s = dt("den_s", [4, 512], F32).ap()
    c.sgp_s = dt("sgp_s", [H, TOWN], F32).ap()
    c.sgz_s = dt("sgz_s", [H, TOWN], F32).ap()
    c.q2n_s = dt("q2n_s", [H, TOWN], F32).ap()
    c.mqh_s = dt("mqh_s", [NCH, 128, TOWN], BF16).ap()
    c.mql_s = dt("mql_s", [NCH, 128, TOWN], BF16).ap()
    c.xqf_s = dt("xqf_s", [128, NCH * TOWN], BF16).ap()
    return c
def build_program(upto=4, debug=()):
    nc = bass.Bass("TRN2", target_bir_lowering=False)
    c = _declare_io(nc)
    dbg_out = {}
    dbg_sbuf = []
    for nm in debug:
        if not hasattr(c, nm):
            dbg_sbuf.append(nm)
            continue
        src = getattr(c, nm)
        shp = list(src.shape)
        flat = [int(np.prod(shp[:-1])), shp[-1]]
        dbg_out[nm] = nc.dram_tensor("dbg_" + nm, flat, src.dtype,
                                     kind="ExternalOutput").ap()

    root = contextlib.ExitStack()
    with root:
        r = types.SimpleNamespace()

        def sbR(name, shape, dtyp=F32):
            t = root.enter_context(nc.sbuf_tensor(name, shape, dtyp))
            setattr(r, name, t)
            return t

        sbR("ident", [128, 128])
        sbR("identb", [128, 128], BF16)
        sbR("ones128b", [128, 1], BF16)
        sbR("ones128r", [1, 128], BF16)
        sbR("biases", [128, 4 * NCH])
        sbR("upzh", [128, 2 * H], BF16)
        sbR("upzl", [128, 2 * H], BF16)
        sbR("scbc", [128, 16])
        sbR("kmax2", [1, H])
        sbR("xbar", [128, NCH])
        sbR("sepA", [128, 2])
        _SEP_AP[0] = r.sepA
        _SEP_COUNT[0] = 0

        sD = Sem(root, nc, "sD")
        sG = Sem(root, nc, "sG")
        sT = Sem(root, nc, "sT")
        sV = Sem(root, nc, "sV")
        sSc = Sem(root, nc, "sSc")
        sQ = Sem(root, nc, "sQ")
        sO = Sem(root, nc, "sO")
        sems = (sD, sG, sT, sV, sSc, sQ, sO)

        gq_stack = contextlib.ExitStack()
        gq_al = _wquant_alloc(nc, gq_stack)
        wq_builder = lambda: _wquant_ops(nc, gq_al, c, r, sems)

        _emit_b0(nc, c, r, sems)
        if upto >= 1:
            _emit_b1(nc, c, r, sems)
        if upto >= 2:
            _emit_b2(nc, c, r, sems, wq_builder)
        gq_done_val = sQ.n
        if upto >= 3:
            _emit_b3(nc, c, r, sems)
        gq_stack.close()
        if upto >= 4:
            _emit_b4(nc, c, r, sems, gq_done_val)

        sch = Sched()
        Y = mk_adder(sch.q["y"])
        for nm in dbg_sbuf:
            src = getattr(r, nm)
            outap = nc.dram_tensor("dbg_" + nm, list(src.shape), src.dtype,
                                   kind="ExternalOutput").ap()
            Y(lambda e, src=src, outap=outap: sO.dma(e.dma_start(outap, src[:])))
        for nm, outap in dbg_out.items():
            src = getattr(c, nm)
            if len(src.shape) == 3:
                p_ = src.shape[1]
                for i0 in range(src.shape[0]):
                    Y(lambda e, src=src, outap=outap, i0=i0, p_=p_:
                      sO.dma(e.dma_start(outap[i0 * p_:(i0 + 1) * p_, :], src[i0])))
            else:
                Y(lambda e, src=src, outap=outap: sO.dma(e.dma_start(outap, src)))
        Y(lambda e, v=sO.n: e.wait_ge(sO.h, v))
        Y(lambda e, v=sD.n: e.wait_ge(sD.h, v))
        Y(lambda e, v=sG.n: e.wait_ge(sG.h, v))
        if upto >= 2:
            Y(lambda e, v=sQ.n: e.wait_ge(sQ.h, v))
        sch.run(nc)
    print(f"[build] raw-hazard separators inserted: {_SEP_COUNT[0]}")
    return nc


def _wquant_alloc(nc, stk):
    W2 = 1024
    al = {}
    al["gq1"] = [stk.enter_context(nc.sbuf_tensor(f"gq1_{i}", [128, W2], F32))
                 for i in range(2)]
    al["gqtA"] = stk.enter_context(nc.sbuf_tensor("gqtA", [128, W2], F32))
    al["gqtB"] = stk.enter_context(nc.sbuf_tensor("gqtB", [128, W2], F32))
    al["gqs1"] = stk.enter_context(nc.sbuf_tensor("gqs1", [128, W2 // 16], F32))
    al["gqs2"] = stk.enter_context(nc.sbuf_tensor("gqs2", [128, W2 // 16], F32))
    al["gqs3"] = stk.enter_context(nc.sbuf_tensor("gqs3", [128, W2 // 16], F32))
    al["gqs4"] = stk.enter_context(nc.sbuf_tensor("gqs4", [128, W2 // 16], F32))
    al["gmx"] = stk.enter_context(nc.sbuf_tensor("gmx", [128, 4], F32))
    al["ggs"] = stk.enter_context(nc.sbuf_tensor("ggs", [128, 4], F32))
    al["grow"] = stk.enter_context(nc.sbuf_tensor("grow", [1, 128], F32))
    al["gout"] = [stk.enter_context(nc.sbuf_tensor(f"gout{i}", [128, W2], BF16))
                  for i in range(2)]
    return al


def _wquant_ops(nc, al, c, r, sems):
    sD, sG, sT, sV, sSc, sQ, sO = sems
    v2l, y2l = [], []
    v2, y2 = mk_adder(v2l, track=True), mk_adder(y2l)
    W2 = 1024
    gq1 = al["gq1"]
    gqtA, gqtB = al["gqtA"], al["gqtB"]
    gqs1, gqs2, gqs3, gqs4 = al["gqs1"], al["gqs2"], al["gqs3"], al["gqs4"]
    gmx, ggs, grow, gout = al["gmx"], al["ggs"], al["grow"], al["gout"]
    jobs = [(c.gw_in, c.wgq_s, FI, DM, 0), (c.uw_in, c.wuq_s, FI, DM, 1),
            (c.dw_in, c.wdq_s, DM, FI, 2)]

    vmark = 0
    nt = 0
    for win, wout_s, nrows, wwidth, wi in jobs:
        v2(lambda e, wi=wi: e.memset(gmx[:, wi:wi + 1], 0.0))
        for cr in range(nrows // 128):
            for cc in range(wwidth // W2):
                par = nt % 2
                nt += 1
                if nt > 2:
                    y2(lambda e, v=vmark: e.wait_ge(sV.h, v))
                y2(lambda e, cr=cr, cc=cc, win=win, par=par: sG.dma(
                    e.dma_start(gq1[par][:],
                                win[cr * 128:(cr + 1) * 128,
                                    cc * W2:(cc + 1) * W2])))
                dn = sG.n
                v2(lambda e, v=dn: e.wait_ge(sG.h, v))
                v2(lambda e, par=par, wi=wi: sV.inc(e.tensor_reduce(
                    gmx[:, 3:4], gq1[par][:], AX.X, ALU.max,
                    apply_absolute_value=True)))
                v2(lambda e, wi=wi: e.tensor_tensor(
                    gmx[:, wi:wi + 1], gmx[:, wi:wi + 1], gmx[:, 3:4], ALU.max))
                vmark = sV.n
    for win, wout_s, nrows, wwidth, wi in jobs:
        v2(lambda e, wi=wi, v=vmark: e.wait_ge(sV.h, v))
        v2(lambda e, wi=wi: sV.inc(e.tensor_copy(gmx[:, 3:4],
                                                        gmx[:, wi:wi + 1])))
        vv = sV.n
        y2(lambda e, v=vv: e.wait_ge(sV.h, v))
        y2(lambda e, wi=wi: sD.dma(e.dma_start(
            c.scal_s[10 + wi, 0:128], gmx[:, 3:4])))
        dn = sD.n
        y2(lambda e, v=dn: e.wait_ge(sD.h, v))
        y2(lambda e, wi=wi: sD.dma(e.dma_start(
            grow[:], c.scal_s[10 + wi].unsqueeze(0))))
        dn = sD.n
        v2(lambda e, v=dn: e.wait_ge(sD.h, v))
        v2(lambda e: e.tensor_reduce(grow[0:1, 0:1], grow[:], AX.X, ALU.max))
        v2(lambda e: e.reciprocal(grow[0:1, 1:2], grow[0:1, 0:1]))
        v2(lambda e: e.tensor_scalar(grow[0:1, 1:2], grow[0:1, 1:2],
                                            2688.0, None, ALU.mult))
        v2(lambda e: sV.inc(e.tensor_scalar(
            grow[0:1, 2:3], grow[0:1, 0:1], float(np.float32(1.0 / 2688.0)),
            None, ALU.mult)))
        vv = sV.n
        y2(lambda e, v=vv: e.wait_ge(sV.h, v))
        y2(lambda e, wi=wi: sD.dma(e.dma_start(
            c.scal_s[10 + wi, 0:1], grow[0:1, 1:2])))
        y2(lambda e, wi=wi: sD.dma(e.dma_start(
            c.scal_s[13 + wi, 0:1], grow[0:1, 2:3])))
        dn = sD.n
        y2(lambda e, v=dn: e.wait_ge(sD.h, v))
        vmark = sV.n
    for k in range(7):
        w = 1 << k
        for rr in range(10, 16):
            y2(lambda e, w=w, rr=rr: sD.dma(e.dma_start(
                c.scal_s[rr, w:2 * w], c.scal_s[rr, 0:w])))
        dn = sD.n
        y2(lambda e, v=dn: e.wait_ge(sD.h, v))
    for wi in range(3):
        y2(lambda e, wi=wi: sD.dma(e.dma_start(
            ggs[:, wi:wi + 1], c.scal_s[10 + wi].unsqueeze(1))))
        y2(lambda e, wi=wi: sD.dma(e.dma_start(
            r.scbc[:, 2 + wi:3 + wi], c.scal_s[13 + wi].unsqueeze(1))))
    y1_done = sD.n

    v2(lambda e, v=y1_done: e.wait_ge(sD.h, v))
    nt = 0
    dmark = {}
    for win, wout_s, nrows, wwidth, wi in jobs:
        for cr in range(nrows // 128):
            for cc in range(wwidth // W2):
                par = nt % 2
                if nt >= 2:
                    y2(lambda e, v=dmark[nt - 2][0]: e.wait_ge(sV.h, v))
                y2(lambda e, cr=cr, cc=cc, win=win, par=par: sG.dma(
                    e.dma_start(gq1[par][:],
                                win[cr * 128:(cr + 1) * 128,
                                    cc * W2:(cc + 1) * W2])))
                dn = sG.n
                v2(lambda e, v=dn: e.wait_ge(sG.h, v))
                if nt >= 2:
                    v2(lambda e, v=dmark[nt - 2][1]: e.wait_ge(sG.h, v))
                last = emit_nvfp4(v2, gq1[par][:], gout[par][:],
                                  gqs1[:], gqs2[:], gqs3[:], gqs4[:],
                                  gqtA[:], gqtB[:], ggs[:, wi:wi + 1], eng="v")
                v2(lambda e, last=last: sV.inc(last(e)))
                vv = sV.n
                y2(lambda e, v=vv: e.wait_ge(sV.h, v))
                y2(lambda e, cr=cr, cc=cc, wout_s=wout_s, par=par:
                          sG.dma(e.dma_start(
                              wout_s[cr * 128:(cr + 1) * 128,
                                     cc * W2:(cc + 1) * W2], gout[par][:])))
                dmark[nt] = (vv, sG.n)
                nt += 1
    v2(lambda e: sQ.inc(e.engine_nop()))
    return v2l, y2l
def _emit_b0(nc, c, r, sems):
    sD, sG, sT, sV, sSc, sQ, sO = sems
    ph = contextlib.ExitStack()
    with ph:
        sch = Sched()
        Y = mk_adder(sch.q["y"])
        V = mk_adder(sch.q["v"], track=True)
        wstgA = ph.enter_context(nc.sbuf_tensor("b0_wstgA", [128, DM], F32))
        woh = ph.enter_context(nc.sbuf_tensor("b0_woh", [128, NCH * DM], BF16))
        wol = ph.enter_context(nc.sbuf_tensor("b0_wol", [128, NCH * DM], BF16))
        jstg = ph.enter_context(nc.sbuf_tensor("b0_jstg", [4, S], F32))
        jrows = ph.enter_context(nc.sbuf_tensor("b0_jrows", [4, S], BF16))
        onesrow = ph.enter_context(nc.sbuf_tensor("b0_onesrow", [1, S], BF16))
        sstg = ph.enter_context(nc.sbuf_tensor("b0_sstg", [1, 8], F32))
        upzf = ph.enter_context(nc.sbuf_tensor("b0_upzf", [128, 2 * H], F32))

        Y(lambda e: sD.dma(e.dma_start(r.ident[:], c.ident_in)))
        Y(lambda e: sD.dma(e.dma_start(jstg[:], c.jc_in)))
        Y(lambda e: sD.dma(e.dma_start(r.biases[:, 0:NCH], c.bq_in)))
        Y(lambda e: sD.dma(e.dma_start(r.biases[:, NCH:2 * NCH], c.bk_in)))
        Y(lambda e: sD.dma(e.dma_start(r.biases[:, 2 * NCH:3 * NCH], c.bfq_in)))
        Y(lambda e: sD.dma(e.dma_start(r.biases[:, 3 * NCH:4 * NCH], c.bfg_in)))

        Y(lambda e: sD.dma(e.dma_start(upzf[:], c.upz_in)))
        Y(lambda e: sD.dma(e.dma_start(sstg[0:1, 0:1], c.sin_in)))
        Y(lambda e: sD.dma(e.dma_start(sstg[0:1, 1:2], c.sind_in)))
        d0 = sD.n
        V(lambda e: e.memset(r.sepA[:], 0.0))
        V(lambda e, v=d0: e.wait_ge(sD.h, v))
        V(lambda e: e.tensor_copy(r.identb[:], r.ident[:]))
        V(lambda e: e.memset(r.ones128b[:], 1.0))
        V(lambda e: e.memset(r.ones128r[:], 1.0))
        V(lambda e: e.tensor_copy(jrows[:], jstg[:]))
        V(lambda e: e.memset(onesrow[:], 1.0))
        V(lambda e: e.tensor_copy(r.upzh[:], upzf[:]))
        V(lambda e: e.tensor_tensor(r.upzl[:], upzf[:], r.upzh[:], ALU.subtract))
        V(lambda e: e.memset(r.kmax2[:], 0.0))
        V(lambda e: e.memset(r.xbar[:], 0.0))
        V(lambda e: e.reciprocal(sstg[0:1, 2:3], sstg[0:1, 0:1]))
        V(lambda e: sV.inc(e.reciprocal(sstg[0:1, 3:4], sstg[0:1, 1:2])))
        v0 = sV.n
        Y(lambda e, v=v0: e.wait_ge(sV.h, v))
        Y(lambda e: sD.dma(e.dma_start(c.scal_s[0, 0:1], sstg[0:1, 2:3])))
        Y(lambda e: sD.dma(e.dma_start(c.scal_s[1, 0:1], sstg[0:1, 3:4])))
        Y(lambda e: sD.dma(e.dma_start(c.scal_s[8, 0:1], sstg[0:1, 0:1])))
        Y(lambda e: sD.dma(e.dma_start(c.scal_s[9, 0:1], sstg[0:1, 1:2])))
        dn = sD.n
        Y(lambda e, v=dn: e.wait_ge(sD.h, v))
        for k in range(7):
            w = 1 << k
            for row in (0, 1, 8, 9):
                Y(lambda e, w=w, row=row: sD.dma(e.dma_start(
                    c.scal_s[row, w:2 * w], c.scal_s[row, 0:w])))
            dn = sD.n
            Y(lambda e, v=dn: e.wait_ge(sD.h, v))
        Y(lambda e: sD.dma(e.dma_start(r.scbc[:, 0:1], c.scal_s[0].unsqueeze(1))))
        Y(lambda e: sD.dma(e.dma_start(r.scbc[:, 1:2], c.scal_s[1].unsqueeze(1))))
        Y(lambda e: sD.dma(e.dma_start(r.scbc[:, 5:6], c.scal_s[8].unsqueeze(1))))
        Y(lambda e: sD.dma(e.dma_start(r.scbc[:, 6:7], c.scal_s[9].unsqueeze(1))))

        vprev = 0
        for ch in range(NCH):
            if ch >= 1:
                Y(lambda e, v=vprev: e.wait_ge(sV.h, v))
            Y(lambda e, ch=ch: sD.dma(e.dma_start(
                wstgA[:], c.wo_in[ch * 128:(ch + 1) * 128, :])))
            dn = sD.n
            V(lambda e, v=dn: e.wait_ge(sD.h, v))
            V(lambda e, ch=ch: e.tensor_copy(woh[:, ch * DM:(ch + 1) * DM], wstgA[:]))
            V(lambda e, ch=ch: sV.inc(e.tensor_tensor(
                wol[:, ch * DM:(ch + 1) * DM], wstgA[:],
                woh[:, ch * DM:(ch + 1) * DM], ALU.subtract)))
            vprev = sV.n
        V(lambda e: sV.inc(e.memset(wstgA[0:1, :], 0.0)))
        vz = sV.n
        Y(lambda e, v=vz: e.wait_ge(sV.h, v))
        for h in range(H):
            ch, r0 = divmod(h * D, 128)
            hi_sl = woh[r0:r0 + 64, ch * DM:(ch + 1) * DM]
            lo_sl = wol[r0:r0 + 64, ch * DM:(ch + 1) * DM]
            Y(lambda e, h=h, s_=hi_sl: sD.dma(e.dma_start(c.woar1_s[h, 0:64], s_)))
            Y(lambda e, h=h, s_=hi_sl: sD.dma(e.dma_start(c.woar1_s[h, 64:128], s_)))
            Y(lambda e, h=h, s_=lo_sl: sD.dma(e.dma_start(c.woar2_s[h, 0:64], s_)))
            Y(lambda e, h=h: sD.dma(e.dma_start(
                c.woar2_s[h, 64:65], wstgA[0:1, 0:DM // 2].bitcast(BF16))))
            Y(lambda e, h=h, s_=hi_sl: sD.dma(
                e.dma_start(c.woar2_s[h, 65:128], s_[0:63, :])))
        for ri, si in enumerate([0, 1, 0, 2, 3, 2]):
            Y(lambda e, ri=ri, si=si: sD.dma(e.dma_start(
                c.jarr_s[ri], jrows[si:si + 1, :])))
        for ri in (6, 7):
            Y(lambda e, ri=ri: sD.dma(e.dma_start(c.jarr_s[ri], onesrow[:])))
        dn = sD.n
        Y(lambda e, v=dn: e.wait_ge(sD.h, v))
        sch.run(nc)


def _emit_b1(nc, c, r, sems):
    sD, sG, sT, sV, sSc, sQ, sO = sems
    ph = contextlib.ExitStack()
    with ph:
        sch = Sched()
        Y = mk_adder(sch.q["y"])
        V = mk_adder(sch.q["v"], track=True)
        T = mk_adder(sch.q["t"])
        SC = mk_adder(sch.q["s"])

        def sbP(name, shape, dtyp=F32):
            return ph.enter_context(nc.sbuf_tensor(name, shape, dtyp))

        bvbc = sbP("b1_bvbc", [128, DM])
        wkh = sbP("wkh", [128, NCH * DM], BF16)
        wkl = sbP("wkl", [128, NCH * DM], BF16)
        wvh = sbP("wvh", [128, NCH * DM], BF16)
        wvl = sbP("wvl", [128, NCH * DM], BF16)
        wstg = sbP("b1_wstg", [128, DM])
        xst = [sbP(f"xst{i}", [128, DM]) for i in range(2)]
        sqt = [sbP(f"sqt{i}", [128, DM]) for i in range(2)]
        lnst = sbP("lnst", [128, 8])
        xnt = [sbP(f"xnt{i}", [128, DM]) for i in range(2)]
        xqo = sbP("xqo", [128, DM], BF16)
        qsA = sbP("qsA", [128, 64])
        qsB = sbP("qsB", [128, 64])
        qsC = sbP("qsC", [128, 64])
        qsD = sbP("qsD", [128, 64])
        qtA = sbP("qtA", [128, DM])
        qtB = sbP("qtB", [128, DM])
        xhi = [sbP(f"xhi{i}", [128, 128], BF16) for i in range(2)]
        xlo = [sbP(f"xlo{i}", [128, 128], BF16) for i in range(2)]
        xnfh = sbP("xnfh", [128, NCH * 512], BF16)
        xnfl = sbP("xnfl", [128, NCH * 512], BF16)
        kevt = sbP("kevt", [128, 512])
        kevh = [sbP(f"kevh{i}", [128, 512], BF16) for i in range(2)]
        kevl = [sbP(f"kevl{i}", [128, 512], BF16) for i in range(2)]
        ksq = sbP("ksq", [128, 512], BF16)
        k2st = sbP("k2st", [1, 4])
        xqstg = [sbP(f"xqstg{i}", [128, 128], BF16) for i in range(2)]
        vevh = [sbP(f"vevh{i}", [128, 512], BF16) for i in range(2)]
        vevl = [sbP(f"vevl{i}", [128, 512], BF16) for i in range(2)]
        vstg = [sbP(f"vstg{i}", [128, 8 * 128], BF16) for i in range(2)]
        xbsum = sbP("xbsum", [128, 512])
        ptr_ = ph.enter_context(nc.psum_tensor("ptr_", [128, 128], F32))
        pk = ph.enter_context(nc.psum_tensor("pk", [128, 512], F32))
        pvp = ph.enter_context(nc.psum_tensor("pvp", [128, 512], F32))
        pk2a = ph.enter_context(nc.psum_tensor("pk2a", [1, 512], F32))
        pk2b = ph.enter_context(nc.psum_tensor("pk2b", [1, 512], F32))
        pxq = ph.enter_context(nc.psum_tensor("pxq", [128, 128], BF16))

        Y(lambda e: sD.dma(e.dma_start(bvbc[:], c.bv_bc_in)))
        vprev = 0
        for wi, (win, whi, wlo) in enumerate([(c.wk_in, wkh, wkl),
                                              (c.wv_in, wvh, wvl)]):
            for ch in range(NCH):
                if wi + ch > 0:
                    Y(lambda e, v=vprev: e.wait_ge(sV.h, v))
                Y(lambda e, win=win, ch=ch: sD.dma(
                    e.dma_start(wstg[:], win[ch * 128:(ch + 1) * 128, :])))
                dn = sD.n
                V(lambda e, v=dn: e.wait_ge(sD.h, v))
                V(lambda e, whi=whi, ch=ch: e.tensor_copy(
                    whi[:, ch * DM:(ch + 1) * DM], wstg[:]))
                V(lambda e, whi=whi, wlo=wlo, ch=ch: sV.inc(e.tensor_tensor(
                    wlo[:, ch * DM:(ch + 1) * DM], wstg[:],
                    whi[:, ch * DM:(ch + 1) * DM], ALU.subtract)))
                vprev = sV.n
        for i in range(2):
            V(lambda e, i=i: e.memset(vstg[i][:], 0.0))
            for hh in range(8):
                V(lambda e, i=i, hh=hh: e.memset(
                    vstg[i][:, hh * 128 + 64:hh * 128 + 65], 1.0))
        V(lambda e: sV.inc(e.memset(lnst[:], 0.0)))

        last_kv_mm = sT.n
        vln = {}
        dma_mark = {}
        vq = 0
        vxq = 0
        v_xqev = 0
        v_ptr = 0
        for blk in range(NB):
            V(lambda e, v=last_kv_mm: e.wait_ge(sT.h, v))
            for tsub in range(4):
                idx = blk * 4 + tsub
                tok0 = blk * 512 + tsub * 128
                par = idx % 2
                xt, sq, xn_ = xst[par], sqt[par], xnt[par]
                if idx >= 2:
                    Y(lambda e, v=vln.get(idx - 2, 0): e.wait_ge(sV.h, v))
                Y(lambda e, xt=xt, tok0=tok0: sD.dma(
                    e.dma_start(xt[:], c.x_in[tok0:tok0 + 128, :])))
                dn = sD.n
                SC(lambda e, v=dn: e.wait_ge(sD.h, v))
                if idx >= 2:
                    SC(lambda e, v=vln.get(idx - 2, 0): e.wait_ge(sV.h, v))
                SC(lambda e, xt=xt, sq=sq: sSc.inc(
                    e.activation(sq[:], xt[:], ACT.Square)))
                ssq = sSc.n
                V(lambda e, v=dn: e.wait_ge(sD.h, v))
                V(lambda e, v=ssq: e.wait_ge(sSc.h, v))
                V(lambda e, xt=xt: e.reduce_sum(lnst[:, 0:1], xt[:], AX.X))
                V(lambda e, sq=sq: e.reduce_sum(lnst[:, 1:2], sq[:], AX.X))
                V(lambda e: e.tensor_scalar(lnst[:, 2:3], lnst[:, 0:1],
                                            1.0 / DM, None, ALU.mult))
                V(lambda e: e.tensor_scalar(lnst[:, 3:4], lnst[:, 1:2],
                                            1.0 / DM, None, ALU.mult))
                V(lambda e: e.tensor_tensor(lnst[:, 4:5], lnst[:, 2:3],
                                            lnst[:, 2:3], ALU.mult))
                V(lambda e: e.tensor_tensor(lnst[:, 5:6], lnst[:, 3:4],
                                             lnst[:, 4:5], ALU.subtract))
                V(lambda e: sV.inc(e.tensor_scalar(lnst[:, 5:6], lnst[:, 5:6],
                                                   EPS, None, ALU.add)))
                vv = sV.n
                SC(lambda e, v=vv: e.wait_ge(sV.h, v))
                SC(lambda e: sSc.inc(e.activation(lnst[:, 6:7], lnst[:, 5:6],
                                                  ACT.Sqrt)))
                srt = sSc.n
                V(lambda e, v=srt: e.wait_ge(sSc.h, v))
                V(lambda e: e.reciprocal(lnst[:, 7:8], lnst[:, 6:7]))
                V(lambda e, xt=xt, xn_=xn_: sV.inc(e.tensor_scalar(
                    xn_[:], xt[:], lnst[:, 2:3], lnst[:, 7:8],
                    ALU.subtract, ALU.mult)))
                vln[idx] = sV.n
                if blk < 4:
                    V(lambda e, v=vln[idx]: e.wait_ge(sV.h, v))
                    if vxq:
                        V(lambda e, v=vxq: e.wait_ge(sT.h, v))
                    last = emit_nvfp4(V, xn_[:], xqo[:], qsA[:], qsB[:], qsC[:],
                                      qsD[:], qtA[:], qtB[:], r.scbc[:, 5:6])
                    V(lambda e, last=last: sV.inc(last(e)))
                    vq = sV.n
                    for ch in range(NCH):
                        nxq = idx * NCH + ch
                        spar = nxq % 2
                        T(lambda e, v=vq: e.wait_ge(sV.h, v))
                        T(lambda e, v=v_xqev: e.wait_ge(sV.h, v))
                        T(lambda e, ch=ch: sT.inc(e.transpose(
                            pxq[:], xqo[:, ch * 128:ch * 128 + 128],
                            r.identb[:])))
                        tt = sT.n
                        col = ch * TOWN + blk * 512 + tsub * 128
                        V(lambda e, v=tt: e.wait_ge(sT.h, v))
                        if nxq >= 2:
                            V(lambda e, v=dma_mark.get(("xq", nxq - 2), 0):
                              e.wait_ge(sD.h, v))
                        V(lambda e, spar=spar: sV.inc(e.tensor_copy(
                            xqstg[spar][:], pxq[:])))
                        v_xqev = sV.n
                        Y(lambda e, v=v_xqev: e.wait_ge(sV.h, v))
                        Y(lambda e, col=col, spar=spar: sD.dma(e.dma_start(
                            c.xqf_s[:, col:col + 128], xqstg[spar][:])))
                        dma_mark[("xq", nxq)] = sD.n
                    vxq = sT.n
                    vln[idx] = sV.n
                vlni = vln[idx]
                for ch in range(NCH):
                    n_tr = idx * NCH + ch
                    cpar = n_tr % 2
                    T(lambda e, v=vlni: e.wait_ge(sV.h, v))
                    if n_tr >= 1:
                        T(lambda e, v=v_ptr: e.wait_ge(sV.h, v))
                    T(lambda e, ch=ch, xn_=xn_: sT.inc(e.transpose(
                        ptr_[:], xn_[:, ch * 128:(ch + 1) * 128], r.ident[:])))
                    ttr = sT.n
                    V(lambda e, v=ttr: e.wait_ge(sT.h, v))
                    if n_tr >= 2:
                        V(lambda e, v=dma_mark.get(("x", n_tr - 2), 0):
                          e.wait_ge(sD.h, v))
                    V(lambda e, cpar=cpar: e.tensor_copy(xhi[cpar][:], ptr_[:]))
                    V(lambda e, cpar=cpar: sV.inc(e.tensor_tensor(
                        xlo[cpar][:], ptr_[:], xhi[cpar][:], ALU.subtract)))
                    v_ptr = sV.n
                    col = ch * 512 + tsub * 128
                    V(lambda e, col=col, cpar=cpar: e.tensor_copy(
                        xnfh[:, col:col + 128], xhi[cpar][:]))
                    V(lambda e, col=col, cpar=cpar: sV.inc(e.tensor_copy(
                        xnfl[:, col:col + 128], xlo[cpar][:])))
                    vst2 = sV.n
                    Y(lambda e, v=vst2: e.wait_ge(sV.h, v))
                    Y(lambda e, ch=ch, tok0=tok0, cpar=cpar: sD.dma(e.dma_start(
                        c.xnh_s[ch, :, tok0:tok0 + 128], xhi[cpar][:])))
                    Y(lambda e, ch=ch, tok0=tok0, cpar=cpar: sD.dma(e.dma_start(
                        c.xnl_s[ch, :, tok0:tok0 + 128], xlo[cpar][:])))
                    dma_mark[("x", n_tr)] = sD.n
            v_stageA = sV.n
            for ch in range(NCH):
                V(lambda e, ch=ch: e.tensor_tensor(
                    xbsum[:], xnfh[:, ch * 512:(ch + 1) * 512],
                    xnfl[:, ch * 512:(ch + 1) * 512], ALU.add))
                V(lambda e: e.reduce_sum(lnst[:, 0:1], xbsum[:], AX.X))
                V(lambda e, ch=ch: sV.inc(e.tensor_tensor(
                    r.xbar[:, ch:ch + 1], r.xbar[:, ch:ch + 1],
                    lnst[:, 0:1], ALU.add)))
            v_xnf = sV.n

            v_kev = sV.n
            for m in range(NCH):
                kpar = m % 2
                T(lambda e, v=v_xnf: e.wait_ge(sV.h, v))
                T(lambda e, v=v_kev: e.wait_ge(sV.h, v))
                for ch in range(NCH):
                    w0 = ch * DM + m * 128
                    x0 = ch * 512
                    T(lambda e, w0=w0, x0=x0, st=(ch == 0): e.matmul(
                        pk[:], wkh[:, w0:w0 + 128], xnfh[:, x0:x0 + 512],
                        start=st, stop=False))
                    T(lambda e, w0=w0, x0=x0: e.matmul(
                        pk[:], wkl[:, w0:w0 + 128], xnfh[:, x0:x0 + 512],
                        start=False, stop=False))
                    if ch == NCH - 1:
                        T(lambda e, w0=w0, x0=x0: sT.inc(e.matmul(
                            pk[:], wkh[:, w0:w0 + 128], xnfl[:, x0:x0 + 512],
                            start=False, stop=True)))
                    else:
                        T(lambda e, w0=w0, x0=x0: e.matmul(
                            pk[:], wkh[:, w0:w0 + 128], xnfl[:, x0:x0 + 512],
                            start=False, stop=False))
                tk = sT.n
                V(lambda e, v=tk: e.wait_ge(sT.h, v))
                if blk * NCH + m >= 2:
                    V(lambda e, v=dma_mark.get(("k", blk * NCH + m - 2), 0):
                      e.wait_ge(sD.h, v))
                V(lambda e, m=m: e.tensor_scalar(
                    kevt[:], pk[:], r.biases[:, NCH + m:NCH + m + 1], None, ALU.add))
                V(lambda e, kpar=kpar: e.tensor_copy(kevh[kpar][:], kevt[:]))
                V(lambda e, kpar=kpar: e.tensor_tensor(
                    kevl[kpar][:], kevt[:], kevh[kpar][:], ALU.subtract))
                V(lambda e: e.tensor_tensor(kevt[:], kevt[:], kevt[:], ALU.mult))
                V(lambda e: sV.inc(e.tensor_copy(ksq[:], kevt[:])))
                v_kev = sV.n
                T(lambda e, v=v_kev: e.wait_ge(sV.h, v))
                T(lambda e: sT.inc(e.matmul(pk2a[:], r.ones128b[0:64, 0:1],
                                            ksq[0:64, :], start=True, stop=True)))
                T(lambda e: sT.inc(e.matmul(pk2b[:], r.ones128b[64:128, 0:1],
                                            ksq[64:128, :], start=True, stop=True)))
                tk2 = sT.n
                V(lambda e, v=tk2: e.wait_ge(sT.h, v))
                h0, h1 = 2 * m, 2 * m + 1
                V(lambda e: e.tensor_reduce(k2st[0:1, 0:1], pk2a[:], AX.X, ALU.max))
                V(lambda e: e.tensor_reduce(k2st[0:1, 1:2], pk2b[:], AX.X, ALU.max))
                V(lambda e, h0=h0: e.tensor_tensor(
                    r.kmax2[0:1, h0:h0 + 1], r.kmax2[0:1, h0:h0 + 1],
                    k2st[0:1, 0:1], ALU.max))
                V(lambda e, h1=h1: sV.inc(e.tensor_tensor(
                    r.kmax2[0:1, h1:h1 + 1], r.kmax2[0:1, h1:h1 + 1],
                    k2st[0:1, 1:2], ALU.max)))
                v_kev = sV.n
                t0 = blk * 512
                Y(lambda e, v=v_kev: e.wait_ge(sV.h, v))
                Y(lambda e, h0=h0, t0=t0, kpar=kpar: sD.dma(e.dma_start(
                    c.karr_s[h0, 0:64, t0:t0 + 512], kevh[kpar][0:64, :])))
                Y(lambda e, h0=h0, t0=t0, kpar=kpar: sD.dma(e.dma_start(
                    c.karr_s[h0, 64:128, t0:t0 + 512], kevl[kpar][0:64, :])))
                Y(lambda e, h1=h1, t0=t0, kpar=kpar: sD.dma(e.dma_start(
                    c.karr_s[h1, 0:64, t0:t0 + 512], kevh[kpar][64:128, :])))
                Y(lambda e, h1=h1, t0=t0, kpar=kpar: sD.dma(e.dma_start(
                    c.karr_s[h1, 64:128, t0:t0 + 512], kevl[kpar][64:128, :])))
                dma_mark[("k", blk * NCH + m)] = sD.n

            v_vev = sV.n
            for tsub in range(4):
                t0 = blk * 512 + tsub * 128
                for ot in range(2):
                    rnd = (blk * 4 + tsub) * 2 + ot
                    vpar = rnd % 2
                    T(lambda e, v=v_vev: e.wait_ge(sV.h, v))
                    for ch in range(NCH):
                        x0 = ch * 512 + tsub * 128
                        w0 = ch * DM + ot * 512
                        T(lambda e, x0=x0, w0=w0, st=(ch == 0): e.matmul(
                            pvp[:], xnfh[:, x0:x0 + 128], wvh[:, w0:w0 + 512],
                            start=st, stop=False))
                        T(lambda e, x0=x0, w0=w0: e.matmul(
                            pvp[:], xnfl[:, x0:x0 + 128], wvh[:, w0:w0 + 512],
                            start=False, stop=False))
                        if ch == NCH - 1:
                            T(lambda e, x0=x0, w0=w0: sT.inc(e.matmul(
                                pvp[:], xnfh[:, x0:x0 + 128], wvl[:, w0:w0 + 512],
                                start=False, stop=True)))
                        else:
                            T(lambda e, x0=x0, w0=w0: e.matmul(
                                pvp[:], xnfh[:, x0:x0 + 128], wvl[:, w0:w0 + 512],
                                start=False, stop=False))
                    tv = sT.n
                    V(lambda e, v=tv: e.wait_ge(sT.h, v))
                    if rnd >= 2:
                        V(lambda e, v=dma_mark.get(("v", rnd - 2), 0):
                          e.wait_ge(sD.h, v))
                    V(lambda e, ot=ot: e.tensor_tensor(
                        kevt[:], pvp[:], bvbc[:, ot * 512:ot * 512 + 512], ALU.add))
                    V(lambda e, vpar=vpar: e.tensor_copy(vevh[vpar][:], kevt[:]))
                    V(lambda e, vpar=vpar: sV.inc(e.tensor_tensor(
                        vevl[vpar][:], kevt[:], vevh[vpar][:], ALU.subtract)))
                    for hh in range(8):
                        V(lambda e, hh=hh, vpar=vpar: e.tensor_copy(
                            vstg[vpar][:, hh * 128:hh * 128 + 64],
                            vevh[vpar][:, hh * 64:hh * 64 + 64]))
                        V(lambda e, hh=hh, vpar=vpar: sV.inc(e.tensor_copy(
                            vstg[vpar][:, hh * 128 + 65:hh * 128 + 128],
                            vevl[vpar][:, hh * 64:hh * 64 + 63])))
                    v_vev = sV.n
                    Y(lambda e, v=v_vev: e.wait_ge(sV.h, v))
                    for hh in range(8):
                        h = ot * 8 + hh
                        Y(lambda e, h=h, hh=hh, t0=t0, vpar=vpar: sD.dma(e.dma_start(
                            c.varr_s[h, t0:t0 + 128, :],
                            vstg[vpar][:, hh * 128:hh * 128 + 128])))
                    dma_mark[("v", rnd)] = sD.n

            last_kv_mm = sT.n
        sch.run(nc)
def _emit_b2(nc, c, r, sems, wq_builder):
    sD, sG, sT, sV, sSc, sQ, sO = sems
    ph = contextlib.ExitStack()
    with ph:
        sch = Sched()
        Y = mk_adder(sch.q["y"])
        V = mk_adder(sch.q["v"], track=True)
        T = mk_adder(sch.q["t"])
        SC = mk_adder(sch.q["s"])

        def sbP(name, shape, dtyp=F32):
            return ph.enter_context(nc.sbuf_tensor(name, shape, dtyp))

        wqh = sbP("wqh", [128, NCH * DM], BF16)
        wql = sbP("wql", [128, NCH * DM], BF16)
        wfqh = sbP("wfqh", [128, NCH * DM], BF16)
        wfql = sbP("wfql", [128, NCH * DM], BF16)
        wstg = sbP("b2_wstg", [128, DM])
        wfgh = sbP("wfgh", [128, DM], BF16)
        wfgl = sbP("wfgl", [128, DM], BF16)
        xrh = [sbP(f"xrh{i}", [128, 512], BF16) for i in range(2)]
        xrl = [sbP(f"xrl{i}", [128, 512], BF16) for i in range(2)]
        qevt = sbP("qevt", [128, 512])
        qhi = [sbP(f"qhi{i}", [128, 512], BF16) for i in range(2)]
        qlo = [sbP(f"qlo{i}", [128, 512], BF16) for i in range(2)]
        qsqt = sbP("qsqt", [128, 512], BF16)
        xbh = sbP("xbh", [128, NCH], BF16)
        xbl = sbP("xbl", [128, NCH], BF16)
        glof = sbP("glof", [128, NCH])
        gloh = sbP("gloh", [128, NCH], BF16)
        glol = sbP("glol", [128, NCH], BF16)
        bpp = sbP("bpp", [128, NCH])
        ctf = sbP("ctf", [128, 512])
        cth = sbP("cth", [128, 512], BF16)
        ctl = sbP("ctl", [128, 512], BF16)
        przA = sbP("przA", [1, 512])
        przB = sbP("przB", [1, 512])
        przC = sbP("przC", [1, 512])
        featq = sbP("featq", [1, 3 * 1024])
        featw = sbP("featw", [1, 6 * 1024])
        featb = sbP("featb", [1, 8 * 1024], BF16)
        kmaxr = sbP("kmaxr", [1, H])
        pq = ph.enter_context(nc.psum_tensor("pq", [128, 512], F32))
        pcp = ph.enter_context(nc.psum_tensor("pcp", [128, 512], F32))
        pbp = ph.enter_context(nc.psum_tensor("pbp", [128, NCH], F32))
        ppq = ph.enter_context(nc.psum_tensor("ppq", [1, 512], F32))
        pzq = ph.enter_context(nc.psum_tensor("pzq", [1, 512], F32))
        pq2a = ph.enter_context(nc.psum_tensor("pq2a", [1, 512], F32))
        pq2b = ph.enter_context(nc.psum_tensor("pq2b", [1, 512], F32))

        vprev = 0
        for wi, (win, whi, wlo) in enumerate([(c.wq_in, wqh, wql),
                                              (c.wfq_in, wfqh, wfql)]):
            for ch in range(NCH):
                if wi + ch > 0:
                    Y(lambda e, v=vprev: e.wait_ge(sV.h, v))
                Y(lambda e, win=win, ch=ch: sD.dma(
                    e.dma_start(wstg[:], win[ch * 128:(ch + 1) * 128, :])))
                dn = sD.n
                V(lambda e, v=dn: e.wait_ge(sD.h, v))
                if wi == 1:
                    V(lambda e: e.tensor_scalar(wstg[:], wstg[:], 8.0, None,
                                                ALU.mult))
                V(lambda e, whi=whi, ch=ch: e.tensor_copy(
                    whi[:, ch * DM:(ch + 1) * DM], wstg[:]))
                V(lambda e, whi=whi, wlo=wlo, ch=ch: sV.inc(e.tensor_tensor(
                    wlo[:, ch * DM:(ch + 1) * DM], wstg[:],
                    whi[:, ch * DM:(ch + 1) * DM], ALU.subtract)))
                vprev = sV.n

        V(lambda e: e.tensor_scalar(r.xbar[:], r.xbar[:], 1.0 / S, None, ALU.mult))
        V(lambda e: e.tensor_copy(xbh[:], r.xbar[:]))
        V(lambda e: sV.inc(e.tensor_tensor(xbl[:], r.xbar[:], xbh[:], ALU.subtract)))
        v_xb = sV.n

        T(lambda e, v=v_xb: e.wait_ge(sV.h, v))
        for m in range(NCH):
            for ch in range(NCH):
                w0 = ch * DM + m * 128
                T(lambda e, w0=w0, ch=ch, m=m, st=(m == 0 and ch == 0): e.matmul(
                    pbp[:, m:m + 1],
                    wqh[:, w0:w0 + 128], xbh[:, ch:ch + 1],
                    start=st, stop=False))
                T(lambda e, w0=w0, ch=ch, m=m: e.matmul(
                    pbp[:, m:m + 1], wql[:, w0:w0 + 128], xbh[:, ch:ch + 1],
                    start=False, stop=False))
                T(lambda e, w0=w0, ch=ch, m=m: (sT.inc(e.matmul(
                    pbp[:, m:m + 1], wqh[:, w0:w0 + 128], xbl[:, ch:ch + 1],
                    start=False, stop=True))
                    if (ch == NCH - 1 and m == NCH - 1) else e.matmul(
                    pbp[:, m:m + 1], wqh[:, w0:w0 + 128], xbl[:, ch:ch + 1],
                    start=False, stop=False)))
        t_glo = sT.n
        V(lambda e, v=t_glo: e.wait_ge(sT.h, v))
        V(lambda e: e.tensor_tensor(glof[:], pbp[:], r.biases[:, 0:NCH], ALU.add))
        V(lambda e: e.tensor_copy(gloh[:], glof[:]))
        V(lambda e: sV.inc(e.tensor_tensor(glol[:], glof[:], gloh[:], ALU.subtract)))
        v_glo = sV.n

        v_fgs = v_glo
        for ch in range(NCH):
            Y(lambda e, v=v_fgs: e.wait_ge(sV.h, v))
            Y(lambda e, ch=ch: sD.dma(e.dma_start(
                wstg[:], c.wfg_in[ch * 128:(ch + 1) * 128, :])))
            dn = sD.n
            V(lambda e, v=dn: e.wait_ge(sD.h, v))
            V(lambda e: e.tensor_copy(wfgh[:], wstg[:]))
            V(lambda e: sV.inc(e.tensor_tensor(wfgl[:], wstg[:], wfgh[:],
                                               ALU.subtract)))
            v_fgs = sV.n
            T(lambda e, v=v_fgs: e.wait_ge(sV.h, v))
            if ch == 0:
                T(lambda e, v=v_glo: e.wait_ge(sV.h, v))
            for m in range(NCH):
                w0 = m * 128
                T(lambda e, w0=w0, ch=ch, m=m, st=(ch == 0 and m == 0): e.matmul(
                    pq[:, m:m + 1], wfgh[:, w0:w0 + 128], gloh[:, ch:ch + 1],
                    start=st, stop=False))
                T(lambda e, w0=w0, ch=ch, m=m: e.matmul(
                    pq[:, m:m + 1], wfgl[:, w0:w0 + 128], gloh[:, ch:ch + 1],
                    start=False, stop=False))
                T(lambda e, w0=w0, ch=ch, m=m: (sT.inc(e.matmul(
                    pq[:, m:m + 1], wfgh[:, w0:w0 + 128], glol[:, ch:ch + 1],
                    start=False, stop=True))
                    if (ch == NCH - 1 and m == NCH - 1) else e.matmul(
                    pq[:, m:m + 1], wfgh[:, w0:w0 + 128], glol[:, ch:ch + 1],
                    start=False, stop=False)))
        t_bpp = sT.n
        V(lambda e, v=t_bpp: e.wait_ge(sT.h, v))
        V(lambda e: e.tensor_tensor(bpp[:], pq[:, 0:NCH], r.biases[:, 3 * NCH:4 * NCH],
                                    ALU.add))
        V(lambda e: sV.inc(e.tensor_tensor(bpp[:], bpp[:],
                                           r.biases[:, 2 * NCH:3 * NCH], ALU.add)))
        v_bpp = sV.n

        dma_q = {}
        qk_mm = {}
        v_qev = sV.n
        for qch in range(4):
            q0 = qch * 512
            for m in range(NCH):
                rnd = qch * NCH + m
                for ch in range(NCH):
                    par = ch % 2
                    nld = rnd * NCH + ch
                    if nld >= 2:
                        Y(lambda e, v=qk_mm.get(nld - 2, 0): e.wait_ge(sT.h, v))
                    Y(lambda e, ch=ch, q0=q0, par=par: sD.dma(e.dma_start(
                        xrh[par][:], c.xnh_s[ch, :, q0:q0 + 512])))
                    Y(lambda e, ch=ch, q0=q0, par=par: sD.dma(e.dma_start(
                        xrl[par][:], c.xnl_s[ch, :, q0:q0 + 512])))
                    dn = sD.n
                    w0 = ch * DM + m * 128
                    T(lambda e, v=dn: e.wait_ge(sD.h, v))
                    if ch == 0:
                        T(lambda e, v=v_qev: e.wait_ge(sV.h, v))
                    T(lambda e, w0=w0, par=par, st=(ch == 0): e.matmul(
                        pq[:], wqh[:, w0:w0 + 128], xrh[par][:],
                        start=st, stop=False))
                    T(lambda e, w0=w0, par=par: e.matmul(
                        pq[:], wql[:, w0:w0 + 128], xrh[par][:],
                        start=False, stop=False))
                    T(lambda e, w0=w0, par=par, sp=(ch == NCH - 1): sT.inc(
                        e.matmul(pq[:], wqh[:, w0:w0 + 128], xrl[par][:],
                                 start=False, stop=sp)))
                    qk_mm[nld] = sT.n
                tq = sT.n
                qpar = rnd % 2
                V(lambda e, v=tq: e.wait_ge(sT.h, v))
                if rnd >= 2:
                    V(lambda e, v=dma_q.get(rnd - 2, 0): e.wait_ge(sD.h, v))
                V(lambda e, m=m: e.tensor_scalar(
                    qevt[:], pq[:], r.biases[:, m:m + 1], 0.125, ALU.add, ALU.mult))
                V(lambda e, qpar=qpar: e.tensor_copy(qhi[qpar][:], qevt[:]))
                V(lambda e, qpar=qpar: e.tensor_tensor(
                    qlo[qpar][:], qevt[:], qhi[qpar][:], ALU.subtract))
                V(lambda e: e.tensor_tensor(qevt[:], qevt[:], qevt[:], ALU.mult))
                V(lambda e: sV.inc(e.tensor_copy(qsqt[:], qevt[:])))
                v_qev = sV.n
                T(lambda e, v=v_qev: e.wait_ge(sV.h, v))
                T(lambda e: sT.inc(e.matmul(pq2a[:], r.ones128b[0:64, 0:1],
                                            qsqt[0:64, :], start=True, stop=True)))
                T(lambda e: sT.inc(e.matmul(pq2b[:], r.ones128b[64:128, 0:1],
                                            qsqt[64:128, :], start=True, stop=True)))
                tq2 = sT.n
                h0, h1 = 2 * m, 2 * m + 1
                V(lambda e, v=tq2: e.wait_ge(sT.h, v))
                V(lambda e, v=dma_q.get(rnd - 1, 0): e.wait_ge(sD.h, v))
                V(lambda e: e.tensor_copy(przA[:], pq2a[:]))
                V(lambda e: sV.inc(e.tensor_copy(przB[:], pq2b[:])))
                v_qev = sV.n
                Y(lambda e, v=v_qev: e.wait_ge(sV.h, v))
                Y(lambda e, h0=h0, q0=q0: sD.dma(e.dma_start(
                    c.q2n_s[h0, q0:q0 + 512], przA[:])))
                Y(lambda e, h1=h1, q0=q0: sD.dma(e.dma_start(
                    c.q2n_s[h1, q0:q0 + 512], przB[:])))
                Y(lambda e, m=m, q0=q0, qpar=qpar: sD.dma(e.dma_start(
                    c.mqh_s[m, :, q0:q0 + 512], qhi[qpar][:])))
                Y(lambda e, m=m, q0=q0, qpar=qpar: sD.dma(e.dma_start(
                    c.mql_s[m, :, q0:q0 + 512], qlo[qpar][:])))
                dma_q[rnd] = sD.n

        dma_c = {}
        c_mm = {}
        v_cev = sSc.n
        v_ctf = sV.n
        v_pzr = sSc.n
        for qch in range(4):
            q0 = qch * 512
            for m in range(NCH):
                rnd = qch * NCH + m
                for ch in range(NCH):
                    par = ch % 2
                    nld = rnd * NCH + ch
                    if nld >= 2:
                        Y(lambda e, v=c_mm.get(nld - 2, 0): e.wait_ge(sT.h, v))
                    Y(lambda e, ch=ch, q0=q0, par=par: sD.dma(e.dma_start(
                        xrh[par][:], c.mqh_s[ch, :, q0:q0 + 512])))
                    Y(lambda e, ch=ch, q0=q0, par=par: sD.dma(e.dma_start(
                        xrl[par][:], c.mql_s[ch, :, q0:q0 + 512])))
                    dn = sD.n
                    w0 = ch * DM + m * 128
                    T(lambda e, v=dn: e.wait_ge(sD.h, v))
                    if ch == 0:
                        T(lambda e, v=v_cev: e.wait_ge(sSc.h, v))
                        T(lambda e, v=v_pzr: e.wait_ge(sSc.h, v))
                    T(lambda e, w0=w0, par=par, st=(ch == 0): e.matmul(
                        pcp[:], wfqh[:, w0:w0 + 128], xrh[par][:],
                        start=st, stop=False))
                    T(lambda e, w0=w0, par=par: e.matmul(
                        pcp[:], wfql[:, w0:w0 + 128], xrh[par][:],
                        start=False, stop=False))
                    T(lambda e, w0=w0, par=par, sp=(ch == NCH - 1): sT.inc(
                        e.matmul(pcp[:], wfqh[:, w0:w0 + 128], xrl[par][:],
                                 start=False, stop=sp)))
                    c_mm[nld] = sT.n
                tc = sT.n
                SC(lambda e, v=tc: e.wait_ge(sT.h, v))
                SC(lambda e, v=v_ctf: e.wait_ge(sV.h, v))
                SC(lambda e, m=m: sSc.inc(e.activation(
                    ctf[:], pcp[:], ACT.Tanh, bias=bpp[:, m:m + 1])))
                v_cev = sSc.n
                V(lambda e, v=v_cev: e.wait_ge(sSc.h, v))
                V(lambda e: e.tensor_copy(cth[:], ctf[:]))
                V(lambda e: sV.inc(e.tensor_tensor(ctl[:], ctf[:], cth[:],
                                                   ALU.subtract)))
                v_ctf = sV.n
                for hh in range(2):
                    h = 2 * m + hh
                    b0_ = hh * 64
                    T(lambda e, v=v_ctf: e.wait_ge(sV.h, v))
                    T(lambda e, v=v_pzr: e.wait_ge(sSc.h, v))
                    for ps_, col in ((ppq, h), (pzq, H + h)):
                        T(lambda e, ps_=ps_, col=col, b0_=b0_: e.matmul(
                            ps_[:], r.upzh[b0_:b0_ + 64, col:col + 1],
                            cth[b0_:b0_ + 64, :], start=True, stop=False))
                        T(lambda e, ps_=ps_, col=col, b0_=b0_: e.matmul(
                            ps_[:], r.upzl[b0_:b0_ + 64, col:col + 1],
                            cth[b0_:b0_ + 64, :], start=False, stop=False))
                        T(lambda e, ps_=ps_, col=col, b0_=b0_: sT.inc(e.matmul(
                            ps_[:], r.upzh[b0_:b0_ + 64, col:col + 1],
                            ctl[b0_:b0_ + 64, :], start=False, stop=True)))
                    tpz = sT.n
                    SC(lambda e, v=tpz: e.wait_ge(sT.h, v))
                    SC(lambda e, v=dma_c.get(("pz", rnd, hh - 1) if hh else ("pz", rnd - 1, 1), 0):
                       e.wait_ge(sD.h, v))
                    SC(lambda e: e.activation(przB[:], ppq[:], ACT.Sigmoid))
                    SC(lambda e: sSc.inc(e.activation(przC[:], pzq[:], ACT.Sigmoid)))
                    v_pzr = sSc.n
                    Y(lambda e, v=v_pzr: e.wait_ge(sSc.h, v))
                    Y(lambda e, h=h, q0=q0: sD.dma(e.dma_start(
                        c.sgp_s[h, q0:q0 + 512], przB[:])))
                    Y(lambda e, h=h, q0=q0: sD.dma(e.dma_start(
                        c.sgz_s[h, q0:q0 + 512], przC[:])))
                    dma_c[("pz", rnd, hh)] = sD.n

        SC(lambda e: sSc.inc(e.activation(kmaxr[:], r.kmax2[:], ACT.Sqrt)))
        s_kx = sSc.n
        V(lambda e, v=s_kx: e.wait_ge(sSc.h, v))
        fW = 1024
        dprev = 0
        for h in range(H):
            for fh in range(2):
                f0 = fh * fW
                if h + fh >= 1:
                    Y(lambda e, v=dprev: e.wait_ge(sV.h, v))
                Y(lambda e, h=h, f0=f0: sD.dma(e.dma_start(
                    featq[0:1, 0:fW], c.sgp_s[h, f0:f0 + fW].unsqueeze(0))))
                Y(lambda e, h=h, f0=f0: sD.dma(e.dma_start(
                    featq[0:1, fW:2 * fW], c.sgz_s[h, f0:f0 + fW].unsqueeze(0))))
                Y(lambda e, h=h, f0=f0: sD.dma(e.dma_start(
                    featq[0:1, 2 * fW:3 * fW], c.q2n_s[h, f0:f0 + fW].unsqueeze(0))))
                dln = sD.n
                V(lambda e, v=dln: e.wait_ge(sD.h, v))
                V(lambda e: e.reciprocal(featw[0:1, 0:fW], featq[0:1, fW:2 * fW]))
                V(lambda e: e.tensor_tensor(featw[0:1, 0:fW], featw[0:1, 0:fW],
                                            featw[0:1, 0:fW], ALU.mult))
                V(lambda e: e.tensor_tensor(featw[0:1, fW:2 * fW],
                                            featq[0:1, 0:fW],
                                            featw[0:1, 0:fW], ALU.mult))
                V(lambda e: e.tensor_scalar(featw[0:1, 2 * fW:3 * fW],
                                            featw[0:1, fW:2 * fW],
                                            float(np.float32(4.0 / 8192.0)),
                                            None, ALU.mult))
                V(lambda e: e.tensor_scalar(featw[0:1, 3 * fW:4 * fW],
                                            featw[0:1, 0:fW],
                                            float(np.float32(-2.0 / (8192.0 * 8192.0))),
                                            None, ALU.mult))
                V(lambda e: e.tensor_tensor(featw[0:1, 4 * fW:5 * fW],
                                            featq[0:1, 0:fW],
                                            featw[0:1, fW:2 * fW], ALU.mult))
                V(lambda e: e.tensor_scalar(featw[0:1, 4 * fW:5 * fW],
                                            featw[0:1, 4 * fW:5 * fW], -2.0,
                                            None, ALU.mult))
                sq_in = featw[0:1, 5 * fW:6 * fW]
                V(lambda e: sV.inc(e.tensor_copy(sq_in, featq[0:1, 2 * fW:3 * fW])))
                vq2 = sV.n
                SC(lambda e, v=vq2: e.wait_ge(sV.h, v))
                SC(lambda e, sq_in=sq_in: sSc.inc(e.activation(sq_in, sq_in, ACT.Sqrt)))
                sq2 = sSc.n
                V(lambda e, v=sq2: e.wait_ge(sSc.h, v))
                V(lambda e, h=h: e.tensor_scalar(sq_in, sq_in,
                                                 kmaxr[0:1, h:h + 1], None, ALU.mult))
                V(lambda e: e.tensor_tensor(featw[0:1, 4 * fW:5 * fW],
                                            featw[0:1, 4 * fW:5 * fW], sq_in,
                                            ALU.subtract))
                for i_, src0 in ((0, 2 * fW), (1, 3 * fW), (2, 4 * fW)):
                    V(lambda e, i_=i_, s0=src0: e.tensor_copy(
                        featb[0:1, (2 * i_) * fW:(2 * i_) * fW + fW],
                        featw[0:1, s0:s0 + fW]))
                    V(lambda e, i_=i_, s0=src0: e.tensor_tensor(
                        featb[0:1, (2 * i_ + 1) * fW:(2 * i_ + 1) * fW + fW],
                        featw[0:1, s0:s0 + fW],
                        featb[0:1, (2 * i_) * fW:(2 * i_) * fW + fW], ALU.subtract))
                V(lambda e: sV.inc(e.engine_nop()))
                vfeat = sV.n
                dprev = vfeat
                Y(lambda e, v=vfeat: e.wait_ge(sV.h, v))
                rowmap = [0, 0, 1, 2, 2, 3, 4, 5]
                for ri, si in enumerate(rowmap):
                    Y(lambda e, h=h, ri=ri, si=si, f0=f0: sD.dma(e.dma_start(
                        c.qarr2_s[h, ri, f0:f0 + fW],
                        featb[0:1, si * fW:(si + 1) * fW])))
        v2l, y2l = wq_builder()
        sch.q["v"].extend(v2l)
        sch.q["y"].extend(y2l)
        dnf = sD.n
        Y(lambda e, v=dnf: e.wait_ge(sD.h, v))
        sch.run(nc)
def _emit_b3(nc, c, r, sems):
    sD, sG, sT, sV, sSc, sQ, sO = sems
    ph = contextlib.ExitStack()
    with ph:
        sch = Sched()
        Y = mk_adder(sch.q["y"])
        V = mk_adder(sch.q["v"], track=True)
        T = mk_adder(sch.q["t"])
        SC = mk_adder(sch.q["s"])

        def sbP(name, shape, dtyp=F32):
            return ph.enter_context(nc.sbuf_tensor(name, shape, dtyp))

        qR = [sbP(f"qR{i}", [128, TOWN], BF16) for i in range(2)]
        q2R = [sbP(f"q2R{i}", [72, TOWN], BF16) for i in range(2)]
        wo1R = [sbP(f"wo1R{i}", [128, DM], BF16) for i in range(2)]
        wo2R = [sbP(f"wo2R{i}", [128, DM], BF16) for i in range(2)]
        kR = [sbP(f"kR{i}", [128, 1024], BF16) for i in range(2)]
        k2R = [sbP(f"k2R{i}", [72, 1024], BF16) for i in range(2)]
        vR = [sbP(f"vR{i}", [128, 1024], BF16) for i in range(2)]
        ptmp = [sbP(f"ptmp{i}", [128, 1024]) for i in range(2)]
        pH = [sbP(f"pH{i}", [128, 1024], BF16) for i in range(2)]
        pL = [sbP(f"pL{i}", [128, 1024], BF16) for i in range(2)]
        sacc = sbP("sacc", [128, 16 * DM])
        S1 = sbP("S1", [128, TOWN], BF16)
        S2 = sbP("S2", [128, TOWN], BF16)
        dsb = sbP("dsb", [128, 512])
        drow = sbP("drow", [1, 512])
        rrh = sbP("rrh", [1, 512], BF16)
        rrl = sbP("rrl", [1, 512], BF16)
        rbsb = sbP("rbsb", [128, 512])
        aft = sbP("aft", [128, 512])
        a1t = sbP("a1t", [128, 512], BF16)
        a2t = sbP("a2t", [128, 512], BF16)
        pS = [ph.enter_context(nc.psum_tensor(f"pS{i}", [128, 1024], F32))
              for i in range(2)]
        pctx = [ph.enter_context(nc.psum_tensor(f"pctx{i}", [128, 512], F32))
                for i in range(2)]
        prb = ph.enter_context(nc.psum_tensor("prb", [128, 512], F32))
        pso = ph.enter_context(nc.psum_tensor("pso", [128, 512], F32))

        KGRP = 8
        NG = 64 // KGRP

        q_load = {}
        ring_load = {}
        qk_mark = {}
        ex_mark = {}
        sp_mark = {}
        pv_mark = {}
        v_sacc = 0
        d_sacc_spill = 0

        def load_head(h, par):
            ch, rr = h // 2, (h % 2) * 64
            Y(lambda e, ch=ch, rr=rr, par=par: sD.dma(e.dma_start(
                qR[par][0:64, :], c.mqh_s[ch, rr:rr + 64, 0:TOWN])))
            Y(lambda e, ch=ch, rr=rr, par=par: sD.dma(e.dma_start(
                qR[par][64:128, :], c.mqh_s[ch, rr:rr + 64, 0:TOWN])))
            Y(lambda e, ch=ch, rr=rr, par=par: sD.dma(e.dma_start(
                q2R[par][0:64, :], c.mql_s[ch, rr:rr + 64, 0:TOWN])))
            Y(lambda e, h=h, par=par: sD.dma(e.dma_start(
                q2R[par][64:72, :], c.qarr2_s[h])))
            Y(lambda e, h=h, par=par: sD.dma(e.dma_start(
                wo1R[par][:], c.woar1_s[h])))
            Y(lambda e, h=h, par=par: sD.dma(e.dma_start(
                wo2R[par][:], c.woar2_s[h])))
            q_load[h] = sD.n

        def load_group(h, g, par, u=0):
            k0 = g * KGRP * 128
            Y(lambda e, h=h, k0=k0, par=par: sD.dma(e.dma_start(
                kR[par][:], c.karr_s[h, :, k0:k0 + 1024])))
            Y(lambda e, h=h, k0=k0, par=par: sD.dma(e.dma_start(
                k2R[par][0:64, :], c.karr_s[h, 0:64, k0:k0 + 1024])))
            Y(lambda e, k0=k0, par=par: sD.dma(e.dma_start(
                k2R[par][64:72, :], c.jarr_s[:, k0:k0 + 1024])))
            for kbl in range(KGRP):
                Y(lambda e, h=h, k0=k0, kbl=kbl, par=par: sD.dma(e.dma_start(
                    vR[par][:, kbl * 128:(kbl + 1) * 128],
                    c.varr_s[h, k0 + kbl * 128:k0 + kbl * 128 + 128, :])))
            ring_load[(h, g, u)] = sD.n

        def entry(j):
            hh = j // (2 * NG)
            uu = (j // NG) % 2
            gg = j % NG
            return hh, uu, gg

        NENT = H * 2 * NG
        load_head(0, 0)
        for j0 in range(2):
            hh, uu, gg = entry(j0)
            load_group(hh, gg, j0 % 2, uu)

        it = 0
        oproj_mark = {}
        for h in range(H):
            hpar = h % 2
            if h + 1 < H:
                if h >= 1:
                    Y(lambda e, v=oproj_mark[h - 1]: e.wait_ge(sT.h, v))
                load_head(h + 1, (h + 1) % 2)
            for u in range(2):
                for kb in range(64):
                    g, kbl = kb // KGRP, kb % KGRP
                    j = (h * 2 + u) * NG + g
                    gpar = j % 2
                    b = it % 2
                    if kbl == 0:
                        T(lambda e, v=ring_load[(h, g, u)]: e.wait_ge(sD.h, v))
                    if it >= 2:
                        T(lambda e, v=ex_mark[it - 2]: e.wait_ge(sSc.h, v))
                    for qc in range(2):
                        q0 = u * 1024 + qc * 512
                        T(lambda e, b=b, qc=qc, kbl=kbl, q0=q0, hpar=hpar,
                          gpar=gpar: e.matmul(
                              pS[b][:, qc * 512:qc * 512 + 512],
                              kR[gpar][:, kbl * 128:kbl * 128 + 128],
                              qR[hpar][:, q0:q0 + 512], start=True, stop=False))
                        mm = lambda e, b=b, qc=qc, kbl=kbl, q0=q0, hpar=hpar, \
                            gpar=gpar: e.matmul(
                                pS[b][:, qc * 512:qc * 512 + 512],
                                k2R[gpar][0:72, kbl * 128:kbl * 128 + 128],
                                q2R[hpar][0:72, q0:q0 + 512],
                                start=False, stop=True)
                        if qc == 1:
                            T(lambda e, mm=mm: sT.inc(mm(e)))
                        else:
                            T(mm)
                    qk_mark[it] = sT.n
                    SC(lambda e, v=qk_mark[it]: e.wait_ge(sT.h, v))
                    if it >= 2:
                        SC(lambda e, v=sp_mark[it - 2]: e.wait_ge(sV.h, v))
                    SC(lambda e, b=b: sSc.inc(e.activation(
                        ptmp[b][:], pS[b][:], ACT.Exp)))
                    ex_mark[it] = sSc.n
                    V(lambda e, v=ex_mark[it]: e.wait_ge(sSc.h, v))
                    if it >= 2:
                        V(lambda e, v=pv_mark[it - 2]: e.wait_ge(sT.h, v))
                    V(lambda e, b=b: e.tensor_copy(pH[b][:], ptmp[b][:]))
                    V(lambda e, b=b: sV.inc(e.tensor_tensor(
                        pL[b][:], ptmp[b][:], pH[b][:], ALU.subtract)))
                    sp_mark[it] = sV.n
                    T(lambda e, v=sp_mark[it]: e.wait_ge(sV.h, v))
                    for qc in range(2):
                        T(lambda e, b=b, qc=qc, kbl=kbl, gpar=gpar, st=(kb == 0): e.matmul(
                            pctx[qc][:], vR[gpar][:, kbl * 128:kbl * 128 + 128],
                            pH[b][:, qc * 512:qc * 512 + 512],
                            start=st, stop=False))
                        mm = lambda e, b=b, qc=qc, kbl=kbl, gpar=gpar, \
                            sp=(kb == 63): e.matmul(
                            pctx[qc][:], vR[gpar][:, kbl * 128:kbl * 128 + 128],
                            pL[b][:, qc * 512:qc * 512 + 512],
                            start=False, stop=sp)
                        if qc == 1:
                            T(lambda e, mm=mm: sT.inc(mm(e)))
                        else:
                            T(mm)
                    pv_mark[it] = sT.n
                    if kbl == KGRP - 1 and j + 2 < NENT:
                        nh, nu, ng = entry(j + 2)
                        Y(lambda e, v=pv_mark[it]: e.wait_ge(sT.h, v))
                        load_group(nh, ng, gpar, nu)
                    it += 1
                for qc in range(2):
                    V(lambda e, v=pv_mark[it - 1]: e.wait_ge(sT.h, v))
                    V(lambda e, qc=qc: sV.inc(e.tensor_copy(
                        dsb[64:65, :], pctx[qc][64:65, :])))
                    vden = sV.n
                    Y(lambda e, v=vden: e.wait_ge(sV.h, v))
                    Y(lambda e: sD.dma(e.dma_start(c.den_s[0], dsb[64:65, :])))
                    dn = sD.n
                    Y(lambda e, v=dn: e.wait_ge(sD.h, v))
                    Y(lambda e: sD.dma(e.dma_start(drow[:], c.den_s[0].unsqueeze(0))))
                    dn = sD.n
                    V(lambda e, v=dn: e.wait_ge(sD.h, v))
                    V(lambda e: e.reciprocal(drow[:], drow[:]))
                    V(lambda e: e.tensor_copy(rrh[:], drow[:]))
                    V(lambda e: sV.inc(e.tensor_tensor(rrl[:], drow[:], rrh[:],
                                                       ALU.subtract)))
                    vr = sV.n
                    T(lambda e, v=vr: e.wait_ge(sV.h, v))
                    T(lambda e: e.matmul(prb[:], r.ones128r[0:1, 0:128], rrh[:],
                                         start=True, stop=False))
                    T(lambda e: sT.inc(e.matmul(prb[:], r.ones128r[0:1, 0:128],
                                                rrl[:], start=False, stop=True)))
                    trb = sT.n
                    V(lambda e, v=trb: e.wait_ge(sT.h, v))
                    V(lambda e: e.tensor_copy(rbsb[:], prb[:]))
                    V(lambda e, qc=qc: e.tensor_tensor(
                        aft[0:64, :], pctx[qc][0:64, :], rbsb[0:64, :], ALU.mult))
                    V(lambda e: e.tensor_copy(a1t[0:64, :], aft[0:64, :]))
                    V(lambda e: e.tensor_tensor(a2t[0:64, :], aft[0:64, :],
                                                a1t[0:64, :], ALU.subtract))
                    V(lambda e, qc=qc, u=u: sV.inc(e.tensor_tensor(
                        S2[64:128, u * 1024 + qc * 512:u * 1024 + qc * 512 + 512],
                        pctx[qc][64:128, :], rbsb[64:128, :], ALU.mult)))
                    vev = sV.n
                    col = u * 1024 + qc * 512
                    V(lambda e, col=col: e.tensor_copy(
                        S1[0:64, col:col + 512], a1t[0:64, :]))
                    V(lambda e, col=col: sV.inc(e.tensor_copy(
                        S2[0:64, col:col + 512], a1t[0:64, :])))
                    vev = sV.n
                    Y(lambda e, v=vev: e.wait_ge(sV.h, v))
                    Y(lambda e, col=col: sD.dma(e.dma_start(
                        S1[64:128, col:col + 512], a2t[0:64, :])))
                    dS1 = sD.n
                    T(lambda e, v=vev: e.wait_ge(sV.h, v))
                    T(lambda e, v=dS1: e.wait_ge(sD.h, v))
            T(lambda e, v=q_load[h]: e.wait_ge(sD.h, v))
            for tb in range(16):
                for ot in range(2):
                    T(lambda e, v=v_sacc: e.wait_ge(sV.h, v))
                    T(lambda e, tb=tb, ot=ot, hpar=hpar: e.matmul(
                        pso[:], S1[:, tb * 128:tb * 128 + 128],
                        wo1R[hpar][:, ot * 512:ot * 512 + 512],
                        start=True, stop=False))
                    T(lambda e, tb=tb, ot=ot, hpar=hpar: sT.inc(e.matmul(
                        pso[:], S2[:, tb * 128:tb * 128 + 128],
                        wo2R[hpar][:, ot * 512:ot * 512 + 512],
                        start=False, stop=True)))
                    tso = sT.n
                    V(lambda e, v=tso: e.wait_ge(sT.h, v))
                    col = tb * DM + ot * 512
                    if h == 0:
                        V(lambda e, col=col: sV.inc(e.tensor_copy(
                            sacc[:, col:col + 512], pso[:])))
                    else:
                        V(lambda e, col=col: sV.inc(e.tensor_tensor(
                            sacc[:, col:col + 512], sacc[:, col:col + 512],
                            pso[:], ALU.add)))
                    v_sacc = sV.n
            oproj_mark[h] = sT.n
        Y(lambda e, v=v_sacc: e.wait_ge(sV.h, v))
        for tb in range(16):
            Y(lambda e, tb=tb: sO.dma(e.dma_start(
                c.sacc_s[tb * 128:(tb + 1) * 128, :],
                sacc[:, tb * DM:(tb + 1) * DM])))
        Y(lambda e, v=sO.n: e.wait_ge(sO.h, v))
        sch.run(nc)
def _emit_b4(nc, c, r, sems, gq_done_val):
    sD, sG, sT, sV, sSc, sQ, sO = sems
    ph = contextlib.ExitStack()
    with ph:
        sch = Sched()
        Y = mk_adder(sch.q["y"])
        V = mk_adder(sch.q["v"], track=True)
        T = mk_adder(sch.q["t"])
        SC = mk_adder(sch.q["s"])

        def sbP(name, shape, dtyp=F32):
            return ph.enter_context(nc.sbuf_tensor(name, shape, dtyp))

        bobc = sbP("b4_bobc", [128, DM])
        wtin = [sbP(f"wtin{i}", [128, FI], BF16) for i in range(2)]
        wtst = [sbP(f"wtst{i}", [128, 128], BF16) for i in range(2)]
        xqR = sbP("xqR", [128, NCH * 512], BF16)
        wgR = [sbP(f"wgR{i}", [128, NCH * 512], BF16) for i in range(2)]
        wuR = [sbP(f"wuR{i}", [128, NCH * 512], BF16) for i in range(2)]
        wdH = [sbP(f"wdH{i}", [128, 16 * 512], BF16) for i in range(2)]
        sil = sbP("sil", [128, 512])
        hidt = sbP("hidt", [128, 512])
        hq = sbP("hq", [128, 512], BF16)
        hqf = sbP("hqf", [128, 32 * 512], BF16)
        qsA = sbP("b4qsA", [128, 32])
        qsB = sbP("b4qsB", [128, 32])
        qsC = sbP("b4qsC", [128, 32])
        qsD = sbP("b4qsD", [128, 32])
        qtA = sbP("b4qtA", [128, 512])
        qtB = sbP("b4qtB", [128, 512])
        gt4 = sbP("gt4", [128, 4 * DM])
        xt = [sbP(f"b4xt{i}", [128, DM]) for i in range(2)]
        st_ = [sbP(f"b4st{i}", [128, DM]) for i in range(2)]
        ot_ = [sbP(f"b4ot{i}", [128, DM]) for i in range(2)]
        ptrB = ph.enter_context(nc.psum_tensor("ptrB", [128, 128], BF16))
        pga = ph.enter_context(nc.psum_tensor("pga", [128, 512], F32))
        pgb = ph.enter_context(nc.psum_tensor("pgb", [128, 512], F32))
        pd4 = [ph.enter_context(nc.psum_tensor(f"pd{i}", [128, 512], F32))
               for i in range(4)]

        Y(lambda e: sD.dma(e.dma_start(bobc[:], c.bo_bc_in)))
        d_bo = sD.n
        V(lambda e, v=gq_done_val: e.wait_ge(sQ.h, v))
        V(lambda e, v=d_bo: e.wait_ge(sD.h, v))
        V(lambda e: e.tensor_tensor(r.scbc[:, 7:8], r.scbc[:, 0:1],
                                    r.scbc[:, 2:3], ALU.mult))
        V(lambda e: e.tensor_tensor(r.scbc[:, 8:9], r.scbc[:, 0:1],
                                    r.scbc[:, 3:4], ALU.mult))
        V(lambda e: sV.inc(e.tensor_tensor(r.scbc[:, 9:10], r.scbc[:, 1:2],
                                           r.scbc[:, 4:5], ALU.mult)))
        v_al = sV.n
        SC(lambda e, v=v_al: e.wait_ge(sV.h, v))

        tr_jobs = [(c.wgq_s, c.wgT_s, FI, DM), (c.wuq_s, c.wuT_s, FI, DM),
                   (c.wdq_s, c.wdT_s, DM, FI)]
        v_trev = 0
        d_trst = {}
        t_trin = {}
        n_tr = 0
        n_rb = 0
        for win, wout, nr, ncl in tr_jobs:
            for rb in range(nr // 128):
                n_rb += 1
                par = n_rb % 2
                if n_rb >= 3:
                    Y(lambda e, v=t_trin.get(n_rb - 2, 0): e.wait_ge(sT.h, v))
                Y(lambda e, win=win, rb=rb, par=par, ncl=ncl: sD.dma(e.dma_start(
                    wtin[par][:, 0:ncl], win[rb * 128:(rb + 1) * 128, :])))
                dwin = sD.n
                for cb in range(ncl // 128):
                    n_tr += 1
                    spar = n_tr % 2
                    T(lambda e, v=dwin: e.wait_ge(sD.h, v))
                    T(lambda e, v=v_trev: e.wait_ge(sV.h, v))
                    T(lambda e, cb=cb, par=par: sT.inc(e.transpose(
                        ptrB[:], wtin[par][:, cb * 128:cb * 128 + 128],
                        r.identb[:])))
                    tt = sT.n
                    V(lambda e, v=tt: e.wait_ge(sT.h, v))
                    if n_tr >= 3:
                        V(lambda e, v=d_trst.get(n_tr - 2, 0): e.wait_ge(sD.h, v))
                    V(lambda e, spar=spar: sV.inc(e.tensor_copy(
                        wtst[spar][:], ptrB[:])))
                    v_trev = sV.n
                    Y(lambda e, v=v_trev: e.wait_ge(sV.h, v))
                    Y(lambda e, wout=wout, rb=rb, cb=cb, spar=spar: sD.dma(
                        e.dma_start(wout[cb * 128:(cb + 1) * 128,
                                         rb * 128:(rb + 1) * 128], wtst[spar][:])))
                    d_trst[n_tr] = sD.n
                t_trin[n_rb] = sT.n
        d_tr_all = sD.n

        d_wg = {}
        d_wd = {}
        v_hid = 0
        v_hqf = 0
        t_hqf_used = 0
        v_comb = 0
        d_ocp = {}
        s_gt = 0
        for t4 in range(4):
            tk0 = t4 * 512
            Y(lambda e, v=d_tr_all: e.wait_ge(sD.h, v))
            if t4 >= 1:
                Y(lambda e, v=t_hqf_used: e.wait_ge(sT.h, v))
            for ch in range(NCH):
                Y(lambda e, ch=ch, tk0=tk0: sD.dma(e.dma_start(
                    xqR[:, ch * 512:(ch + 1) * 512],
                    c.xqf_s[:, ch * TOWN + tk0:ch * TOWN + tk0 + 512])))
            d_xq = sD.n
            for ot in range(8):
                rnd = t4 * 8 + ot
                opar = rnd % 2
                if rnd >= 2:
                    Y(lambda e, v=d_wg.get(rnd - 2, 0): e.wait_ge(sT.h, v))
                for ch in range(NCH):
                    Y(lambda e, ch=ch, ot=ot, opar=opar: sD.dma(e.dma_start(
                        wgR[opar][:, ch * 512:(ch + 1) * 512],
                        c.wgT_s[ch * 128:(ch + 1) * 128, ot * 512:ot * 512 + 512])))
                    Y(lambda e, ch=ch, ot=ot, opar=opar: sD.dma(e.dma_start(
                        wuR[opar][:, ch * 512:(ch + 1) * 512],
                        c.wuT_s[ch * 128:(ch + 1) * 128, ot * 512:ot * 512 + 512])))
                d_w = sD.n
                for ts in range(4):
                    T(lambda e, v=d_w: e.wait_ge(sD.h, v))
                    T(lambda e, v=d_xq: e.wait_ge(sD.h, v))
                    T(lambda e, v=v_hid: e.wait_ge(sV.h, v))
                    for ch in range(NCH):
                        x0 = ch * 512 + ts * 128
                        mm = lambda e, x0=x0, ch=ch, opar=opar: e.matmul(
                            pga[:], xqR[:, x0:x0 + 128],
                            wgR[opar][:, ch * 512:(ch + 1) * 512],
                            start=(ch == 0), stop=(ch == NCH - 1))
                        if ch == NCH - 1:
                            T(lambda e, mm=mm: sT.inc(mm(e)))
                        else:
                            T(mm)
                    tga = sT.n
                    for ch in range(NCH):
                        x0 = ch * 512 + ts * 128
                        mm = lambda e, x0=x0, ch=ch, opar=opar: e.matmul(
                            pgb[:], xqR[:, x0:x0 + 128],
                            wuR[opar][:, ch * 512:(ch + 1) * 512],
                            start=(ch == 0), stop=(ch == NCH - 1))
                        if ch == NCH - 1:
                            T(lambda e, mm=mm: sT.inc(mm(e)))
                        else:
                            T(mm)
                    tgb = sT.n
                    SC(lambda e, v=tga: e.wait_ge(sT.h, v))
                    SC(lambda e, v=v_hid: e.wait_ge(sV.h, v))
                    SC(lambda e: sSc.inc(e.activation(
                        sil[:], pga[:], ACT.Silu, bias=0.0,
                        scale=r.scbc[:, 7:8])))
                    sgl = sSc.n
                    V(lambda e, v=sgl: e.wait_ge(sSc.h, v))
                    V(lambda e, v=tgb: e.wait_ge(sT.h, v))
                    V(lambda e: e.tensor_scalar(hidt[:], pgb[:], r.scbc[:, 8:9],
                                                None, ALU.mult))
                    V(lambda e: sV.inc(e.tensor_tensor(hidt[:], hidt[:], sil[:],
                                                       ALU.mult)))
                    v_hid = sV.n
                    last = emit_nvfp4(V, hidt[:], hq[:], qsA[:], qsB[:],
                                      qsC[:], qsD[:], qtA[:], qtB[:],
                                      r.scbc[:, 6:7])
                    V(lambda e, last=last: sV.inc(last(e)))
                    vhq = sV.n
                    for cb in range(4):
                        T(lambda e, v=vhq: e.wait_ge(sV.h, v))
                        T(lambda e, v=v_hqf: e.wait_ge(sV.h, v))
                        T(lambda e, cb=cb: sT.inc(e.transpose(
                            ptrB[:], hq[:, cb * 128:cb * 128 + 128], r.identb[:])))
                        tt = sT.n
                        ic = ot * 4 + cb
                        V(lambda e, v=tt: e.wait_ge(sT.h, v))
                        V(lambda e, ic=ic, ts=ts: sV.inc(e.tensor_copy(
                            hqf[:, ic * 512 + ts * 128:ic * 512 + ts * 128 + 128],
                            ptrB[:])))
                        v_hqf = sV.n
                d_wg[rnd] = sT.n
            for oc in range(2):
                for icg in range(2):
                    wpar = (t4 * 4 + oc * 2 + icg) % 2
                    if t4 * 4 + oc * 2 + icg >= 2:
                        Y(lambda e, v=d_wd.get(t4 * 4 + oc * 2 + icg - 2, 0):
                          e.wait_ge(sT.h, v))
                    for i16 in range(16):
                        icabs = icg * 16 + i16
                        Y(lambda e, icabs=icabs, i16=i16, oc=oc, wpar=wpar:
                          sD.dma(e.dma_start(
                              wdH[wpar][:, i16 * 512:(i16 + 1) * 512],
                              c.wdT_s[icabs * 128:(icabs + 1) * 128,
                                      oc * 512:oc * 512 + 512])))
                    d_wdn = sD.n
                    for ts in range(4):
                        T(lambda e, v=d_wdn: e.wait_ge(sD.h, v))
                        if icg == 0:
                            T(lambda e, v=v_hqf: e.wait_ge(sV.h, v))
                            if t4 > 0 or oc > 0:
                                T(lambda e, v=s_gt: e.wait_ge(sSc.h, v))
                        for i16 in range(16):
                            icabs = icg * 16 + i16
                            mm = lambda e, icabs=icabs, i16=i16, ts=ts, \
                                wpar=wpar: e.matmul(
                                    pd4[ts][:],
                                    hqf[:, icabs * 512 + ts * 128:
                                         icabs * 512 + ts * 128 + 128],
                                    wdH[wpar][:, i16 * 512:(i16 + 1) * 512],
                                    start=(icabs == 0), stop=(icabs == 31))
                            if i16 == 15:
                                T(lambda e, mm=mm: sT.inc(mm(e)))
                            else:
                                T(mm)
                    d_wd[t4 * 4 + oc * 2 + icg] = sT.n
                t_dn = sT.n
                for ts in range(4):
                    SC(lambda e, v=t_dn: e.wait_ge(sT.h, v))
                    if t4 > 0:
                        SC(lambda e, v=v_comb: e.wait_ge(sV.h, v))
                    SC(lambda e, ts=ts, oc=oc: sSc.inc(e.activation(
                        gt4[:, ts * DM + oc * 512:ts * DM + oc * 512 + 512],
                        pd4[ts][:], ACT.Silu, bias=0.0, scale=r.scbc[:, 9:10])))
                s_gt = sSc.n
            t_hqf_used = sT.n
            for ts in range(4):
                tok0 = tk0 + ts * 128
                cpar = ts % 2
                if t4 * 4 + ts >= 2:
                    Y(lambda e, v=d_ocp.get(t4 * 4 + ts - 2, 0): e.wait_ge(sO.h, v))
                Y(lambda e, tok0=tok0, cpar=cpar: sD.dma(e.dma_start(
                    xt[cpar][:], c.x_in[tok0:tok0 + 128, :])))
                Y(lambda e, tok0=tok0, cpar=cpar: sD.dma(e.dma_start(
                    st_[cpar][:], c.sacc_s[tok0:tok0 + 128, :])))
                d_xs = sD.n
                V(lambda e, v=d_xs: e.wait_ge(sD.h, v))
                V(lambda e, v=s_gt: e.wait_ge(sSc.h, v))
                V(lambda e, cpar=cpar: e.tensor_tensor(
                    st_[cpar][:], st_[cpar][:], bobc[:], ALU.add))
                V(lambda e, cpar=cpar, ts=ts: e.tensor_tensor(
                    ot_[cpar][:], st_[cpar][:], gt4[:, ts * DM:(ts + 1) * DM],
                    ALU.mult))
                V(lambda e, cpar=cpar: sV.inc(e.tensor_tensor(
                    ot_[cpar][:], ot_[cpar][:], xt[cpar][:], ALU.add)))
                v_comb = sV.n
                Y(lambda e, v=v_comb: e.wait_ge(sV.h, v))
                Y(lambda e, tok0=tok0, cpar=cpar: sO.dma(e.dma_start(
                    c.out_hbm[tok0:tok0 + 128, :], ot_[cpar][:])))
                d_ocp[t4 * 4 + ts] = sO.n
        Y(lambda e, v=sO.n: e.wait_ge(sO.h, v))
        sch.run(nc)


def kernel(**inputs):
    bf16 = ml_dtypes.bfloat16
    x = np.asarray(inputs["L0_attn_hidden_states"], np.float32)
    wq = np.ascontiguousarray(np.asarray(inputs["L0_attn_wq"], np.float32))
    wk = np.ascontiguousarray(np.asarray(inputs["L0_attn_wk"], np.float32))
    wv = np.ascontiguousarray(np.asarray(inputs["L0_attn_wv"], np.float32))
    wo = np.ascontiguousarray(np.asarray(inputs["L0_attn_wo"], np.float32))
    wfq = np.ascontiguousarray(np.asarray(inputs["L0_attn_wfq"], np.float32))
    wfg = np.ascontiguousarray(np.asarray(inputs["L0_attn_wfg"], np.float32))

    def bias_fold(b):
        return np.ascontiguousarray(np.asarray(b, np.float32).reshape(NCH, 128).T)

    bq = bias_fold(inputs["L0_attn_bq"])
    bk = bias_fold(inputs["L0_attn_bk"])
    bfq = bias_fold(inputs["L0_attn_bfq"])
    bfg = bias_fold(inputs["L0_attn_bfg"])
    bv_bc = np.ascontiguousarray(np.broadcast_to(
        np.asarray(inputs["L0_attn_bv"], np.float32)[None, :], (128, DM)))
    bo_bc = np.ascontiguousarray(np.broadcast_to(
        np.asarray(inputs["L0_attn_bo"], np.float32)[None, :], (128, DM)))
    up2 = np.asarray(inputs["L0_attn_up"], np.float32).reshape(H, D).T
    uz2 = np.asarray(inputs["L0_attn_uz"], np.float32).reshape(H, D).T
    upz_half = np.concatenate([up2, uz2], axis=1)
    upz = np.ascontiguousarray(np.concatenate([upz_half, upz_half], axis=0))
    gw = np.ascontiguousarray(np.asarray(inputs["L0_ffn_gate_w"], np.float32))
    uw = np.ascontiguousarray(np.asarray(inputs["L0_ffn_up_w"], np.float32))
    dw = np.ascontiguousarray(np.asarray(inputs["L0_ffn_down_w"], np.float32))
    s_in = np.asarray(inputs["L0_ffn_s_in"], np.float32).reshape(1, 1)
    s_in_down = np.asarray(inputs["L0_ffn_s_in_down"], np.float32).reshape(1, 1)
    ident = np.eye(128, dtype=np.float32)

    nc = build_program(4)
    in_maps = []
    for core in range(8):
        b, qq = core // 4, core % 4
        xr = np.ascontiguousarray(np.roll(x[b], -TOWN * qq, axis=0))
        pos = ((np.arange(S) + TOWN * qq) % S).astype(np.float64)
        j_hi = pos.astype(bf16).astype(np.float64)
        j_lo = (pos - j_hi).astype(bf16).astype(np.float64)
        jj = pos * pos
        jj_hi = jj.astype(bf16).astype(np.float64)
        jj_lo = (jj - jj_hi).astype(bf16).astype(np.float64)
        jconst = np.ascontiguousarray(
            np.stack([j_hi, j_lo, jj_hi, jj_lo]).astype(np.float32))
        in_maps.append({
            "x_roll": xr, "wq": wq, "wk": wk, "wv": wv, "wo": wo,
            "wfq": wfq, "wfg": wfg, "bq": bq, "bk": bk, "bfq": bfq,
            "bfg": bfg, "bv_bc": bv_bc, "bo_bc": bo_bc, "upz": upz,
            "jconst": jconst, "ident_in": ident, "gate_w": gw, "up_w": uw,
            "down_w": dw, "s_in": s_in, "s_in_down": s_in_down,
        })
    res = run_bass_kernel_spmd(nc, in_maps, core_ids=list(range(8)))
    out = np.empty((2, S, DM), np.float32)
    for core in range(8):
        b, qq = core // 4, core % 4
        out[b, TOWN * qq:TOWN * (qq + 1)] = np.asarray(res.results[core]["out"])
    return out


# version 02
# speedup vs baseline: 1.3424x; 1.3424x over previous
import contextlib
import types
import numpy as np
import ml_dtypes
import concourse.bass as bass
import concourse.mybir as mybir
from concourse.bass_utils import run_bass_kernel_spmd

F32 = mybir.dt.float32
BF16 = mybir.dt.bfloat16
AX = mybir.AxisListType
ALU = mybir.AluOpType
ACT = mybir.ActivationFunctionType

S = 8192
TOWN = 2048
DM = 1024
H = 16
D = 64
FI = 4096
NCH = 8
NB = 16
EPS = 1e-5


_COUNTING = [False]
_CAPTURE = [None]
_SEP_AP = [None]
_SEP_COUNT = [0]
_HZW = 128


class _MockInst:
    def then_inc(self, h, k=1):
        return self

    def __getattr__(self, name):
        return lambda *a, **kw: self


class _Cap:
    __slots__ = ("reads", "writes", "reduce", "touched")

    def __init__(self):
        self.reads = []
        self.writes = []
        self.reduce = False
        self.touched = False


def _is_ap(x):
    return hasattr(x, "offset") and hasattr(x, "free_size") and hasattr(x, "ap")


class _MockEng:
    def __getattr__(self, name):
        def call(*a, **kw):
            cap = _CAPTURE[0]
            if cap is not None and name not in ("wait_ge", "engine_nop"):
                cap.touched = True
                if "reduce" in name:
                    cap.reduce = True
                aps = [x for x in a if _is_ap(x)]
                if aps:
                    cap.writes.append(aps[0])
                    cap.reads += aps[1:]
                cap.reads += [v for v in kw.values() if _is_ap(v)]
            return _MockInst()
        return call


_MOCK = _MockEng()


def _ap_rng(ap):
    try:
        return (ap.name, int(ap.offset), int(ap.free_size()))
    except Exception:
        return None


def _hazard(prev, cap):
    if prev is None:
        return False
    for w in prev.writes:
        wi = _ap_rng(w)
        if wi is None:
            continue
        wt, wo, ww = wi
        for rap in cap.reads:
            ri = _ap_rng(rap)
            if ri is None:
                continue
            rt, ro, rw = ri
            if rt != wt or ro + rw <= wo or wo + ww <= ro:
                continue
            if prev.reduce or ww < _HZW or (wo + ww) - ro < _HZW:
                return True
    return False


def _sep_op(e):
    s = _SEP_AP[0]
    e.tensor_copy(s[:, 1:2], s[:, 0:1])


class Sem:

    def __init__(self, st, nc, name):
        self.h = st.enter_context(nc.semaphore(name))
        self.n = 0

    def inc(self, inst, k=1):
        if _COUNTING[0]:
            self.n += k
        inst.then_inc(self.h, k)
        return self.n

    def dma(self, inst):
        return self.inc(inst, 16)


def mk_adder(lst, track=False):
    state = {"last": None}

    def add(fn):
        cap = _Cap() if track else None
        _CAPTURE[0] = cap
        _COUNTING[0] = True
        try:
            fn(_MOCK)
        finally:
            _COUNTING[0] = False
            _CAPTURE[0] = None
        if track and cap.touched:
            if _hazard(state["last"], cap):
                lst.append(_sep_op)
                _SEP_COUNT[0] += 1
            state["last"] = cap
        lst.append(fn)
    return add


class Sched:
    def __init__(self):
        self.q = {"t": [], "v": [], "s": [], "g": [], "y": []}

    def run(self, nc):
        with nc.Block() as blk:
            if self.q["y"]:
                @blk.sync
                def _(e):
                    for f in self.q["y"]:
                        f(e)
            if self.q["t"]:
                @blk.tensor
                def _(e):
                    for f in self.q["t"]:
                        f(e)
            if self.q["v"]:
                @blk.vector
                def _(e):
                    for f in self.q["v"]:
                        f(e)
            if self.q["s"]:
                @blk.scalar
                def _(e):
                    for f in self.q["s"]:
                        f(e)
            if self.q["g"]:
                @blk.gpsimd
                def _(e):
                    for f in self.q["g"]:
                        f(e)


def emit_nvfp4(E, x_ap, out_ap, sA, sB, sC, sT_, tA, tB, gs_ap, eng="v"):
    W = x_ap.shape[-1]
    nsc = W // 16
    P = x_ap.shape[0]
    E(lambda e: e.tensor_scalar(tA, x_ap, -1.0, None, ALU.mult))
    E(lambda e: e.tensor_tensor(tA, tA, x_ap, ALU.max))
    tA3 = tA.rearrange("p (a b) -> p a b", b=16)
    if eng == "v":
        E(lambda e: e.tensor_reduce(sA, tA3, AX.X, ALU.max))
    else:
        tB3 = tB.rearrange("p (a b) -> p a b", b=16)
        E(lambda e: e.tensor_tensor(tB3[:, :, 0:8], tA3[:, :, 0:8],
                                    tA3[:, :, 8:16], ALU.max))
        E(lambda e: e.tensor_tensor(tB3[:, :, 0:4], tB3[:, :, 0:4],
                                    tB3[:, :, 4:8], ALU.max))
        E(lambda e: e.tensor_tensor(tB3[:, :, 0:2], tB3[:, :, 0:2],
                                    tB3[:, :, 2:4], ALU.max))
        E(lambda e: e.tensor_tensor(tB3[:, :, 0:1], tB3[:, :, 0:1],
                                    tB3[:, :, 1:2], ALU.max))
        E(lambda e: e.tensor_copy(sA, tB3[:, :, 0:1].squeeze(2)))
    E(lambda e: e.tensor_scalar(sA, sA, float(np.float32(1.0 / 6.0)), None, ALU.mult))
    E(lambda e: e.tensor_scalar(sA, sA, gs_ap, 448.0, ALU.mult, ALU.min))
    E(lambda e: e.memset(sB, 2.0 ** 9))
    E(lambda e: e.memset(sC, 2.0 ** -9))
    for p in range(-5, 9):
        cinv = 2.0 ** (3 - p) - 2.0 ** (4 - p)
        cs = 2.0 ** (p - 3) - 2.0 ** (p - 4)
        E(lambda e, p=p, cv=cinv: e.tensor_scalar(sT_, sA, 2.0 ** p, cv, ALU.is_ge, ALU.mult))
        E(lambda e: e.tensor_tensor(sB, sB, sT_, ALU.add))
        E(lambda e, p=p, cv=cs: e.tensor_scalar(sT_, sA, 2.0 ** p, cv, ALU.is_ge, ALU.mult))
        E(lambda e: e.tensor_tensor(sC, sC, sT_, ALU.add))
    E(lambda e: e.tensor_tensor(sT_, sA, sB, ALU.mult))
    E(lambda e: e.memset(sA, 0.0))
    for k in range(16):
        op = ALU.is_gt if (k % 2 == 0) else ALU.is_ge
        E(lambda e, k=k, op=op: e.tensor_scalar(tB[:, 0:nsc], sT_, k + 0.5, None, op))
        E(lambda e: e.tensor_tensor(sA, sA, tB[:, 0:nsc], ALU.add))
    E(lambda e: e.tensor_tensor(sT_, sA, sC, ALU.mult))
    if eng == "v":
        E(lambda e: e.reciprocal(sC, sT_))
        E(lambda e: e.tensor_scalar(sC, sC, gs_ap, None, ALU.mult))
        rdq_bc = sC.unsqueeze(2).broadcast_to([P, nsc, 16])
        E(lambda e: e.tensor_tensor(tA3, tA3, rdq_bc, ALU.mult))
        E(lambda e: e.tensor_scalar(tB, tA, 0.25, 0.5, ALU.is_ge, ALU.mult))
        for m, cadd in ((0.75, 0.5), (1.25, 0.5), (1.75, 0.5),
                        (2.5, 1.0), (3.5, 1.0), (5.0, 2.0)):
            E(lambda e, m=m, cv=cadd: e.tensor_scalar(out_ap, tA, m, cv, ALU.is_ge, ALU.mult))
            E(lambda e: e.tensor_tensor(tB, tB, out_ap, ALU.add))
    else:
        E(lambda e: e.tensor_scalar(sB, sB, gs_ap, None, ALU.mult))
        gi_bc = sB.unsqueeze(2).broadcast_to([P, nsc, 16])
        E(lambda e: e.tensor_tensor(tA3, tA3, gi_bc, ALU.mult))
        first = True
        for m, cadd in ((0.25, 0.5), (0.75, 0.5), (1.25, 0.5), (1.75, 0.5),
                        (2.5, 1.0), (3.5, 1.0), (5.0, 2.0)):
            E(lambda e, m=m: e.tensor_scalar(sC, sA, m, None, ALU.mult))
            thr_bc = sC.unsqueeze(2).broadcast_to([P, nsc, 16])
            o3_ = out_ap.rearrange("p (a b) -> p a b", b=16)
            E(lambda e, o3_=o3_, thr_bc=thr_bc: e.tensor_tensor(
                o3_, tA3, thr_bc, ALU.is_ge))
            if first:
                E(lambda e, cv=cadd: e.tensor_scalar(tB, out_ap, cv, None, ALU.mult))
                first = False
            else:
                E(lambda e, cv=cadd: e.tensor_scalar(out_ap, out_ap, cv, None, ALU.mult))
                E(lambda e: e.tensor_tensor(tB, tB, out_ap, ALU.add))
    E(lambda e: e.tensor_scalar(tA, x_ap, 0.0, 2.0, ALU.is_ge, ALU.mult))
    E(lambda e: e.tensor_scalar(tA, tA, -1.0, None, ALU.add))
    E(lambda e: e.tensor_tensor(tB, tB, tA, ALU.mult))
    sc8_bc = sT_.unsqueeze(2).broadcast_to([P, nsc, 16])
    tB3f = tB.rearrange("p (a b) -> p a b", b=16)
    o3 = out_ap.rearrange("p (a b) -> p a b", b=16)
    return lambda e: e.tensor_tensor(o3, tB3f, sc8_bc, ALU.mult)


def _declare_io(nc):
    dt = nc.dram_tensor
    c = types.SimpleNamespace()
    c.x_in = dt("x_roll", [S, DM], F32, kind="ExternalInput").ap()
    for nm in ("wq", "wk", "wv", "wo", "wfq", "wfg"):
        setattr(c, nm + "_in", dt(nm, [DM, DM], F32, kind="ExternalInput").ap())
    for nm in ("bq", "bk", "bfq", "bfg"):
        setattr(c, nm + "_in", dt(nm, [128, NCH], F32, kind="ExternalInput").ap())
    c.bv_bc_in = dt("bv_bc", [128, DM], F32, kind="ExternalInput").ap()
    c.bo_bc_in = dt("bo_bc", [128, DM], F32, kind="ExternalInput").ap()
    c.upz_in = dt("upz", [128, 2 * H], F32, kind="ExternalInput").ap()
    c.jc_in = dt("jconst", [4, S], F32, kind="ExternalInput").ap()
    c.ident_in = dt("ident_in", [128, 128], F32, kind="ExternalInput").ap()
    c.gw_in = dt("gate_w", [FI, DM], F32, kind="ExternalInput").ap()
    c.uw_in = dt("up_w", [FI, DM], F32, kind="ExternalInput").ap()
    c.dw_in = dt("down_w", [DM, FI], F32, kind="ExternalInput").ap()
    c.sin_in = dt("s_in", [1, 1], F32, kind="ExternalInput").ap()
    c.sind_in = dt("s_in_down", [1, 1], F32, kind="ExternalInput").ap()
    c.out_hbm = dt("out", [TOWN, DM], F32, kind="ExternalOutput").ap()

    c.xnh_s = dt("xnh_s", [NCH, 128, S], BF16).ap()
    c.xnl_s = dt("xnl_s", [NCH, 128, S], BF16).ap()
    c.karr_s = dt("karr_s", [H, 128, S], BF16).ap()
    c.varr_s = dt("varr_s", [H, S, 128], BF16).ap()
    c.qarr2_s = dt("qarr2_s", [H, 8, TOWN], BF16).ap()
    c.jarr_s = dt("jarr_s", [8, S], BF16).ap()
    c.woar1_s = dt("woar1_s", [H, 128, DM], BF16).ap()
    c.woar2_s = dt("woar2_s", [H, 128, DM], BF16).ap()
    c.wgq_s = dt("wgq_s", [FI, DM], BF16).ap()
    c.wuq_s = dt("wuq_s", [FI, DM], BF16).ap()
    c.wdq_s = dt("wdq_s", [DM, FI], BF16).ap()
    c.wgT_s = dt("wgT_s", [DM, FI], BF16).ap()
    c.wuT_s = dt("wuT_s", [DM, FI], BF16).ap()
    c.wdT_s = dt("wdT_s", [FI, DM], BF16).ap()
    c.sacc_s = dt("sacc_s", [TOWN, DM], F32).ap()
    c.scal_s = dt("scal_s", [16, 128], F32).ap()
    c.den_s = dt("den_s", [4, 512], F32).ap()
    c.sgp_s = dt("sgp_s", [H, TOWN], F32).ap()
    c.sgz_s = dt("sgz_s", [H, TOWN], F32).ap()
    c.q2n_s = dt("q2n_s", [H, TOWN], F32).ap()
    c.mqh_s = dt("mqh_s", [NCH, 128, TOWN], BF16).ap()
    c.mql_s = dt("mql_s", [NCH, 128, TOWN], BF16).ap()
    c.xqf_s = dt("xqf_s", [128, NCH * TOWN], BF16).ap()
    return c
def build_program(upto=4, debug=()):
    nc = bass.Bass("TRN2", target_bir_lowering=False)
    c = _declare_io(nc)
    dbg_out = {}
    dbg_sbuf = []
    for nm in debug:
        if not hasattr(c, nm):
            dbg_sbuf.append(nm)
            continue
        src = getattr(c, nm)
        shp = list(src.shape)
        flat = [int(np.prod(shp[:-1])), shp[-1]]
        dbg_out[nm] = nc.dram_tensor("dbg_" + nm, flat, src.dtype,
                                     kind="ExternalOutput").ap()

    root = contextlib.ExitStack()
    with root:
        r = types.SimpleNamespace()

        def sbR(name, shape, dtyp=F32):
            t = root.enter_context(nc.sbuf_tensor(name, shape, dtyp))
            setattr(r, name, t)
            return t

        sbR("ident", [128, 128])
        sbR("identb", [128, 128], BF16)
        sbR("ones128b", [128, 1], BF16)
        sbR("ones128r", [1, 128], BF16)
        sbR("biases", [128, 4 * NCH])
        sbR("upzh", [128, 2 * H], BF16)
        sbR("upzl", [128, 2 * H], BF16)
        sbR("scbc", [128, 16])
        sbR("kmax2", [1, H])
        sbR("xbar", [128, NCH])
        sbR("sepA", [128, 2])
        _SEP_AP[0] = r.sepA
        _SEP_COUNT[0] = 0

        sD = Sem(root, nc, "sD")
        sG = Sem(root, nc, "sG")
        sT = Sem(root, nc, "sT")
        sV = Sem(root, nc, "sV")
        sSc = Sem(root, nc, "sSc")
        sQ = Sem(root, nc, "sQ")
        sO = Sem(root, nc, "sO")
        sems = (sD, sG, sT, sV, sSc, sQ, sO)

        gq_stack = contextlib.ExitStack()
        gq_al = _wquant_alloc(nc, gq_stack)
        wq_builder = lambda: _wquant_ops(nc, gq_al, c, r, sems)

        _emit_b0(nc, c, r, sems)
        if upto >= 1:
            _emit_b1(nc, c, r, sems)
        if upto >= 2:
            _emit_b2(nc, c, r, sems, wq_builder)
        gq_done_val = sQ.n
        if upto >= 3:
            _emit_b3(nc, c, r, sems)
        gq_stack.close()
        if upto >= 4:
            _emit_b4(nc, c, r, sems, gq_done_val)

        sch = Sched()
        Y = mk_adder(sch.q["y"])
        for nm in dbg_sbuf:
            src = getattr(r, nm)
            outap = nc.dram_tensor("dbg_" + nm, list(src.shape), src.dtype,
                                   kind="ExternalOutput").ap()
            Y(lambda e, src=src, outap=outap: sO.dma(e.dma_start(outap, src[:])))
        for nm, outap in dbg_out.items():
            src = getattr(c, nm)
            if len(src.shape) == 3:
                p_ = src.shape[1]
                for i0 in range(src.shape[0]):
                    Y(lambda e, src=src, outap=outap, i0=i0, p_=p_:
                      sO.dma(e.dma_start(outap[i0 * p_:(i0 + 1) * p_, :], src[i0])))
            else:
                Y(lambda e, src=src, outap=outap: sO.dma(e.dma_start(outap, src)))
        Y(lambda e, v=sO.n: e.wait_ge(sO.h, v))
        Y(lambda e, v=sD.n: e.wait_ge(sD.h, v))
        Y(lambda e, v=sG.n: e.wait_ge(sG.h, v))
        if upto >= 2:
            Y(lambda e, v=sQ.n: e.wait_ge(sQ.h, v))
        sch.run(nc)
    print(f"[build] raw-hazard separators inserted: {_SEP_COUNT[0]}")
    return nc


def _wquant_alloc(nc, stk):
    W2 = 1024
    al = {}
    al["gq1"] = [stk.enter_context(nc.sbuf_tensor(f"gq1_{i}", [128, W2], F32))
                 for i in range(2)]
    al["gqtA"] = stk.enter_context(nc.sbuf_tensor("gqtA", [128, W2], F32))
    al["gqtB"] = stk.enter_context(nc.sbuf_tensor("gqtB", [128, W2], F32))
    al["gqs1"] = stk.enter_context(nc.sbuf_tensor("gqs1", [128, W2 // 16], F32))
    al["gqs2"] = stk.enter_context(nc.sbuf_tensor("gqs2", [128, W2 // 16], F32))
    al["gqs3"] = stk.enter_context(nc.sbuf_tensor("gqs3", [128, W2 // 16], F32))
    al["gqs4"] = stk.enter_context(nc.sbuf_tensor("gqs4", [128, W2 // 16], F32))
    al["gmx"] = stk.enter_context(nc.sbuf_tensor("gmx", [128, 4], F32))
    al["ggs"] = stk.enter_context(nc.sbuf_tensor("ggs", [128, 4], F32))
    al["grow"] = stk.enter_context(nc.sbuf_tensor("grow", [1, 128], F32))
    al["gout"] = [stk.enter_context(nc.sbuf_tensor(f"gout{i}", [128, W2], BF16))
                  for i in range(2)]
    return al


def _wquant_ops(nc, al, c, r, sems):
    sD, sG, sT, sV, sSc, sQ, sO = sems
    v2l, y2l = [], []
    v2, y2 = mk_adder(v2l, track=True), mk_adder(y2l)
    W2 = 1024
    gq1 = al["gq1"]
    gqtA, gqtB = al["gqtA"], al["gqtB"]
    gqs1, gqs2, gqs3, gqs4 = al["gqs1"], al["gqs2"], al["gqs3"], al["gqs4"]
    gmx, ggs, grow, gout = al["gmx"], al["ggs"], al["grow"], al["gout"]
    jobs = [(c.gw_in, c.wgq_s, FI, DM, 0), (c.uw_in, c.wuq_s, FI, DM, 1),
            (c.dw_in, c.wdq_s, DM, FI, 2)]

    vmark = 0
    nt = 0
    for win, wout_s, nrows, wwidth, wi in jobs:
        v2(lambda e, wi=wi: e.memset(gmx[:, wi:wi + 1], 0.0))
        for cr in range(nrows // 128):
            for cc in range(wwidth // W2):
                par = nt % 2
                nt += 1
                if nt > 2:
                    y2(lambda e, v=vmark: e.wait_ge(sV.h, v))
                y2(lambda e, cr=cr, cc=cc, win=win, par=par: sG.dma(
                    e.dma_start(gq1[par][:],
                                win[cr * 128:(cr + 1) * 128,
                                    cc * W2:(cc + 1) * W2])))
                dn = sG.n
                v2(lambda e, v=dn: e.wait_ge(sG.h, v))
                v2(lambda e, par=par, wi=wi: sV.inc(e.tensor_reduce(
                    gmx[:, 3:4], gq1[par][:], AX.X, ALU.max,
                    apply_absolute_value=True)))
                v2(lambda e, wi=wi: e.tensor_tensor(
                    gmx[:, wi:wi + 1], gmx[:, wi:wi + 1], gmx[:, 3:4], ALU.max))
                vmark = sV.n
    for win, wout_s, nrows, wwidth, wi in jobs:
        v2(lambda e, wi=wi, v=vmark: e.wait_ge(sV.h, v))
        v2(lambda e, wi=wi: sV.inc(e.tensor_copy(gmx[:, 3:4],
                                                        gmx[:, wi:wi + 1])))
        vv = sV.n
        y2(lambda e, v=vv: e.wait_ge(sV.h, v))
        y2(lambda e, wi=wi: sD.dma(e.dma_start(
            c.scal_s[10 + wi, 0:128], gmx[:, 3:4])))
        dn = sD.n
        y2(lambda e, v=dn: e.wait_ge(sD.h, v))
        y2(lambda e, wi=wi: sD.dma(e.dma_start(
            grow[:], c.scal_s[10 + wi].unsqueeze(0))))
        dn = sD.n
        v2(lambda e, v=dn: e.wait_ge(sD.h, v))
        v2(lambda e: e.tensor_reduce(grow[0:1, 0:1], grow[:], AX.X, ALU.max))
        v2(lambda e: e.reciprocal(grow[0:1, 1:2], grow[0:1, 0:1]))
        v2(lambda e: e.tensor_scalar(grow[0:1, 1:2], grow[0:1, 1:2],
                                            2688.0, None, ALU.mult))
        v2(lambda e: sV.inc(e.tensor_scalar(
            grow[0:1, 2:3], grow[0:1, 0:1], float(np.float32(1.0 / 2688.0)),
            None, ALU.mult)))
        vv = sV.n
        y2(lambda e, v=vv: e.wait_ge(sV.h, v))
        y2(lambda e, wi=wi: sD.dma(e.dma_start(
            c.scal_s[10 + wi, 0:1], grow[0:1, 1:2])))
        y2(lambda e, wi=wi: sD.dma(e.dma_start(
            c.scal_s[13 + wi, 0:1], grow[0:1, 2:3])))
        dn = sD.n
        y2(lambda e, v=dn: e.wait_ge(sD.h, v))
        vmark = sV.n
    for k in range(7):
        w = 1 << k
        for rr in range(10, 16):
            y2(lambda e, w=w, rr=rr: sD.dma(e.dma_start(
                c.scal_s[rr, w:2 * w], c.scal_s[rr, 0:w])))
        dn = sD.n
        y2(lambda e, v=dn: e.wait_ge(sD.h, v))
    for wi in range(3):
        y2(lambda e, wi=wi: sD.dma(e.dma_start(
            ggs[:, wi:wi + 1], c.scal_s[10 + wi].unsqueeze(1))))
        y2(lambda e, wi=wi: sD.dma(e.dma_start(
            r.scbc[:, 2 + wi:3 + wi], c.scal_s[13 + wi].unsqueeze(1))))
    y1_done = sD.n

    v2(lambda e, v=y1_done: e.wait_ge(sD.h, v))
    nt = 0
    dmark = {}
    for win, wout_s, nrows, wwidth, wi in jobs:
        for cr in range(nrows // 128):
            for cc in range(wwidth // W2):
                par = nt % 2
                if nt >= 2:
                    y2(lambda e, v=dmark[nt - 2][0]: e.wait_ge(sV.h, v))
                y2(lambda e, cr=cr, cc=cc, win=win, par=par: sG.dma(
                    e.dma_start(gq1[par][:],
                                win[cr * 128:(cr + 1) * 128,
                                    cc * W2:(cc + 1) * W2])))
                dn = sG.n
                v2(lambda e, v=dn: e.wait_ge(sG.h, v))
                if nt >= 2:
                    v2(lambda e, v=dmark[nt - 2][1]: e.wait_ge(sG.h, v))
                last = emit_nvfp4(v2, gq1[par][:], gout[par][:],
                                  gqs1[:], gqs2[:], gqs3[:], gqs4[:],
                                  gqtA[:], gqtB[:], ggs[:, wi:wi + 1], eng="v")
                v2(lambda e, last=last: sV.inc(last(e)))
                vv = sV.n
                y2(lambda e, v=vv: e.wait_ge(sV.h, v))
                y2(lambda e, cr=cr, cc=cc, wout_s=wout_s, par=par:
                          sG.dma(e.dma_start(
                              wout_s[cr * 128:(cr + 1) * 128,
                                     cc * W2:(cc + 1) * W2], gout[par][:])))
                dmark[nt] = (vv, sG.n)
                nt += 1
    v2(lambda e: sQ.inc(e.engine_nop()))
    return v2l, y2l
def _emit_b0(nc, c, r, sems):
    sD, sG, sT, sV, sSc, sQ, sO = sems
    ph = contextlib.ExitStack()
    with ph:
        sch = Sched()
        Y = mk_adder(sch.q["y"])
        V = mk_adder(sch.q["v"], track=True)
        wstgA = ph.enter_context(nc.sbuf_tensor("b0_wstgA", [128, DM], F32))
        woh = ph.enter_context(nc.sbuf_tensor("b0_woh", [128, NCH * DM], BF16))
        wol = ph.enter_context(nc.sbuf_tensor("b0_wol", [128, NCH * DM], BF16))
        jstg = ph.enter_context(nc.sbuf_tensor("b0_jstg", [4, S], F32))
        jrows = ph.enter_context(nc.sbuf_tensor("b0_jrows", [4, S], BF16))
        onesrow = ph.enter_context(nc.sbuf_tensor("b0_onesrow", [1, S], BF16))
        sstg = ph.enter_context(nc.sbuf_tensor("b0_sstg", [1, 8], F32))
        upzf = ph.enter_context(nc.sbuf_tensor("b0_upzf", [128, 2 * H], F32))

        Y(lambda e: sD.dma(e.dma_start(r.ident[:], c.ident_in)))
        Y(lambda e: sD.dma(e.dma_start(jstg[:], c.jc_in)))
        Y(lambda e: sD.dma(e.dma_start(r.biases[:, 0:NCH], c.bq_in)))
        Y(lambda e: sD.dma(e.dma_start(r.biases[:, NCH:2 * NCH], c.bk_in)))
        Y(lambda e: sD.dma(e.dma_start(r.biases[:, 2 * NCH:3 * NCH], c.bfq_in)))
        Y(lambda e: sD.dma(e.dma_start(r.biases[:, 3 * NCH:4 * NCH], c.bfg_in)))

        Y(lambda e: sD.dma(e.dma_start(upzf[:], c.upz_in)))
        Y(lambda e: sD.dma(e.dma_start(sstg[0:1, 0:1], c.sin_in)))
        Y(lambda e: sD.dma(e.dma_start(sstg[0:1, 1:2], c.sind_in)))
        d0 = sD.n
        V(lambda e: e.memset(r.sepA[:], 0.0))
        V(lambda e, v=d0: e.wait_ge(sD.h, v))
        V(lambda e: e.tensor_copy(r.identb[:], r.ident[:]))
        V(lambda e: e.memset(r.ones128b[:], 1.0))
        V(lambda e: e.memset(r.ones128r[:], 1.0))
        V(lambda e: e.tensor_copy(jrows[:], jstg[:]))
        V(lambda e: e.memset(onesrow[:], 1.0))
        V(lambda e: e.tensor_copy(r.upzh[:], upzf[:]))
        V(lambda e: e.tensor_tensor(r.upzl[:], upzf[:], r.upzh[:], ALU.subtract))
        V(lambda e: e.memset(r.kmax2[:], 0.0))
        V(lambda e: e.memset(r.xbar[:], 0.0))
        V(lambda e: e.reciprocal(sstg[0:1, 2:3], sstg[0:1, 0:1]))
        V(lambda e: sV.inc(e.reciprocal(sstg[0:1, 3:4], sstg[0:1, 1:2])))
        v0 = sV.n
        Y(lambda e, v=v0: e.wait_ge(sV.h, v))
        Y(lambda e: sD.dma(e.dma_start(c.scal_s[0, 0:1], sstg[0:1, 2:3])))
        Y(lambda e: sD.dma(e.dma_start(c.scal_s[1, 0:1], sstg[0:1, 3:4])))
        Y(lambda e: sD.dma(e.dma_start(c.scal_s[8, 0:1], sstg[0:1, 0:1])))
        Y(lambda e: sD.dma(e.dma_start(c.scal_s[9, 0:1], sstg[0:1, 1:2])))
        dn = sD.n
        Y(lambda e, v=dn: e.wait_ge(sD.h, v))
        for k in range(7):
            w = 1 << k
            for row in (0, 1, 8, 9):
                Y(lambda e, w=w, row=row: sD.dma(e.dma_start(
                    c.scal_s[row, w:2 * w], c.scal_s[row, 0:w])))
            dn = sD.n
            Y(lambda e, v=dn: e.wait_ge(sD.h, v))
        Y(lambda e: sD.dma(e.dma_start(r.scbc[:, 0:1], c.scal_s[0].unsqueeze(1))))
        Y(lambda e: sD.dma(e.dma_start(r.scbc[:, 1:2], c.scal_s[1].unsqueeze(1))))
        Y(lambda e: sD.dma(e.dma_start(r.scbc[:, 5:6], c.scal_s[8].unsqueeze(1))))
        Y(lambda e: sD.dma(e.dma_start(r.scbc[:, 6:7], c.scal_s[9].unsqueeze(1))))

        vprev = 0
        for ch in range(NCH):
            if ch >= 1:
                Y(lambda e, v=vprev: e.wait_ge(sV.h, v))
            Y(lambda e, ch=ch: sD.dma(e.dma_start(
                wstgA[:], c.wo_in[ch * 128:(ch + 1) * 128, :])))
            dn = sD.n
            V(lambda e, v=dn: e.wait_ge(sD.h, v))
            V(lambda e, ch=ch: e.tensor_copy(woh[:, ch * DM:(ch + 1) * DM], wstgA[:]))
            V(lambda e, ch=ch: sV.inc(e.tensor_tensor(
                wol[:, ch * DM:(ch + 1) * DM], wstgA[:],
                woh[:, ch * DM:(ch + 1) * DM], ALU.subtract)))
            vprev = sV.n
        V(lambda e: sV.inc(e.memset(wstgA[0:1, :], 0.0)))
        vz = sV.n
        Y(lambda e, v=vz: e.wait_ge(sV.h, v))
        for h in range(H):
            ch, r0 = divmod(h * D, 128)
            hi_sl = woh[r0:r0 + 64, ch * DM:(ch + 1) * DM]
            lo_sl = wol[r0:r0 + 64, ch * DM:(ch + 1) * DM]
            Y(lambda e, h=h, s_=hi_sl: sD.dma(e.dma_start(c.woar1_s[h, 0:64], s_)))
            Y(lambda e, h=h, s_=hi_sl: sD.dma(e.dma_start(c.woar1_s[h, 64:128], s_)))
            Y(lambda e, h=h, s_=lo_sl: sD.dma(e.dma_start(c.woar2_s[h, 0:64], s_)))
            Y(lambda e, h=h: sD.dma(e.dma_start(
                c.woar2_s[h, 64:65], wstgA[0:1, 0:DM // 2].bitcast(BF16))))
            Y(lambda e, h=h, s_=hi_sl: sD.dma(
                e.dma_start(c.woar2_s[h, 65:128], s_[0:63, :])))
        for ri, si in enumerate([0, 1, 0, 2, 3, 2]):
            Y(lambda e, ri=ri, si=si: sD.dma(e.dma_start(
                c.jarr_s[ri], jrows[si:si + 1, :])))
        for ri in (6, 7):
            Y(lambda e, ri=ri: sD.dma(e.dma_start(c.jarr_s[ri], onesrow[:])))
        dn = sD.n
        Y(lambda e, v=dn: e.wait_ge(sD.h, v))
        sch.run(nc)


def _emit_b1(nc, c, r, sems):
    sD, sG, sT, sV, sSc, sQ, sO = sems
    ph = contextlib.ExitStack()
    with ph:
        sch = Sched()
        Y = mk_adder(sch.q["y"])
        V = mk_adder(sch.q["v"], track=True)
        T = mk_adder(sch.q["t"])
        SC = mk_adder(sch.q["s"])

        def sbP(name, shape, dtyp=F32):
            return ph.enter_context(nc.sbuf_tensor(name, shape, dtyp))

        bvbc = sbP("b1_bvbc", [128, DM])
        wkh = sbP("wkh", [128, NCH * DM], BF16)
        wkl = sbP("wkl", [128, NCH * DM], BF16)
        wvh = sbP("wvh", [128, NCH * DM], BF16)
        wvl = sbP("wvl", [128, NCH * DM], BF16)
        wstg = sbP("b1_wstg", [128, DM])
        xst = [sbP(f"xst{i}", [128, DM]) for i in range(2)]
        sqt = [sbP(f"sqt{i}", [128, DM]) for i in range(2)]
        lnst = sbP("lnst", [128, 8])
        xnt = [sbP(f"xnt{i}", [128, DM]) for i in range(2)]
        xqo = sbP("xqo", [128, DM], BF16)
        qsA = sbP("qsA", [128, 64])
        qsB = sbP("qsB", [128, 64])
        qsC = sbP("qsC", [128, 64])
        qsD = sbP("qsD", [128, 64])
        qtA = sbP("qtA", [128, DM])
        qtB = sbP("qtB", [128, DM])
        xhi = [sbP(f"xhi{i}", [128, 128], BF16) for i in range(2)]
        xlo = [sbP(f"xlo{i}", [128, 128], BF16) for i in range(2)]
        xnfh = sbP("xnfh", [128, NCH * 512], BF16)
        xnfl = sbP("xnfl", [128, NCH * 512], BF16)
        kevt = sbP("kevt", [128, 512])
        kevh = [sbP(f"kevh{i}", [128, 512], BF16) for i in range(2)]
        kevl = [sbP(f"kevl{i}", [128, 512], BF16) for i in range(2)]
        ksq = sbP("ksq", [128, 512], BF16)
        k2st = sbP("k2st", [1, 4])
        xqstg = [sbP(f"xqstg{i}", [128, 128], BF16) for i in range(2)]
        vevh = [sbP(f"vevh{i}", [128, 512], BF16) for i in range(2)]
        vevl = [sbP(f"vevl{i}", [128, 512], BF16) for i in range(2)]
        vstg = [sbP(f"vstg{i}", [128, 8 * 128], BF16) for i in range(2)]
        xbsum = sbP("xbsum", [128, 512])
        ptr_ = ph.enter_context(nc.psum_tensor("ptr_", [128, 128], F32))
        pk = ph.enter_context(nc.psum_tensor("pk", [128, 512], F32))
        pvp = ph.enter_context(nc.psum_tensor("pvp", [128, 512], F32))
        pk2a = ph.enter_context(nc.psum_tensor("pk2a", [1, 512], F32))
        pk2b = ph.enter_context(nc.psum_tensor("pk2b", [1, 512], F32))
        pxq = ph.enter_context(nc.psum_tensor("pxq", [128, 128], BF16))

        Y(lambda e: sD.dma(e.dma_start(bvbc[:], c.bv_bc_in)))
        vprev = 0
        for wi, (win, whi, wlo) in enumerate([(c.wk_in, wkh, wkl),
                                              (c.wv_in, wvh, wvl)]):
            for ch in range(NCH):
                if wi + ch > 0:
                    Y(lambda e, v=vprev: e.wait_ge(sV.h, v))
                Y(lambda e, win=win, ch=ch: sD.dma(
                    e.dma_start(wstg[:], win[ch * 128:(ch + 1) * 128, :])))
                dn = sD.n
                V(lambda e, v=dn: e.wait_ge(sD.h, v))
                V(lambda e, whi=whi, ch=ch: e.tensor_copy(
                    whi[:, ch * DM:(ch + 1) * DM], wstg[:]))
                V(lambda e, whi=whi, wlo=wlo, ch=ch: sV.inc(e.tensor_tensor(
                    wlo[:, ch * DM:(ch + 1) * DM], wstg[:],
                    whi[:, ch * DM:(ch + 1) * DM], ALU.subtract)))
                vprev = sV.n
        for i in range(2):
            V(lambda e, i=i: e.memset(vstg[i][:], 0.0))
            for hh in range(8):
                V(lambda e, i=i, hh=hh: e.memset(
                    vstg[i][:, hh * 128 + 64:hh * 128 + 65], 1.0))
        V(lambda e: sV.inc(e.memset(lnst[:], 0.0)))

        last_kv_mm = sT.n
        vln = {}
        dma_mark = {}
        vq = 0
        vxq = 0
        v_xqev = 0
        v_ptr = 0
        for blk in range(NB):
            V(lambda e, v=last_kv_mm: e.wait_ge(sT.h, v))
            for tsub in range(4):
                idx = blk * 4 + tsub
                tok0 = blk * 512 + tsub * 128
                par = idx % 2
                xt, sq, xn_ = xst[par], sqt[par], xnt[par]
                if idx >= 2:
                    Y(lambda e, v=vln.get(idx - 2, 0): e.wait_ge(sV.h, v))
                Y(lambda e, xt=xt, tok0=tok0: sD.dma(
                    e.dma_start(xt[:], c.x_in[tok0:tok0 + 128, :])))
                dn = sD.n
                SC(lambda e, v=dn: e.wait_ge(sD.h, v))
                if idx >= 2:
                    SC(lambda e, v=vln.get(idx - 2, 0): e.wait_ge(sV.h, v))
                SC(lambda e, xt=xt, sq=sq: sSc.inc(
                    e.activation(sq[:], xt[:], ACT.Square)))
                ssq = sSc.n
                V(lambda e, v=dn: e.wait_ge(sD.h, v))
                V(lambda e, v=ssq: e.wait_ge(sSc.h, v))
                V(lambda e, xt=xt: e.reduce_sum(lnst[:, 0:1], xt[:], AX.X))
                V(lambda e, sq=sq: e.reduce_sum(lnst[:, 1:2], sq[:], AX.X))
                V(lambda e: e.tensor_scalar(lnst[:, 2:3], lnst[:, 0:1],
                                            1.0 / DM, None, ALU.mult))
                V(lambda e: e.tensor_scalar(lnst[:, 3:4], lnst[:, 1:2],
                                            1.0 / DM, None, ALU.mult))
                V(lambda e: e.tensor_tensor(lnst[:, 4:5], lnst[:, 2:3],
                                            lnst[:, 2:3], ALU.mult))
                V(lambda e: e.tensor_tensor(lnst[:, 5:6], lnst[:, 3:4],
                                             lnst[:, 4:5], ALU.subtract))
                V(lambda e: sV.inc(e.tensor_scalar(lnst[:, 5:6], lnst[:, 5:6],
                                                   EPS, None, ALU.add)))
                vv = sV.n
                SC(lambda e, v=vv: e.wait_ge(sV.h, v))
                SC(lambda e: sSc.inc(e.activation(lnst[:, 6:7], lnst[:, 5:6],
                                                  ACT.Sqrt)))
                srt = sSc.n
                V(lambda e, v=srt: e.wait_ge(sSc.h, v))
                V(lambda e: e.reciprocal(lnst[:, 7:8], lnst[:, 6:7]))
                V(lambda e, xt=xt, xn_=xn_: sV.inc(e.tensor_scalar(
                    xn_[:], xt[:], lnst[:, 2:3], lnst[:, 7:8],
                    ALU.subtract, ALU.mult)))
                vln[idx] = sV.n
                if blk < 4:
                    V(lambda e, v=vln[idx]: e.wait_ge(sV.h, v))
                    if vxq:
                        V(lambda e, v=vxq: e.wait_ge(sT.h, v))
                    last = emit_nvfp4(V, xn_[:], xqo[:], qsA[:], qsB[:], qsC[:],
                                      qsD[:], qtA[:], qtB[:], r.scbc[:, 5:6])
                    V(lambda e, last=last: sV.inc(last(e)))
                    vq = sV.n
                    for ch in range(NCH):
                        nxq = idx * NCH + ch
                        spar = nxq % 2
                        T(lambda e, v=vq: e.wait_ge(sV.h, v))
                        T(lambda e, v=v_xqev: e.wait_ge(sV.h, v))
                        T(lambda e, ch=ch: sT.inc(e.transpose(
                            pxq[:], xqo[:, ch * 128:ch * 128 + 128],
                            r.identb[:])))
                        tt = sT.n
                        col = ch * TOWN + blk * 512 + tsub * 128
                        V(lambda e, v=tt: e.wait_ge(sT.h, v))
                        if nxq >= 2:
                            V(lambda e, v=dma_mark.get(("xq", nxq - 2), 0):
                              e.wait_ge(sD.h, v))
                        V(lambda e, spar=spar: sV.inc(e.tensor_copy(
                            xqstg[spar][:], pxq[:])))
                        v_xqev = sV.n
                        Y(lambda e, v=v_xqev: e.wait_ge(sV.h, v))
                        Y(lambda e, col=col, spar=spar: sD.dma(e.dma_start(
                            c.xqf_s[:, col:col + 128], xqstg[spar][:])))
                        dma_mark[("xq", nxq)] = sD.n
                    vxq = sT.n
                    vln[idx] = sV.n
                vlni = vln[idx]
                for ch in range(NCH):
                    n_tr = idx * NCH + ch
                    cpar = n_tr % 2
                    T(lambda e, v=vlni: e.wait_ge(sV.h, v))
                    if n_tr >= 1:
                        T(lambda e, v=v_ptr: e.wait_ge(sV.h, v))
                    T(lambda e, ch=ch, xn_=xn_: sT.inc(e.transpose(
                        ptr_[:], xn_[:, ch * 128:(ch + 1) * 128], r.ident[:])))
                    ttr = sT.n
                    V(lambda e, v=ttr: e.wait_ge(sT.h, v))
                    if n_tr >= 2:
                        V(lambda e, v=dma_mark.get(("x", n_tr - 2), 0):
                          e.wait_ge(sD.h, v))
                    V(lambda e, cpar=cpar: e.tensor_copy(xhi[cpar][:], ptr_[:]))
                    V(lambda e, cpar=cpar: sV.inc(e.tensor_tensor(
                        xlo[cpar][:], ptr_[:], xhi[cpar][:], ALU.subtract)))
                    v_ptr = sV.n
                    col = ch * 512 + tsub * 128
                    V(lambda e, col=col, cpar=cpar: e.tensor_copy(
                        xnfh[:, col:col + 128], xhi[cpar][:]))
                    V(lambda e, col=col, cpar=cpar: sV.inc(e.tensor_copy(
                        xnfl[:, col:col + 128], xlo[cpar][:])))
                    vst2 = sV.n
                    Y(lambda e, v=vst2: e.wait_ge(sV.h, v))
                    Y(lambda e, ch=ch, tok0=tok0, cpar=cpar: sD.dma(e.dma_start(
                        c.xnh_s[ch, :, tok0:tok0 + 128], xhi[cpar][:])))
                    Y(lambda e, ch=ch, tok0=tok0, cpar=cpar: sD.dma(e.dma_start(
                        c.xnl_s[ch, :, tok0:tok0 + 128], xlo[cpar][:])))
                    dma_mark[("x", n_tr)] = sD.n
            v_stageA = sV.n
            for ch in range(NCH):
                V(lambda e, ch=ch: e.tensor_tensor(
                    xbsum[:], xnfh[:, ch * 512:(ch + 1) * 512],
                    xnfl[:, ch * 512:(ch + 1) * 512], ALU.add))
                V(lambda e: e.reduce_sum(lnst[:, 0:1], xbsum[:], AX.X))
                V(lambda e, ch=ch: sV.inc(e.tensor_tensor(
                    r.xbar[:, ch:ch + 1], r.xbar[:, ch:ch + 1],
                    lnst[:, 0:1], ALU.add)))
            v_xnf = sV.n

            v_kev = sV.n
            for m in range(NCH):
                kpar = m % 2
                T(lambda e, v=v_xnf: e.wait_ge(sV.h, v))
                T(lambda e, v=v_kev: e.wait_ge(sV.h, v))
                for ch in range(NCH):
                    w0 = ch * DM + m * 128
                    x0 = ch * 512
                    T(lambda e, w0=w0, x0=x0, st=(ch == 0): e.matmul(
                        pk[:], wkh[:, w0:w0 + 128], xnfh[:, x0:x0 + 512],
                        start=st, stop=False))
                    T(lambda e, w0=w0, x0=x0: e.matmul(
                        pk[:], wkl[:, w0:w0 + 128], xnfh[:, x0:x0 + 512],
                        start=False, stop=False))
                    if ch == NCH - 1:
                        T(lambda e, w0=w0, x0=x0: sT.inc(e.matmul(
                            pk[:], wkh[:, w0:w0 + 128], xnfl[:, x0:x0 + 512],
                            start=False, stop=True)))
                    else:
                        T(lambda e, w0=w0, x0=x0: e.matmul(
                            pk[:], wkh[:, w0:w0 + 128], xnfl[:, x0:x0 + 512],
                            start=False, stop=False))
                tk = sT.n
                V(lambda e, v=tk: e.wait_ge(sT.h, v))
                if blk * NCH + m >= 2:
                    V(lambda e, v=dma_mark.get(("k", blk * NCH + m - 2), 0):
                      e.wait_ge(sD.h, v))
                V(lambda e, m=m: e.tensor_scalar(
                    kevt[:], pk[:], r.biases[:, NCH + m:NCH + m + 1], None, ALU.add))
                V(lambda e, kpar=kpar: e.tensor_copy(kevh[kpar][:], kevt[:]))
                V(lambda e, kpar=kpar: e.tensor_tensor(
                    kevl[kpar][:], kevt[:], kevh[kpar][:], ALU.subtract))
                V(lambda e: e.tensor_tensor(kevt[:], kevt[:], kevt[:], ALU.mult))
                V(lambda e: sV.inc(e.tensor_copy(ksq[:], kevt[:])))
                v_kev = sV.n
                T(lambda e, v=v_kev: e.wait_ge(sV.h, v))
                T(lambda e: sT.inc(e.matmul(pk2a[:], r.ones128b[0:64, 0:1],
                                            ksq[0:64, :], start=True, stop=True)))
                T(lambda e: sT.inc(e.matmul(pk2b[:], r.ones128b[64:128, 0:1],
                                            ksq[64:128, :], start=True, stop=True)))
                tk2 = sT.n
                V(lambda e, v=tk2: e.wait_ge(sT.h, v))
                h0, h1 = 2 * m, 2 * m + 1
                V(lambda e: e.tensor_reduce(k2st[0:1, 0:1], pk2a[:], AX.X, ALU.max))
                V(lambda e: e.tensor_reduce(k2st[0:1, 1:2], pk2b[:], AX.X, ALU.max))
                V(lambda e, h0=h0: e.tensor_tensor(
                    r.kmax2[0:1, h0:h0 + 1], r.kmax2[0:1, h0:h0 + 1],
                    k2st[0:1, 0:1], ALU.max))
                V(lambda e, h1=h1: sV.inc(e.tensor_tensor(
                    r.kmax2[0:1, h1:h1 + 1], r.kmax2[0:1, h1:h1 + 1],
                    k2st[0:1, 1:2], ALU.max)))
                v_kev = sV.n
                t0 = blk * 512
                Y(lambda e, v=v_kev: e.wait_ge(sV.h, v))
                Y(lambda e, h0=h0, t0=t0, kpar=kpar: sD.dma(e.dma_start(
                    c.karr_s[h0, 0:64, t0:t0 + 512], kevh[kpar][0:64, :])))
                Y(lambda e, h0=h0, t0=t0, kpar=kpar: sD.dma(e.dma_start(
                    c.karr_s[h0, 64:128, t0:t0 + 512], kevl[kpar][0:64, :])))
                Y(lambda e, h1=h1, t0=t0, kpar=kpar: sD.dma(e.dma_start(
                    c.karr_s[h1, 0:64, t0:t0 + 512], kevh[kpar][64:128, :])))
                Y(lambda e, h1=h1, t0=t0, kpar=kpar: sD.dma(e.dma_start(
                    c.karr_s[h1, 64:128, t0:t0 + 512], kevl[kpar][64:128, :])))
                dma_mark[("k", blk * NCH + m)] = sD.n

            v_vev = sV.n
            for tsub in range(4):
                t0 = blk * 512 + tsub * 128
                for ot in range(2):
                    rnd = (blk * 4 + tsub) * 2 + ot
                    vpar = rnd % 2
                    T(lambda e, v=v_vev: e.wait_ge(sV.h, v))
                    for ch in range(NCH):
                        x0 = ch * 512 + tsub * 128
                        w0 = ch * DM + ot * 512
                        T(lambda e, x0=x0, w0=w0, st=(ch == 0): e.matmul(
                            pvp[:], xnfh[:, x0:x0 + 128], wvh[:, w0:w0 + 512],
                            start=st, stop=False))
                        T(lambda e, x0=x0, w0=w0: e.matmul(
                            pvp[:], xnfl[:, x0:x0 + 128], wvh[:, w0:w0 + 512],
                            start=False, stop=False))
                        if ch == NCH - 1:
                            T(lambda e, x0=x0, w0=w0: sT.inc(e.matmul(
                                pvp[:], xnfh[:, x0:x0 + 128], wvl[:, w0:w0 + 512],
                                start=False, stop=True)))
                        else:
                            T(lambda e, x0=x0, w0=w0: e.matmul(
                                pvp[:], xnfh[:, x0:x0 + 128], wvl[:, w0:w0 + 512],
                                start=False, stop=False))
                    tv = sT.n
                    V(lambda e, v=tv: e.wait_ge(sT.h, v))
                    if rnd >= 2:
                        V(lambda e, v=dma_mark.get(("v", rnd - 2), 0):
                          e.wait_ge(sD.h, v))
                    V(lambda e, ot=ot: e.tensor_tensor(
                        kevt[:], pvp[:], bvbc[:, ot * 512:ot * 512 + 512], ALU.add))
                    V(lambda e, vpar=vpar: e.tensor_copy(vevh[vpar][:], kevt[:]))
                    V(lambda e, vpar=vpar: sV.inc(e.tensor_tensor(
                        vevl[vpar][:], kevt[:], vevh[vpar][:], ALU.subtract)))
                    for hh in range(8):
                        V(lambda e, hh=hh, vpar=vpar: e.tensor_copy(
                            vstg[vpar][:, hh * 128:hh * 128 + 64],
                            vevh[vpar][:, hh * 64:hh * 64 + 64]))
                        V(lambda e, hh=hh, vpar=vpar: sV.inc(e.tensor_copy(
                            vstg[vpar][:, hh * 128 + 65:hh * 128 + 128],
                            vevl[vpar][:, hh * 64:hh * 64 + 63])))
                    v_vev = sV.n
                    Y(lambda e, v=v_vev: e.wait_ge(sV.h, v))
                    for hh in range(8):
                        h = ot * 8 + hh
                        Y(lambda e, h=h, hh=hh, t0=t0, vpar=vpar: sD.dma(e.dma_start(
                            c.varr_s[h, t0:t0 + 128, :],
                            vstg[vpar][:, hh * 128:hh * 128 + 128])))
                    dma_mark[("v", rnd)] = sD.n

            last_kv_mm = sT.n
        sch.run(nc)
def _emit_b2(nc, c, r, sems, wq_builder):
    sD, sG, sT, sV, sSc, sQ, sO = sems
    ph = contextlib.ExitStack()
    with ph:
        sch = Sched()
        Y = mk_adder(sch.q["y"])
        V = mk_adder(sch.q["v"], track=True)
        T = mk_adder(sch.q["t"])
        SC = mk_adder(sch.q["s"])

        def sbP(name, shape, dtyp=F32):
            return ph.enter_context(nc.sbuf_tensor(name, shape, dtyp))

        wqh = sbP("wqh", [128, NCH * DM], BF16)
        wql = sbP("wql", [128, NCH * DM], BF16)
        wfqh = sbP("wfqh", [128, NCH * DM], BF16)
        wfql = sbP("wfql", [128, NCH * DM], BF16)
        wstg = sbP("b2_wstg", [128, DM])
        wfgh = sbP("wfgh", [128, DM], BF16)
        wfgl = sbP("wfgl", [128, DM], BF16)
        xrh = [sbP(f"xrh{i}", [128, 512], BF16) for i in range(2)]
        xrl = [sbP(f"xrl{i}", [128, 512], BF16) for i in range(2)]
        qevt = sbP("qevt", [128, 512])
        qhi = [sbP(f"qhi{i}", [128, 512], BF16) for i in range(2)]
        qlo = [sbP(f"qlo{i}", [128, 512], BF16) for i in range(2)]
        qsqt = sbP("qsqt", [128, 512], BF16)
        xbh = sbP("xbh", [128, NCH], BF16)
        xbl = sbP("xbl", [128, NCH], BF16)
        glof = sbP("glof", [128, NCH])
        gloh = sbP("gloh", [128, NCH], BF16)
        glol = sbP("glol", [128, NCH], BF16)
        bpp = sbP("bpp", [128, NCH])
        ctf = sbP("ctf", [128, 512])
        cth = sbP("cth", [128, 512], BF16)
        ctl = sbP("ctl", [128, 512], BF16)
        przA = sbP("przA", [1, 512])
        przB = sbP("przB", [1, 512])
        przC = sbP("przC", [1, 512])
        featq = sbP("featq", [1, 3 * 1024])
        featw = sbP("featw", [1, 6 * 1024])
        featb = sbP("featb", [1, 8 * 1024], BF16)
        kmaxr = sbP("kmaxr", [1, H])
        pq = ph.enter_context(nc.psum_tensor("pq", [128, 512], F32))
        pcp = ph.enter_context(nc.psum_tensor("pcp", [128, 512], F32))
        pbp = ph.enter_context(nc.psum_tensor("pbp", [128, NCH], F32))
        ppq = ph.enter_context(nc.psum_tensor("ppq", [1, 512], F32))
        pzq = ph.enter_context(nc.psum_tensor("pzq", [1, 512], F32))
        pq2a = ph.enter_context(nc.psum_tensor("pq2a", [1, 512], F32))
        pq2b = ph.enter_context(nc.psum_tensor("pq2b", [1, 512], F32))

        vprev = 0
        for wi, (win, whi, wlo) in enumerate([(c.wq_in, wqh, wql),
                                              (c.wfq_in, wfqh, wfql)]):
            for ch in range(NCH):
                if wi + ch > 0:
                    Y(lambda e, v=vprev: e.wait_ge(sV.h, v))
                Y(lambda e, win=win, ch=ch: sD.dma(
                    e.dma_start(wstg[:], win[ch * 128:(ch + 1) * 128, :])))
                dn = sD.n
                V(lambda e, v=dn: e.wait_ge(sD.h, v))
                if wi == 1:
                    V(lambda e: e.tensor_scalar(wstg[:], wstg[:], 8.0, None,
                                                ALU.mult))
                V(lambda e, whi=whi, ch=ch: e.tensor_copy(
                    whi[:, ch * DM:(ch + 1) * DM], wstg[:]))
                V(lambda e, whi=whi, wlo=wlo, ch=ch: sV.inc(e.tensor_tensor(
                    wlo[:, ch * DM:(ch + 1) * DM], wstg[:],
                    whi[:, ch * DM:(ch + 1) * DM], ALU.subtract)))
                vprev = sV.n

        V(lambda e: e.tensor_scalar(r.xbar[:], r.xbar[:], 1.0 / S, None, ALU.mult))
        V(lambda e: e.tensor_copy(xbh[:], r.xbar[:]))
        V(lambda e: sV.inc(e.tensor_tensor(xbl[:], r.xbar[:], xbh[:], ALU.subtract)))
        v_xb = sV.n

        T(lambda e, v=v_xb: e.wait_ge(sV.h, v))
        for m in range(NCH):
            for ch in range(NCH):
                w0 = ch * DM + m * 128
                T(lambda e, w0=w0, ch=ch, m=m, st=(m == 0 and ch == 0): e.matmul(
                    pbp[:, m:m + 1],
                    wqh[:, w0:w0 + 128], xbh[:, ch:ch + 1],
                    start=st, stop=False))
                T(lambda e, w0=w0, ch=ch, m=m: e.matmul(
                    pbp[:, m:m + 1], wql[:, w0:w0 + 128], xbh[:, ch:ch + 1],
                    start=False, stop=False))
                T(lambda e, w0=w0, ch=ch, m=m: (sT.inc(e.matmul(
                    pbp[:, m:m + 1], wqh[:, w0:w0 + 128], xbl[:, ch:ch + 1],
                    start=False, stop=True))
                    if (ch == NCH - 1 and m == NCH - 1) else e.matmul(
                    pbp[:, m:m + 1], wqh[:, w0:w0 + 128], xbl[:, ch:ch + 1],
                    start=False, stop=False)))
        t_glo = sT.n
        V(lambda e, v=t_glo: e.wait_ge(sT.h, v))
        V(lambda e: e.tensor_tensor(glof[:], pbp[:], r.biases[:, 0:NCH], ALU.add))
        V(lambda e: e.tensor_copy(gloh[:], glof[:]))
        V(lambda e: sV.inc(e.tensor_tensor(glol[:], glof[:], gloh[:], ALU.subtract)))
        v_glo = sV.n

        v_fgs = v_glo
        for ch in range(NCH):
            Y(lambda e, v=v_fgs: e.wait_ge(sV.h, v))
            Y(lambda e, ch=ch: sD.dma(e.dma_start(
                wstg[:], c.wfg_in[ch * 128:(ch + 1) * 128, :])))
            dn = sD.n
            V(lambda e, v=dn: e.wait_ge(sD.h, v))
            V(lambda e: e.tensor_copy(wfgh[:], wstg[:]))
            V(lambda e: sV.inc(e.tensor_tensor(wfgl[:], wstg[:], wfgh[:],
                                               ALU.subtract)))
            v_fgs = sV.n
            T(lambda e, v=v_fgs: e.wait_ge(sV.h, v))
            if ch == 0:
                T(lambda e, v=v_glo: e.wait_ge(sV.h, v))
            for m in range(NCH):
                w0 = m * 128
                T(lambda e, w0=w0, ch=ch, m=m, st=(ch == 0 and m == 0): e.matmul(
                    pq[:, m:m + 1], wfgh[:, w0:w0 + 128], gloh[:, ch:ch + 1],
                    start=st, stop=False))
                T(lambda e, w0=w0, ch=ch, m=m: e.matmul(
                    pq[:, m:m + 1], wfgl[:, w0:w0 + 128], gloh[:, ch:ch + 1],
                    start=False, stop=False))
                T(lambda e, w0=w0, ch=ch, m=m: (sT.inc(e.matmul(
                    pq[:, m:m + 1], wfgh[:, w0:w0 + 128], glol[:, ch:ch + 1],
                    start=False, stop=True))
                    if (ch == NCH - 1 and m == NCH - 1) else e.matmul(
                    pq[:, m:m + 1], wfgh[:, w0:w0 + 128], glol[:, ch:ch + 1],
                    start=False, stop=False)))
        t_bpp = sT.n
        V(lambda e, v=t_bpp: e.wait_ge(sT.h, v))
        V(lambda e: e.tensor_tensor(bpp[:], pq[:, 0:NCH], r.biases[:, 3 * NCH:4 * NCH],
                                    ALU.add))
        V(lambda e: sV.inc(e.tensor_tensor(bpp[:], bpp[:],
                                           r.biases[:, 2 * NCH:3 * NCH], ALU.add)))
        v_bpp = sV.n

        dma_q = {}
        qk_mm = {}
        v_qev = sV.n
        for qch in range(4):
            q0 = qch * 512
            for m in range(NCH):
                rnd = qch * NCH + m
                for ch in range(NCH):
                    par = ch % 2
                    nld = rnd * NCH + ch
                    if nld >= 2:
                        Y(lambda e, v=qk_mm.get(nld - 2, 0): e.wait_ge(sT.h, v))
                    Y(lambda e, ch=ch, q0=q0, par=par: sD.dma(e.dma_start(
                        xrh[par][:], c.xnh_s[ch, :, q0:q0 + 512])))
                    Y(lambda e, ch=ch, q0=q0, par=par: sD.dma(e.dma_start(
                        xrl[par][:], c.xnl_s[ch, :, q0:q0 + 512])))
                    dn = sD.n
                    w0 = ch * DM + m * 128
                    T(lambda e, v=dn: e.wait_ge(sD.h, v))
                    if ch == 0:
                        T(lambda e, v=v_qev: e.wait_ge(sV.h, v))
                    T(lambda e, w0=w0, par=par, st=(ch == 0): e.matmul(
                        pq[:], wqh[:, w0:w0 + 128], xrh[par][:],
                        start=st, stop=False))
                    T(lambda e, w0=w0, par=par: e.matmul(
                        pq[:], wql[:, w0:w0 + 128], xrh[par][:],
                        start=False, stop=False))
                    T(lambda e, w0=w0, par=par, sp=(ch == NCH - 1): sT.inc(
                        e.matmul(pq[:], wqh[:, w0:w0 + 128], xrl[par][:],
                                 start=False, stop=sp)))
                    qk_mm[nld] = sT.n
                tq = sT.n
                qpar = rnd % 2
                V(lambda e, v=tq: e.wait_ge(sT.h, v))
                if rnd >= 2:
                    V(lambda e, v=dma_q.get(rnd - 2, 0): e.wait_ge(sD.h, v))
                V(lambda e, m=m: e.tensor_scalar(
                    qevt[:], pq[:], r.biases[:, m:m + 1], 0.125, ALU.add, ALU.mult))
                V(lambda e, qpar=qpar: e.tensor_copy(qhi[qpar][:], qevt[:]))
                V(lambda e, qpar=qpar: e.tensor_tensor(
                    qlo[qpar][:], qevt[:], qhi[qpar][:], ALU.subtract))
                V(lambda e: e.tensor_tensor(qevt[:], qevt[:], qevt[:], ALU.mult))
                V(lambda e: sV.inc(e.tensor_copy(qsqt[:], qevt[:])))
                v_qev = sV.n
                T(lambda e, v=v_qev: e.wait_ge(sV.h, v))
                T(lambda e: sT.inc(e.matmul(pq2a[:], r.ones128b[0:64, 0:1],
                                            qsqt[0:64, :], start=True, stop=True)))
                T(lambda e: sT.inc(e.matmul(pq2b[:], r.ones128b[64:128, 0:1],
                                            qsqt[64:128, :], start=True, stop=True)))
                tq2 = sT.n
                h0, h1 = 2 * m, 2 * m + 1
                V(lambda e, v=tq2: e.wait_ge(sT.h, v))
                V(lambda e, v=dma_q.get(rnd - 1, 0): e.wait_ge(sD.h, v))
                V(lambda e: e.tensor_copy(przA[:], pq2a[:]))
                V(lambda e: sV.inc(e.tensor_copy(przB[:], pq2b[:])))
                v_qev = sV.n
                Y(lambda e, v=v_qev: e.wait_ge(sV.h, v))
                Y(lambda e, h0=h0, q0=q0: sD.dma(e.dma_start(
                    c.q2n_s[h0, q0:q0 + 512], przA[:])))
                Y(lambda e, h1=h1, q0=q0: sD.dma(e.dma_start(
                    c.q2n_s[h1, q0:q0 + 512], przB[:])))
                Y(lambda e, m=m, q0=q0, qpar=qpar: sD.dma(e.dma_start(
                    c.mqh_s[m, :, q0:q0 + 512], qhi[qpar][:])))
                Y(lambda e, m=m, q0=q0, qpar=qpar: sD.dma(e.dma_start(
                    c.mql_s[m, :, q0:q0 + 512], qlo[qpar][:])))
                dma_q[rnd] = sD.n

        dma_c = {}
        c_mm = {}
        v_cev = sSc.n
        v_ctf = sV.n
        v_pzr = sSc.n
        for qch in range(4):
            q0 = qch * 512
            for m in range(NCH):
                rnd = qch * NCH + m
                for ch in range(NCH):
                    par = ch % 2
                    nld = rnd * NCH + ch
                    if nld >= 2:
                        Y(lambda e, v=c_mm.get(nld - 2, 0): e.wait_ge(sT.h, v))
                    Y(lambda e, ch=ch, q0=q0, par=par: sD.dma(e.dma_start(
                        xrh[par][:], c.mqh_s[ch, :, q0:q0 + 512])))
                    Y(lambda e, ch=ch, q0=q0, par=par: sD.dma(e.dma_start(
                        xrl[par][:], c.mql_s[ch, :, q0:q0 + 512])))
                    dn = sD.n
                    w0 = ch * DM + m * 128
                    T(lambda e, v=dn: e.wait_ge(sD.h, v))
                    if ch == 0:
                        T(lambda e, v=v_cev: e.wait_ge(sSc.h, v))
                        T(lambda e, v=v_pzr: e.wait_ge(sSc.h, v))
                    T(lambda e, w0=w0, par=par, st=(ch == 0): e.matmul(
                        pcp[:], wfqh[:, w0:w0 + 128], xrh[par][:],
                        start=st, stop=False))
                    T(lambda e, w0=w0, par=par: e.matmul(
                        pcp[:], wfql[:, w0:w0 + 128], xrh[par][:],
                        start=False, stop=False))
                    T(lambda e, w0=w0, par=par, sp=(ch == NCH - 1): sT.inc(
                        e.matmul(pcp[:], wfqh[:, w0:w0 + 128], xrl[par][:],
                                 start=False, stop=sp)))
                    c_mm[nld] = sT.n
                tc = sT.n
                SC(lambda e, v=tc: e.wait_ge(sT.h, v))
                SC(lambda e, v=v_ctf: e.wait_ge(sV.h, v))
                SC(lambda e, m=m: sSc.inc(e.activation(
                    ctf[:], pcp[:], ACT.Tanh, bias=bpp[:, m:m + 1])))
                v_cev = sSc.n
                V(lambda e, v=v_cev: e.wait_ge(sSc.h, v))
                V(lambda e: e.tensor_copy(cth[:], ctf[:]))
                V(lambda e: sV.inc(e.tensor_tensor(ctl[:], ctf[:], cth[:],
                                                   ALU.subtract)))
                v_ctf = sV.n
                for hh in range(2):
                    h = 2 * m + hh
                    b0_ = hh * 64
                    T(lambda e, v=v_ctf: e.wait_ge(sV.h, v))
                    T(lambda e, v=v_pzr: e.wait_ge(sSc.h, v))
                    for ps_, col in ((ppq, h), (pzq, H + h)):
                        T(lambda e, ps_=ps_, col=col, b0_=b0_: e.matmul(
                            ps_[:], r.upzh[b0_:b0_ + 64, col:col + 1],
                            cth[b0_:b0_ + 64, :], start=True, stop=False))
                        T(lambda e, ps_=ps_, col=col, b0_=b0_: e.matmul(
                            ps_[:], r.upzl[b0_:b0_ + 64, col:col + 1],
                            cth[b0_:b0_ + 64, :], start=False, stop=False))
                        T(lambda e, ps_=ps_, col=col, b0_=b0_: sT.inc(e.matmul(
                            ps_[:], r.upzh[b0_:b0_ + 64, col:col + 1],
                            ctl[b0_:b0_ + 64, :], start=False, stop=True)))
                    tpz = sT.n
                    SC(lambda e, v=tpz: e.wait_ge(sT.h, v))
                    SC(lambda e, v=dma_c.get(("pz", rnd, hh - 1) if hh else ("pz", rnd - 1, 1), 0):
                       e.wait_ge(sD.h, v))
                    SC(lambda e: e.activation(przB[:], ppq[:], ACT.Sigmoid))
                    SC(lambda e: sSc.inc(e.activation(przC[:], pzq[:], ACT.Sigmoid)))
                    v_pzr = sSc.n
                    Y(lambda e, v=v_pzr: e.wait_ge(sSc.h, v))
                    Y(lambda e, h=h, q0=q0: sD.dma(e.dma_start(
                        c.sgp_s[h, q0:q0 + 512], przB[:])))
                    Y(lambda e, h=h, q0=q0: sD.dma(e.dma_start(
                        c.sgz_s[h, q0:q0 + 512], przC[:])))
                    dma_c[("pz", rnd, hh)] = sD.n

        SC(lambda e: sSc.inc(e.activation(kmaxr[:], r.kmax2[:], ACT.Sqrt)))
        s_kx = sSc.n
        V(lambda e, v=s_kx: e.wait_ge(sSc.h, v))
        fW = 1024
        dprev = 0
        for h in range(H):
            for fh in range(2):
                f0 = fh * fW
                if h + fh >= 1:
                    Y(lambda e, v=dprev: e.wait_ge(sV.h, v))
                Y(lambda e, h=h, f0=f0: sD.dma(e.dma_start(
                    featq[0:1, 0:fW], c.sgp_s[h, f0:f0 + fW].unsqueeze(0))))
                Y(lambda e, h=h, f0=f0: sD.dma(e.dma_start(
                    featq[0:1, fW:2 * fW], c.sgz_s[h, f0:f0 + fW].unsqueeze(0))))
                Y(lambda e, h=h, f0=f0: sD.dma(e.dma_start(
                    featq[0:1, 2 * fW:3 * fW], c.q2n_s[h, f0:f0 + fW].unsqueeze(0))))
                dln = sD.n
                V(lambda e, v=dln: e.wait_ge(sD.h, v))
                V(lambda e: e.reciprocal(featw[0:1, 0:fW], featq[0:1, fW:2 * fW]))
                V(lambda e: e.tensor_tensor(featw[0:1, 0:fW], featw[0:1, 0:fW],
                                            featw[0:1, 0:fW], ALU.mult))
                V(lambda e: e.tensor_tensor(featw[0:1, fW:2 * fW],
                                            featq[0:1, 0:fW],
                                            featw[0:1, 0:fW], ALU.mult))
                V(lambda e: e.tensor_scalar(featw[0:1, 2 * fW:3 * fW],
                                            featw[0:1, fW:2 * fW],
                                            float(np.float32(4.0 / 8192.0)),
                                            None, ALU.mult))
                V(lambda e: e.tensor_scalar(featw[0:1, 3 * fW:4 * fW],
                                            featw[0:1, 0:fW],
                                            float(np.float32(-2.0 / (8192.0 * 8192.0))),
                                            None, ALU.mult))
                V(lambda e: e.tensor_tensor(featw[0:1, 4 * fW:5 * fW],
                                            featq[0:1, 0:fW],
                                            featw[0:1, fW:2 * fW], ALU.mult))
                V(lambda e: e.tensor_scalar(featw[0:1, 4 * fW:5 * fW],
                                            featw[0:1, 4 * fW:5 * fW], -2.0,
                                            None, ALU.mult))
                sq_in = featw[0:1, 5 * fW:6 * fW]
                V(lambda e: sV.inc(e.tensor_copy(sq_in, featq[0:1, 2 * fW:3 * fW])))
                vq2 = sV.n
                SC(lambda e, v=vq2: e.wait_ge(sV.h, v))
                SC(lambda e, sq_in=sq_in: sSc.inc(e.activation(sq_in, sq_in, ACT.Sqrt)))
                sq2 = sSc.n
                V(lambda e, v=sq2: e.wait_ge(sSc.h, v))
                V(lambda e, h=h: e.tensor_scalar(sq_in, sq_in,
                                                 kmaxr[0:1, h:h + 1], None, ALU.mult))
                V(lambda e: e.tensor_tensor(featw[0:1, 4 * fW:5 * fW],
                                            featw[0:1, 4 * fW:5 * fW], sq_in,
                                            ALU.subtract))
                for i_, src0 in ((0, 2 * fW), (1, 3 * fW), (2, 4 * fW)):
                    V(lambda e, i_=i_, s0=src0: e.tensor_copy(
                        featb[0:1, (2 * i_) * fW:(2 * i_) * fW + fW],
                        featw[0:1, s0:s0 + fW]))
                    V(lambda e, i_=i_, s0=src0: e.tensor_tensor(
                        featb[0:1, (2 * i_ + 1) * fW:(2 * i_ + 1) * fW + fW],
                        featw[0:1, s0:s0 + fW],
                        featb[0:1, (2 * i_) * fW:(2 * i_) * fW + fW], ALU.subtract))
                V(lambda e: sV.inc(e.engine_nop()))
                vfeat = sV.n
                dprev = vfeat
                Y(lambda e, v=vfeat: e.wait_ge(sV.h, v))
                rowmap = [0, 0, 1, 2, 2, 3, 4, 5]
                for ri, si in enumerate(rowmap):
                    Y(lambda e, h=h, ri=ri, si=si, f0=f0: sD.dma(e.dma_start(
                        c.qarr2_s[h, ri, f0:f0 + fW],
                        featb[0:1, si * fW:(si + 1) * fW])))
        v2l, y2l = wq_builder()
        sch.q["v"].extend(v2l)
        sch.q["y"].extend(y2l)
        dnf = sD.n
        Y(lambda e, v=dnf: e.wait_ge(sD.h, v))
        sch.run(nc)
def _emit_b3(nc, c, r, sems):
    sD, sG, sT, sV, sSc, sQ, sO = sems
    ph = contextlib.ExitStack()
    with ph:
        sch = Sched()
        Y = mk_adder(sch.q["y"])
        V = mk_adder(sch.q["v"], track=True)
        T = mk_adder(sch.q["t"])
        SC = mk_adder(sch.q["s"])

        def sbP(name, shape, dtyp=F32):
            return ph.enter_context(nc.sbuf_tensor(name, shape, dtyp))

        qR = [sbP(f"qR{i}", [128, TOWN], BF16) for i in range(2)]
        q2R = [sbP(f"q2R{i}", [72, TOWN], BF16) for i in range(2)]
        wo1R = [sbP(f"wo1R{i}", [128, DM], BF16) for i in range(2)]
        wo2R = [sbP(f"wo2R{i}", [128, DM], BF16) for i in range(2)]
        kR = [sbP(f"kR{i}", [128, 1024], BF16) for i in range(2)]
        k2R = [sbP(f"k2R{i}", [72, 1024], BF16) for i in range(2)]
        vR = [sbP(f"vR{i}", [128, 1024], BF16) for i in range(2)]
        ptmp = [sbP(f"ptmp{i}", [128, 1024]) for i in range(2)]
        pH = [sbP(f"pH{i}", [128, 1024], BF16) for i in range(2)]
        pL = [sbP(f"pL{i}", [128, 1024], BF16) for i in range(2)]
        sacc = sbP("sacc", [128, 16 * DM])
        S1 = sbP("S1", [128, TOWN], BF16)
        S2 = sbP("S2", [128, TOWN], BF16)
        dsb = sbP("dsb", [128, 512])
        drow = sbP("drow", [1, 512])
        rrh = sbP("rrh", [1, 512], BF16)
        rrl = sbP("rrl", [1, 512], BF16)
        rbsb = sbP("rbsb", [128, 512])
        aft = sbP("aft", [128, 512])
        a1t = sbP("a1t", [128, 512], BF16)
        a2t = sbP("a2t", [128, 512], BF16)
        pS = [ph.enter_context(nc.psum_tensor(f"pS{i}", [128, 1024], F32))
              for i in range(2)]
        pctx = [ph.enter_context(nc.psum_tensor(f"pctx{i}", [128, 512], F32))
                for i in range(2)]
        prb = ph.enter_context(nc.psum_tensor("prb", [128, 512], F32))
        pso = ph.enter_context(nc.psum_tensor("pso", [128, 512], F32))
        psoB = [pso, prb]

        KGRP = 8
        NG = 64 // KGRP

        q_load = {}
        ring_load = {}
        qk_mark = {}
        ex_mark = {}
        sp_mark = {}
        pv_mark = {}
        v_sacc = 0
        d_sacc_spill = 0

        def load_head(h, par):
            ch, rr = h // 2, (h % 2) * 64
            Y(lambda e, ch=ch, rr=rr, par=par: sD.dma(e.dma_start(
                qR[par][0:64, :], c.mqh_s[ch, rr:rr + 64, 0:TOWN])))
            Y(lambda e, ch=ch, rr=rr, par=par: sD.dma(e.dma_start(
                qR[par][64:128, :], c.mqh_s[ch, rr:rr + 64, 0:TOWN])))
            Y(lambda e, ch=ch, rr=rr, par=par: sD.dma(e.dma_start(
                q2R[par][0:64, :], c.mql_s[ch, rr:rr + 64, 0:TOWN])))
            Y(lambda e, h=h, par=par: sD.dma(e.dma_start(
                q2R[par][64:72, :], c.qarr2_s[h])))
            Y(lambda e, h=h, par=par: sD.dma(e.dma_start(
                wo1R[par][:], c.woar1_s[h])))
            Y(lambda e, h=h, par=par: sD.dma(e.dma_start(
                wo2R[par][:], c.woar2_s[h])))
            q_load[h] = sD.n

        def load_group(h, g, par, u=0):
            k0 = g * KGRP * 128
            Y(lambda e, h=h, k0=k0, par=par: sD.dma(e.dma_start(
                kR[par][:], c.karr_s[h, :, k0:k0 + 1024])))
            Y(lambda e, h=h, k0=k0, par=par: sD.dma(e.dma_start(
                k2R[par][0:64, :], c.karr_s[h, 0:64, k0:k0 + 1024])))
            Y(lambda e, k0=k0, par=par: sD.dma(e.dma_start(
                k2R[par][64:72, :], c.jarr_s[:, k0:k0 + 1024])))
            for kbl in range(KGRP):
                Y(lambda e, h=h, k0=k0, kbl=kbl, par=par: sD.dma(e.dma_start(
                    vR[par][:, kbl * 128:(kbl + 1) * 128],
                    c.varr_s[h, k0 + kbl * 128:k0 + kbl * 128 + 128, :])))
            ring_load[(h, g, u)] = sD.n

        def entry(j):
            hh = j // (2 * NG)
            uu = (j // NG) % 2
            gg = j % NG
            return hh, uu, gg

        NENT = H * 2 * NG
        load_head(0, 0)
        for j0 in range(2):
            hh, uu, gg = entry(j0)
            load_group(hh, gg, j0 % 2, uu)

        it = 0
        pend_pv = None
        oproj_mark = {}
        for h in range(H):
            hpar = h % 2
            if h + 1 < H:
                if h >= 1:
                    Y(lambda e, v=oproj_mark[h - 1]: e.wait_ge(sT.h, v))
                load_head(h + 1, (h + 1) % 2)
            for u in range(2):
                def emit_pv(pit, pb, pkb, pgpar, pj, pkbl):
                    T(lambda e, v=sp_mark[pit]: e.wait_ge(sV.h, v))
                    for qc in range(2):
                        T(lambda e, b=pb, qc=qc, kbl=pkbl, gpar=pgpar,
                          st=(pkb == 0): e.matmul(
                            pctx[qc][:], vR[gpar][:, kbl * 128:kbl * 128 + 128],
                            pH[b][:, qc * 512:qc * 512 + 512],
                            start=st, stop=False))
                        mm = lambda e, b=pb, qc=qc, kbl=pkbl, gpar=pgpar, \
                            sp=(pkb == 63): e.matmul(
                            pctx[qc][:], vR[gpar][:, kbl * 128:kbl * 128 + 128],
                            pL[b][:, qc * 512:qc * 512 + 512],
                            start=False, stop=sp)
                        if qc == 1:
                            T(lambda e, mm=mm: sT.inc(mm(e)))
                        else:
                            T(mm)
                    pv_mark[pit] = sT.n
                    if pkbl == KGRP - 1 and pj + 2 < NENT:
                        nh, nu, ng = entry(pj + 2)
                        Y(lambda e, v=pv_mark[pit]: e.wait_ge(sT.h, v))
                        load_group(nh, ng, pgpar, nu)

                for kb in range(64):
                    g, kbl = kb // KGRP, kb % KGRP
                    j = (h * 2 + u) * NG + g
                    gpar = j % 2
                    b = it % 2
                    if kbl == 0:
                        T(lambda e, v=ring_load[(h, g, u)]: e.wait_ge(sD.h, v))
                    if it >= 2:
                        T(lambda e, v=ex_mark[it - 2]: e.wait_ge(sSc.h, v))
                    for qc in range(2):
                        q0 = u * 1024 + qc * 512
                        T(lambda e, b=b, qc=qc, kbl=kbl, q0=q0, hpar=hpar,
                          gpar=gpar: e.matmul(
                              pS[b][:, qc * 512:qc * 512 + 512],
                              kR[gpar][:, kbl * 128:kbl * 128 + 128],
                              qR[hpar][:, q0:q0 + 512], start=True, stop=False))
                        mm = lambda e, b=b, qc=qc, kbl=kbl, q0=q0, hpar=hpar, \
                            gpar=gpar: e.matmul(
                                pS[b][:, qc * 512:qc * 512 + 512],
                                k2R[gpar][0:72, kbl * 128:kbl * 128 + 128],
                                q2R[hpar][0:72, q0:q0 + 512],
                                start=False, stop=True)
                        if qc == 1:
                            T(lambda e, mm=mm: sT.inc(mm(e)))
                        else:
                            T(mm)
                    qk_mark[it] = sT.n
                    SC(lambda e, v=qk_mark[it]: e.wait_ge(sT.h, v))
                    if it >= 2:
                        SC(lambda e, v=sp_mark[it - 2]: e.wait_ge(sV.h, v))
                    SC(lambda e, b=b: sSc.inc(e.activation(
                        ptmp[b][:], pS[b][:], ACT.Exp)))
                    ex_mark[it] = sSc.n
                    V(lambda e, v=ex_mark[it]: e.wait_ge(sSc.h, v))
                    if it >= 2:
                        V(lambda e, v=pv_mark[it - 2]: e.wait_ge(sT.h, v))
                    V(lambda e, b=b: e.tensor_copy(pH[b][:], ptmp[b][:]))
                    V(lambda e, b=b: sV.inc(e.tensor_tensor(
                        pL[b][:], ptmp[b][:], pH[b][:], ALU.subtract)))
                    sp_mark[it] = sV.n
                    if pend_pv is not None:
                        emit_pv(*pend_pv)
                    pend_pv = (it, b, kb, gpar, j, kbl)
                    it += 1
                if pend_pv is not None:
                    emit_pv(*pend_pv)
                    pend_pv = None
                for qc in range(2):
                    V(lambda e, v=pv_mark[it - 1]: e.wait_ge(sT.h, v))
                    V(lambda e, qc=qc: sV.inc(e.tensor_copy(
                        dsb[64:65, :], pctx[qc][64:65, :])))
                    vden = sV.n
                    Y(lambda e, v=vden: e.wait_ge(sV.h, v))
                    Y(lambda e: sD.dma(e.dma_start(c.den_s[0], dsb[64:65, :])))
                    dn = sD.n
                    Y(lambda e, v=dn: e.wait_ge(sD.h, v))
                    Y(lambda e: sD.dma(e.dma_start(drow[:], c.den_s[0].unsqueeze(0))))
                    dn = sD.n
                    V(lambda e, v=dn: e.wait_ge(sD.h, v))
                    V(lambda e: e.reciprocal(drow[:], drow[:]))
                    V(lambda e: e.tensor_copy(rrh[:], drow[:]))
                    V(lambda e: sV.inc(e.tensor_tensor(rrl[:], drow[:], rrh[:],
                                                       ALU.subtract)))
                    vr = sV.n
                    T(lambda e, v=vr: e.wait_ge(sV.h, v))
                    T(lambda e: e.matmul(prb[:], r.ones128r[0:1, 0:128], rrh[:],
                                         start=True, stop=False))
                    T(lambda e: sT.inc(e.matmul(prb[:], r.ones128r[0:1, 0:128],
                                                rrl[:], start=False, stop=True)))
                    trb = sT.n
                    V(lambda e, v=trb: e.wait_ge(sT.h, v))
                    V(lambda e: e.tensor_copy(rbsb[:], prb[:]))
                    V(lambda e, qc=qc: e.tensor_tensor(
                        aft[0:64, :], pctx[qc][0:64, :], rbsb[0:64, :], ALU.mult))
                    V(lambda e: e.tensor_copy(a1t[0:64, :], aft[0:64, :]))
                    V(lambda e: e.tensor_tensor(a2t[0:64, :], aft[0:64, :],
                                                a1t[0:64, :], ALU.subtract))
                    V(lambda e, qc=qc, u=u: sV.inc(e.tensor_tensor(
                        S2[64:128, u * 1024 + qc * 512:u * 1024 + qc * 512 + 512],
                        pctx[qc][64:128, :], rbsb[64:128, :], ALU.mult)))
                    vev = sV.n
                    col = u * 1024 + qc * 512
                    V(lambda e, col=col: e.tensor_copy(
                        S1[0:64, col:col + 512], a1t[0:64, :]))
                    V(lambda e, col=col: sV.inc(e.tensor_copy(
                        S2[0:64, col:col + 512], a1t[0:64, :])))
                    vev = sV.n
                    Y(lambda e, v=vev: e.wait_ge(sV.h, v))
                    Y(lambda e, col=col: sD.dma(e.dma_start(
                        S1[64:128, col:col + 512], a2t[0:64, :])))
                    dS1 = sD.n
                    T(lambda e, v=vev: e.wait_ge(sV.h, v))
                    T(lambda e, v=dS1: e.wait_ge(sD.h, v))
            T(lambda e, v=q_load[h]: e.wait_ge(sD.h, v))
            v_sacc_hist = {}
            for tb in range(16):
                for ot in range(2):
                    rop = tb * 2 + ot
                    opar = rop % 2
                    if rop >= 2:
                        T(lambda e, v=v_sacc_hist[rop - 2]: e.wait_ge(sV.h, v))
                    else:
                        T(lambda e, v=v_sacc: e.wait_ge(sV.h, v))
                    T(lambda e, tb=tb, ot=ot, hpar=hpar, opar=opar: e.matmul(
                        psoB[opar][:], S1[:, tb * 128:tb * 128 + 128],
                        wo1R[hpar][:, ot * 512:ot * 512 + 512],
                        start=True, stop=False))
                    T(lambda e, tb=tb, ot=ot, hpar=hpar, opar=opar: sT.inc(e.matmul(
                        psoB[opar][:], S2[:, tb * 128:tb * 128 + 128],
                        wo2R[hpar][:, ot * 512:ot * 512 + 512],
                        start=False, stop=True)))
                    tso = sT.n
                    V(lambda e, v=tso: e.wait_ge(sT.h, v))
                    col = tb * DM + ot * 512
                    if h == 0:
                        V(lambda e, col=col, opar=opar: sV.inc(e.tensor_copy(
                            sacc[:, col:col + 512], psoB[opar][:])))
                    else:
                        V(lambda e, col=col, opar=opar: sV.inc(e.tensor_tensor(
                            sacc[:, col:col + 512], sacc[:, col:col + 512],
                            psoB[opar][:], ALU.add)))
                    v_sacc_hist[rop] = sV.n
                    v_sacc = sV.n
            oproj_mark[h] = sT.n
        Y(lambda e, v=v_sacc: e.wait_ge(sV.h, v))
        for tb in range(16):
            Y(lambda e, tb=tb: sO.dma(e.dma_start(
                c.sacc_s[tb * 128:(tb + 1) * 128, :],
                sacc[:, tb * DM:(tb + 1) * DM])))
        Y(lambda e, v=sO.n: e.wait_ge(sO.h, v))
        sch.run(nc)
def _emit_b4(nc, c, r, sems, gq_done_val):
    sD, sG, sT, sV, sSc, sQ, sO = sems
    ph = contextlib.ExitStack()
    with ph:
        sch = Sched()
        Y = mk_adder(sch.q["y"])
        V = mk_adder(sch.q["v"], track=True)
        T = mk_adder(sch.q["t"])
        SC = mk_adder(sch.q["s"])

        def sbP(name, shape, dtyp=F32):
            return ph.enter_context(nc.sbuf_tensor(name, shape, dtyp))

        bobc = sbP("b4_bobc", [128, DM])
        wtin = [sbP(f"wtin{i}", [128, FI], BF16) for i in range(2)]
        wtst = [sbP(f"wtst{i}", [128, 128], BF16) for i in range(2)]
        xqR = sbP("xqR", [128, NCH * 512], BF16)
        wgR = [sbP(f"wgR{i}", [128, NCH * 512], BF16) for i in range(2)]
        wuR = [sbP(f"wuR{i}", [128, NCH * 512], BF16) for i in range(2)]
        wdH = [sbP(f"wdH{i}", [128, 16 * 512], BF16) for i in range(2)]
        sil = sbP("sil", [128, 512])
        hidt = sbP("hidt", [128, 512])
        hq = sbP("hq", [128, 512], BF16)
        hqf = sbP("hqf", [128, 32 * 512], BF16)
        qsA = sbP("b4qsA", [128, 32])
        qsB = sbP("b4qsB", [128, 32])
        qsC = sbP("b4qsC", [128, 32])
        qsD = sbP("b4qsD", [128, 32])
        qtA = sbP("b4qtA", [128, 512])
        qtB = sbP("b4qtB", [128, 512])
        gt4 = sbP("gt4", [128, 4 * DM])
        xt = [sbP(f"b4xt{i}", [128, DM]) for i in range(2)]
        st_ = [sbP(f"b4st{i}", [128, DM]) for i in range(2)]
        ot_ = [sbP(f"b4ot{i}", [128, DM]) for i in range(2)]
        ptrB = ph.enter_context(nc.psum_tensor("ptrB", [128, 128], BF16))
        pga = ph.enter_context(nc.psum_tensor("pga", [128, 512], F32))
        pgb = ph.enter_context(nc.psum_tensor("pgb", [128, 512], F32))
        pd4 = [ph.enter_context(nc.psum_tensor(f"pd{i}", [128, 512], F32))
               for i in range(4)]

        Y(lambda e: sD.dma(e.dma_start(bobc[:], c.bo_bc_in)))
        d_bo = sD.n
        V(lambda e, v=gq_done_val: e.wait_ge(sQ.h, v))
        V(lambda e, v=d_bo: e.wait_ge(sD.h, v))
        V(lambda e: e.tensor_tensor(r.scbc[:, 7:8], r.scbc[:, 0:1],
                                    r.scbc[:, 2:3], ALU.mult))
        V(lambda e: e.tensor_tensor(r.scbc[:, 8:9], r.scbc[:, 0:1],
                                    r.scbc[:, 3:4], ALU.mult))
        V(lambda e: sV.inc(e.tensor_tensor(r.scbc[:, 9:10], r.scbc[:, 1:2],
                                           r.scbc[:, 4:5], ALU.mult)))
        v_al = sV.n
        SC(lambda e, v=v_al: e.wait_ge(sV.h, v))

        tr_jobs = [(c.wgq_s, c.wgT_s, FI, DM), (c.wuq_s, c.wuT_s, FI, DM),
                   (c.wdq_s, c.wdT_s, DM, FI)]
        v_trev = 0
        d_trst = {}
        t_trin = {}
        n_tr = 0
        n_rb = 0
        for win, wout, nr, ncl in tr_jobs:
            for rb in range(nr // 128):
                n_rb += 1
                par = n_rb % 2
                if n_rb >= 3:
                    Y(lambda e, v=t_trin.get(n_rb - 2, 0): e.wait_ge(sT.h, v))
                Y(lambda e, win=win, rb=rb, par=par, ncl=ncl: sD.dma(e.dma_start(
                    wtin[par][:, 0:ncl], win[rb * 128:(rb + 1) * 128, :])))
                dwin = sD.n
                for cb in range(ncl // 128):
                    n_tr += 1
                    spar = n_tr % 2
                    T(lambda e, v=dwin: e.wait_ge(sD.h, v))
                    T(lambda e, v=v_trev: e.wait_ge(sV.h, v))
                    T(lambda e, cb=cb, par=par: sT.inc(e.transpose(
                        ptrB[:], wtin[par][:, cb * 128:cb * 128 + 128],
                        r.identb[:])))
                    tt = sT.n
                    V(lambda e, v=tt: e.wait_ge(sT.h, v))
                    if n_tr >= 3:
                        V(lambda e, v=d_trst.get(n_tr - 2, 0): e.wait_ge(sD.h, v))
                    V(lambda e, spar=spar: sV.inc(e.tensor_copy(
                        wtst[spar][:], ptrB[:])))
                    v_trev = sV.n
                    Y(lambda e, v=v_trev: e.wait_ge(sV.h, v))
                    Y(lambda e, wout=wout, rb=rb, cb=cb, spar=spar: sD.dma(
                        e.dma_start(wout[cb * 128:(cb + 1) * 128,
                                         rb * 128:(rb + 1) * 128], wtst[spar][:])))
                    d_trst[n_tr] = sD.n
                t_trin[n_rb] = sT.n
        d_tr_all = sD.n

        d_wg = {}
        d_wd = {}
        v_hid = 0
        v_hqf = 0
        t_hqf_used = 0
        v_comb = 0
        d_ocp = {}
        s_gt = 0
        for t4 in range(4):
            tk0 = t4 * 512
            Y(lambda e, v=d_tr_all: e.wait_ge(sD.h, v))
            if t4 >= 1:
                Y(lambda e, v=t_hqf_used: e.wait_ge(sT.h, v))
            for ch in range(NCH):
                Y(lambda e, ch=ch, tk0=tk0: sD.dma(e.dma_start(
                    xqR[:, ch * 512:(ch + 1) * 512],
                    c.xqf_s[:, ch * TOWN + tk0:ch * TOWN + tk0 + 512])))
            d_xq = sD.n
            for ot in range(8):
                rnd = t4 * 8 + ot
                opar = rnd % 2
                if rnd >= 2:
                    Y(lambda e, v=d_wg.get(rnd - 2, 0): e.wait_ge(sT.h, v))
                for ch in range(NCH):
                    Y(lambda e, ch=ch, ot=ot, opar=opar: sD.dma(e.dma_start(
                        wgR[opar][:, ch * 512:(ch + 1) * 512],
                        c.wgT_s[ch * 128:(ch + 1) * 128, ot * 512:ot * 512 + 512])))
                    Y(lambda e, ch=ch, ot=ot, opar=opar: sD.dma(e.dma_start(
                        wuR[opar][:, ch * 512:(ch + 1) * 512],
                        c.wuT_s[ch * 128:(ch + 1) * 128, ot * 512:ot * 512 + 512])))
                d_w = sD.n
                for ts in range(4):
                    T(lambda e, v=d_w: e.wait_ge(sD.h, v))
                    T(lambda e, v=d_xq: e.wait_ge(sD.h, v))
                    T(lambda e, v=v_hid: e.wait_ge(sV.h, v))
                    for ch in range(NCH):
                        x0 = ch * 512 + ts * 128
                        mm = lambda e, x0=x0, ch=ch, opar=opar: e.matmul(
                            pga[:], xqR[:, x0:x0 + 128],
                            wgR[opar][:, ch * 512:(ch + 1) * 512],
                            start=(ch == 0), stop=(ch == NCH - 1))
                        if ch == NCH - 1:
                            T(lambda e, mm=mm: sT.inc(mm(e)))
                        else:
                            T(mm)
                    tga = sT.n
                    for ch in range(NCH):
                        x0 = ch * 512 + ts * 128
                        mm = lambda e, x0=x0, ch=ch, opar=opar: e.matmul(
                            pgb[:], xqR[:, x0:x0 + 128],
                            wuR[opar][:, ch * 512:(ch + 1) * 512],
                            start=(ch == 0), stop=(ch == NCH - 1))
                        if ch == NCH - 1:
                            T(lambda e, mm=mm: sT.inc(mm(e)))
                        else:
                            T(mm)
                    tgb = sT.n
                    SC(lambda e, v=tga: e.wait_ge(sT.h, v))
                    SC(lambda e, v=v_hid: e.wait_ge(sV.h, v))
                    SC(lambda e: sSc.inc(e.activation(
                        sil[:], pga[:], ACT.Silu, bias=0.0,
                        scale=r.scbc[:, 7:8])))
                    sgl = sSc.n
                    V(lambda e, v=sgl: e.wait_ge(sSc.h, v))
                    V(lambda e, v=tgb: e.wait_ge(sT.h, v))
                    V(lambda e: e.tensor_scalar(hidt[:], pgb[:], r.scbc[:, 8:9],
                                                None, ALU.mult))
                    V(lambda e: sV.inc(e.tensor_tensor(hidt[:], hidt[:], sil[:],
                                                       ALU.mult)))
                    v_hid = sV.n
                    last = emit_nvfp4(V, hidt[:], hq[:], qsA[:], qsB[:],
                                      qsC[:], qsD[:], qtA[:], qtB[:],
                                      r.scbc[:, 6:7])
                    V(lambda e, last=last: sV.inc(last(e)))
                    vhq = sV.n
                    for cb in range(4):
                        T(lambda e, v=vhq: e.wait_ge(sV.h, v))
                        T(lambda e, v=v_hqf: e.wait_ge(sV.h, v))
                        T(lambda e, cb=cb: sT.inc(e.transpose(
                            ptrB[:], hq[:, cb * 128:cb * 128 + 128], r.identb[:])))
                        tt = sT.n
                        ic = ot * 4 + cb
                        V(lambda e, v=tt: e.wait_ge(sT.h, v))
                        V(lambda e, ic=ic, ts=ts: sV.inc(e.tensor_copy(
                            hqf[:, ic * 512 + ts * 128:ic * 512 + ts * 128 + 128],
                            ptrB[:])))
                        v_hqf = sV.n
                d_wg[rnd] = sT.n
            for oc in range(2):
                for icg in range(2):
                    wpar = (t4 * 4 + oc * 2 + icg) % 2
                    if t4 * 4 + oc * 2 + icg >= 2:
                        Y(lambda e, v=d_wd.get(t4 * 4 + oc * 2 + icg - 2, 0):
                          e.wait_ge(sT.h, v))
                    for i16 in range(16):
                        icabs = icg * 16 + i16
                        Y(lambda e, icabs=icabs, i16=i16, oc=oc, wpar=wpar:
                          sD.dma(e.dma_start(
                              wdH[wpar][:, i16 * 512:(i16 + 1) * 512],
                              c.wdT_s[icabs * 128:(icabs + 1) * 128,
                                      oc * 512:oc * 512 + 512])))
                    d_wdn = sD.n
                    for ts in range(4):
                        T(lambda e, v=d_wdn: e.wait_ge(sD.h, v))
                        if icg == 0:
                            T(lambda e, v=v_hqf: e.wait_ge(sV.h, v))
                            if t4 > 0 or oc > 0:
                                T(lambda e, v=s_gt: e.wait_ge(sSc.h, v))
                        for i16 in range(16):
                            icabs = icg * 16 + i16
                            mm = lambda e, icabs=icabs, i16=i16, ts=ts, \
                                wpar=wpar: e.matmul(
                                    pd4[ts][:],
                                    hqf[:, icabs * 512 + ts * 128:
                                         icabs * 512 + ts * 128 + 128],
                                    wdH[wpar][:, i16 * 512:(i16 + 1) * 512],
                                    start=(icabs == 0), stop=(icabs == 31))
                            if i16 == 15:
                                T(lambda e, mm=mm: sT.inc(mm(e)))
                            else:
                                T(mm)
                    d_wd[t4 * 4 + oc * 2 + icg] = sT.n
                t_dn = sT.n
                for ts in range(4):
                    SC(lambda e, v=t_dn: e.wait_ge(sT.h, v))
                    if t4 > 0:
                        SC(lambda e, v=v_comb: e.wait_ge(sV.h, v))
                    SC(lambda e, ts=ts, oc=oc: sSc.inc(e.activation(
                        gt4[:, ts * DM + oc * 512:ts * DM + oc * 512 + 512],
                        pd4[ts][:], ACT.Silu, bias=0.0, scale=r.scbc[:, 9:10])))
                s_gt = sSc.n
            t_hqf_used = sT.n
            for ts in range(4):
                tok0 = tk0 + ts * 128
                cpar = ts % 2
                if t4 * 4 + ts >= 2:
                    Y(lambda e, v=d_ocp.get(t4 * 4 + ts - 2, 0): e.wait_ge(sO.h, v))
                Y(lambda e, tok0=tok0, cpar=cpar: sD.dma(e.dma_start(
                    xt[cpar][:], c.x_in[tok0:tok0 + 128, :])))
                Y(lambda e, tok0=tok0, cpar=cpar: sD.dma(e.dma_start(
                    st_[cpar][:], c.sacc_s[tok0:tok0 + 128, :])))
                d_xs = sD.n
                V(lambda e, v=d_xs: e.wait_ge(sD.h, v))
                V(lambda e, v=s_gt: e.wait_ge(sSc.h, v))
                V(lambda e, cpar=cpar: e.tensor_tensor(
                    st_[cpar][:], st_[cpar][:], bobc[:], ALU.add))
                V(lambda e, cpar=cpar, ts=ts: e.tensor_tensor(
                    ot_[cpar][:], st_[cpar][:], gt4[:, ts * DM:(ts + 1) * DM],
                    ALU.mult))
                V(lambda e, cpar=cpar: sV.inc(e.tensor_tensor(
                    ot_[cpar][:], ot_[cpar][:], xt[cpar][:], ALU.add)))
                v_comb = sV.n
                Y(lambda e, v=v_comb: e.wait_ge(sV.h, v))
                Y(lambda e, tok0=tok0, cpar=cpar: sO.dma(e.dma_start(
                    c.out_hbm[tok0:tok0 + 128, :], ot_[cpar][:])))
                d_ocp[t4 * 4 + ts] = sO.n
        Y(lambda e, v=sO.n: e.wait_ge(sO.h, v))
        sch.run(nc)


def kernel(**inputs):
    bf16 = ml_dtypes.bfloat16
    x = np.asarray(inputs["L0_attn_hidden_states"], np.float32)
    wq = np.ascontiguousarray(np.asarray(inputs["L0_attn_wq"], np.float32))
    wk = np.ascontiguousarray(np.asarray(inputs["L0_attn_wk"], np.float32))
    wv = np.ascontiguousarray(np.asarray(inputs["L0_attn_wv"], np.float32))
    wo = np.ascontiguousarray(np.asarray(inputs["L0_attn_wo"], np.float32))
    wfq = np.ascontiguousarray(np.asarray(inputs["L0_attn_wfq"], np.float32))
    wfg = np.ascontiguousarray(np.asarray(inputs["L0_attn_wfg"], np.float32))

    def bias_fold(b):
        return np.ascontiguousarray(np.asarray(b, np.float32).reshape(NCH, 128).T)

    bq = bias_fold(inputs["L0_attn_bq"])
    bk = bias_fold(inputs["L0_attn_bk"])
    bfq = bias_fold(inputs["L0_attn_bfq"])
    bfg = bias_fold(inputs["L0_attn_bfg"])
    bv_bc = np.ascontiguousarray(np.broadcast_to(
        np.asarray(inputs["L0_attn_bv"], np.float32)[None, :], (128, DM)))
    bo_bc = np.ascontiguousarray(np.broadcast_to(
        np.asarray(inputs["L0_attn_bo"], np.float32)[None, :], (128, DM)))
    up2 = np.asarray(inputs["L0_attn_up"], np.float32).reshape(H, D).T
    uz2 = np.asarray(inputs["L0_attn_uz"], np.float32).reshape(H, D).T
    upz_half = np.concatenate([up2, uz2], axis=1)
    upz = np.ascontiguousarray(np.concatenate([upz_half, upz_half], axis=0))
    gw = np.ascontiguousarray(np.asarray(inputs["L0_ffn_gate_w"], np.float32))
    uw = np.ascontiguousarray(np.asarray(inputs["L0_ffn_up_w"], np.float32))
    dw = np.ascontiguousarray(np.asarray(inputs["L0_ffn_down_w"], np.float32))
    s_in = np.asarray(inputs["L0_ffn_s_in"], np.float32).reshape(1, 1)
    s_in_down = np.asarray(inputs["L0_ffn_s_in_down"], np.float32).reshape(1, 1)
    ident = np.eye(128, dtype=np.float32)

    nc = build_program(4)
    in_maps = []
    for core in range(8):
        b, qq = core // 4, core % 4
        xr = np.ascontiguousarray(np.roll(x[b], -TOWN * qq, axis=0))
        pos = ((np.arange(S) + TOWN * qq) % S).astype(np.float64)
        j_hi = pos.astype(bf16).astype(np.float64)
        j_lo = (pos - j_hi).astype(bf16).astype(np.float64)
        jj = pos * pos
        jj_hi = jj.astype(bf16).astype(np.float64)
        jj_lo = (jj - jj_hi).astype(bf16).astype(np.float64)
        jconst = np.ascontiguousarray(
            np.stack([j_hi, j_lo, jj_hi, jj_lo]).astype(np.float32))
        in_maps.append({
            "x_roll": xr, "wq": wq, "wk": wk, "wv": wv, "wo": wo,
            "wfq": wfq, "wfg": wfg, "bq": bq, "bk": bk, "bfq": bfq,
            "bfg": bfg, "bv_bc": bv_bc, "bo_bc": bo_bc, "upz": upz,
            "jconst": jconst, "ident_in": ident, "gate_w": gw, "up_w": uw,
            "down_w": dw, "s_in": s_in, "s_in_down": s_in_down,
        })
    res = run_bass_kernel_spmd(nc, in_maps, core_ids=list(range(8)))
    out = np.empty((2, S, DM), np.float32)
    for core in range(8):
        b, qq = core // 4, core % 4
        out[b, TOWN * qq:TOWN * (qq + 1)] = np.asarray(res.results[core]["out"])
    return out


# version 03
# speedup vs baseline: 1.3456x; 1.0023x over previous
import contextlib
import types
import numpy as np
import ml_dtypes
import concourse.bass as bass
import concourse.mybir as mybir
from concourse.bass_utils import run_bass_kernel_spmd

F32 = mybir.dt.float32
BF16 = mybir.dt.bfloat16
AX = mybir.AxisListType
ALU = mybir.AluOpType
ACT = mybir.ActivationFunctionType

S = 8192
TOWN = 2048
DM = 1024
H = 16
D = 64
FI = 4096
NCH = 8
NB = 16
EPS = 1e-5


_COUNTING = [False]
_CAPTURE = [None]
_SEP_AP = [None]
_SEP_COUNT = [0]
_HZW = 128


class _MockInst:
    def then_inc(self, h, k=1):
        return self

    def __getattr__(self, name):
        return lambda *a, **kw: self


class _Cap:
    __slots__ = ("reads", "writes", "reduce", "touched")

    def __init__(self):
        self.reads = []
        self.writes = []
        self.reduce = False
        self.touched = False


def _is_ap(x):
    return hasattr(x, "offset") and hasattr(x, "free_size") and hasattr(x, "ap")


class _MockEng:
    def __getattr__(self, name):
        def call(*a, **kw):
            cap = _CAPTURE[0]
            if cap is not None and name not in ("wait_ge", "engine_nop"):
                cap.touched = True
                if "reduce" in name:
                    cap.reduce = True
                aps = [x for x in a if _is_ap(x)]
                if aps:
                    cap.writes.append(aps[0])
                    cap.reads += aps[1:]
                cap.reads += [v for v in kw.values() if _is_ap(v)]
            return _MockInst()
        return call


_MOCK = _MockEng()


def _ap_rng(ap):
    try:
        return (ap.name, int(ap.offset), int(ap.free_size()))
    except Exception:
        return None


def _hazard(prev, cap):
    if prev is None:
        return False
    for w in prev.writes:
        wi = _ap_rng(w)
        if wi is None:
            continue
        wt, wo, ww = wi
        for rap in cap.reads:
            ri = _ap_rng(rap)
            if ri is None:
                continue
            rt, ro, rw = ri
            if rt != wt or ro + rw <= wo or wo + ww <= ro:
                continue
            if prev.reduce or ww < _HZW or (wo + ww) - ro < _HZW:
                return True
    return False


def _sep_op(e):
    s = _SEP_AP[0]
    e.tensor_copy(s[:, 1:2], s[:, 0:1])


class Sem:

    def __init__(self, st, nc, name):
        self.h = st.enter_context(nc.semaphore(name))
        self.n = 0

    def inc(self, inst, k=1):
        if _COUNTING[0]:
            self.n += k
        inst.then_inc(self.h, k)
        return self.n

    def dma(self, inst):
        return self.inc(inst, 16)


def mk_adder(lst, track=False):
    state = {"last": None}

    def add(fn):
        cap = _Cap() if track else None
        _CAPTURE[0] = cap
        _COUNTING[0] = True
        try:
            fn(_MOCK)
        finally:
            _COUNTING[0] = False
            _CAPTURE[0] = None
        if track and cap.touched:
            if _hazard(state["last"], cap):
                lst.append(_sep_op)
                _SEP_COUNT[0] += 1
            state["last"] = cap
        lst.append(fn)
    return add


class Sched:
    def __init__(self):
        self.q = {"t": [], "v": [], "s": [], "g": [], "y": []}

    def run(self, nc):
        with nc.Block() as blk:
            if self.q["y"]:
                @blk.sync
                def _(e):
                    for f in self.q["y"]:
                        f(e)
            if self.q["t"]:
                @blk.tensor
                def _(e):
                    for f in self.q["t"]:
                        f(e)
            if self.q["v"]:
                @blk.vector
                def _(e):
                    for f in self.q["v"]:
                        f(e)
            if self.q["s"]:
                @blk.scalar
                def _(e):
                    for f in self.q["s"]:
                        f(e)
            if self.q["g"]:
                @blk.gpsimd
                def _(e):
                    for f in self.q["g"]:
                        f(e)


def emit_nvfp4(E, x_ap, out_ap, sA, sB, sC, sT_, tA, tB, gs_ap, eng="v"):
    W = x_ap.shape[-1]
    nsc = W // 16
    P = x_ap.shape[0]
    E(lambda e: e.tensor_scalar(tA, x_ap, -1.0, None, ALU.mult))
    E(lambda e: e.tensor_tensor(tA, tA, x_ap, ALU.max))
    tA3 = tA.rearrange("p (a b) -> p a b", b=16)
    if eng == "v":
        E(lambda e: e.tensor_reduce(sA, tA3, AX.X, ALU.max))
    else:
        tB3 = tB.rearrange("p (a b) -> p a b", b=16)
        E(lambda e: e.tensor_tensor(tB3[:, :, 0:8], tA3[:, :, 0:8],
                                    tA3[:, :, 8:16], ALU.max))
        E(lambda e: e.tensor_tensor(tB3[:, :, 0:4], tB3[:, :, 0:4],
                                    tB3[:, :, 4:8], ALU.max))
        E(lambda e: e.tensor_tensor(tB3[:, :, 0:2], tB3[:, :, 0:2],
                                    tB3[:, :, 2:4], ALU.max))
        E(lambda e: e.tensor_tensor(tB3[:, :, 0:1], tB3[:, :, 0:1],
                                    tB3[:, :, 1:2], ALU.max))
        E(lambda e: e.tensor_copy(sA, tB3[:, :, 0:1].squeeze(2)))
    E(lambda e: e.tensor_scalar(sA, sA, float(np.float32(1.0 / 6.0)), None, ALU.mult))
    E(lambda e: e.tensor_scalar(sA, sA, gs_ap, 448.0, ALU.mult, ALU.min))
    E(lambda e: e.memset(sB, 2.0 ** 9))
    E(lambda e: e.memset(sC, 2.0 ** -9))
    for p in range(-5, 9):
        cinv = 2.0 ** (3 - p) - 2.0 ** (4 - p)
        cs = 2.0 ** (p - 3) - 2.0 ** (p - 4)
        E(lambda e, p=p, cv=cinv: e.tensor_scalar(sT_, sA, 2.0 ** p, cv, ALU.is_ge, ALU.mult))
        E(lambda e: e.tensor_tensor(sB, sB, sT_, ALU.add))
        E(lambda e, p=p, cv=cs: e.tensor_scalar(sT_, sA, 2.0 ** p, cv, ALU.is_ge, ALU.mult))
        E(lambda e: e.tensor_tensor(sC, sC, sT_, ALU.add))
    E(lambda e: e.tensor_tensor(sT_, sA, sB, ALU.mult))
    E(lambda e: e.memset(sA, 0.0))
    for k in range(16):
        op = ALU.is_gt if (k % 2 == 0) else ALU.is_ge
        E(lambda e, k=k, op=op: e.tensor_scalar(tB[:, 0:nsc], sT_, k + 0.5, None, op))
        E(lambda e: e.tensor_tensor(sA, sA, tB[:, 0:nsc], ALU.add))
    E(lambda e: e.tensor_tensor(sT_, sA, sC, ALU.mult))
    if eng == "v":
        E(lambda e: e.reciprocal(sC, sT_))
        E(lambda e: e.tensor_scalar(sC, sC, gs_ap, None, ALU.mult))
        rdq_bc = sC.unsqueeze(2).broadcast_to([P, nsc, 16])
        E(lambda e: e.tensor_tensor(tA3, tA3, rdq_bc, ALU.mult))
        E(lambda e: e.tensor_scalar(tB, tA, 0.25, 0.5, ALU.is_ge, ALU.mult))
        for m, cadd in ((0.75, 0.5), (1.25, 0.5), (1.75, 0.5),
                        (2.5, 1.0), (3.5, 1.0), (5.0, 2.0)):
            E(lambda e, m=m, cv=cadd: e.tensor_scalar(out_ap, tA, m, cv, ALU.is_ge, ALU.mult))
            E(lambda e: e.tensor_tensor(tB, tB, out_ap, ALU.add))
    else:
        E(lambda e: e.tensor_scalar(sB, sB, gs_ap, None, ALU.mult))
        gi_bc = sB.unsqueeze(2).broadcast_to([P, nsc, 16])
        E(lambda e: e.tensor_tensor(tA3, tA3, gi_bc, ALU.mult))
        first = True
        for m, cadd in ((0.25, 0.5), (0.75, 0.5), (1.25, 0.5), (1.75, 0.5),
                        (2.5, 1.0), (3.5, 1.0), (5.0, 2.0)):
            E(lambda e, m=m: e.tensor_scalar(sC, sA, m, None, ALU.mult))
            thr_bc = sC.unsqueeze(2).broadcast_to([P, nsc, 16])
            o3_ = out_ap.rearrange("p (a b) -> p a b", b=16)
            E(lambda e, o3_=o3_, thr_bc=thr_bc: e.tensor_tensor(
                o3_, tA3, thr_bc, ALU.is_ge))
            if first:
                E(lambda e, cv=cadd: e.tensor_scalar(tB, out_ap, cv, None, ALU.mult))
                first = False
            else:
                E(lambda e, cv=cadd: e.tensor_scalar(out_ap, out_ap, cv, None, ALU.mult))
                E(lambda e: e.tensor_tensor(tB, tB, out_ap, ALU.add))
    E(lambda e: e.tensor_scalar(tA, x_ap, 0.0, 2.0, ALU.is_ge, ALU.mult))
    E(lambda e: e.tensor_scalar(tA, tA, -1.0, None, ALU.add))
    E(lambda e: e.tensor_tensor(tB, tB, tA, ALU.mult))
    sc8_bc = sT_.unsqueeze(2).broadcast_to([P, nsc, 16])
    tB3f = tB.rearrange("p (a b) -> p a b", b=16)
    o3 = out_ap.rearrange("p (a b) -> p a b", b=16)
    return lambda e: e.tensor_tensor(o3, tB3f, sc8_bc, ALU.mult)


def _declare_io(nc):
    dt = nc.dram_tensor
    c = types.SimpleNamespace()
    c.x_in = dt("x_roll", [S, DM], F32, kind="ExternalInput").ap()
    for nm in ("wq", "wk", "wv", "wo", "wfq", "wfg"):
        setattr(c, nm + "_in", dt(nm, [DM, DM], F32, kind="ExternalInput").ap())
    for nm in ("bq", "bk", "bfq", "bfg"):
        setattr(c, nm + "_in", dt(nm, [128, NCH], F32, kind="ExternalInput").ap())
    c.bv_bc_in = dt("bv_bc", [128, DM], F32, kind="ExternalInput").ap()
    c.bo_bc_in = dt("bo_bc", [128, DM], F32, kind="ExternalInput").ap()
    c.upz_in = dt("upz", [128, 2 * H], F32, kind="ExternalInput").ap()
    c.jc_in = dt("jconst", [4, S], F32, kind="ExternalInput").ap()
    c.ident_in = dt("ident_in", [128, 128], F32, kind="ExternalInput").ap()
    c.gw_in = dt("gate_w", [FI, DM], F32, kind="ExternalInput").ap()
    c.uw_in = dt("up_w", [FI, DM], F32, kind="ExternalInput").ap()
    c.dw_in = dt("down_w", [DM, FI], F32, kind="ExternalInput").ap()
    c.sin_in = dt("s_in", [1, 1], F32, kind="ExternalInput").ap()
    c.sind_in = dt("s_in_down", [1, 1], F32, kind="ExternalInput").ap()
    c.out_hbm = dt("out", [TOWN, DM], F32, kind="ExternalOutput").ap()

    c.xnh_s = dt("xnh_s", [NCH, 128, S], BF16).ap()
    c.xnl_s = dt("xnl_s", [NCH, 128, S], BF16).ap()
    c.karr_s = dt("karr_s", [H, 128, S], BF16).ap()
    c.varr_s = dt("varr_s", [H, S, 128], BF16).ap()
    c.qarr2_s = dt("qarr2_s", [H, 8, TOWN], BF16).ap()
    c.jarr_s = dt("jarr_s", [8, S], BF16).ap()
    c.woar1_s = dt("woar1_s", [H, 128, DM], BF16).ap()
    c.woar2_s = dt("woar2_s", [H, 128, DM], BF16).ap()
    c.wgq_s = dt("wgq_s", [FI, DM], BF16).ap()
    c.wuq_s = dt("wuq_s", [FI, DM], BF16).ap()
    c.wdq_s = dt("wdq_s", [DM, FI], BF16).ap()
    c.wgT_s = dt("wgT_s", [DM, FI], BF16).ap()
    c.wuT_s = dt("wuT_s", [DM, FI], BF16).ap()
    c.wdT_s = dt("wdT_s", [FI, DM], BF16).ap()
    c.sacc_s = dt("sacc_s", [TOWN, DM], F32).ap()
    c.scal_s = dt("scal_s", [16, 128], F32).ap()
    c.den_s = dt("den_s", [4, 512], F32).ap()
    c.sgp_s = dt("sgp_s", [H, TOWN], F32).ap()
    c.sgz_s = dt("sgz_s", [H, TOWN], F32).ap()
    c.q2n_s = dt("q2n_s", [H, TOWN], F32).ap()
    c.mqh_s = dt("mqh_s", [NCH, 128, TOWN], BF16).ap()
    c.mql_s = dt("mql_s", [NCH, 128, TOWN], BF16).ap()
    c.xqf_s = dt("xqf_s", [128, NCH * TOWN], BF16).ap()
    return c
def build_program(upto=4, debug=()):
    nc = bass.Bass("TRN2", target_bir_lowering=False)
    c = _declare_io(nc)
    dbg_out = {}
    dbg_sbuf = []
    for nm in debug:
        if not hasattr(c, nm):
            dbg_sbuf.append(nm)
            continue
        src = getattr(c, nm)
        shp = list(src.shape)
        flat = [int(np.prod(shp[:-1])), shp[-1]]
        dbg_out[nm] = nc.dram_tensor("dbg_" + nm, flat, src.dtype,
                                     kind="ExternalOutput").ap()

    root = contextlib.ExitStack()
    with root:
        r = types.SimpleNamespace()

        def sbR(name, shape, dtyp=F32):
            t = root.enter_context(nc.sbuf_tensor(name, shape, dtyp))
            setattr(r, name, t)
            return t

        sbR("ident", [128, 128])
        sbR("identb", [128, 128], BF16)
        sbR("ones128b", [128, 1], BF16)
        sbR("ones128r", [1, 128], BF16)
        sbR("biases", [128, 4 * NCH])
        sbR("upzh", [128, 2 * H], BF16)
        sbR("upzl", [128, 2 * H], BF16)
        sbR("scbc", [128, 16])
        sbR("kmax2", [1, H])
        sbR("xbar", [128, NCH])
        sbR("sepA", [128, 2])
        _SEP_AP[0] = r.sepA
        _SEP_COUNT[0] = 0

        sD = Sem(root, nc, "sD")
        sG = Sem(root, nc, "sG")
        sT = Sem(root, nc, "sT")
        sV = Sem(root, nc, "sV")
        sSc = Sem(root, nc, "sSc")
        sQ = Sem(root, nc, "sQ")
        sO = Sem(root, nc, "sO")
        sems = (sD, sG, sT, sV, sSc, sQ, sO)

        gq_stack = contextlib.ExitStack()
        gq_al = _wquant_alloc(nc, gq_stack)
        wq_factory = lambda V, Y: _wquant_gen(nc, gq_al, c, r, sems, V, Y)

        _emit_b0(nc, c, r, sems)
        if upto >= 1:
            _emit_b1(nc, c, r, sems)
        if upto >= 2:
            _emit_b2(nc, c, r, sems)
        if upto >= 3:
            _emit_b3(nc, c, r, sems, wq_factory)
        gq_done_val = sQ.n
        gq_stack.close()
        if upto >= 4:
            _emit_b4(nc, c, r, sems, gq_done_val)

        sch = Sched()
        Y = mk_adder(sch.q["y"])
        for nm in dbg_sbuf:
            src = getattr(r, nm)
            outap = nc.dram_tensor("dbg_" + nm, list(src.shape), src.dtype,
                                   kind="ExternalOutput").ap()
            Y(lambda e, src=src, outap=outap: sO.dma(e.dma_start(outap, src[:])))
        for nm, outap in dbg_out.items():
            src = getattr(c, nm)
            if len(src.shape) == 3:
                p_ = src.shape[1]
                for i0 in range(src.shape[0]):
                    Y(lambda e, src=src, outap=outap, i0=i0, p_=p_:
                      sO.dma(e.dma_start(outap[i0 * p_:(i0 + 1) * p_, :], src[i0])))
            else:
                Y(lambda e, src=src, outap=outap: sO.dma(e.dma_start(outap, src)))
        Y(lambda e, v=sO.n: e.wait_ge(sO.h, v))
        Y(lambda e, v=sD.n: e.wait_ge(sD.h, v))
        Y(lambda e, v=sG.n: e.wait_ge(sG.h, v))
        if upto >= 2:
            Y(lambda e, v=sQ.n: e.wait_ge(sQ.h, v))
        sch.run(nc)
    print(f"[build] raw-hazard separators inserted: {_SEP_COUNT[0]}")
    return nc


def _wquant_alloc(nc, stk):
    W2 = 1024
    al = {}
    al["gq1"] = [stk.enter_context(nc.sbuf_tensor(f"gq1_{i}", [128, W2], F32))
                 for i in range(2)]
    al["gqtA"] = stk.enter_context(nc.sbuf_tensor("gqtA", [128, W2], F32))
    al["gqtB"] = stk.enter_context(nc.sbuf_tensor("gqtB", [128, W2], F32))
    al["gqs1"] = stk.enter_context(nc.sbuf_tensor("gqs1", [128, W2 // 16], F32))
    al["gqs2"] = stk.enter_context(nc.sbuf_tensor("gqs2", [128, W2 // 16], F32))
    al["gqs3"] = stk.enter_context(nc.sbuf_tensor("gqs3", [128, W2 // 16], F32))
    al["gqs4"] = stk.enter_context(nc.sbuf_tensor("gqs4", [128, W2 // 16], F32))
    al["gmx"] = stk.enter_context(nc.sbuf_tensor("gmx", [128, 4], F32))
    al["ggs"] = stk.enter_context(nc.sbuf_tensor("ggs", [128, 4], F32))
    al["grow"] = stk.enter_context(nc.sbuf_tensor("grow", [1, 128], F32))
    al["gout"] = [stk.enter_context(nc.sbuf_tensor(f"gout{i}", [128, W2], BF16))
                  for i in range(2)]
    return al


def _wquant_gen(nc, al, c, r, sems, v2, y2):
    sD, sG, sT, sV, sSc, sQ, sO = sems
    W2 = 1024
    gq1 = al["gq1"]
    gqtA, gqtB = al["gqtA"], al["gqtB"]
    gqs1, gqs2, gqs3, gqs4 = al["gqs1"], al["gqs2"], al["gqs3"], al["gqs4"]
    gmx, ggs, grow, gout = al["gmx"], al["ggs"], al["grow"], al["gout"]
    jobs = [(c.gw_in, c.wgq_s, FI, DM, 0), (c.uw_in, c.wuq_s, FI, DM, 1),
            (c.dw_in, c.wdq_s, DM, FI, 2)]

    vmark = 0
    nt = 0
    for win, wout_s, nrows, wwidth, wi in jobs:
        v2(lambda e, wi=wi: e.memset(gmx[:, wi:wi + 1], 0.0))
        for cr in range(nrows // 128):
            for cc in range(wwidth // W2):
                par = nt % 2
                nt += 1
                if nt > 2:
                    y2(lambda e, v=vmark: e.wait_ge(sV.h, v))
                y2(lambda e, cr=cr, cc=cc, win=win, par=par: sG.dma(
                    e.dma_start(gq1[par][:],
                                win[cr * 128:(cr + 1) * 128,
                                    cc * W2:(cc + 1) * W2])))
                dn = sG.n
                v2(lambda e, v=dn: e.wait_ge(sG.h, v))
                v2(lambda e, par=par, wi=wi: sV.inc(e.tensor_reduce(
                    gmx[:, 3:4], gq1[par][:], AX.X, ALU.max,
                    apply_absolute_value=True)))
                v2(lambda e, wi=wi: e.tensor_tensor(
                    gmx[:, wi:wi + 1], gmx[:, wi:wi + 1], gmx[:, 3:4], ALU.max))
                vmark = sV.n
                yield
    for win, wout_s, nrows, wwidth, wi in jobs:
        v2(lambda e, wi=wi, v=vmark: e.wait_ge(sV.h, v))
        v2(lambda e, wi=wi: sV.inc(e.tensor_copy(gmx[:, 3:4],
                                                        gmx[:, wi:wi + 1])))
        vv = sV.n
        y2(lambda e, v=vv: e.wait_ge(sV.h, v))
        y2(lambda e, wi=wi: sD.dma(e.dma_start(
            c.scal_s[10 + wi, 0:128], gmx[:, 3:4])))
        dn = sD.n
        y2(lambda e, v=dn: e.wait_ge(sD.h, v))
        y2(lambda e, wi=wi: sD.dma(e.dma_start(
            grow[:], c.scal_s[10 + wi].unsqueeze(0))))
        dn = sD.n
        v2(lambda e, v=dn: e.wait_ge(sD.h, v))
        v2(lambda e: e.tensor_reduce(grow[0:1, 0:1], grow[:], AX.X, ALU.max))
        v2(lambda e: e.reciprocal(grow[0:1, 1:2], grow[0:1, 0:1]))
        v2(lambda e: e.tensor_scalar(grow[0:1, 1:2], grow[0:1, 1:2],
                                            2688.0, None, ALU.mult))
        v2(lambda e: sV.inc(e.tensor_scalar(
            grow[0:1, 2:3], grow[0:1, 0:1], float(np.float32(1.0 / 2688.0)),
            None, ALU.mult)))
        vv = sV.n
        y2(lambda e, v=vv: e.wait_ge(sV.h, v))
        y2(lambda e, wi=wi: sD.dma(e.dma_start(
            c.scal_s[10 + wi, 0:1], grow[0:1, 1:2])))
        y2(lambda e, wi=wi: sD.dma(e.dma_start(
            c.scal_s[13 + wi, 0:1], grow[0:1, 2:3])))
        dn = sD.n
        y2(lambda e, v=dn: e.wait_ge(sD.h, v))
        vmark = sV.n
        yield
    for k in range(7):
        w = 1 << k
        for rr in range(10, 16):
            y2(lambda e, w=w, rr=rr: sD.dma(e.dma_start(
                c.scal_s[rr, w:2 * w], c.scal_s[rr, 0:w])))
        dn = sD.n
        y2(lambda e, v=dn: e.wait_ge(sD.h, v))
    for wi in range(3):
        y2(lambda e, wi=wi: sD.dma(e.dma_start(
            ggs[:, wi:wi + 1], c.scal_s[10 + wi].unsqueeze(1))))
        y2(lambda e, wi=wi: sD.dma(e.dma_start(
            r.scbc[:, 2 + wi:3 + wi], c.scal_s[13 + wi].unsqueeze(1))))
    y1_done = sD.n

    v2(lambda e, v=y1_done: e.wait_ge(sD.h, v))
    nt = 0
    dmark = {}
    for win, wout_s, nrows, wwidth, wi in jobs:
        for cr in range(nrows // 128):
            for cc in range(wwidth // W2):
                par = nt % 2
                if nt >= 2:
                    y2(lambda e, v=dmark[nt - 2][0]: e.wait_ge(sV.h, v))
                y2(lambda e, cr=cr, cc=cc, win=win, par=par: sG.dma(
                    e.dma_start(gq1[par][:],
                                win[cr * 128:(cr + 1) * 128,
                                    cc * W2:(cc + 1) * W2])))
                dn = sG.n
                v2(lambda e, v=dn: e.wait_ge(sG.h, v))
                if nt >= 2:
                    v2(lambda e, v=dmark[nt - 2][1]: e.wait_ge(sG.h, v))
                last = emit_nvfp4(v2, gq1[par][:], gout[par][:],
                                  gqs1[:], gqs2[:], gqs3[:], gqs4[:],
                                  gqtA[:], gqtB[:], ggs[:, wi:wi + 1], eng="v")
                v2(lambda e, last=last: sV.inc(last(e)))
                vv = sV.n
                y2(lambda e, v=vv: e.wait_ge(sV.h, v))
                y2(lambda e, cr=cr, cc=cc, wout_s=wout_s, par=par:
                          sG.dma(e.dma_start(
                              wout_s[cr * 128:(cr + 1) * 128,
                                     cc * W2:(cc + 1) * W2], gout[par][:])))
                dmark[nt] = (vv, sG.n)
                nt += 1
                yield
    v2(lambda e: sQ.inc(e.engine_nop()))
def _emit_b0(nc, c, r, sems):
    sD, sG, sT, sV, sSc, sQ, sO = sems
    ph = contextlib.ExitStack()
    with ph:
        sch = Sched()
        Y = mk_adder(sch.q["y"])
        V = mk_adder(sch.q["v"], track=True)
        wstgA = ph.enter_context(nc.sbuf_tensor("b0_wstgA", [128, DM], F32))
        woh = ph.enter_context(nc.sbuf_tensor("b0_woh", [128, NCH * DM], BF16))
        wol = ph.enter_context(nc.sbuf_tensor("b0_wol", [128, NCH * DM], BF16))
        jstg = ph.enter_context(nc.sbuf_tensor("b0_jstg", [4, S], F32))
        jrows = ph.enter_context(nc.sbuf_tensor("b0_jrows", [4, S], BF16))
        onesrow = ph.enter_context(nc.sbuf_tensor("b0_onesrow", [1, S], BF16))
        sstg = ph.enter_context(nc.sbuf_tensor("b0_sstg", [1, 8], F32))
        upzf = ph.enter_context(nc.sbuf_tensor("b0_upzf", [128, 2 * H], F32))

        Y(lambda e: sD.dma(e.dma_start(r.ident[:], c.ident_in)))
        Y(lambda e: sD.dma(e.dma_start(jstg[:], c.jc_in)))
        Y(lambda e: sD.dma(e.dma_start(r.biases[:, 0:NCH], c.bq_in)))
        Y(lambda e: sD.dma(e.dma_start(r.biases[:, NCH:2 * NCH], c.bk_in)))
        Y(lambda e: sD.dma(e.dma_start(r.biases[:, 2 * NCH:3 * NCH], c.bfq_in)))
        Y(lambda e: sD.dma(e.dma_start(r.biases[:, 3 * NCH:4 * NCH], c.bfg_in)))

        Y(lambda e: sD.dma(e.dma_start(upzf[:], c.upz_in)))
        Y(lambda e: sD.dma(e.dma_start(sstg[0:1, 0:1], c.sin_in)))
        Y(lambda e: sD.dma(e.dma_start(sstg[0:1, 1:2], c.sind_in)))
        d0 = sD.n
        V(lambda e: e.memset(r.sepA[:], 0.0))
        V(lambda e, v=d0: e.wait_ge(sD.h, v))
        V(lambda e: e.tensor_copy(r.identb[:], r.ident[:]))
        V(lambda e: e.memset(r.ones128b[:], 1.0))
        V(lambda e: e.memset(r.ones128r[:], 1.0))
        V(lambda e: e.tensor_copy(jrows[:], jstg[:]))
        V(lambda e: e.memset(onesrow[:], 1.0))
        V(lambda e: e.tensor_copy(r.upzh[:], upzf[:]))
        V(lambda e: e.tensor_tensor(r.upzl[:], upzf[:], r.upzh[:], ALU.subtract))
        V(lambda e: e.memset(r.kmax2[:], 0.0))
        V(lambda e: e.memset(r.xbar[:], 0.0))
        V(lambda e: e.reciprocal(sstg[0:1, 2:3], sstg[0:1, 0:1]))
        V(lambda e: sV.inc(e.reciprocal(sstg[0:1, 3:4], sstg[0:1, 1:2])))
        v0 = sV.n
        Y(lambda e, v=v0: e.wait_ge(sV.h, v))
        Y(lambda e: sD.dma(e.dma_start(c.scal_s[0, 0:1], sstg[0:1, 2:3])))
        Y(lambda e: sD.dma(e.dma_start(c.scal_s[1, 0:1], sstg[0:1, 3:4])))
        Y(lambda e: sD.dma(e.dma_start(c.scal_s[8, 0:1], sstg[0:1, 0:1])))
        Y(lambda e: sD.dma(e.dma_start(c.scal_s[9, 0:1], sstg[0:1, 1:2])))
        dn = sD.n
        Y(lambda e, v=dn: e.wait_ge(sD.h, v))
        for k in range(7):
            w = 1 << k
            for row in (0, 1, 8, 9):
                Y(lambda e, w=w, row=row: sD.dma(e.dma_start(
                    c.scal_s[row, w:2 * w], c.scal_s[row, 0:w])))
            dn = sD.n
            Y(lambda e, v=dn: e.wait_ge(sD.h, v))
        Y(lambda e: sD.dma(e.dma_start(r.scbc[:, 0:1], c.scal_s[0].unsqueeze(1))))
        Y(lambda e: sD.dma(e.dma_start(r.scbc[:, 1:2], c.scal_s[1].unsqueeze(1))))
        Y(lambda e: sD.dma(e.dma_start(r.scbc[:, 5:6], c.scal_s[8].unsqueeze(1))))
        Y(lambda e: sD.dma(e.dma_start(r.scbc[:, 6:7], c.scal_s[9].unsqueeze(1))))

        vprev = 0
        for ch in range(NCH):
            if ch >= 1:
                Y(lambda e, v=vprev: e.wait_ge(sV.h, v))
            Y(lambda e, ch=ch: sD.dma(e.dma_start(
                wstgA[:], c.wo_in[ch * 128:(ch + 1) * 128, :])))
            dn = sD.n
            V(lambda e, v=dn: e.wait_ge(sD.h, v))
            V(lambda e, ch=ch: e.tensor_copy(woh[:, ch * DM:(ch + 1) * DM], wstgA[:]))
            V(lambda e, ch=ch: sV.inc(e.tensor_tensor(
                wol[:, ch * DM:(ch + 1) * DM], wstgA[:],
                woh[:, ch * DM:(ch + 1) * DM], ALU.subtract)))
            vprev = sV.n
        V(lambda e: sV.inc(e.memset(wstgA[0:1, :], 0.0)))
        vz = sV.n
        Y(lambda e, v=vz: e.wait_ge(sV.h, v))
        for h in range(H):
            ch, r0 = divmod(h * D, 128)
            hi_sl = woh[r0:r0 + 64, ch * DM:(ch + 1) * DM]
            lo_sl = wol[r0:r0 + 64, ch * DM:(ch + 1) * DM]
            Y(lambda e, h=h, s_=hi_sl: sD.dma(e.dma_start(c.woar1_s[h, 0:64], s_)))
            Y(lambda e, h=h, s_=hi_sl: sD.dma(e.dma_start(c.woar1_s[h, 64:128], s_)))
            Y(lambda e, h=h, s_=lo_sl: sD.dma(e.dma_start(c.woar2_s[h, 0:64], s_)))
            Y(lambda e, h=h: sD.dma(e.dma_start(
                c.woar2_s[h, 64:65], wstgA[0:1, 0:DM // 2].bitcast(BF16))))
            Y(lambda e, h=h, s_=hi_sl: sD.dma(
                e.dma_start(c.woar2_s[h, 65:128], s_[0:63, :])))
        for ri, si in enumerate([0, 1, 0, 2, 3, 2]):
            Y(lambda e, ri=ri, si=si: sD.dma(e.dma_start(
                c.jarr_s[ri], jrows[si:si + 1, :])))
        for ri in (6, 7):
            Y(lambda e, ri=ri: sD.dma(e.dma_start(c.jarr_s[ri], onesrow[:])))
        dn = sD.n
        Y(lambda e, v=dn: e.wait_ge(sD.h, v))
        sch.run(nc)


def _emit_b1(nc, c, r, sems):
    sD, sG, sT, sV, sSc, sQ, sO = sems
    ph = contextlib.ExitStack()
    with ph:
        sch = Sched()
        Y = mk_adder(sch.q["y"])
        V = mk_adder(sch.q["v"], track=True)
        T = mk_adder(sch.q["t"])
        SC = mk_adder(sch.q["s"])

        def sbP(name, shape, dtyp=F32):
            return ph.enter_context(nc.sbuf_tensor(name, shape, dtyp))

        bvbc = sbP("b1_bvbc", [128, DM])
        wkh = sbP("wkh", [128, NCH * DM], BF16)
        wkl = sbP("wkl", [128, NCH * DM], BF16)
        wvh = sbP("wvh", [128, NCH * DM], BF16)
        wvl = sbP("wvl", [128, NCH * DM], BF16)
        wstg = sbP("b1_wstg", [128, DM])
        xst = [sbP(f"xst{i}", [128, DM]) for i in range(2)]
        sqt = [sbP(f"sqt{i}", [128, DM]) for i in range(2)]
        lnst = sbP("lnst", [128, 8])
        xnt = [sbP(f"xnt{i}", [128, DM]) for i in range(2)]
        xqo = sbP("xqo", [128, DM], BF16)
        qsA = sbP("qsA", [128, 64])
        qsB = sbP("qsB", [128, 64])
        qsC = sbP("qsC", [128, 64])
        qsD = sbP("qsD", [128, 64])
        qtA = sbP("qtA", [128, DM])
        qtB = sbP("qtB", [128, DM])
        xhi = [sbP(f"xhi{i}", [128, 128], BF16) for i in range(2)]
        xlo = [sbP(f"xlo{i}", [128, 128], BF16) for i in range(2)]
        xnfh = sbP("xnfh", [128, NCH * 512], BF16)
        xnfl = sbP("xnfl", [128, NCH * 512], BF16)
        kevt = sbP("kevt", [128, 512])
        kevh = [sbP(f"kevh{i}", [128, 512], BF16) for i in range(2)]
        kevl = [sbP(f"kevl{i}", [128, 512], BF16) for i in range(2)]
        ksq = sbP("ksq", [128, 512], BF16)
        k2st = sbP("k2st", [1, 4])
        xqstg = [sbP(f"xqstg{i}", [128, 128], BF16) for i in range(2)]
        vevh = [sbP(f"vevh{i}", [128, 512], BF16) for i in range(2)]
        vevl = [sbP(f"vevl{i}", [128, 512], BF16) for i in range(2)]
        vstg = [sbP(f"vstg{i}", [128, 8 * 128], BF16) for i in range(2)]
        xbsum = sbP("xbsum", [128, 512])
        ptr_ = ph.enter_context(nc.psum_tensor("ptr_", [128, 128], F32))
        pk = ph.enter_context(nc.psum_tensor("pk", [128, 512], F32))
        pvp = ph.enter_context(nc.psum_tensor("pvp", [128, 512], F32))
        pk2a = ph.enter_context(nc.psum_tensor("pk2a", [1, 512], F32))
        pk2b = ph.enter_context(nc.psum_tensor("pk2b", [1, 512], F32))
        pxq = ph.enter_context(nc.psum_tensor("pxq", [128, 128], BF16))

        Y(lambda e: sD.dma(e.dma_start(bvbc[:], c.bv_bc_in)))
        vprev = 0
        for wi, (win, whi, wlo) in enumerate([(c.wk_in, wkh, wkl),
                                              (c.wv_in, wvh, wvl)]):
            for ch in range(NCH):
                if wi + ch > 0:
                    Y(lambda e, v=vprev: e.wait_ge(sV.h, v))
                Y(lambda e, win=win, ch=ch: sD.dma(
                    e.dma_start(wstg[:], win[ch * 128:(ch + 1) * 128, :])))
                dn = sD.n
                V(lambda e, v=dn: e.wait_ge(sD.h, v))
                V(lambda e, whi=whi, ch=ch: e.tensor_copy(
                    whi[:, ch * DM:(ch + 1) * DM], wstg[:]))
                V(lambda e, whi=whi, wlo=wlo, ch=ch: sV.inc(e.tensor_tensor(
                    wlo[:, ch * DM:(ch + 1) * DM], wstg[:],
                    whi[:, ch * DM:(ch + 1) * DM], ALU.subtract)))
                vprev = sV.n
        for i in range(2):
            V(lambda e, i=i: e.memset(vstg[i][:], 0.0))
            for hh in range(8):
                V(lambda e, i=i, hh=hh: e.memset(
                    vstg[i][:, hh * 128 + 64:hh * 128 + 65], 1.0))
        V(lambda e: sV.inc(e.memset(lnst[:], 0.0)))

        last_kv_mm = sT.n
        vln = {}
        dma_mark = {}
        vq = 0
        vxq = 0
        v_xqev = 0
        v_ptr = 0
        for blk in range(NB):
            V(lambda e, v=last_kv_mm: e.wait_ge(sT.h, v))
            for tsub in range(4):
                idx = blk * 4 + tsub
                tok0 = blk * 512 + tsub * 128
                par = idx % 2
                xt, sq, xn_ = xst[par], sqt[par], xnt[par]
                if idx >= 2:
                    Y(lambda e, v=vln.get(idx - 2, 0): e.wait_ge(sV.h, v))
                Y(lambda e, xt=xt, tok0=tok0: sD.dma(
                    e.dma_start(xt[:], c.x_in[tok0:tok0 + 128, :])))
                dn = sD.n
                SC(lambda e, v=dn: e.wait_ge(sD.h, v))
                if idx >= 2:
                    SC(lambda e, v=vln.get(idx - 2, 0): e.wait_ge(sV.h, v))
                SC(lambda e, xt=xt, sq=sq: sSc.inc(
                    e.activation(sq[:], xt[:], ACT.Square)))
                ssq = sSc.n
                V(lambda e, v=dn: e.wait_ge(sD.h, v))
                V(lambda e, v=ssq: e.wait_ge(sSc.h, v))
                V(lambda e, xt=xt: e.reduce_sum(lnst[:, 0:1], xt[:], AX.X))
                V(lambda e, sq=sq: e.reduce_sum(lnst[:, 1:2], sq[:], AX.X))
                V(lambda e: e.tensor_scalar(lnst[:, 2:3], lnst[:, 0:1],
                                            1.0 / DM, None, ALU.mult))
                V(lambda e: e.tensor_scalar(lnst[:, 3:4], lnst[:, 1:2],
                                            1.0 / DM, None, ALU.mult))
                V(lambda e: e.tensor_tensor(lnst[:, 4:5], lnst[:, 2:3],
                                            lnst[:, 2:3], ALU.mult))
                V(lambda e: e.tensor_tensor(lnst[:, 5:6], lnst[:, 3:4],
                                             lnst[:, 4:5], ALU.subtract))
                V(lambda e: sV.inc(e.tensor_scalar(lnst[:, 5:6], lnst[:, 5:6],
                                                   EPS, None, ALU.add)))
                vv = sV.n
                SC(lambda e, v=vv: e.wait_ge(sV.h, v))
                SC(lambda e: sSc.inc(e.activation(lnst[:, 6:7], lnst[:, 5:6],
                                                  ACT.Sqrt)))
                srt = sSc.n
                V(lambda e, v=srt: e.wait_ge(sSc.h, v))
                V(lambda e: e.reciprocal(lnst[:, 7:8], lnst[:, 6:7]))
                V(lambda e, xt=xt, xn_=xn_: sV.inc(e.tensor_scalar(
                    xn_[:], xt[:], lnst[:, 2:3], lnst[:, 7:8],
                    ALU.subtract, ALU.mult)))
                vln[idx] = sV.n
                if blk < 4:
                    V(lambda e, v=vln[idx]: e.wait_ge(sV.h, v))
                    if vxq:
                        V(lambda e, v=vxq: e.wait_ge(sT.h, v))
                    last = emit_nvfp4(V, xn_[:], xqo[:], qsA[:], qsB[:], qsC[:],
                                      qsD[:], qtA[:], qtB[:], r.scbc[:, 5:6])
                    V(lambda e, last=last: sV.inc(last(e)))
                    vq = sV.n
                    for ch in range(NCH):
                        nxq = idx * NCH + ch
                        spar = nxq % 2
                        T(lambda e, v=vq: e.wait_ge(sV.h, v))
                        T(lambda e, v=v_xqev: e.wait_ge(sV.h, v))
                        T(lambda e, ch=ch: sT.inc(e.transpose(
                            pxq[:], xqo[:, ch * 128:ch * 128 + 128],
                            r.identb[:])))
                        tt = sT.n
                        col = ch * TOWN + blk * 512 + tsub * 128
                        V(lambda e, v=tt: e.wait_ge(sT.h, v))
                        if nxq >= 2:
                            V(lambda e, v=dma_mark.get(("xq", nxq - 2), 0):
                              e.wait_ge(sD.h, v))
                        V(lambda e, spar=spar: sV.inc(e.tensor_copy(
                            xqstg[spar][:], pxq[:])))
                        v_xqev = sV.n
                        Y(lambda e, v=v_xqev: e.wait_ge(sV.h, v))
                        Y(lambda e, col=col, spar=spar: sD.dma(e.dma_start(
                            c.xqf_s[:, col:col + 128], xqstg[spar][:])))
                        dma_mark[("xq", nxq)] = sD.n
                    vxq = sT.n
                    vln[idx] = sV.n
                vlni = vln[idx]
                for ch in range(NCH):
                    n_tr = idx * NCH + ch
                    cpar = n_tr % 2
                    T(lambda e, v=vlni: e.wait_ge(sV.h, v))
                    if n_tr >= 1:
                        T(lambda e, v=v_ptr: e.wait_ge(sV.h, v))
                    T(lambda e, ch=ch, xn_=xn_: sT.inc(e.transpose(
                        ptr_[:], xn_[:, ch * 128:(ch + 1) * 128], r.ident[:])))
                    ttr = sT.n
                    V(lambda e, v=ttr: e.wait_ge(sT.h, v))
                    if n_tr >= 2:
                        V(lambda e, v=dma_mark.get(("x", n_tr - 2), 0):
                          e.wait_ge(sD.h, v))
                    V(lambda e, cpar=cpar: e.tensor_copy(xhi[cpar][:], ptr_[:]))
                    V(lambda e, cpar=cpar: sV.inc(e.tensor_tensor(
                        xlo[cpar][:], ptr_[:], xhi[cpar][:], ALU.subtract)))
                    v_ptr = sV.n
                    col = ch * 512 + tsub * 128
                    V(lambda e, col=col, cpar=cpar: e.tensor_copy(
                        xnfh[:, col:col + 128], xhi[cpar][:]))
                    V(lambda e, col=col, cpar=cpar: sV.inc(e.tensor_copy(
                        xnfl[:, col:col + 128], xlo[cpar][:])))
                    vst2 = sV.n
                    Y(lambda e, v=vst2: e.wait_ge(sV.h, v))
                    Y(lambda e, ch=ch, tok0=tok0, cpar=cpar: sD.dma(e.dma_start(
                        c.xnh_s[ch, :, tok0:tok0 + 128], xhi[cpar][:])))
                    Y(lambda e, ch=ch, tok0=tok0, cpar=cpar: sD.dma(e.dma_start(
                        c.xnl_s[ch, :, tok0:tok0 + 128], xlo[cpar][:])))
                    dma_mark[("x", n_tr)] = sD.n
            v_stageA = sV.n
            for ch in range(NCH):
                V(lambda e, ch=ch: e.tensor_tensor(
                    xbsum[:], xnfh[:, ch * 512:(ch + 1) * 512],
                    xnfl[:, ch * 512:(ch + 1) * 512], ALU.add))
                V(lambda e: e.reduce_sum(lnst[:, 0:1], xbsum[:], AX.X))
                V(lambda e, ch=ch: sV.inc(e.tensor_tensor(
                    r.xbar[:, ch:ch + 1], r.xbar[:, ch:ch + 1],
                    lnst[:, 0:1], ALU.add)))
            v_xnf = sV.n

            v_kev = sV.n
            for m in range(NCH):
                kpar = m % 2
                T(lambda e, v=v_xnf: e.wait_ge(sV.h, v))
                T(lambda e, v=v_kev: e.wait_ge(sV.h, v))
                for ch in range(NCH):
                    w0 = ch * DM + m * 128
                    x0 = ch * 512
                    T(lambda e, w0=w0, x0=x0, st=(ch == 0): e.matmul(
                        pk[:], wkh[:, w0:w0 + 128], xnfh[:, x0:x0 + 512],
                        start=st, stop=False))
                    T(lambda e, w0=w0, x0=x0: e.matmul(
                        pk[:], wkl[:, w0:w0 + 128], xnfh[:, x0:x0 + 512],
                        start=False, stop=False))
                    if ch == NCH - 1:
                        T(lambda e, w0=w0, x0=x0: sT.inc(e.matmul(
                            pk[:], wkh[:, w0:w0 + 128], xnfl[:, x0:x0 + 512],
                            start=False, stop=True)))
                    else:
                        T(lambda e, w0=w0, x0=x0: e.matmul(
                            pk[:], wkh[:, w0:w0 + 128], xnfl[:, x0:x0 + 512],
                            start=False, stop=False))
                tk = sT.n
                V(lambda e, v=tk: e.wait_ge(sT.h, v))
                if blk * NCH + m >= 2:
                    V(lambda e, v=dma_mark.get(("k", blk * NCH + m - 2), 0):
                      e.wait_ge(sD.h, v))
                V(lambda e, m=m: e.tensor_scalar(
                    kevt[:], pk[:], r.biases[:, NCH + m:NCH + m + 1], None, ALU.add))
                V(lambda e, kpar=kpar: e.tensor_copy(kevh[kpar][:], kevt[:]))
                V(lambda e, kpar=kpar: e.tensor_tensor(
                    kevl[kpar][:], kevt[:], kevh[kpar][:], ALU.subtract))
                V(lambda e: e.tensor_tensor(kevt[:], kevt[:], kevt[:], ALU.mult))
                V(lambda e: sV.inc(e.tensor_copy(ksq[:], kevt[:])))
                v_kev = sV.n
                T(lambda e, v=v_kev: e.wait_ge(sV.h, v))
                T(lambda e: sT.inc(e.matmul(pk2a[:], r.ones128b[0:64, 0:1],
                                            ksq[0:64, :], start=True, stop=True)))
                T(lambda e: sT.inc(e.matmul(pk2b[:], r.ones128b[64:128, 0:1],
                                            ksq[64:128, :], start=True, stop=True)))
                tk2 = sT.n
                V(lambda e, v=tk2: e.wait_ge(sT.h, v))
                h0, h1 = 2 * m, 2 * m + 1
                V(lambda e: e.tensor_reduce(k2st[0:1, 0:1], pk2a[:], AX.X, ALU.max))
                V(lambda e: e.tensor_reduce(k2st[0:1, 1:2], pk2b[:], AX.X, ALU.max))
                V(lambda e, h0=h0: e.tensor_tensor(
                    r.kmax2[0:1, h0:h0 + 1], r.kmax2[0:1, h0:h0 + 1],
                    k2st[0:1, 0:1], ALU.max))
                V(lambda e, h1=h1: sV.inc(e.tensor_tensor(
                    r.kmax2[0:1, h1:h1 + 1], r.kmax2[0:1, h1:h1 + 1],
                    k2st[0:1, 1:2], ALU.max)))
                v_kev = sV.n
                t0 = blk * 512
                Y(lambda e, v=v_kev: e.wait_ge(sV.h, v))
                Y(lambda e, h0=h0, t0=t0, kpar=kpar: sD.dma(e.dma_start(
                    c.karr_s[h0, 0:64, t0:t0 + 512], kevh[kpar][0:64, :])))
                Y(lambda e, h0=h0, t0=t0, kpar=kpar: sD.dma(e.dma_start(
                    c.karr_s[h0, 64:128, t0:t0 + 512], kevl[kpar][0:64, :])))
                Y(lambda e, h1=h1, t0=t0, kpar=kpar: sD.dma(e.dma_start(
                    c.karr_s[h1, 0:64, t0:t0 + 512], kevh[kpar][64:128, :])))
                Y(lambda e, h1=h1, t0=t0, kpar=kpar: sD.dma(e.dma_start(
                    c.karr_s[h1, 64:128, t0:t0 + 512], kevl[kpar][64:128, :])))
                dma_mark[("k", blk * NCH + m)] = sD.n

            v_vev = sV.n
            for tsub in range(4):
                t0 = blk * 512 + tsub * 128
                for ot in range(2):
                    rnd = (blk * 4 + tsub) * 2 + ot
                    vpar = rnd % 2
                    T(lambda e, v=v_vev: e.wait_ge(sV.h, v))
                    for ch in range(NCH):
                        x0 = ch * 512 + tsub * 128
                        w0 = ch * DM + ot * 512
                        T(lambda e, x0=x0, w0=w0, st=(ch == 0): e.matmul(
                            pvp[:], xnfh[:, x0:x0 + 128], wvh[:, w0:w0 + 512],
                            start=st, stop=False))
                        T(lambda e, x0=x0, w0=w0: e.matmul(
                            pvp[:], xnfl[:, x0:x0 + 128], wvh[:, w0:w0 + 512],
                            start=False, stop=False))
                        if ch == NCH - 1:
                            T(lambda e, x0=x0, w0=w0: sT.inc(e.matmul(
                                pvp[:], xnfh[:, x0:x0 + 128], wvl[:, w0:w0 + 512],
                                start=False, stop=True)))
                        else:
                            T(lambda e, x0=x0, w0=w0: e.matmul(
                                pvp[:], xnfh[:, x0:x0 + 128], wvl[:, w0:w0 + 512],
                                start=False, stop=False))
                    tv = sT.n
                    V(lambda e, v=tv: e.wait_ge(sT.h, v))
                    if rnd >= 2:
                        V(lambda e, v=dma_mark.get(("v", rnd - 2), 0):
                          e.wait_ge(sD.h, v))
                    V(lambda e, ot=ot: e.tensor_tensor(
                        kevt[:], pvp[:], bvbc[:, ot * 512:ot * 512 + 512], ALU.add))
                    V(lambda e, vpar=vpar: e.tensor_copy(vevh[vpar][:], kevt[:]))
                    V(lambda e, vpar=vpar: sV.inc(e.tensor_tensor(
                        vevl[vpar][:], kevt[:], vevh[vpar][:], ALU.subtract)))
                    for hh in range(8):
                        V(lambda e, hh=hh, vpar=vpar: e.tensor_copy(
                            vstg[vpar][:, hh * 128:hh * 128 + 64],
                            vevh[vpar][:, hh * 64:hh * 64 + 64]))
                        V(lambda e, hh=hh, vpar=vpar: sV.inc(e.tensor_copy(
                            vstg[vpar][:, hh * 128 + 65:hh * 128 + 128],
                            vevl[vpar][:, hh * 64:hh * 64 + 63])))
                    v_vev = sV.n
                    Y(lambda e, v=v_vev: e.wait_ge(sV.h, v))
                    for hh in range(8):
                        h = ot * 8 + hh
                        Y(lambda e, h=h, hh=hh, t0=t0, vpar=vpar: sD.dma(e.dma_start(
                            c.varr_s[h, t0:t0 + 128, :],
                            vstg[vpar][:, hh * 128:hh * 128 + 128])))
                    dma_mark[("v", rnd)] = sD.n

            last_kv_mm = sT.n
        sch.run(nc)
def _emit_b2(nc, c, r, sems):
    sD, sG, sT, sV, sSc, sQ, sO = sems
    ph = contextlib.ExitStack()
    with ph:
        sch = Sched()
        Y = mk_adder(sch.q["y"])
        V = mk_adder(sch.q["v"], track=True)
        T = mk_adder(sch.q["t"])
        SC = mk_adder(sch.q["s"])

        def sbP(name, shape, dtyp=F32):
            return ph.enter_context(nc.sbuf_tensor(name, shape, dtyp))

        wqh = sbP("wqh", [128, NCH * DM], BF16)
        wql = sbP("wql", [128, NCH * DM], BF16)
        wfqh = sbP("wfqh", [128, NCH * DM], BF16)
        wfql = sbP("wfql", [128, NCH * DM], BF16)
        wstg = sbP("b2_wstg", [128, DM])
        wfgh = sbP("wfgh", [128, DM], BF16)
        wfgl = sbP("wfgl", [128, DM], BF16)
        xrh = [sbP(f"xrh{i}", [128, 512], BF16) for i in range(2)]
        xrl = [sbP(f"xrl{i}", [128, 512], BF16) for i in range(2)]
        qevt = sbP("qevt", [128, 512])
        qhi = [sbP(f"qhi{i}", [128, 512], BF16) for i in range(2)]
        qlo = [sbP(f"qlo{i}", [128, 512], BF16) for i in range(2)]
        qsqt = sbP("qsqt", [128, 512], BF16)
        xbh = sbP("xbh", [128, NCH], BF16)
        xbl = sbP("xbl", [128, NCH], BF16)
        glof = sbP("glof", [128, NCH])
        gloh = sbP("gloh", [128, NCH], BF16)
        glol = sbP("glol", [128, NCH], BF16)
        bpp = sbP("bpp", [128, NCH])
        ctf = sbP("ctf", [128, 512])
        cth = sbP("cth", [128, 512], BF16)
        ctl = sbP("ctl", [128, 512], BF16)
        przA = sbP("przA", [1, 512])
        przB = sbP("przB", [1, 512])
        przC = sbP("przC", [1, 512])
        featq = sbP("featq", [1, 3 * 1024])
        featw = sbP("featw", [1, 6 * 1024])
        featb = sbP("featb", [1, 8 * 1024], BF16)
        kmaxr = sbP("kmaxr", [1, H])
        pq = ph.enter_context(nc.psum_tensor("pq", [128, 512], F32))
        pcp = ph.enter_context(nc.psum_tensor("pcp", [128, 512], F32))
        pbp = ph.enter_context(nc.psum_tensor("pbp", [128, NCH], F32))
        ppq = ph.enter_context(nc.psum_tensor("ppq", [1, 512], F32))
        pzq = ph.enter_context(nc.psum_tensor("pzq", [1, 512], F32))
        pq2a = ph.enter_context(nc.psum_tensor("pq2a", [1, 512], F32))
        pq2b = ph.enter_context(nc.psum_tensor("pq2b", [1, 512], F32))

        vprev = 0
        for wi, (win, whi, wlo) in enumerate([(c.wq_in, wqh, wql),
                                              (c.wfq_in, wfqh, wfql)]):
            for ch in range(NCH):
                if wi + ch > 0:
                    Y(lambda e, v=vprev: e.wait_ge(sV.h, v))
                Y(lambda e, win=win, ch=ch: sD.dma(
                    e.dma_start(wstg[:], win[ch * 128:(ch + 1) * 128, :])))
                dn = sD.n
                V(lambda e, v=dn: e.wait_ge(sD.h, v))
                if wi == 1:
                    V(lambda e: e.tensor_scalar(wstg[:], wstg[:], 8.0, None,
                                                ALU.mult))
                V(lambda e, whi=whi, ch=ch: e.tensor_copy(
                    whi[:, ch * DM:(ch + 1) * DM], wstg[:]))
                V(lambda e, whi=whi, wlo=wlo, ch=ch: sV.inc(e.tensor_tensor(
                    wlo[:, ch * DM:(ch + 1) * DM], wstg[:],
                    whi[:, ch * DM:(ch + 1) * DM], ALU.subtract)))
                vprev = sV.n

        V(lambda e: e.tensor_scalar(r.xbar[:], r.xbar[:], 1.0 / S, None, ALU.mult))
        V(lambda e: e.tensor_copy(xbh[:], r.xbar[:]))
        V(lambda e: sV.inc(e.tensor_tensor(xbl[:], r.xbar[:], xbh[:], ALU.subtract)))
        v_xb = sV.n

        T(lambda e, v=v_xb: e.wait_ge(sV.h, v))
        for m in range(NCH):
            for ch in range(NCH):
                w0 = ch * DM + m * 128
                T(lambda e, w0=w0, ch=ch, m=m, st=(m == 0 and ch == 0): e.matmul(
                    pbp[:, m:m + 1],
                    wqh[:, w0:w0 + 128], xbh[:, ch:ch + 1],
                    start=st, stop=False))
                T(lambda e, w0=w0, ch=ch, m=m: e.matmul(
                    pbp[:, m:m + 1], wql[:, w0:w0 + 128], xbh[:, ch:ch + 1],
                    start=False, stop=False))
                T(lambda e, w0=w0, ch=ch, m=m: (sT.inc(e.matmul(
                    pbp[:, m:m + 1], wqh[:, w0:w0 + 128], xbl[:, ch:ch + 1],
                    start=False, stop=True))
                    if (ch == NCH - 1 and m == NCH - 1) else e.matmul(
                    pbp[:, m:m + 1], wqh[:, w0:w0 + 128], xbl[:, ch:ch + 1],
                    start=False, stop=False)))
        t_glo = sT.n
        V(lambda e, v=t_glo: e.wait_ge(sT.h, v))
        V(lambda e: e.tensor_tensor(glof[:], pbp[:], r.biases[:, 0:NCH], ALU.add))
        V(lambda e: e.tensor_copy(gloh[:], glof[:]))
        V(lambda e: sV.inc(e.tensor_tensor(glol[:], glof[:], gloh[:], ALU.subtract)))
        v_glo = sV.n

        v_fgs = v_glo
        for ch in range(NCH):
            Y(lambda e, v=v_fgs: e.wait_ge(sV.h, v))
            Y(lambda e, ch=ch: sD.dma(e.dma_start(
                wstg[:], c.wfg_in[ch * 128:(ch + 1) * 128, :])))
            dn = sD.n
            V(lambda e, v=dn: e.wait_ge(sD.h, v))
            V(lambda e: e.tensor_copy(wfgh[:], wstg[:]))
            V(lambda e: sV.inc(e.tensor_tensor(wfgl[:], wstg[:], wfgh[:],
                                               ALU.subtract)))
            v_fgs = sV.n
            T(lambda e, v=v_fgs: e.wait_ge(sV.h, v))
            if ch == 0:
                T(lambda e, v=v_glo: e.wait_ge(sV.h, v))
            for m in range(NCH):
                w0 = m * 128
                T(lambda e, w0=w0, ch=ch, m=m, st=(ch == 0 and m == 0): e.matmul(
                    pq[:, m:m + 1], wfgh[:, w0:w0 + 128], gloh[:, ch:ch + 1],
                    start=st, stop=False))
                T(lambda e, w0=w0, ch=ch, m=m: e.matmul(
                    pq[:, m:m + 1], wfgl[:, w0:w0 + 128], gloh[:, ch:ch + 1],
                    start=False, stop=False))
                T(lambda e, w0=w0, ch=ch, m=m: (sT.inc(e.matmul(
                    pq[:, m:m + 1], wfgh[:, w0:w0 + 128], glol[:, ch:ch + 1],
                    start=False, stop=True))
                    if (ch == NCH - 1 and m == NCH - 1) else e.matmul(
                    pq[:, m:m + 1], wfgh[:, w0:w0 + 128], glol[:, ch:ch + 1],
                    start=False, stop=False)))
        t_bpp = sT.n
        V(lambda e, v=t_bpp: e.wait_ge(sT.h, v))
        V(lambda e: e.tensor_tensor(bpp[:], pq[:, 0:NCH], r.biases[:, 3 * NCH:4 * NCH],
                                    ALU.add))
        V(lambda e: sV.inc(e.tensor_tensor(bpp[:], bpp[:],
                                           r.biases[:, 2 * NCH:3 * NCH], ALU.add)))
        v_bpp = sV.n

        dma_q = {}
        qk_mm = {}
        v_qev = sV.n
        for qch in range(4):
            q0 = qch * 512
            for m in range(NCH):
                rnd = qch * NCH + m
                for ch in range(NCH):
                    par = ch % 2
                    nld = rnd * NCH + ch
                    if nld >= 2:
                        Y(lambda e, v=qk_mm.get(nld - 2, 0): e.wait_ge(sT.h, v))
                    Y(lambda e, ch=ch, q0=q0, par=par: sD.dma(e.dma_start(
                        xrh[par][:], c.xnh_s[ch, :, q0:q0 + 512])))
                    Y(lambda e, ch=ch, q0=q0, par=par: sD.dma(e.dma_start(
                        xrl[par][:], c.xnl_s[ch, :, q0:q0 + 512])))
                    dn = sD.n
                    w0 = ch * DM + m * 128
                    T(lambda e, v=dn: e.wait_ge(sD.h, v))
                    if ch == 0:
                        T(lambda e, v=v_qev: e.wait_ge(sV.h, v))
                    T(lambda e, w0=w0, par=par, st=(ch == 0): e.matmul(
                        pq[:], wqh[:, w0:w0 + 128], xrh[par][:],
                        start=st, stop=False))
                    T(lambda e, w0=w0, par=par: e.matmul(
                        pq[:], wql[:, w0:w0 + 128], xrh[par][:],
                        start=False, stop=False))
                    T(lambda e, w0=w0, par=par, sp=(ch == NCH - 1): sT.inc(
                        e.matmul(pq[:], wqh[:, w0:w0 + 128], xrl[par][:],
                                 start=False, stop=sp)))
                    qk_mm[nld] = sT.n
                tq = sT.n
                qpar = rnd % 2
                V(lambda e, v=tq: e.wait_ge(sT.h, v))
                if rnd >= 2:
                    V(lambda e, v=dma_q.get(rnd - 2, 0): e.wait_ge(sD.h, v))
                V(lambda e, m=m: e.tensor_scalar(
                    qevt[:], pq[:], r.biases[:, m:m + 1], 0.125, ALU.add, ALU.mult))
                V(lambda e, qpar=qpar: e.tensor_copy(qhi[qpar][:], qevt[:]))
                V(lambda e, qpar=qpar: e.tensor_tensor(
                    qlo[qpar][:], qevt[:], qhi[qpar][:], ALU.subtract))
                V(lambda e: e.tensor_tensor(qevt[:], qevt[:], qevt[:], ALU.mult))
                V(lambda e: sV.inc(e.tensor_copy(qsqt[:], qevt[:])))
                v_qev = sV.n
                T(lambda e, v=v_qev: e.wait_ge(sV.h, v))
                T(lambda e: sT.inc(e.matmul(pq2a[:], r.ones128b[0:64, 0:1],
                                            qsqt[0:64, :], start=True, stop=True)))
                T(lambda e: sT.inc(e.matmul(pq2b[:], r.ones128b[64:128, 0:1],
                                            qsqt[64:128, :], start=True, stop=True)))
                tq2 = sT.n
                h0, h1 = 2 * m, 2 * m + 1
                V(lambda e, v=tq2: e.wait_ge(sT.h, v))
                V(lambda e, v=dma_q.get(rnd - 1, 0): e.wait_ge(sD.h, v))
                V(lambda e: e.tensor_copy(przA[:], pq2a[:]))
                V(lambda e: sV.inc(e.tensor_copy(przB[:], pq2b[:])))
                v_qev = sV.n
                Y(lambda e, v=v_qev: e.wait_ge(sV.h, v))
                Y(lambda e, h0=h0, q0=q0: sD.dma(e.dma_start(
                    c.q2n_s[h0, q0:q0 + 512], przA[:])))
                Y(lambda e, h1=h1, q0=q0: sD.dma(e.dma_start(
                    c.q2n_s[h1, q0:q0 + 512], przB[:])))
                Y(lambda e, m=m, q0=q0, qpar=qpar: sD.dma(e.dma_start(
                    c.mqh_s[m, :, q0:q0 + 512], qhi[qpar][:])))
                Y(lambda e, m=m, q0=q0, qpar=qpar: sD.dma(e.dma_start(
                    c.mql_s[m, :, q0:q0 + 512], qlo[qpar][:])))
                dma_q[rnd] = sD.n

        dma_c = {}
        c_mm = {}
        v_cev = sSc.n
        v_ctf = sV.n
        v_pzr = sSc.n
        for qch in range(4):
            q0 = qch * 512
            for m in range(NCH):
                rnd = qch * NCH + m
                for ch in range(NCH):
                    par = ch % 2
                    nld = rnd * NCH + ch
                    if nld >= 2:
                        Y(lambda e, v=c_mm.get(nld - 2, 0): e.wait_ge(sT.h, v))
                    Y(lambda e, ch=ch, q0=q0, par=par: sD.dma(e.dma_start(
                        xrh[par][:], c.mqh_s[ch, :, q0:q0 + 512])))
                    Y(lambda e, ch=ch, q0=q0, par=par: sD.dma(e.dma_start(
                        xrl[par][:], c.mql_s[ch, :, q0:q0 + 512])))
                    dn = sD.n
                    w0 = ch * DM + m * 128
                    T(lambda e, v=dn: e.wait_ge(sD.h, v))
                    if ch == 0:
                        T(lambda e, v=v_cev: e.wait_ge(sSc.h, v))
                        T(lambda e, v=v_pzr: e.wait_ge(sSc.h, v))
                    T(lambda e, w0=w0, par=par, st=(ch == 0): e.matmul(
                        pcp[:], wfqh[:, w0:w0 + 128], xrh[par][:],
                        start=st, stop=False))
                    T(lambda e, w0=w0, par=par: e.matmul(
                        pcp[:], wfql[:, w0:w0 + 128], xrh[par][:],
                        start=False, stop=False))
                    T(lambda e, w0=w0, par=par, sp=(ch == NCH - 1): sT.inc(
                        e.matmul(pcp[:], wfqh[:, w0:w0 + 128], xrl[par][:],
                                 start=False, stop=sp)))
                    c_mm[nld] = sT.n
                tc = sT.n
                SC(lambda e, v=tc: e.wait_ge(sT.h, v))
                SC(lambda e, v=v_ctf: e.wait_ge(sV.h, v))
                SC(lambda e, m=m: sSc.inc(e.activation(
                    ctf[:], pcp[:], ACT.Tanh, bias=bpp[:, m:m + 1])))
                v_cev = sSc.n
                V(lambda e, v=v_cev: e.wait_ge(sSc.h, v))
                V(lambda e: e.tensor_copy(cth[:], ctf[:]))
                V(lambda e: sV.inc(e.tensor_tensor(ctl[:], ctf[:], cth[:],
                                                   ALU.subtract)))
                v_ctf = sV.n
                for hh in range(2):
                    h = 2 * m + hh
                    b0_ = hh * 64
                    T(lambda e, v=v_ctf: e.wait_ge(sV.h, v))
                    T(lambda e, v=v_pzr: e.wait_ge(sSc.h, v))
                    for ps_, col in ((ppq, h), (pzq, H + h)):
                        T(lambda e, ps_=ps_, col=col, b0_=b0_: e.matmul(
                            ps_[:], r.upzh[b0_:b0_ + 64, col:col + 1],
                            cth[b0_:b0_ + 64, :], start=True, stop=False))
                        T(lambda e, ps_=ps_, col=col, b0_=b0_: e.matmul(
                            ps_[:], r.upzl[b0_:b0_ + 64, col:col + 1],
                            cth[b0_:b0_ + 64, :], start=False, stop=False))
                        T(lambda e, ps_=ps_, col=col, b0_=b0_: sT.inc(e.matmul(
                            ps_[:], r.upzh[b0_:b0_ + 64, col:col + 1],
                            ctl[b0_:b0_ + 64, :], start=False, stop=True)))
                    tpz = sT.n
                    SC(lambda e, v=tpz: e.wait_ge(sT.h, v))
                    SC(lambda e, v=dma_c.get(("pz", rnd, hh - 1) if hh else ("pz", rnd - 1, 1), 0):
                       e.wait_ge(sD.h, v))
                    SC(lambda e: e.activation(przB[:], ppq[:], ACT.Sigmoid))
                    SC(lambda e: sSc.inc(e.activation(przC[:], pzq[:], ACT.Sigmoid)))
                    v_pzr = sSc.n
                    Y(lambda e, v=v_pzr: e.wait_ge(sSc.h, v))
                    Y(lambda e, h=h, q0=q0: sD.dma(e.dma_start(
                        c.sgp_s[h, q0:q0 + 512], przB[:])))
                    Y(lambda e, h=h, q0=q0: sD.dma(e.dma_start(
                        c.sgz_s[h, q0:q0 + 512], przC[:])))
                    dma_c[("pz", rnd, hh)] = sD.n

        SC(lambda e: sSc.inc(e.activation(kmaxr[:], r.kmax2[:], ACT.Sqrt)))
        s_kx = sSc.n
        V(lambda e, v=s_kx: e.wait_ge(sSc.h, v))
        fW = 1024
        dprev = 0
        for h in range(H):
            for fh in range(2):
                f0 = fh * fW
                if h + fh >= 1:
                    Y(lambda e, v=dprev: e.wait_ge(sV.h, v))
                Y(lambda e, h=h, f0=f0: sD.dma(e.dma_start(
                    featq[0:1, 0:fW], c.sgp_s[h, f0:f0 + fW].unsqueeze(0))))
                Y(lambda e, h=h, f0=f0: sD.dma(e.dma_start(
                    featq[0:1, fW:2 * fW], c.sgz_s[h, f0:f0 + fW].unsqueeze(0))))
                Y(lambda e, h=h, f0=f0: sD.dma(e.dma_start(
                    featq[0:1, 2 * fW:3 * fW], c.q2n_s[h, f0:f0 + fW].unsqueeze(0))))
                dln = sD.n
                V(lambda e, v=dln: e.wait_ge(sD.h, v))
                V(lambda e: e.reciprocal(featw[0:1, 0:fW], featq[0:1, fW:2 * fW]))
                V(lambda e: e.tensor_tensor(featw[0:1, 0:fW], featw[0:1, 0:fW],
                                            featw[0:1, 0:fW], ALU.mult))
                V(lambda e: e.tensor_tensor(featw[0:1, fW:2 * fW],
                                            featq[0:1, 0:fW],
                                            featw[0:1, 0:fW], ALU.mult))
                V(lambda e: e.tensor_scalar(featw[0:1, 2 * fW:3 * fW],
                                            featw[0:1, fW:2 * fW],
                                            float(np.float32(4.0 / 8192.0)),
                                            None, ALU.mult))
                V(lambda e: e.tensor_scalar(featw[0:1, 3 * fW:4 * fW],
                                            featw[0:1, 0:fW],
                                            float(np.float32(-2.0 / (8192.0 * 8192.0))),
                                            None, ALU.mult))
                V(lambda e: e.tensor_tensor(featw[0:1, 4 * fW:5 * fW],
                                            featq[0:1, 0:fW],
                                            featw[0:1, fW:2 * fW], ALU.mult))
                V(lambda e: e.tensor_scalar(featw[0:1, 4 * fW:5 * fW],
                                            featw[0:1, 4 * fW:5 * fW], -2.0,
                                            None, ALU.mult))
                sq_in = featw[0:1, 5 * fW:6 * fW]
                V(lambda e: sV.inc(e.tensor_copy(sq_in, featq[0:1, 2 * fW:3 * fW])))
                vq2 = sV.n
                SC(lambda e, v=vq2: e.wait_ge(sV.h, v))
                SC(lambda e, sq_in=sq_in: sSc.inc(e.activation(sq_in, sq_in, ACT.Sqrt)))
                sq2 = sSc.n
                V(lambda e, v=sq2: e.wait_ge(sSc.h, v))
                V(lambda e, h=h: e.tensor_scalar(sq_in, sq_in,
                                                 kmaxr[0:1, h:h + 1], None, ALU.mult))
                V(lambda e: e.tensor_tensor(featw[0:1, 4 * fW:5 * fW],
                                            featw[0:1, 4 * fW:5 * fW], sq_in,
                                            ALU.subtract))
                for i_, src0 in ((0, 2 * fW), (1, 3 * fW), (2, 4 * fW)):
                    V(lambda e, i_=i_, s0=src0: e.tensor_copy(
                        featb[0:1, (2 * i_) * fW:(2 * i_) * fW + fW],
                        featw[0:1, s0:s0 + fW]))
                    V(lambda e, i_=i_, s0=src0: e.tensor_tensor(
                        featb[0:1, (2 * i_ + 1) * fW:(2 * i_ + 1) * fW + fW],
                        featw[0:1, s0:s0 + fW],
                        featb[0:1, (2 * i_) * fW:(2 * i_) * fW + fW], ALU.subtract))
                V(lambda e: sV.inc(e.engine_nop()))
                vfeat = sV.n
                dprev = vfeat
                Y(lambda e, v=vfeat: e.wait_ge(sV.h, v))
                rowmap = [0, 0, 1, 2, 2, 3, 4, 5]
                for ri, si in enumerate(rowmap):
                    Y(lambda e, h=h, ri=ri, si=si, f0=f0: sD.dma(e.dma_start(
                        c.qarr2_s[h, ri, f0:f0 + fW],
                        featb[0:1, si * fW:(si + 1) * fW])))
        dnf = sD.n
        Y(lambda e, v=dnf: e.wait_ge(sD.h, v))
        sch.run(nc)
def _emit_b3(nc, c, r, sems, wq_factory=None):
    sD, sG, sT, sV, sSc, sQ, sO = sems
    ph = contextlib.ExitStack()
    with ph:
        sch = Sched()
        Y = mk_adder(sch.q["y"])
        V = mk_adder(sch.q["v"], track=True)
        T = mk_adder(sch.q["t"])
        SC = mk_adder(sch.q["s"])

        def sbP(name, shape, dtyp=F32):
            return ph.enter_context(nc.sbuf_tensor(name, shape, dtyp))

        qR = [sbP(f"qR{i}", [128, TOWN], BF16) for i in range(2)]
        q2R = [sbP(f"q2R{i}", [72, TOWN], BF16) for i in range(2)]
        wo1R = [sbP(f"wo1R{i}", [128, DM], BF16) for i in range(2)]
        wo2R = [sbP(f"wo2R{i}", [128, DM], BF16) for i in range(2)]
        kR = [sbP(f"kR{i}", [128, 1024], BF16) for i in range(2)]
        k2R = [sbP(f"k2R{i}", [72, 1024], BF16) for i in range(2)]
        vR = [sbP(f"vR{i}", [128, 1024], BF16) for i in range(2)]
        ptmp = [sbP(f"ptmp{i}", [128, 1024]) for i in range(2)]
        pH = [sbP(f"pH{i}", [128, 1024], BF16) for i in range(2)]
        pL = [sbP(f"pL{i}", [128, 1024], BF16) for i in range(2)]
        sacc = sbP("sacc", [128, 16 * DM])
        S1 = sbP("S1", [128, TOWN], BF16)
        S2 = sbP("S2", [128, TOWN], BF16)
        dsb = sbP("dsb", [128, 512])
        drow = sbP("drow", [1, 512])
        rrh = sbP("rrh", [1, 512], BF16)
        rrl = sbP("rrl", [1, 512], BF16)
        rbsb = sbP("rbsb", [128, 512])
        aft = sbP("aft", [128, 512])
        a1t = sbP("a1t", [128, 512], BF16)
        a2t = sbP("a2t", [128, 512], BF16)
        pS = [ph.enter_context(nc.psum_tensor(f"pS{i}", [128, 1024], F32))
              for i in range(2)]
        pctx = [ph.enter_context(nc.psum_tensor(f"pctx{i}", [128, 512], F32))
                for i in range(2)]
        prb = ph.enter_context(nc.psum_tensor("prb", [128, 512], F32))
        pso = ph.enter_context(nc.psum_tensor("pso", [128, 512], F32))
        psoB = [pso, prb]

        KGRP = 8
        NG = 64 // KGRP

        q_load = {}
        ring_load = {}
        qk_mark = {}
        ex_mark = {}
        sp_mark = {}
        pv_mark = {}
        v_sacc = 0
        d_sacc_spill = 0

        def load_head(h, par):
            ch, rr = h // 2, (h % 2) * 64
            Y(lambda e, ch=ch, rr=rr, par=par: sD.dma(e.dma_start(
                qR[par][0:64, :], c.mqh_s[ch, rr:rr + 64, 0:TOWN])))
            Y(lambda e, ch=ch, rr=rr, par=par: sD.dma(e.dma_start(
                qR[par][64:128, :], c.mqh_s[ch, rr:rr + 64, 0:TOWN])))
            Y(lambda e, ch=ch, rr=rr, par=par: sD.dma(e.dma_start(
                q2R[par][0:64, :], c.mql_s[ch, rr:rr + 64, 0:TOWN])))
            Y(lambda e, h=h, par=par: sD.dma(e.dma_start(
                q2R[par][64:72, :], c.qarr2_s[h])))
            Y(lambda e, h=h, par=par: sD.dma(e.dma_start(
                wo1R[par][:], c.woar1_s[h])))
            Y(lambda e, h=h, par=par: sD.dma(e.dma_start(
                wo2R[par][:], c.woar2_s[h])))
            q_load[h] = sD.n

        def load_group(h, g, par, u=0):
            k0 = g * KGRP * 128
            Y(lambda e, h=h, k0=k0, par=par: sD.dma(e.dma_start(
                kR[par][:], c.karr_s[h, :, k0:k0 + 1024])))
            Y(lambda e, h=h, k0=k0, par=par: sD.dma(e.dma_start(
                k2R[par][0:64, :], c.karr_s[h, 0:64, k0:k0 + 1024])))
            Y(lambda e, k0=k0, par=par: sD.dma(e.dma_start(
                k2R[par][64:72, :], c.jarr_s[:, k0:k0 + 1024])))
            for kbl in range(KGRP):
                Y(lambda e, h=h, k0=k0, kbl=kbl, par=par: sD.dma(e.dma_start(
                    vR[par][:, kbl * 128:(kbl + 1) * 128],
                    c.varr_s[h, k0 + kbl * 128:k0 + kbl * 128 + 128, :])))
            ring_load[(h, g, u)] = sD.n

        def entry(j):
            hh = j // (2 * NG)
            uu = (j // NG) % 2
            gg = j % NG
            return hh, uu, gg

        NENT = H * 2 * NG
        load_head(0, 0)
        for j0 in range(2):
            hh, uu, gg = entry(j0)
            load_group(hh, gg, j0 % 2, uu)

        it = 0
        pend_pv = None
        oproj_mark = {}
        wq_gen = wq_factory(V, Y) if wq_factory is not None else iter(())
        for h in range(H):
            hpar = h % 2
            if h + 1 < H:
                if h >= 1:
                    Y(lambda e, v=oproj_mark[h - 1]: e.wait_ge(sT.h, v))
                load_head(h + 1, (h + 1) % 2)
            for u in range(2):
                def emit_pv(pit, pb, pkb, pgpar, pj, pkbl):
                    T(lambda e, v=sp_mark[pit]: e.wait_ge(sV.h, v))
                    for qc in range(2):
                        T(lambda e, b=pb, qc=qc, kbl=pkbl, gpar=pgpar,
                          st=(pkb == 0): e.matmul(
                            pctx[qc][:], vR[gpar][:, kbl * 128:kbl * 128 + 128],
                            pH[b][:, qc * 512:qc * 512 + 512],
                            start=st, stop=False))
                        mm = lambda e, b=pb, qc=qc, kbl=pkbl, gpar=pgpar, \
                            sp=(pkb == 63): e.matmul(
                            pctx[qc][:], vR[gpar][:, kbl * 128:kbl * 128 + 128],
                            pL[b][:, qc * 512:qc * 512 + 512],
                            start=False, stop=sp)
                        if qc == 1:
                            T(lambda e, mm=mm: sT.inc(mm(e)))
                        else:
                            T(mm)
                    pv_mark[pit] = sT.n
                    if pkbl == KGRP - 1 and pj + 2 < NENT:
                        nh, nu, ng = entry(pj + 2)
                        Y(lambda e, v=pv_mark[pit]: e.wait_ge(sT.h, v))
                        load_group(nh, ng, pgpar, nu)

                for kb in range(64):
                    g, kbl = kb // KGRP, kb % KGRP
                    j = (h * 2 + u) * NG + g
                    gpar = j % 2
                    b = it % 2
                    if kbl == 0:
                        T(lambda e, v=ring_load[(h, g, u)]: e.wait_ge(sD.h, v))
                    if it >= 2:
                        T(lambda e, v=ex_mark[it - 2]: e.wait_ge(sSc.h, v))
                    for qc in range(2):
                        q0 = u * 1024 + qc * 512
                        T(lambda e, b=b, qc=qc, kbl=kbl, q0=q0, hpar=hpar,
                          gpar=gpar: e.matmul(
                              pS[b][:, qc * 512:qc * 512 + 512],
                              kR[gpar][:, kbl * 128:kbl * 128 + 128],
                              qR[hpar][:, q0:q0 + 512], start=True, stop=False))
                        mm = lambda e, b=b, qc=qc, kbl=kbl, q0=q0, hpar=hpar, \
                            gpar=gpar: e.matmul(
                                pS[b][:, qc * 512:qc * 512 + 512],
                                k2R[gpar][0:72, kbl * 128:kbl * 128 + 128],
                                q2R[hpar][0:72, q0:q0 + 512],
                                start=False, stop=True)
                        if qc == 1:
                            T(lambda e, mm=mm: sT.inc(mm(e)))
                        else:
                            T(mm)
                    qk_mark[it] = sT.n
                    SC(lambda e, v=qk_mark[it]: e.wait_ge(sT.h, v))
                    if it >= 2:
                        SC(lambda e, v=sp_mark[it - 2]: e.wait_ge(sV.h, v))
                    SC(lambda e, b=b: sSc.inc(e.activation(
                        ptmp[b][:], pS[b][:], ACT.Exp)))
                    ex_mark[it] = sSc.n
                    V(lambda e, v=ex_mark[it]: e.wait_ge(sSc.h, v))
                    if it >= 2:
                        V(lambda e, v=pv_mark[it - 2]: e.wait_ge(sT.h, v))
                    V(lambda e, b=b: e.tensor_copy(pH[b][:], ptmp[b][:]))
                    V(lambda e, b=b: sV.inc(e.tensor_tensor(
                        pL[b][:], ptmp[b][:], pH[b][:], ALU.subtract)))
                    sp_mark[it] = sV.n
                    if pend_pv is not None:
                        emit_pv(*pend_pv)
                    pend_pv = (it, b, kb, gpar, j, kbl)
                    it += 1
                if pend_pv is not None:
                    emit_pv(*pend_pv)
                    pend_pv = None
                for qc in range(2):
                    V(lambda e, v=pv_mark[it - 1]: e.wait_ge(sT.h, v))
                    V(lambda e, qc=qc: sV.inc(e.tensor_copy(
                        dsb[64:65, :], pctx[qc][64:65, :])))
                    vden = sV.n
                    Y(lambda e, v=vden: e.wait_ge(sV.h, v))
                    Y(lambda e: sD.dma(e.dma_start(c.den_s[0], dsb[64:65, :])))
                    dn = sD.n
                    Y(lambda e, v=dn: e.wait_ge(sD.h, v))
                    Y(lambda e: sD.dma(e.dma_start(drow[:], c.den_s[0].unsqueeze(0))))
                    dn = sD.n
                    V(lambda e, v=dn: e.wait_ge(sD.h, v))
                    V(lambda e: e.reciprocal(drow[:], drow[:]))
                    V(lambda e: e.tensor_copy(rrh[:], drow[:]))
                    V(lambda e: sV.inc(e.tensor_tensor(rrl[:], drow[:], rrh[:],
                                                       ALU.subtract)))
                    vr = sV.n
                    T(lambda e, v=vr: e.wait_ge(sV.h, v))
                    T(lambda e: e.matmul(prb[:], r.ones128r[0:1, 0:128], rrh[:],
                                         start=True, stop=False))
                    T(lambda e: sT.inc(e.matmul(prb[:], r.ones128r[0:1, 0:128],
                                                rrl[:], start=False, stop=True)))
                    trb = sT.n
                    V(lambda e, v=trb: e.wait_ge(sT.h, v))
                    V(lambda e: e.tensor_copy(rbsb[:], prb[:]))
                    V(lambda e, qc=qc: e.tensor_tensor(
                        aft[0:64, :], pctx[qc][0:64, :], rbsb[0:64, :], ALU.mult))
                    V(lambda e: e.tensor_copy(a1t[0:64, :], aft[0:64, :]))
                    V(lambda e: e.tensor_tensor(a2t[0:64, :], aft[0:64, :],
                                                a1t[0:64, :], ALU.subtract))
                    V(lambda e, qc=qc, u=u: sV.inc(e.tensor_tensor(
                        S2[64:128, u * 1024 + qc * 512:u * 1024 + qc * 512 + 512],
                        pctx[qc][64:128, :], rbsb[64:128, :], ALU.mult)))
                    vev = sV.n
                    col = u * 1024 + qc * 512
                    V(lambda e, col=col: e.tensor_copy(
                        S1[0:64, col:col + 512], a1t[0:64, :]))
                    V(lambda e, col=col: sV.inc(e.tensor_copy(
                        S2[0:64, col:col + 512], a1t[0:64, :])))
                    vev = sV.n
                    Y(lambda e, v=vev: e.wait_ge(sV.h, v))
                    Y(lambda e, col=col: sD.dma(e.dma_start(
                        S1[64:128, col:col + 512], a2t[0:64, :])))
                    dS1 = sD.n
                    T(lambda e, v=vev: e.wait_ge(sV.h, v))
                    T(lambda e, v=dS1: e.wait_ge(sD.h, v))
            for _ in range(7):
                next(wq_gen, None)
            T(lambda e, v=q_load[h]: e.wait_ge(sD.h, v))
            v_sacc_hist = {}
            for tb in range(16):
                for ot in range(2):
                    rop = tb * 2 + ot
                    opar = rop % 2
                    if rop >= 2:
                        T(lambda e, v=v_sacc_hist[rop - 2]: e.wait_ge(sV.h, v))
                    else:
                        T(lambda e, v=v_sacc: e.wait_ge(sV.h, v))
                    T(lambda e, tb=tb, ot=ot, hpar=hpar, opar=opar: e.matmul(
                        psoB[opar][:], S1[:, tb * 128:tb * 128 + 128],
                        wo1R[hpar][:, ot * 512:ot * 512 + 512],
                        start=True, stop=False))
                    T(lambda e, tb=tb, ot=ot, hpar=hpar, opar=opar: sT.inc(e.matmul(
                        psoB[opar][:], S2[:, tb * 128:tb * 128 + 128],
                        wo2R[hpar][:, ot * 512:ot * 512 + 512],
                        start=False, stop=True)))
                    tso = sT.n
                    V(lambda e, v=tso: e.wait_ge(sT.h, v))
                    col = tb * DM + ot * 512
                    if h == 0:
                        V(lambda e, col=col, opar=opar: sV.inc(e.tensor_copy(
                            sacc[:, col:col + 512], psoB[opar][:])))
                    else:
                        V(lambda e, col=col, opar=opar: sV.inc(e.tensor_tensor(
                            sacc[:, col:col + 512], sacc[:, col:col + 512],
                            psoB[opar][:], ALU.add)))
                    v_sacc_hist[rop] = sV.n
                    v_sacc = sV.n
            oproj_mark[h] = sT.n
        for _ in wq_gen:
            pass
        Y(lambda e, v=v_sacc: e.wait_ge(sV.h, v))
        for tb in range(16):
            Y(lambda e, tb=tb: sO.dma(e.dma_start(
                c.sacc_s[tb * 128:(tb + 1) * 128, :],
                sacc[:, tb * DM:(tb + 1) * DM])))
        Y(lambda e, v=sO.n: e.wait_ge(sO.h, v))
        sch.run(nc)
def _emit_b4(nc, c, r, sems, gq_done_val):
    sD, sG, sT, sV, sSc, sQ, sO = sems
    ph = contextlib.ExitStack()
    with ph:
        sch = Sched()
        Y = mk_adder(sch.q["y"])
        V = mk_adder(sch.q["v"], track=True)
        T = mk_adder(sch.q["t"])
        SC = mk_adder(sch.q["s"])

        def sbP(name, shape, dtyp=F32):
            return ph.enter_context(nc.sbuf_tensor(name, shape, dtyp))

        bobc = sbP("b4_bobc", [128, DM])
        wtin = [sbP(f"wtin{i}", [128, FI], BF16) for i in range(2)]
        wtst = [sbP(f"wtst{i}", [128, 128], BF16) for i in range(2)]
        xqR = sbP("xqR", [128, NCH * 512], BF16)
        wgR = [sbP(f"wgR{i}", [128, NCH * 512], BF16) for i in range(2)]
        wuR = [sbP(f"wuR{i}", [128, NCH * 512], BF16) for i in range(2)]
        wdH = [sbP(f"wdH{i}", [128, 16 * 512], BF16) for i in range(2)]
        sil = sbP("sil", [128, 512])
        hidt = sbP("hidt", [128, 512])
        hq = sbP("hq", [128, 512], BF16)
        hqf = sbP("hqf", [128, 32 * 512], BF16)
        qsA = sbP("b4qsA", [128, 32])
        qsB = sbP("b4qsB", [128, 32])
        qsC = sbP("b4qsC", [128, 32])
        qsD = sbP("b4qsD", [128, 32])
        qtA = sbP("b4qtA", [128, 512])
        qtB = sbP("b4qtB", [128, 512])
        gt4 = sbP("gt4", [128, 4 * DM])
        xt = [sbP(f"b4xt{i}", [128, DM]) for i in range(2)]
        st_ = [sbP(f"b4st{i}", [128, DM]) for i in range(2)]
        ot_ = [sbP(f"b4ot{i}", [128, DM]) for i in range(2)]
        ptrB = ph.enter_context(nc.psum_tensor("ptrB", [128, 128], BF16))
        pga = ph.enter_context(nc.psum_tensor("pga", [128, 512], F32))
        pgb = ph.enter_context(nc.psum_tensor("pgb", [128, 512], F32))
        pd4 = [ph.enter_context(nc.psum_tensor(f"pd{i}", [128, 512], F32))
               for i in range(4)]

        Y(lambda e: sD.dma(e.dma_start(bobc[:], c.bo_bc_in)))
        d_bo = sD.n
        V(lambda e, v=gq_done_val: e.wait_ge(sQ.h, v))
        V(lambda e, v=d_bo: e.wait_ge(sD.h, v))
        V(lambda e: e.tensor_tensor(r.scbc[:, 7:8], r.scbc[:, 0:1],
                                    r.scbc[:, 2:3], ALU.mult))
        V(lambda e: e.tensor_tensor(r.scbc[:, 8:9], r.scbc[:, 0:1],
                                    r.scbc[:, 3:4], ALU.mult))
        V(lambda e: sV.inc(e.tensor_tensor(r.scbc[:, 9:10], r.scbc[:, 1:2],
                                           r.scbc[:, 4:5], ALU.mult)))
        v_al = sV.n
        SC(lambda e, v=v_al: e.wait_ge(sV.h, v))

        tr_jobs = [(c.wgq_s, c.wgT_s, FI, DM), (c.wuq_s, c.wuT_s, FI, DM),
                   (c.wdq_s, c.wdT_s, DM, FI)]
        v_trev = 0
        d_trst = {}
        t_trin = {}
        n_tr = 0
        n_rb = 0
        for win, wout, nr, ncl in tr_jobs:
            for rb in range(nr // 128):
                n_rb += 1
                par = n_rb % 2
                if n_rb >= 3:
                    Y(lambda e, v=t_trin.get(n_rb - 2, 0): e.wait_ge(sT.h, v))
                Y(lambda e, win=win, rb=rb, par=par, ncl=ncl: sD.dma(e.dma_start(
                    wtin[par][:, 0:ncl], win[rb * 128:(rb + 1) * 128, :])))
                dwin = sD.n
                for cb in range(ncl // 128):
                    n_tr += 1
                    spar = n_tr % 2
                    T(lambda e, v=dwin: e.wait_ge(sD.h, v))
                    T(lambda e, v=v_trev: e.wait_ge(sV.h, v))
                    T(lambda e, cb=cb, par=par: sT.inc(e.transpose(
                        ptrB[:], wtin[par][:, cb * 128:cb * 128 + 128],
                        r.identb[:])))
                    tt = sT.n
                    V(lambda e, v=tt: e.wait_ge(sT.h, v))
                    if n_tr >= 3:
                        V(lambda e, v=d_trst.get(n_tr - 2, 0): e.wait_ge(sD.h, v))
                    V(lambda e, spar=spar: sV.inc(e.tensor_copy(
                        wtst[spar][:], ptrB[:])))
                    v_trev = sV.n
                    Y(lambda e, v=v_trev: e.wait_ge(sV.h, v))
                    Y(lambda e, wout=wout, rb=rb, cb=cb, spar=spar: sD.dma(
                        e.dma_start(wout[cb * 128:(cb + 1) * 128,
                                         rb * 128:(rb + 1) * 128], wtst[spar][:])))
                    d_trst[n_tr] = sD.n
                t_trin[n_rb] = sT.n
        d_tr_all = sD.n

        d_wg = {}
        d_wd = {}
        v_hid = 0
        v_hqf = 0
        t_hqf_used = 0
        v_comb = 0
        d_ocp = {}
        s_gt = 0
        for t4 in range(4):
            tk0 = t4 * 512
            Y(lambda e, v=d_tr_all: e.wait_ge(sD.h, v))
            if t4 >= 1:
                Y(lambda e, v=t_hqf_used: e.wait_ge(sT.h, v))
            for ch in range(NCH):
                Y(lambda e, ch=ch, tk0=tk0: sD.dma(e.dma_start(
                    xqR[:, ch * 512:(ch + 1) * 512],
                    c.xqf_s[:, ch * TOWN + tk0:ch * TOWN + tk0 + 512])))
            d_xq = sD.n
            for ot in range(8):
                rnd = t4 * 8 + ot
                opar = rnd % 2
                if rnd >= 2:
                    Y(lambda e, v=d_wg.get(rnd - 2, 0): e.wait_ge(sT.h, v))
                for ch in range(NCH):
                    Y(lambda e, ch=ch, ot=ot, opar=opar: sD.dma(e.dma_start(
                        wgR[opar][:, ch * 512:(ch + 1) * 512],
                        c.wgT_s[ch * 128:(ch + 1) * 128, ot * 512:ot * 512 + 512])))
                    Y(lambda e, ch=ch, ot=ot, opar=opar: sD.dma(e.dma_start(
                        wuR[opar][:, ch * 512:(ch + 1) * 512],
                        c.wuT_s[ch * 128:(ch + 1) * 128, ot * 512:ot * 512 + 512])))
                d_w = sD.n
                for ts in range(4):
                    T(lambda e, v=d_w: e.wait_ge(sD.h, v))
                    T(lambda e, v=d_xq: e.wait_ge(sD.h, v))
                    T(lambda e, v=v_hid: e.wait_ge(sV.h, v))
                    for ch in range(NCH):
                        x0 = ch * 512 + ts * 128
                        mm = lambda e, x0=x0, ch=ch, opar=opar: e.matmul(
                            pga[:], xqR[:, x0:x0 + 128],
                            wgR[opar][:, ch * 512:(ch + 1) * 512],
                            start=(ch == 0), stop=(ch == NCH - 1))
                        if ch == NCH - 1:
                            T(lambda e, mm=mm: sT.inc(mm(e)))
                        else:
                            T(mm)
                    tga = sT.n
                    for ch in range(NCH):
                        x0 = ch * 512 + ts * 128
                        mm = lambda e, x0=x0, ch=ch, opar=opar: e.matmul(
                            pgb[:], xqR[:, x0:x0 + 128],
                            wuR[opar][:, ch * 512:(ch + 1) * 512],
                            start=(ch == 0), stop=(ch == NCH - 1))
                        if ch == NCH - 1:
                            T(lambda e, mm=mm: sT.inc(mm(e)))
                        else:
                            T(mm)
                    tgb = sT.n
                    SC(lambda e, v=tga: e.wait_ge(sT.h, v))
                    SC(lambda e, v=v_hid: e.wait_ge(sV.h, v))
                    SC(lambda e: sSc.inc(e.activation(
                        sil[:], pga[:], ACT.Silu, bias=0.0,
                        scale=r.scbc[:, 7:8])))
                    sgl = sSc.n
                    V(lambda e, v=sgl: e.wait_ge(sSc.h, v))
                    V(lambda e, v=tgb: e.wait_ge(sT.h, v))
                    V(lambda e: e.tensor_scalar(hidt[:], pgb[:], r.scbc[:, 8:9],
                                                None, ALU.mult))
                    V(lambda e: sV.inc(e.tensor_tensor(hidt[:], hidt[:], sil[:],
                                                       ALU.mult)))
                    v_hid = sV.n
                    last = emit_nvfp4(V, hidt[:], hq[:], qsA[:], qsB[:],
                                      qsC[:], qsD[:], qtA[:], qtB[:],
                                      r.scbc[:, 6:7])
                    V(lambda e, last=last: sV.inc(last(e)))
                    vhq = sV.n
                    for cb in range(4):
                        T(lambda e, v=vhq: e.wait_ge(sV.h, v))
                        T(lambda e, v=v_hqf: e.wait_ge(sV.h, v))
                        T(lambda e, cb=cb: sT.inc(e.transpose(
                            ptrB[:], hq[:, cb * 128:cb * 128 + 128], r.identb[:])))
                        tt = sT.n
                        ic = ot * 4 + cb
                        V(lambda e, v=tt: e.wait_ge(sT.h, v))
                        V(lambda e, ic=ic, ts=ts: sV.inc(e.tensor_copy(
                            hqf[:, ic * 512 + ts * 128:ic * 512 + ts * 128 + 128],
                            ptrB[:])))
                        v_hqf = sV.n
                d_wg[rnd] = sT.n
            for oc in range(2):
                for icg in range(2):
                    wpar = (t4 * 4 + oc * 2 + icg) % 2
                    if t4 * 4 + oc * 2 + icg >= 2:
                        Y(lambda e, v=d_wd.get(t4 * 4 + oc * 2 + icg - 2, 0):
                          e.wait_ge(sT.h, v))
                    for i16 in range(16):
                        icabs = icg * 16 + i16
                        Y(lambda e, icabs=icabs, i16=i16, oc=oc, wpar=wpar:
                          sD.dma(e.dma_start(
                              wdH[wpar][:, i16 * 512:(i16 + 1) * 512],
                              c.wdT_s[icabs * 128:(icabs + 1) * 128,
                                      oc * 512:oc * 512 + 512])))
                    d_wdn = sD.n
                    for ts in range(4):
                        T(lambda e, v=d_wdn: e.wait_ge(sD.h, v))
                        if icg == 0:
                            T(lambda e, v=v_hqf: e.wait_ge(sV.h, v))
                            if t4 > 0 or oc > 0:
                                T(lambda e, v=s_gt: e.wait_ge(sSc.h, v))
                        for i16 in range(16):
                            icabs = icg * 16 + i16
                            mm = lambda e, icabs=icabs, i16=i16, ts=ts, \
                                wpar=wpar: e.matmul(
                                    pd4[ts][:],
                                    hqf[:, icabs * 512 + ts * 128:
                                         icabs * 512 + ts * 128 + 128],
                                    wdH[wpar][:, i16 * 512:(i16 + 1) * 512],
                                    start=(icabs == 0), stop=(icabs == 31))
                            if i16 == 15:
                                T(lambda e, mm=mm: sT.inc(mm(e)))
                            else:
                                T(mm)
                    d_wd[t4 * 4 + oc * 2 + icg] = sT.n
                t_dn = sT.n
                for ts in range(4):
                    SC(lambda e, v=t_dn: e.wait_ge(sT.h, v))
                    if t4 > 0:
                        SC(lambda e, v=v_comb: e.wait_ge(sV.h, v))
                    SC(lambda e, ts=ts, oc=oc: sSc.inc(e.activation(
                        gt4[:, ts * DM + oc * 512:ts * DM + oc * 512 + 512],
                        pd4[ts][:], ACT.Silu, bias=0.0, scale=r.scbc[:, 9:10])))
                s_gt = sSc.n
            t_hqf_used = sT.n
            for ts in range(4):
                tok0 = tk0 + ts * 128
                cpar = ts % 2
                if t4 * 4 + ts >= 2:
                    Y(lambda e, v=d_ocp.get(t4 * 4 + ts - 2, 0): e.wait_ge(sO.h, v))
                Y(lambda e, tok0=tok0, cpar=cpar: sD.dma(e.dma_start(
                    xt[cpar][:], c.x_in[tok0:tok0 + 128, :])))
                Y(lambda e, tok0=tok0, cpar=cpar: sD.dma(e.dma_start(
                    st_[cpar][:], c.sacc_s[tok0:tok0 + 128, :])))
                d_xs = sD.n
                V(lambda e, v=d_xs: e.wait_ge(sD.h, v))
                V(lambda e, v=s_gt: e.wait_ge(sSc.h, v))
                V(lambda e, cpar=cpar: e.tensor_tensor(
                    st_[cpar][:], st_[cpar][:], bobc[:], ALU.add))
                V(lambda e, cpar=cpar, ts=ts: e.tensor_tensor(
                    ot_[cpar][:], st_[cpar][:], gt4[:, ts * DM:(ts + 1) * DM],
                    ALU.mult))
                V(lambda e, cpar=cpar: sV.inc(e.tensor_tensor(
                    ot_[cpar][:], ot_[cpar][:], xt[cpar][:], ALU.add)))
                v_comb = sV.n
                Y(lambda e, v=v_comb: e.wait_ge(sV.h, v))
                Y(lambda e, tok0=tok0, cpar=cpar: sO.dma(e.dma_start(
                    c.out_hbm[tok0:tok0 + 128, :], ot_[cpar][:])))
                d_ocp[t4 * 4 + ts] = sO.n
        Y(lambda e, v=sO.n: e.wait_ge(sO.h, v))
        sch.run(nc)


def kernel(**inputs):
    bf16 = ml_dtypes.bfloat16
    x = np.asarray(inputs["L0_attn_hidden_states"], np.float32)
    wq = np.ascontiguousarray(np.asarray(inputs["L0_attn_wq"], np.float32))
    wk = np.ascontiguousarray(np.asarray(inputs["L0_attn_wk"], np.float32))
    wv = np.ascontiguousarray(np.asarray(inputs["L0_attn_wv"], np.float32))
    wo = np.ascontiguousarray(np.asarray(inputs["L0_attn_wo"], np.float32))
    wfq = np.ascontiguousarray(np.asarray(inputs["L0_attn_wfq"], np.float32))
    wfg = np.ascontiguousarray(np.asarray(inputs["L0_attn_wfg"], np.float32))

    def bias_fold(b):
        return np.ascontiguousarray(np.asarray(b, np.float32).reshape(NCH, 128).T)

    bq = bias_fold(inputs["L0_attn_bq"])
    bk = bias_fold(inputs["L0_attn_bk"])
    bfq = bias_fold(inputs["L0_attn_bfq"])
    bfg = bias_fold(inputs["L0_attn_bfg"])
    bv_bc = np.ascontiguousarray(np.broadcast_to(
        np.asarray(inputs["L0_attn_bv"], np.float32)[None, :], (128, DM)))
    bo_bc = np.ascontiguousarray(np.broadcast_to(
        np.asarray(inputs["L0_attn_bo"], np.float32)[None, :], (128, DM)))
    up2 = np.asarray(inputs["L0_attn_up"], np.float32).reshape(H, D).T
    uz2 = np.asarray(inputs["L0_attn_uz"], np.float32).reshape(H, D).T
    upz_half = np.concatenate([up2, uz2], axis=1)
    upz = np.ascontiguousarray(np.concatenate([upz_half, upz_half], axis=0))
    gw = np.ascontiguousarray(np.asarray(inputs["L0_ffn_gate_w"], np.float32))
    uw = np.ascontiguousarray(np.asarray(inputs["L0_ffn_up_w"], np.float32))
    dw = np.ascontiguousarray(np.asarray(inputs["L0_ffn_down_w"], np.float32))
    s_in = np.asarray(inputs["L0_ffn_s_in"], np.float32).reshape(1, 1)
    s_in_down = np.asarray(inputs["L0_ffn_s_in_down"], np.float32).reshape(1, 1)
    ident = np.eye(128, dtype=np.float32)

    nc = build_program(4)
    in_maps = []
    for core in range(8):
        b, qq = core // 4, core % 4
        xr = np.ascontiguousarray(np.roll(x[b], -TOWN * qq, axis=0))
        pos = ((np.arange(S) + TOWN * qq) % S).astype(np.float64)
        j_hi = pos.astype(bf16).astype(np.float64)
        j_lo = (pos - j_hi).astype(bf16).astype(np.float64)
        jj = pos * pos
        jj_hi = jj.astype(bf16).astype(np.float64)
        jj_lo = (jj - jj_hi).astype(bf16).astype(np.float64)
        jconst = np.ascontiguousarray(
            np.stack([j_hi, j_lo, jj_hi, jj_lo]).astype(np.float32))
        in_maps.append({
            "x_roll": xr, "wq": wq, "wk": wk, "wv": wv, "wo": wo,
            "wfq": wfq, "wfg": wfg, "bq": bq, "bk": bk, "bfq": bfq,
            "bfg": bfg, "bv_bc": bv_bc, "bo_bc": bo_bc, "upz": upz,
            "jconst": jconst, "ident_in": ident, "gate_w": gw, "up_w": uw,
            "down_w": dw, "s_in": s_in, "s_in_down": s_in_down,
        })
    res = run_bass_kernel_spmd(nc, in_maps, core_ids=list(range(8)))
    out = np.empty((2, S, DM), np.float32)
    for core in range(8):
        b, qq = core // 4, core % 4
        out[b, TOWN * qq:TOWN * (qq + 1)] = np.asarray(res.results[core]["out"])
    return out


# version 04
# speedup vs baseline: 1.3950x; 1.0367x over previous
import contextlib
import types
import numpy as np
import ml_dtypes
import concourse.bass as bass
import concourse.mybir as mybir
from concourse.bass_utils import run_bass_kernel_spmd

F32 = mybir.dt.float32
BF16 = mybir.dt.bfloat16
AX = mybir.AxisListType
ALU = mybir.AluOpType
ACT = mybir.ActivationFunctionType

S = 8192
TOWN = 2048
DM = 1024
H = 16
D = 64
FI = 4096
NCH = 8
NB = 16
EPS = 1e-5


_COUNTING = [False]
_CAPTURE = [None]
_SEP_AP = [None]
_SEP_COUNT = [0]
_HZW = 128


class _MockInst:
    def then_inc(self, h, k=1):
        return self

    def __getattr__(self, name):
        return lambda *a, **kw: self


class _Cap:
    __slots__ = ("reads", "writes", "reduce", "touched")

    def __init__(self):
        self.reads = []
        self.writes = []
        self.reduce = False
        self.touched = False


def _is_ap(x):
    return hasattr(x, "offset") and hasattr(x, "free_size") and hasattr(x, "ap")


class _MockEng:
    def __getattr__(self, name):
        def call(*a, **kw):
            cap = _CAPTURE[0]
            if cap is not None and name not in ("wait_ge", "engine_nop"):
                cap.touched = True
                if "reduce" in name:
                    cap.reduce = True
                aps = [x for x in a if _is_ap(x)]
                if aps:
                    cap.writes.append(aps[0])
                    cap.reads += aps[1:]
                cap.reads += [v for v in kw.values() if _is_ap(v)]
            return _MockInst()
        return call


_MOCK = _MockEng()


def _ap_rng(ap):
    try:
        return (ap.name, int(ap.offset), int(ap.free_size()))
    except Exception:
        return None


def _hazard(prev, cap):
    if prev is None:
        return False
    for w in prev.writes:
        wi = _ap_rng(w)
        if wi is None:
            continue
        wt, wo, ww = wi
        for rap in cap.reads:
            ri = _ap_rng(rap)
            if ri is None:
                continue
            rt, ro, rw = ri
            if rt != wt or ro + rw <= wo or wo + ww <= ro:
                continue
            if prev.reduce or ww < _HZW or (wo + ww) - ro < _HZW:
                return True
    return False


def _sep_op(e):
    s = _SEP_AP[0]
    e.tensor_copy(s[:, 1:2], s[:, 0:1])


class Sem:

    def __init__(self, st, nc, name):
        self.h = st.enter_context(nc.semaphore(name))
        self.n = 0

    def inc(self, inst, k=1):
        if _COUNTING[0]:
            self.n += k
        inst.then_inc(self.h, k)
        return self.n

    def dma(self, inst):
        return self.inc(inst, 16)


def mk_adder(lst, track=False):
    state = {"last": None}

    def add(fn):
        cap = _Cap() if track else None
        _CAPTURE[0] = cap
        _COUNTING[0] = True
        try:
            fn(_MOCK)
        finally:
            _COUNTING[0] = False
            _CAPTURE[0] = None
        if track and cap.touched:
            if _hazard(state["last"], cap):
                lst.append(_sep_op)
                _SEP_COUNT[0] += 1
            state["last"] = cap
        lst.append(fn)
    return add


class Sched:
    def __init__(self):
        self.q = {"t": [], "v": [], "s": [], "g": [], "y": []}

    def run(self, nc):
        with nc.Block() as blk:
            if self.q["y"]:
                @blk.sync
                def _(e):
                    for f in self.q["y"]:
                        f(e)
            if self.q["t"]:
                @blk.tensor
                def _(e):
                    for f in self.q["t"]:
                        f(e)
            if self.q["v"]:
                @blk.vector
                def _(e):
                    for f in self.q["v"]:
                        f(e)
            if self.q["s"]:
                @blk.scalar
                def _(e):
                    for f in self.q["s"]:
                        f(e)
            if self.q["g"]:
                @blk.gpsimd
                def _(e):
                    for f in self.q["g"]:
                        f(e)


def emit_nvfp4(E, x_ap, out_ap, sA, sB, sC, sT_, tA, tB, gs_ap, eng="v"):
    W = x_ap.shape[-1]
    nsc = W // 16
    P = x_ap.shape[0]
    E(lambda e: e.tensor_scalar(tA, x_ap, -1.0, None, ALU.mult))
    E(lambda e: e.tensor_tensor(tA, tA, x_ap, ALU.max))
    tA3 = tA.rearrange("p (a b) -> p a b", b=16)
    if eng == "v":
        E(lambda e: e.tensor_reduce(sA, tA3, AX.X, ALU.max))
    else:
        tB3 = tB.rearrange("p (a b) -> p a b", b=16)
        E(lambda e: e.tensor_tensor(tB3[:, :, 0:8], tA3[:, :, 0:8],
                                    tA3[:, :, 8:16], ALU.max))
        E(lambda e: e.tensor_tensor(tB3[:, :, 0:4], tB3[:, :, 0:4],
                                    tB3[:, :, 4:8], ALU.max))
        E(lambda e: e.tensor_tensor(tB3[:, :, 0:2], tB3[:, :, 0:2],
                                    tB3[:, :, 2:4], ALU.max))
        E(lambda e: e.tensor_tensor(tB3[:, :, 0:1], tB3[:, :, 0:1],
                                    tB3[:, :, 1:2], ALU.max))
        E(lambda e: e.tensor_copy(sA, tB3[:, :, 0:1].squeeze(2)))
    E(lambda e: e.tensor_scalar(sA, sA, float(np.float32(1.0 / 6.0)), None, ALU.mult))
    E(lambda e: e.tensor_scalar(sA, sA, gs_ap, 448.0, ALU.mult, ALU.min))
    E(lambda e: e.memset(sB, 2.0 ** 9))
    E(lambda e: e.memset(sC, 2.0 ** -9))
    for p in range(-5, 9):
        cinv = 2.0 ** (3 - p) - 2.0 ** (4 - p)
        cs = 2.0 ** (p - 3) - 2.0 ** (p - 4)
        E(lambda e, p=p, cv=cinv: e.tensor_scalar(sT_, sA, 2.0 ** p, cv, ALU.is_ge, ALU.mult))
        E(lambda e: e.tensor_tensor(sB, sB, sT_, ALU.add))
        E(lambda e, p=p, cv=cs: e.tensor_scalar(sT_, sA, 2.0 ** p, cv, ALU.is_ge, ALU.mult))
        E(lambda e: e.tensor_tensor(sC, sC, sT_, ALU.add))
    E(lambda e: e.tensor_tensor(sT_, sA, sB, ALU.mult))
    E(lambda e: e.memset(sA, 0.0))
    for k in range(16):
        op = ALU.is_gt if (k % 2 == 0) else ALU.is_ge
        E(lambda e, k=k, op=op: e.tensor_scalar(tB[:, 0:nsc], sT_, k + 0.5, None, op))
        E(lambda e: e.tensor_tensor(sA, sA, tB[:, 0:nsc], ALU.add))
    E(lambda e: e.tensor_tensor(sT_, sA, sC, ALU.mult))
    if eng == "v":
        E(lambda e: e.reciprocal(sC, sT_))
        E(lambda e: e.tensor_scalar(sC, sC, gs_ap, None, ALU.mult))
        rdq_bc = sC.unsqueeze(2).broadcast_to([P, nsc, 16])
        E(lambda e: e.tensor_tensor(tA3, tA3, rdq_bc, ALU.mult))
        E(lambda e: e.tensor_scalar(tB, tA, 0.25, 0.5, ALU.is_ge, ALU.mult))
        for m, cadd in ((0.75, 0.5), (1.25, 0.5), (1.75, 0.5),
                        (2.5, 1.0), (3.5, 1.0), (5.0, 2.0)):
            E(lambda e, m=m, cv=cadd: e.tensor_scalar(out_ap, tA, m, cv, ALU.is_ge, ALU.mult))
            E(lambda e: e.tensor_tensor(tB, tB, out_ap, ALU.add))
    else:
        E(lambda e: e.tensor_scalar(sB, sB, gs_ap, None, ALU.mult))
        gi_bc = sB.unsqueeze(2).broadcast_to([P, nsc, 16])
        E(lambda e: e.tensor_tensor(tA3, tA3, gi_bc, ALU.mult))
        first = True
        for m, cadd in ((0.25, 0.5), (0.75, 0.5), (1.25, 0.5), (1.75, 0.5),
                        (2.5, 1.0), (3.5, 1.0), (5.0, 2.0)):
            E(lambda e, m=m: e.tensor_scalar(sC, sA, m, None, ALU.mult))
            thr_bc = sC.unsqueeze(2).broadcast_to([P, nsc, 16])
            o3_ = out_ap.rearrange("p (a b) -> p a b", b=16)
            E(lambda e, o3_=o3_, thr_bc=thr_bc: e.tensor_tensor(
                o3_, tA3, thr_bc, ALU.is_ge))
            if first:
                E(lambda e, cv=cadd: e.tensor_scalar(tB, out_ap, cv, None, ALU.mult))
                first = False
            else:
                E(lambda e, cv=cadd: e.tensor_scalar(out_ap, out_ap, cv, None, ALU.mult))
                E(lambda e: e.tensor_tensor(tB, tB, out_ap, ALU.add))
    E(lambda e: e.tensor_scalar(tA, x_ap, 0.0, 2.0, ALU.is_ge, ALU.mult))
    E(lambda e: e.tensor_scalar(tA, tA, -1.0, None, ALU.add))
    E(lambda e: e.tensor_tensor(tB, tB, tA, ALU.mult))
    sc8_bc = sT_.unsqueeze(2).broadcast_to([P, nsc, 16])
    tB3f = tB.rearrange("p (a b) -> p a b", b=16)
    o3 = out_ap.rearrange("p (a b) -> p a b", b=16)
    return lambda e: e.tensor_tensor(o3, tB3f, sc8_bc, ALU.mult)


def _declare_io(nc):
    dt = nc.dram_tensor
    c = types.SimpleNamespace()
    c.x_in = dt("x_roll", [S, DM], F32, kind="ExternalInput").ap()
    for nm in ("wq", "wk", "wv", "wo", "wfq", "wfg"):
        setattr(c, nm + "_in", dt(nm, [DM, DM], F32, kind="ExternalInput").ap())
    for nm in ("bq", "bk", "bfq", "bfg"):
        setattr(c, nm + "_in", dt(nm, [128, NCH], F32, kind="ExternalInput").ap())
    c.bv_bc_in = dt("bv_bc", [128, DM], F32, kind="ExternalInput").ap()
    c.bo_bc_in = dt("bo_bc", [128, DM], F32, kind="ExternalInput").ap()
    c.upz_in = dt("upz", [128, 2 * H], F32, kind="ExternalInput").ap()
    c.jc_in = dt("jconst", [4, S], F32, kind="ExternalInput").ap()
    c.ident_in = dt("ident_in", [128, 128], F32, kind="ExternalInput").ap()
    c.gw_in = dt("gate_w", [FI, DM], F32, kind="ExternalInput").ap()
    c.uw_in = dt("up_w", [FI, DM], F32, kind="ExternalInput").ap()
    c.dw_in = dt("down_w", [DM, FI], F32, kind="ExternalInput").ap()
    c.sin_in = dt("s_in", [1, 1], F32, kind="ExternalInput").ap()
    c.sind_in = dt("s_in_down", [1, 1], F32, kind="ExternalInput").ap()
    c.out_hbm = dt("out", [TOWN, DM], F32, kind="ExternalOutput").ap()

    c.xnh_s = dt("xnh_s", [NCH, 128, S], BF16).ap()
    c.xnl_s = dt("xnl_s", [NCH, 128, S], BF16).ap()
    c.karr_s = dt("karr_s", [H, 128, S], BF16).ap()
    c.varr_s = dt("varr_s", [H, S, 128], BF16).ap()
    c.qarr2_s = dt("qarr2_s", [H, 8, TOWN], BF16).ap()
    c.jarr_s = dt("jarr_s", [8, S], BF16).ap()
    c.woar1_s = dt("woar1_s", [H, 128, DM], BF16).ap()
    c.woar2_s = dt("woar2_s", [H, 128, DM], BF16).ap()
    c.wgq_s = dt("wgq_s", [FI, DM], BF16).ap()
    c.wuq_s = dt("wuq_s", [FI, DM], BF16).ap()
    c.wdq_s = dt("wdq_s", [DM, FI], BF16).ap()
    c.wgT_s = dt("wgT_s", [DM, FI], BF16).ap()
    c.wuT_s = dt("wuT_s", [DM, FI], BF16).ap()
    c.wdT_s = dt("wdT_s", [FI, DM], BF16).ap()
    c.sacc_s = dt("sacc_s", [TOWN, DM], F32).ap()
    c.scal_s = dt("scal_s", [16, 128], F32).ap()
    c.den_s = dt("den_s", [4, 512], F32).ap()
    c.sgp_s = dt("sgp_s", [H, TOWN], F32).ap()
    c.sgz_s = dt("sgz_s", [H, TOWN], F32).ap()
    c.q2n_s = dt("q2n_s", [H, TOWN], F32).ap()
    c.mqh_s = dt("mqh_s", [NCH, 128, TOWN], BF16).ap()
    c.mql_s = dt("mql_s", [NCH, 128, TOWN], BF16).ap()
    c.xqf_s = dt("xqf_s", [128, NCH * TOWN], BF16).ap()
    return c
def build_program(upto=4, debug=()):
    nc = bass.Bass("TRN2", target_bir_lowering=False)
    c = _declare_io(nc)
    dbg_out = {}
    dbg_sbuf = []
    for nm in debug:
        if not hasattr(c, nm):
            dbg_sbuf.append(nm)
            continue
        src = getattr(c, nm)
        shp = list(src.shape)
        flat = [int(np.prod(shp[:-1])), shp[-1]]
        dbg_out[nm] = nc.dram_tensor("dbg_" + nm, flat, src.dtype,
                                     kind="ExternalOutput").ap()

    root = contextlib.ExitStack()
    with root:
        r = types.SimpleNamespace()

        def sbR(name, shape, dtyp=F32):
            t = root.enter_context(nc.sbuf_tensor(name, shape, dtyp))
            setattr(r, name, t)
            return t

        sbR("ident", [128, 128])
        sbR("identb", [128, 128], BF16)
        sbR("ones128b", [128, 1], BF16)
        sbR("ones128r", [1, 128], BF16)
        sbR("biases", [128, 4 * NCH])
        sbR("upzh", [128, 2 * H], BF16)
        sbR("upzl", [128, 2 * H], BF16)
        sbR("scbc", [128, 16])
        sbR("kmax2", [1, H])
        sbR("xbar", [128, NCH])
        sbR("sepA", [128, 2])
        _SEP_AP[0] = r.sepA
        _SEP_COUNT[0] = 0

        sD = Sem(root, nc, "sD")
        sG = Sem(root, nc, "sG")
        sT = Sem(root, nc, "sT")
        sV = Sem(root, nc, "sV")
        sSc = Sem(root, nc, "sSc")
        sQ = Sem(root, nc, "sQ")
        sO = Sem(root, nc, "sO")
        sems = (sD, sG, sT, sV, sSc, sQ, sO)

        gq_stack = contextlib.ExitStack()
        gq_al = _wquant_alloc(nc, gq_stack)
        wq_factory = lambda V, Y: _wquant_gen(nc, gq_al, c, r, sems, V, Y)

        _emit_b0(nc, c, r, sems)
        if upto >= 1:
            _emit_b1(nc, c, r, sems)
        if upto >= 2:
            _emit_b2(nc, c, r, sems)
        if upto >= 3:
            _emit_b3(nc, c, r, sems, wq_factory)
        gq_done_val = sQ.n
        gq_stack.close()
        if upto >= 4:
            _emit_b4(nc, c, r, sems, gq_done_val)

        sch = Sched()
        Y = mk_adder(sch.q["y"])
        for nm in dbg_sbuf:
            src = getattr(r, nm)
            outap = nc.dram_tensor("dbg_" + nm, list(src.shape), src.dtype,
                                   kind="ExternalOutput").ap()
            Y(lambda e, src=src, outap=outap: sO.dma(e.dma_start(outap, src[:])))
        for nm, outap in dbg_out.items():
            src = getattr(c, nm)
            if len(src.shape) == 3:
                p_ = src.shape[1]
                for i0 in range(src.shape[0]):
                    Y(lambda e, src=src, outap=outap, i0=i0, p_=p_:
                      sO.dma(e.dma_start(outap[i0 * p_:(i0 + 1) * p_, :], src[i0])))
            else:
                Y(lambda e, src=src, outap=outap: sO.dma(e.dma_start(outap, src)))
        Y(lambda e, v=sO.n: e.wait_ge(sO.h, v))
        Y(lambda e, v=sD.n: e.wait_ge(sD.h, v))
        Y(lambda e, v=sG.n: e.wait_ge(sG.h, v))
        if upto >= 2:
            Y(lambda e, v=sQ.n: e.wait_ge(sQ.h, v))
        sch.run(nc)
    print(f"[build] raw-hazard separators inserted: {_SEP_COUNT[0]}")
    return nc


def _wquant_alloc(nc, stk):
    W2 = 1024
    al = {}
    al["gq1"] = [stk.enter_context(nc.sbuf_tensor(f"gq1_{i}", [128, W2], F32))
                 for i in range(2)]
    al["gqtA"] = stk.enter_context(nc.sbuf_tensor("gqtA", [128, W2], F32))
    al["gqtB"] = stk.enter_context(nc.sbuf_tensor("gqtB", [128, W2], F32))
    al["gqs1"] = stk.enter_context(nc.sbuf_tensor("gqs1", [128, W2 // 16], F32))
    al["gqs2"] = stk.enter_context(nc.sbuf_tensor("gqs2", [128, W2 // 16], F32))
    al["gqs3"] = stk.enter_context(nc.sbuf_tensor("gqs3", [128, W2 // 16], F32))
    al["gqs4"] = stk.enter_context(nc.sbuf_tensor("gqs4", [128, W2 // 16], F32))
    al["gmx"] = stk.enter_context(nc.sbuf_tensor("gmx", [128, 4], F32))
    al["ggs"] = stk.enter_context(nc.sbuf_tensor("ggs", [128, 4], F32))
    al["grow"] = stk.enter_context(nc.sbuf_tensor("grow", [1, 128], F32))
    al["gout"] = [stk.enter_context(nc.sbuf_tensor(f"gout{i}", [128, W2], BF16))
                  for i in range(2)]
    return al


def _wquant_gen(nc, al, c, r, sems, v2, y2):
    sD, sG, sT, sV, sSc, sQ, sO = sems
    W2 = 1024
    gq1 = al["gq1"]
    gqtA, gqtB = al["gqtA"], al["gqtB"]
    gqs1, gqs2, gqs3, gqs4 = al["gqs1"], al["gqs2"], al["gqs3"], al["gqs4"]
    gmx, ggs, grow, gout = al["gmx"], al["ggs"], al["grow"], al["gout"]
    jobs = [(c.gw_in, c.wgq_s, FI, DM, 0), (c.uw_in, c.wuq_s, FI, DM, 1),
            (c.dw_in, c.wdq_s, DM, FI, 2)]

    vmark = 0
    nt = 0
    for win, wout_s, nrows, wwidth, wi in jobs:
        v2(lambda e, wi=wi: e.memset(gmx[:, wi:wi + 1], 0.0))
        for cr in range(nrows // 128):
            for cc in range(wwidth // W2):
                par = nt % 2
                nt += 1
                if nt > 2:
                    y2(lambda e, v=vmark: e.wait_ge(sV.h, v))
                y2(lambda e, cr=cr, cc=cc, win=win, par=par: sG.dma(
                    e.dma_start(gq1[par][:],
                                win[cr * 128:(cr + 1) * 128,
                                    cc * W2:(cc + 1) * W2])))
                dn = sG.n
                v2(lambda e, v=dn: e.wait_ge(sG.h, v))
                v2(lambda e, par=par, wi=wi: sV.inc(e.tensor_reduce(
                    gmx[:, 3:4], gq1[par][:], AX.X, ALU.max,
                    apply_absolute_value=True)))
                v2(lambda e, wi=wi: e.tensor_tensor(
                    gmx[:, wi:wi + 1], gmx[:, wi:wi + 1], gmx[:, 3:4], ALU.max))
                vmark = sV.n
                yield
    for win, wout_s, nrows, wwidth, wi in jobs:
        v2(lambda e, wi=wi, v=vmark: e.wait_ge(sV.h, v))
        v2(lambda e, wi=wi: sV.inc(e.tensor_copy(gmx[:, 3:4],
                                                        gmx[:, wi:wi + 1])))
        vv = sV.n
        y2(lambda e, v=vv: e.wait_ge(sV.h, v))
        y2(lambda e, wi=wi: sD.dma(e.dma_start(
            c.scal_s[10 + wi, 0:128], gmx[:, 3:4])))
        dn = sD.n
        y2(lambda e, v=dn: e.wait_ge(sD.h, v))
        y2(lambda e, wi=wi: sD.dma(e.dma_start(
            grow[:], c.scal_s[10 + wi].unsqueeze(0))))
        dn = sD.n
        v2(lambda e, v=dn: e.wait_ge(sD.h, v))
        v2(lambda e: e.tensor_reduce(grow[0:1, 0:1], grow[:], AX.X, ALU.max))
        v2(lambda e: e.reciprocal(grow[0:1, 1:2], grow[0:1, 0:1]))
        v2(lambda e: e.tensor_scalar(grow[0:1, 1:2], grow[0:1, 1:2],
                                            2688.0, None, ALU.mult))
        v2(lambda e: sV.inc(e.tensor_scalar(
            grow[0:1, 2:3], grow[0:1, 0:1], float(np.float32(1.0 / 2688.0)),
            None, ALU.mult)))
        vv = sV.n
        y2(lambda e, v=vv: e.wait_ge(sV.h, v))
        y2(lambda e, wi=wi: sD.dma(e.dma_start(
            c.scal_s[10 + wi, 0:1], grow[0:1, 1:2])))
        y2(lambda e, wi=wi: sD.dma(e.dma_start(
            c.scal_s[13 + wi, 0:1], grow[0:1, 2:3])))
        dn = sD.n
        y2(lambda e, v=dn: e.wait_ge(sD.h, v))
        vmark = sV.n
        yield
    for k in range(7):
        w = 1 << k
        for rr in range(10, 16):
            y2(lambda e, w=w, rr=rr: sD.dma(e.dma_start(
                c.scal_s[rr, w:2 * w], c.scal_s[rr, 0:w])))
        dn = sD.n
        y2(lambda e, v=dn: e.wait_ge(sD.h, v))
    for wi in range(3):
        y2(lambda e, wi=wi: sD.dma(e.dma_start(
            ggs[:, wi:wi + 1], c.scal_s[10 + wi].unsqueeze(1))))
        y2(lambda e, wi=wi: sD.dma(e.dma_start(
            r.scbc[:, 2 + wi:3 + wi], c.scal_s[13 + wi].unsqueeze(1))))
    y1_done = sD.n

    v2(lambda e, v=y1_done: e.wait_ge(sD.h, v))
    nt = 0
    dmark = {}
    for win, wout_s, nrows, wwidth, wi in jobs:
        for cr in range(nrows // 128):
            for cc in range(wwidth // W2):
                par = nt % 2
                if nt >= 2:
                    y2(lambda e, v=dmark[nt - 2][0]: e.wait_ge(sV.h, v))
                y2(lambda e, cr=cr, cc=cc, win=win, par=par: sG.dma(
                    e.dma_start(gq1[par][:],
                                win[cr * 128:(cr + 1) * 128,
                                    cc * W2:(cc + 1) * W2])))
                dn = sG.n
                v2(lambda e, v=dn: e.wait_ge(sG.h, v))
                if nt >= 2:
                    v2(lambda e, v=dmark[nt - 2][1]: e.wait_ge(sG.h, v))
                _ops = []
                last = emit_nvfp4(_ops.append, gq1[par][:], gout[par][:],
                                  gqs1[:], gqs2[:], gqs3[:], gqs4[:],
                                  gqtA[:], gqtB[:], ggs[:, wi:wi + 1], eng="v")
                for _oi, _op in enumerate(_ops):
                    v2(_op)
                    if _oi % 2 == 1:
                        yield
                v2(lambda e, last=last: sV.inc(last(e)))
                vv = sV.n
                y2(lambda e, v=vv: e.wait_ge(sV.h, v))
                y2(lambda e, cr=cr, cc=cc, wout_s=wout_s, par=par:
                          sG.dma(e.dma_start(
                              wout_s[cr * 128:(cr + 1) * 128,
                                     cc * W2:(cc + 1) * W2], gout[par][:])))
                dmark[nt] = (vv, sG.n)
                nt += 1
                yield
    v2(lambda e: sQ.inc(e.engine_nop()))
def _emit_b0(nc, c, r, sems):
    sD, sG, sT, sV, sSc, sQ, sO = sems
    ph = contextlib.ExitStack()
    with ph:
        sch = Sched()
        Y = mk_adder(sch.q["y"])
        V = mk_adder(sch.q["v"], track=True)
        wstgA = ph.enter_context(nc.sbuf_tensor("b0_wstgA", [128, DM], F32))
        woh = ph.enter_context(nc.sbuf_tensor("b0_woh", [128, NCH * DM], BF16))
        wol = ph.enter_context(nc.sbuf_tensor("b0_wol", [128, NCH * DM], BF16))
        jstg = ph.enter_context(nc.sbuf_tensor("b0_jstg", [4, S], F32))
        jrows = ph.enter_context(nc.sbuf_tensor("b0_jrows", [4, S], BF16))
        onesrow = ph.enter_context(nc.sbuf_tensor("b0_onesrow", [1, S], BF16))
        sstg = ph.enter_context(nc.sbuf_tensor("b0_sstg", [1, 8], F32))
        upzf = ph.enter_context(nc.sbuf_tensor("b0_upzf", [128, 2 * H], F32))

        Y(lambda e: sD.dma(e.dma_start(r.ident[:], c.ident_in)))
        Y(lambda e: sD.dma(e.dma_start(jstg[:], c.jc_in)))
        Y(lambda e: sD.dma(e.dma_start(r.biases[:, 0:NCH], c.bq_in)))
        Y(lambda e: sD.dma(e.dma_start(r.biases[:, NCH:2 * NCH], c.bk_in)))
        Y(lambda e: sD.dma(e.dma_start(r.biases[:, 2 * NCH:3 * NCH], c.bfq_in)))
        Y(lambda e: sD.dma(e.dma_start(r.biases[:, 3 * NCH:4 * NCH], c.bfg_in)))

        Y(lambda e: sD.dma(e.dma_start(upzf[:], c.upz_in)))
        Y(lambda e: sD.dma(e.dma_start(sstg[0:1, 0:1], c.sin_in)))
        Y(lambda e: sD.dma(e.dma_start(sstg[0:1, 1:2], c.sind_in)))
        d0 = sD.n
        V(lambda e: e.memset(r.sepA[:], 0.0))
        V(lambda e, v=d0: e.wait_ge(sD.h, v))
        V(lambda e: e.tensor_copy(r.identb[:], r.ident[:]))
        V(lambda e: e.memset(r.ones128b[:], 1.0))
        V(lambda e: e.memset(r.ones128r[:], 1.0))
        V(lambda e: e.tensor_copy(jrows[:], jstg[:]))
        V(lambda e: e.memset(onesrow[:], 1.0))
        V(lambda e: e.tensor_copy(r.upzh[:], upzf[:]))
        V(lambda e: e.tensor_tensor(r.upzl[:], upzf[:], r.upzh[:], ALU.subtract))
        V(lambda e: e.memset(r.kmax2[:], 0.0))
        V(lambda e: e.memset(r.xbar[:], 0.0))
        V(lambda e: e.reciprocal(sstg[0:1, 2:3], sstg[0:1, 0:1]))
        V(lambda e: sV.inc(e.reciprocal(sstg[0:1, 3:4], sstg[0:1, 1:2])))
        v0 = sV.n
        Y(lambda e, v=v0: e.wait_ge(sV.h, v))
        Y(lambda e: sD.dma(e.dma_start(c.scal_s[0, 0:1], sstg[0:1, 2:3])))
        Y(lambda e: sD.dma(e.dma_start(c.scal_s[1, 0:1], sstg[0:1, 3:4])))
        Y(lambda e: sD.dma(e.dma_start(c.scal_s[8, 0:1], sstg[0:1, 0:1])))
        Y(lambda e: sD.dma(e.dma_start(c.scal_s[9, 0:1], sstg[0:1, 1:2])))
        dn = sD.n
        Y(lambda e, v=dn: e.wait_ge(sD.h, v))
        for k in range(7):
            w = 1 << k
            for row in (0, 1, 8, 9):
                Y(lambda e, w=w, row=row: sD.dma(e.dma_start(
                    c.scal_s[row, w:2 * w], c.scal_s[row, 0:w])))
            dn = sD.n
            Y(lambda e, v=dn: e.wait_ge(sD.h, v))
        Y(lambda e: sD.dma(e.dma_start(r.scbc[:, 0:1], c.scal_s[0].unsqueeze(1))))
        Y(lambda e: sD.dma(e.dma_start(r.scbc[:, 1:2], c.scal_s[1].unsqueeze(1))))
        Y(lambda e: sD.dma(e.dma_start(r.scbc[:, 5:6], c.scal_s[8].unsqueeze(1))))
        Y(lambda e: sD.dma(e.dma_start(r.scbc[:, 6:7], c.scal_s[9].unsqueeze(1))))

        vprev = 0
        for ch in range(NCH):
            if ch >= 1:
                Y(lambda e, v=vprev: e.wait_ge(sV.h, v))
            Y(lambda e, ch=ch: sD.dma(e.dma_start(
                wstgA[:], c.wo_in[ch * 128:(ch + 1) * 128, :])))
            dn = sD.n
            V(lambda e, v=dn: e.wait_ge(sD.h, v))
            V(lambda e, ch=ch: e.tensor_copy(woh[:, ch * DM:(ch + 1) * DM], wstgA[:]))
            V(lambda e, ch=ch: sV.inc(e.tensor_tensor(
                wol[:, ch * DM:(ch + 1) * DM], wstgA[:],
                woh[:, ch * DM:(ch + 1) * DM], ALU.subtract)))
            vprev = sV.n
        V(lambda e: sV.inc(e.memset(wstgA[0:1, :], 0.0)))
        vz = sV.n
        Y(lambda e, v=vz: e.wait_ge(sV.h, v))
        for h in range(H):
            ch, r0 = divmod(h * D, 128)
            hi_sl = woh[r0:r0 + 64, ch * DM:(ch + 1) * DM]
            lo_sl = wol[r0:r0 + 64, ch * DM:(ch + 1) * DM]
            Y(lambda e, h=h, s_=hi_sl: sD.dma(e.dma_start(c.woar1_s[h, 0:64], s_)))
            Y(lambda e, h=h, s_=hi_sl: sD.dma(e.dma_start(c.woar1_s[h, 64:128], s_)))
            Y(lambda e, h=h, s_=lo_sl: sD.dma(e.dma_start(c.woar2_s[h, 0:64], s_)))
            Y(lambda e, h=h: sD.dma(e.dma_start(
                c.woar2_s[h, 64:65], wstgA[0:1, 0:DM // 2].bitcast(BF16))))
            Y(lambda e, h=h, s_=hi_sl: sD.dma(
                e.dma_start(c.woar2_s[h, 65:128], s_[0:63, :])))
        for ri, si in enumerate([0, 1, 0, 2, 3, 2]):
            Y(lambda e, ri=ri, si=si: sD.dma(e.dma_start(
                c.jarr_s[ri], jrows[si:si + 1, :])))
        for ri in (6, 7):
            Y(lambda e, ri=ri: sD.dma(e.dma_start(c.jarr_s[ri], onesrow[:])))
        dn = sD.n
        Y(lambda e, v=dn: e.wait_ge(sD.h, v))
        sch.run(nc)


def _emit_b1(nc, c, r, sems):
    sD, sG, sT, sV, sSc, sQ, sO = sems
    ph = contextlib.ExitStack()
    with ph:
        sch = Sched()
        Y = mk_adder(sch.q["y"])
        V = mk_adder(sch.q["v"], track=True)
        T = mk_adder(sch.q["t"])
        SC = mk_adder(sch.q["s"])

        def sbP(name, shape, dtyp=F32):
            return ph.enter_context(nc.sbuf_tensor(name, shape, dtyp))

        bvbc = sbP("b1_bvbc", [128, DM])
        wkh = sbP("wkh", [128, NCH * DM], BF16)
        wkl = sbP("wkl", [128, NCH * DM], BF16)
        wvh = sbP("wvh", [128, NCH * DM], BF16)
        wvl = sbP("wvl", [128, NCH * DM], BF16)
        wstg = sbP("b1_wstg", [128, DM])
        xst = [sbP(f"xst{i}", [128, DM]) for i in range(2)]
        sqt = [sbP(f"sqt{i}", [128, DM]) for i in range(2)]
        lnst = sbP("lnst", [128, 8])
        xnt = [sbP(f"xnt{i}", [128, DM]) for i in range(2)]
        xqo = sbP("xqo", [128, DM], BF16)
        qsA = sbP("qsA", [128, 64])
        qsB = sbP("qsB", [128, 64])
        qsC = sbP("qsC", [128, 64])
        qsD = sbP("qsD", [128, 64])
        qtA = sbP("qtA", [128, DM])
        qtB = sbP("qtB", [128, DM])
        xhi = [sbP(f"xhi{i}", [128, 128], BF16) for i in range(2)]
        xlo = [sbP(f"xlo{i}", [128, 128], BF16) for i in range(2)]
        xnfh = sbP("xnfh", [128, NCH * 512], BF16)
        xnfl = sbP("xnfl", [128, NCH * 512], BF16)
        kevt = sbP("kevt", [128, 512])
        kevh = [sbP(f"kevh{i}", [128, 512], BF16) for i in range(2)]
        kevl = [sbP(f"kevl{i}", [128, 512], BF16) for i in range(2)]
        ksq = sbP("ksq", [128, 512], BF16)
        k2st = sbP("k2st", [1, 4])
        xqstg = [sbP(f"xqstg{i}", [128, 128], BF16) for i in range(2)]
        vevh = [sbP(f"vevh{i}", [128, 512], BF16) for i in range(2)]
        vevl = [sbP(f"vevl{i}", [128, 512], BF16) for i in range(2)]
        vstg = [sbP(f"vstg{i}", [128, 8 * 128], BF16) for i in range(2)]
        xbsum = sbP("xbsum", [128, 512])
        ptr_ = ph.enter_context(nc.psum_tensor("ptr_", [128, 128], F32))
        pk = ph.enter_context(nc.psum_tensor("pk", [128, 512], F32))
        pvp = ph.enter_context(nc.psum_tensor("pvp", [128, 512], F32))
        pk2a = ph.enter_context(nc.psum_tensor("pk2a", [1, 512], F32))
        pk2b = ph.enter_context(nc.psum_tensor("pk2b", [1, 512], F32))
        pxq = ph.enter_context(nc.psum_tensor("pxq", [128, 128], BF16))

        Y(lambda e: sD.dma(e.dma_start(bvbc[:], c.bv_bc_in)))
        vprev = 0
        for wi, (win, whi, wlo) in enumerate([(c.wk_in, wkh, wkl),
                                              (c.wv_in, wvh, wvl)]):
            for ch in range(NCH):
                if wi + ch > 0:
                    Y(lambda e, v=vprev: e.wait_ge(sV.h, v))
                Y(lambda e, win=win, ch=ch: sD.dma(
                    e.dma_start(wstg[:], win[ch * 128:(ch + 1) * 128, :])))
                dn = sD.n
                V(lambda e, v=dn: e.wait_ge(sD.h, v))
                V(lambda e, whi=whi, ch=ch: e.tensor_copy(
                    whi[:, ch * DM:(ch + 1) * DM], wstg[:]))
                V(lambda e, whi=whi, wlo=wlo, ch=ch: sV.inc(e.tensor_tensor(
                    wlo[:, ch * DM:(ch + 1) * DM], wstg[:],
                    whi[:, ch * DM:(ch + 1) * DM], ALU.subtract)))
                vprev = sV.n
        for i in range(2):
            V(lambda e, i=i: e.memset(vstg[i][:], 0.0))
            for hh in range(8):
                V(lambda e, i=i, hh=hh: e.memset(
                    vstg[i][:, hh * 128 + 64:hh * 128 + 65], 1.0))
        V(lambda e: sV.inc(e.memset(lnst[:], 0.0)))

        last_kv_mm = sT.n
        vln = {}
        dma_mark = {}
        vq = 0
        vxq = 0
        v_xqev = 0
        v_ptr = 0
        for blk in range(NB):
            V(lambda e, v=last_kv_mm: e.wait_ge(sT.h, v))
            for tsub in range(4):
                idx = blk * 4 + tsub
                tok0 = blk * 512 + tsub * 128
                par = idx % 2
                xt, sq, xn_ = xst[par], sqt[par], xnt[par]
                if idx >= 2:
                    Y(lambda e, v=vln.get(idx - 2, 0): e.wait_ge(sV.h, v))
                Y(lambda e, xt=xt, tok0=tok0: sD.dma(
                    e.dma_start(xt[:], c.x_in[tok0:tok0 + 128, :])))
                dn = sD.n
                SC(lambda e, v=dn: e.wait_ge(sD.h, v))
                if idx >= 2:
                    SC(lambda e, v=vln.get(idx - 2, 0): e.wait_ge(sV.h, v))
                SC(lambda e, xt=xt, sq=sq: sSc.inc(
                    e.activation(sq[:], xt[:], ACT.Square)))
                ssq = sSc.n
                V(lambda e, v=dn: e.wait_ge(sD.h, v))
                V(lambda e, v=ssq: e.wait_ge(sSc.h, v))
                V(lambda e, xt=xt: e.reduce_sum(lnst[:, 0:1], xt[:], AX.X))
                V(lambda e, sq=sq: e.reduce_sum(lnst[:, 1:2], sq[:], AX.X))
                V(lambda e: e.tensor_scalar(lnst[:, 2:3], lnst[:, 0:1],
                                            1.0 / DM, None, ALU.mult))
                V(lambda e: e.tensor_scalar(lnst[:, 3:4], lnst[:, 1:2],
                                            1.0 / DM, None, ALU.mult))
                V(lambda e: e.tensor_tensor(lnst[:, 4:5], lnst[:, 2:3],
                                            lnst[:, 2:3], ALU.mult))
                V(lambda e: e.tensor_tensor(lnst[:, 5:6], lnst[:, 3:4],
                                             lnst[:, 4:5], ALU.subtract))
                V(lambda e: sV.inc(e.tensor_scalar(lnst[:, 5:6], lnst[:, 5:6],
                                                   EPS, None, ALU.add)))
                vv = sV.n
                SC(lambda e, v=vv: e.wait_ge(sV.h, v))
                SC(lambda e: sSc.inc(e.activation(lnst[:, 6:7], lnst[:, 5:6],
                                                  ACT.Sqrt)))
                srt = sSc.n
                V(lambda e, v=srt: e.wait_ge(sSc.h, v))
                V(lambda e: e.reciprocal(lnst[:, 7:8], lnst[:, 6:7]))
                V(lambda e, xt=xt, xn_=xn_: sV.inc(e.tensor_scalar(
                    xn_[:], xt[:], lnst[:, 2:3], lnst[:, 7:8],
                    ALU.subtract, ALU.mult)))
                vln[idx] = sV.n
                if blk < 4:
                    V(lambda e, v=vln[idx]: e.wait_ge(sV.h, v))
                    if vxq:
                        V(lambda e, v=vxq: e.wait_ge(sT.h, v))
                    last = emit_nvfp4(V, xn_[:], xqo[:], qsA[:], qsB[:], qsC[:],
                                      qsD[:], qtA[:], qtB[:], r.scbc[:, 5:6])
                    V(lambda e, last=last: sV.inc(last(e)))
                    vq = sV.n
                    for ch in range(NCH):
                        nxq = idx * NCH + ch
                        spar = nxq % 2
                        T(lambda e, v=vq: e.wait_ge(sV.h, v))
                        T(lambda e, v=v_xqev: e.wait_ge(sV.h, v))
                        T(lambda e, ch=ch: sT.inc(e.transpose(
                            pxq[:], xqo[:, ch * 128:ch * 128 + 128],
                            r.identb[:])))
                        tt = sT.n
                        col = ch * TOWN + blk * 512 + tsub * 128
                        V(lambda e, v=tt: e.wait_ge(sT.h, v))
                        if nxq >= 2:
                            V(lambda e, v=dma_mark.get(("xq", nxq - 2), 0):
                              e.wait_ge(sD.h, v))
                        V(lambda e, spar=spar: sV.inc(e.tensor_copy(
                            xqstg[spar][:], pxq[:])))
                        v_xqev = sV.n
                        Y(lambda e, v=v_xqev: e.wait_ge(sV.h, v))
                        Y(lambda e, col=col, spar=spar: sD.dma(e.dma_start(
                            c.xqf_s[:, col:col + 128], xqstg[spar][:])))
                        dma_mark[("xq", nxq)] = sD.n
                    vxq = sT.n
                    vln[idx] = sV.n
                vlni = vln[idx]
                for ch in range(NCH):
                    n_tr = idx * NCH + ch
                    cpar = n_tr % 2
                    T(lambda e, v=vlni: e.wait_ge(sV.h, v))
                    if n_tr >= 1:
                        T(lambda e, v=v_ptr: e.wait_ge(sV.h, v))
                    T(lambda e, ch=ch, xn_=xn_: sT.inc(e.transpose(
                        ptr_[:], xn_[:, ch * 128:(ch + 1) * 128], r.ident[:])))
                    ttr = sT.n
                    V(lambda e, v=ttr: e.wait_ge(sT.h, v))
                    if n_tr >= 2:
                        V(lambda e, v=dma_mark.get(("x", n_tr - 2), 0):
                          e.wait_ge(sD.h, v))
                    V(lambda e, cpar=cpar: e.tensor_copy(xhi[cpar][:], ptr_[:]))
                    V(lambda e, cpar=cpar: sV.inc(e.tensor_tensor(
                        xlo[cpar][:], ptr_[:], xhi[cpar][:], ALU.subtract)))
                    v_ptr = sV.n
                    col = ch * 512 + tsub * 128
                    V(lambda e, col=col, cpar=cpar: e.tensor_copy(
                        xnfh[:, col:col + 128], xhi[cpar][:]))
                    V(lambda e, col=col, cpar=cpar: sV.inc(e.tensor_copy(
                        xnfl[:, col:col + 128], xlo[cpar][:])))
                    vst2 = sV.n
                    Y(lambda e, v=vst2: e.wait_ge(sV.h, v))
                    Y(lambda e, ch=ch, tok0=tok0, cpar=cpar: sD.dma(e.dma_start(
                        c.xnh_s[ch, :, tok0:tok0 + 128], xhi[cpar][:])))
                    Y(lambda e, ch=ch, tok0=tok0, cpar=cpar: sD.dma(e.dma_start(
                        c.xnl_s[ch, :, tok0:tok0 + 128], xlo[cpar][:])))
                    dma_mark[("x", n_tr)] = sD.n
            v_stageA = sV.n
            for ch in range(NCH):
                V(lambda e, ch=ch: e.tensor_tensor(
                    xbsum[:], xnfh[:, ch * 512:(ch + 1) * 512],
                    xnfl[:, ch * 512:(ch + 1) * 512], ALU.add))
                V(lambda e: e.reduce_sum(lnst[:, 0:1], xbsum[:], AX.X))
                V(lambda e, ch=ch: sV.inc(e.tensor_tensor(
                    r.xbar[:, ch:ch + 1], r.xbar[:, ch:ch + 1],
                    lnst[:, 0:1], ALU.add)))
            v_xnf = sV.n

            v_kev = sV.n
            for m in range(NCH):
                kpar = m % 2
                T(lambda e, v=v_xnf: e.wait_ge(sV.h, v))
                T(lambda e, v=v_kev: e.wait_ge(sV.h, v))
                for ch in range(NCH):
                    w0 = ch * DM + m * 128
                    x0 = ch * 512
                    T(lambda e, w0=w0, x0=x0, st=(ch == 0): e.matmul(
                        pk[:], wkh[:, w0:w0 + 128], xnfh[:, x0:x0 + 512],
                        start=st, stop=False))
                    T(lambda e, w0=w0, x0=x0: e.matmul(
                        pk[:], wkl[:, w0:w0 + 128], xnfh[:, x0:x0 + 512],
                        start=False, stop=False))
                    if ch == NCH - 1:
                        T(lambda e, w0=w0, x0=x0: sT.inc(e.matmul(
                            pk[:], wkh[:, w0:w0 + 128], xnfl[:, x0:x0 + 512],
                            start=False, stop=True)))
                    else:
                        T(lambda e, w0=w0, x0=x0: e.matmul(
                            pk[:], wkh[:, w0:w0 + 128], xnfl[:, x0:x0 + 512],
                            start=False, stop=False))
                tk = sT.n
                V(lambda e, v=tk: e.wait_ge(sT.h, v))
                if blk * NCH + m >= 2:
                    V(lambda e, v=dma_mark.get(("k", blk * NCH + m - 2), 0):
                      e.wait_ge(sD.h, v))
                V(lambda e, m=m: e.tensor_scalar(
                    kevt[:], pk[:], r.biases[:, NCH + m:NCH + m + 1], None, ALU.add))
                V(lambda e, kpar=kpar: e.tensor_copy(kevh[kpar][:], kevt[:]))
                V(lambda e, kpar=kpar: e.tensor_tensor(
                    kevl[kpar][:], kevt[:], kevh[kpar][:], ALU.subtract))
                V(lambda e: e.tensor_tensor(kevt[:], kevt[:], kevt[:], ALU.mult))
                V(lambda e: sV.inc(e.tensor_copy(ksq[:], kevt[:])))
                v_kev = sV.n
                T(lambda e, v=v_kev: e.wait_ge(sV.h, v))
                T(lambda e: sT.inc(e.matmul(pk2a[:], r.ones128b[0:64, 0:1],
                                            ksq[0:64, :], start=True, stop=True)))
                T(lambda e: sT.inc(e.matmul(pk2b[:], r.ones128b[64:128, 0:1],
                                            ksq[64:128, :], start=True, stop=True)))
                tk2 = sT.n
                V(lambda e, v=tk2: e.wait_ge(sT.h, v))
                h0, h1 = 2 * m, 2 * m + 1
                V(lambda e: e.tensor_reduce(k2st[0:1, 0:1], pk2a[:], AX.X, ALU.max))
                V(lambda e: e.tensor_reduce(k2st[0:1, 1:2], pk2b[:], AX.X, ALU.max))
                V(lambda e, h0=h0: e.tensor_tensor(
                    r.kmax2[0:1, h0:h0 + 1], r.kmax2[0:1, h0:h0 + 1],
                    k2st[0:1, 0:1], ALU.max))
                V(lambda e, h1=h1: sV.inc(e.tensor_tensor(
                    r.kmax2[0:1, h1:h1 + 1], r.kmax2[0:1, h1:h1 + 1],
                    k2st[0:1, 1:2], ALU.max)))
                v_kev = sV.n
                t0 = blk * 512
                Y(lambda e, v=v_kev: e.wait_ge(sV.h, v))
                Y(lambda e, h0=h0, t0=t0, kpar=kpar: sD.dma(e.dma_start(
                    c.karr_s[h0, 0:64, t0:t0 + 512], kevh[kpar][0:64, :])))
                Y(lambda e, h0=h0, t0=t0, kpar=kpar: sD.dma(e.dma_start(
                    c.karr_s[h0, 64:128, t0:t0 + 512], kevl[kpar][0:64, :])))
                Y(lambda e, h1=h1, t0=t0, kpar=kpar: sD.dma(e.dma_start(
                    c.karr_s[h1, 0:64, t0:t0 + 512], kevh[kpar][64:128, :])))
                Y(lambda e, h1=h1, t0=t0, kpar=kpar: sD.dma(e.dma_start(
                    c.karr_s[h1, 64:128, t0:t0 + 512], kevl[kpar][64:128, :])))
                dma_mark[("k", blk * NCH + m)] = sD.n

            v_vev = sV.n
            for tsub in range(4):
                t0 = blk * 512 + tsub * 128
                for ot in range(2):
                    rnd = (blk * 4 + tsub) * 2 + ot
                    vpar = rnd % 2
                    T(lambda e, v=v_vev: e.wait_ge(sV.h, v))
                    for ch in range(NCH):
                        x0 = ch * 512 + tsub * 128
                        w0 = ch * DM + ot * 512
                        T(lambda e, x0=x0, w0=w0, st=(ch == 0): e.matmul(
                            pvp[:], xnfh[:, x0:x0 + 128], wvh[:, w0:w0 + 512],
                            start=st, stop=False))
                        T(lambda e, x0=x0, w0=w0: e.matmul(
                            pvp[:], xnfl[:, x0:x0 + 128], wvh[:, w0:w0 + 512],
                            start=False, stop=False))
                        if ch == NCH - 1:
                            T(lambda e, x0=x0, w0=w0: sT.inc(e.matmul(
                                pvp[:], xnfh[:, x0:x0 + 128], wvl[:, w0:w0 + 512],
                                start=False, stop=True)))
                        else:
                            T(lambda e, x0=x0, w0=w0: e.matmul(
                                pvp[:], xnfh[:, x0:x0 + 128], wvl[:, w0:w0 + 512],
                                start=False, stop=False))
                    tv = sT.n
                    V(lambda e, v=tv: e.wait_ge(sT.h, v))
                    if rnd >= 2:
                        V(lambda e, v=dma_mark.get(("v", rnd - 2), 0):
                          e.wait_ge(sD.h, v))
                    V(lambda e, ot=ot: e.tensor_tensor(
                        kevt[:], pvp[:], bvbc[:, ot * 512:ot * 512 + 512], ALU.add))
                    V(lambda e, vpar=vpar: e.tensor_copy(vevh[vpar][:], kevt[:]))
                    V(lambda e, vpar=vpar: sV.inc(e.tensor_tensor(
                        vevl[vpar][:], kevt[:], vevh[vpar][:], ALU.subtract)))
                    for hh in range(8):
                        V(lambda e, hh=hh, vpar=vpar: e.tensor_copy(
                            vstg[vpar][:, hh * 128:hh * 128 + 64],
                            vevh[vpar][:, hh * 64:hh * 64 + 64]))
                        V(lambda e, hh=hh, vpar=vpar: sV.inc(e.tensor_copy(
                            vstg[vpar][:, hh * 128 + 65:hh * 128 + 128],
                            vevl[vpar][:, hh * 64:hh * 64 + 63])))
                    v_vev = sV.n
                    Y(lambda e, v=v_vev: e.wait_ge(sV.h, v))
                    for hh in range(8):
                        h = ot * 8 + hh
                        Y(lambda e, h=h, hh=hh, t0=t0, vpar=vpar: sD.dma(e.dma_start(
                            c.varr_s[h, t0:t0 + 128, :],
                            vstg[vpar][:, hh * 128:hh * 128 + 128])))
                    dma_mark[("v", rnd)] = sD.n

            last_kv_mm = sT.n
        sch.run(nc)
def _emit_b2(nc, c, r, sems):
    sD, sG, sT, sV, sSc, sQ, sO = sems
    ph = contextlib.ExitStack()
    with ph:
        sch = Sched()
        Y = mk_adder(sch.q["y"])
        V = mk_adder(sch.q["v"], track=True)
        T = mk_adder(sch.q["t"])
        SC = mk_adder(sch.q["s"])

        def sbP(name, shape, dtyp=F32):
            return ph.enter_context(nc.sbuf_tensor(name, shape, dtyp))

        wqh = sbP("wqh", [128, NCH * DM], BF16)
        wql = sbP("wql", [128, NCH * DM], BF16)
        wfqh = sbP("wfqh", [128, NCH * DM], BF16)
        wfql = sbP("wfql", [128, NCH * DM], BF16)
        wstg = sbP("b2_wstg", [128, DM])
        wfgh = sbP("wfgh", [128, DM], BF16)
        wfgl = sbP("wfgl", [128, DM], BF16)
        xrh = [sbP(f"xrh{i}", [128, 512], BF16) for i in range(2)]
        xrl = [sbP(f"xrl{i}", [128, 512], BF16) for i in range(2)]
        qevt = sbP("qevt", [128, 512])
        qhi = [sbP(f"qhi{i}", [128, 512], BF16) for i in range(2)]
        qlo = [sbP(f"qlo{i}", [128, 512], BF16) for i in range(2)]
        qsqt = sbP("qsqt", [128, 512], BF16)
        xbh = sbP("xbh", [128, NCH], BF16)
        xbl = sbP("xbl", [128, NCH], BF16)
        glof = sbP("glof", [128, NCH])
        gloh = sbP("gloh", [128, NCH], BF16)
        glol = sbP("glol", [128, NCH], BF16)
        bpp = sbP("bpp", [128, NCH])
        ctf = sbP("ctf", [128, 512])
        cth = sbP("cth", [128, 512], BF16)
        ctl = sbP("ctl", [128, 512], BF16)
        przA = sbP("przA", [1, 512])
        przB = sbP("przB", [1, 512])
        przC = sbP("przC", [1, 512])
        featq = sbP("featq", [1, 3 * 1024])
        featw = sbP("featw", [1, 6 * 1024])
        featb = sbP("featb", [1, 8 * 1024], BF16)
        kmaxr = sbP("kmaxr", [1, H])
        pq = ph.enter_context(nc.psum_tensor("pq", [128, 512], F32))
        pcp = ph.enter_context(nc.psum_tensor("pcp", [128, 512], F32))
        pbp = ph.enter_context(nc.psum_tensor("pbp", [128, NCH], F32))
        ppq = ph.enter_context(nc.psum_tensor("ppq", [1, 512], F32))
        pzq = ph.enter_context(nc.psum_tensor("pzq", [1, 512], F32))
        pq2a = ph.enter_context(nc.psum_tensor("pq2a", [1, 512], F32))
        pq2b = ph.enter_context(nc.psum_tensor("pq2b", [1, 512], F32))

        vprev = 0
        for wi, (win, whi, wlo) in enumerate([(c.wq_in, wqh, wql),
                                              (c.wfq_in, wfqh, wfql)]):
            for ch in range(NCH):
                if wi + ch > 0:
                    Y(lambda e, v=vprev: e.wait_ge(sV.h, v))
                Y(lambda e, win=win, ch=ch: sD.dma(
                    e.dma_start(wstg[:], win[ch * 128:(ch + 1) * 128, :])))
                dn = sD.n
                V(lambda e, v=dn: e.wait_ge(sD.h, v))
                if wi == 1:
                    V(lambda e: e.tensor_scalar(wstg[:], wstg[:], 8.0, None,
                                                ALU.mult))
                V(lambda e, whi=whi, ch=ch: e.tensor_copy(
                    whi[:, ch * DM:(ch + 1) * DM], wstg[:]))
                V(lambda e, whi=whi, wlo=wlo, ch=ch: sV.inc(e.tensor_tensor(
                    wlo[:, ch * DM:(ch + 1) * DM], wstg[:],
                    whi[:, ch * DM:(ch + 1) * DM], ALU.subtract)))
                vprev = sV.n

        V(lambda e: e.tensor_scalar(r.xbar[:], r.xbar[:], 1.0 / S, None, ALU.mult))
        V(lambda e: e.tensor_copy(xbh[:], r.xbar[:]))
        V(lambda e: sV.inc(e.tensor_tensor(xbl[:], r.xbar[:], xbh[:], ALU.subtract)))
        v_xb = sV.n

        T(lambda e, v=v_xb: e.wait_ge(sV.h, v))
        for m in range(NCH):
            for ch in range(NCH):
                w0 = ch * DM + m * 128
                T(lambda e, w0=w0, ch=ch, m=m, st=(m == 0 and ch == 0): e.matmul(
                    pbp[:, m:m + 1],
                    wqh[:, w0:w0 + 128], xbh[:, ch:ch + 1],
                    start=st, stop=False))
                T(lambda e, w0=w0, ch=ch, m=m: e.matmul(
                    pbp[:, m:m + 1], wql[:, w0:w0 + 128], xbh[:, ch:ch + 1],
                    start=False, stop=False))
                T(lambda e, w0=w0, ch=ch, m=m: (sT.inc(e.matmul(
                    pbp[:, m:m + 1], wqh[:, w0:w0 + 128], xbl[:, ch:ch + 1],
                    start=False, stop=True))
                    if (ch == NCH - 1 and m == NCH - 1) else e.matmul(
                    pbp[:, m:m + 1], wqh[:, w0:w0 + 128], xbl[:, ch:ch + 1],
                    start=False, stop=False)))
        t_glo = sT.n
        V(lambda e, v=t_glo: e.wait_ge(sT.h, v))
        V(lambda e: e.tensor_tensor(glof[:], pbp[:], r.biases[:, 0:NCH], ALU.add))
        V(lambda e: e.tensor_copy(gloh[:], glof[:]))
        V(lambda e: sV.inc(e.tensor_tensor(glol[:], glof[:], gloh[:], ALU.subtract)))
        v_glo = sV.n

        v_fgs = v_glo
        for ch in range(NCH):
            Y(lambda e, v=v_fgs: e.wait_ge(sV.h, v))
            Y(lambda e, ch=ch: sD.dma(e.dma_start(
                wstg[:], c.wfg_in[ch * 128:(ch + 1) * 128, :])))
            dn = sD.n
            V(lambda e, v=dn: e.wait_ge(sD.h, v))
            V(lambda e: e.tensor_copy(wfgh[:], wstg[:]))
            V(lambda e: sV.inc(e.tensor_tensor(wfgl[:], wstg[:], wfgh[:],
                                               ALU.subtract)))
            v_fgs = sV.n
            T(lambda e, v=v_fgs: e.wait_ge(sV.h, v))
            if ch == 0:
                T(lambda e, v=v_glo: e.wait_ge(sV.h, v))
            for m in range(NCH):
                w0 = m * 128
                T(lambda e, w0=w0, ch=ch, m=m, st=(ch == 0 and m == 0): e.matmul(
                    pq[:, m:m + 1], wfgh[:, w0:w0 + 128], gloh[:, ch:ch + 1],
                    start=st, stop=False))
                T(lambda e, w0=w0, ch=ch, m=m: e.matmul(
                    pq[:, m:m + 1], wfgl[:, w0:w0 + 128], gloh[:, ch:ch + 1],
                    start=False, stop=False))
                T(lambda e, w0=w0, ch=ch, m=m: (sT.inc(e.matmul(
                    pq[:, m:m + 1], wfgh[:, w0:w0 + 128], glol[:, ch:ch + 1],
                    start=False, stop=True))
                    if (ch == NCH - 1 and m == NCH - 1) else e.matmul(
                    pq[:, m:m + 1], wfgh[:, w0:w0 + 128], glol[:, ch:ch + 1],
                    start=False, stop=False)))
        t_bpp = sT.n
        V(lambda e, v=t_bpp: e.wait_ge(sT.h, v))
        V(lambda e: e.tensor_tensor(bpp[:], pq[:, 0:NCH], r.biases[:, 3 * NCH:4 * NCH],
                                    ALU.add))
        V(lambda e: sV.inc(e.tensor_tensor(bpp[:], bpp[:],
                                           r.biases[:, 2 * NCH:3 * NCH], ALU.add)))
        v_bpp = sV.n

        dma_q = {}
        qk_mm = {}
        v_qev = sV.n
        for qch in range(4):
            q0 = qch * 512
            for m in range(NCH):
                rnd = qch * NCH + m
                for ch in range(NCH):
                    par = ch % 2
                    nld = rnd * NCH + ch
                    if nld >= 2:
                        Y(lambda e, v=qk_mm.get(nld - 2, 0): e.wait_ge(sT.h, v))
                    Y(lambda e, ch=ch, q0=q0, par=par: sD.dma(e.dma_start(
                        xrh[par][:], c.xnh_s[ch, :, q0:q0 + 512])))
                    Y(lambda e, ch=ch, q0=q0, par=par: sD.dma(e.dma_start(
                        xrl[par][:], c.xnl_s[ch, :, q0:q0 + 512])))
                    dn = sD.n
                    w0 = ch * DM + m * 128
                    T(lambda e, v=dn: e.wait_ge(sD.h, v))
                    if ch == 0:
                        T(lambda e, v=v_qev: e.wait_ge(sV.h, v))
                    T(lambda e, w0=w0, par=par, st=(ch == 0): e.matmul(
                        pq[:], wqh[:, w0:w0 + 128], xrh[par][:],
                        start=st, stop=False))
                    T(lambda e, w0=w0, par=par: e.matmul(
                        pq[:], wql[:, w0:w0 + 128], xrh[par][:],
                        start=False, stop=False))
                    T(lambda e, w0=w0, par=par, sp=(ch == NCH - 1): sT.inc(
                        e.matmul(pq[:], wqh[:, w0:w0 + 128], xrl[par][:],
                                 start=False, stop=sp)))
                    qk_mm[nld] = sT.n
                tq = sT.n
                qpar = rnd % 2
                V(lambda e, v=tq: e.wait_ge(sT.h, v))
                if rnd >= 2:
                    V(lambda e, v=dma_q.get(rnd - 2, 0): e.wait_ge(sD.h, v))
                V(lambda e, m=m: e.tensor_scalar(
                    qevt[:], pq[:], r.biases[:, m:m + 1], 0.125, ALU.add, ALU.mult))
                V(lambda e, qpar=qpar: e.tensor_copy(qhi[qpar][:], qevt[:]))
                V(lambda e, qpar=qpar: e.tensor_tensor(
                    qlo[qpar][:], qevt[:], qhi[qpar][:], ALU.subtract))
                V(lambda e: e.tensor_tensor(qevt[:], qevt[:], qevt[:], ALU.mult))
                V(lambda e: sV.inc(e.tensor_copy(qsqt[:], qevt[:])))
                v_qev = sV.n
                T(lambda e, v=v_qev: e.wait_ge(sV.h, v))
                T(lambda e: sT.inc(e.matmul(pq2a[:], r.ones128b[0:64, 0:1],
                                            qsqt[0:64, :], start=True, stop=True)))
                T(lambda e: sT.inc(e.matmul(pq2b[:], r.ones128b[64:128, 0:1],
                                            qsqt[64:128, :], start=True, stop=True)))
                tq2 = sT.n
                h0, h1 = 2 * m, 2 * m + 1
                V(lambda e, v=tq2: e.wait_ge(sT.h, v))
                V(lambda e, v=dma_q.get(rnd - 1, 0): e.wait_ge(sD.h, v))
                V(lambda e: e.tensor_copy(przA[:], pq2a[:]))
                V(lambda e: sV.inc(e.tensor_copy(przB[:], pq2b[:])))
                v_qev = sV.n
                Y(lambda e, v=v_qev: e.wait_ge(sV.h, v))
                Y(lambda e, h0=h0, q0=q0: sD.dma(e.dma_start(
                    c.q2n_s[h0, q0:q0 + 512], przA[:])))
                Y(lambda e, h1=h1, q0=q0: sD.dma(e.dma_start(
                    c.q2n_s[h1, q0:q0 + 512], przB[:])))
                Y(lambda e, m=m, q0=q0, qpar=qpar: sD.dma(e.dma_start(
                    c.mqh_s[m, :, q0:q0 + 512], qhi[qpar][:])))
                Y(lambda e, m=m, q0=q0, qpar=qpar: sD.dma(e.dma_start(
                    c.mql_s[m, :, q0:q0 + 512], qlo[qpar][:])))
                dma_q[rnd] = sD.n

        dma_c = {}
        c_mm = {}
        v_cev = sSc.n
        v_ctf = sV.n
        v_pzr = sSc.n
        for qch in range(4):
            q0 = qch * 512
            for m in range(NCH):
                rnd = qch * NCH + m
                for ch in range(NCH):
                    par = ch % 2
                    nld = rnd * NCH + ch
                    if nld >= 2:
                        Y(lambda e, v=c_mm.get(nld - 2, 0): e.wait_ge(sT.h, v))
                    Y(lambda e, ch=ch, q0=q0, par=par: sD.dma(e.dma_start(
                        xrh[par][:], c.mqh_s[ch, :, q0:q0 + 512])))
                    Y(lambda e, ch=ch, q0=q0, par=par: sD.dma(e.dma_start(
                        xrl[par][:], c.mql_s[ch, :, q0:q0 + 512])))
                    dn = sD.n
                    w0 = ch * DM + m * 128
                    T(lambda e, v=dn: e.wait_ge(sD.h, v))
                    if ch == 0:
                        T(lambda e, v=v_cev: e.wait_ge(sSc.h, v))
                        T(lambda e, v=v_pzr: e.wait_ge(sSc.h, v))
                    T(lambda e, w0=w0, par=par, st=(ch == 0): e.matmul(
                        pcp[:], wfqh[:, w0:w0 + 128], xrh[par][:],
                        start=st, stop=False))
                    T(lambda e, w0=w0, par=par: e.matmul(
                        pcp[:], wfql[:, w0:w0 + 128], xrh[par][:],
                        start=False, stop=False))
                    T(lambda e, w0=w0, par=par, sp=(ch == NCH - 1): sT.inc(
                        e.matmul(pcp[:], wfqh[:, w0:w0 + 128], xrl[par][:],
                                 start=False, stop=sp)))
                    c_mm[nld] = sT.n
                tc = sT.n
                SC(lambda e, v=tc: e.wait_ge(sT.h, v))
                SC(lambda e, v=v_ctf: e.wait_ge(sV.h, v))
                SC(lambda e, m=m: sSc.inc(e.activation(
                    ctf[:], pcp[:], ACT.Tanh, bias=bpp[:, m:m + 1])))
                v_cev = sSc.n
                V(lambda e, v=v_cev: e.wait_ge(sSc.h, v))
                V(lambda e: e.tensor_copy(cth[:], ctf[:]))
                V(lambda e: sV.inc(e.tensor_tensor(ctl[:], ctf[:], cth[:],
                                                   ALU.subtract)))
                v_ctf = sV.n
                for hh in range(2):
                    h = 2 * m + hh
                    b0_ = hh * 64
                    T(lambda e, v=v_ctf: e.wait_ge(sV.h, v))
                    T(lambda e, v=v_pzr: e.wait_ge(sSc.h, v))
                    for ps_, col in ((ppq, h), (pzq, H + h)):
                        T(lambda e, ps_=ps_, col=col, b0_=b0_: e.matmul(
                            ps_[:], r.upzh[b0_:b0_ + 64, col:col + 1],
                            cth[b0_:b0_ + 64, :], start=True, stop=False))
                        T(lambda e, ps_=ps_, col=col, b0_=b0_: e.matmul(
                            ps_[:], r.upzl[b0_:b0_ + 64, col:col + 1],
                            cth[b0_:b0_ + 64, :], start=False, stop=False))
                        T(lambda e, ps_=ps_, col=col, b0_=b0_: sT.inc(e.matmul(
                            ps_[:], r.upzh[b0_:b0_ + 64, col:col + 1],
                            ctl[b0_:b0_ + 64, :], start=False, stop=True)))
                    tpz = sT.n
                    SC(lambda e, v=tpz: e.wait_ge(sT.h, v))
                    SC(lambda e, v=dma_c.get(("pz", rnd, hh - 1) if hh else ("pz", rnd - 1, 1), 0):
                       e.wait_ge(sD.h, v))
                    SC(lambda e: e.activation(przB[:], ppq[:], ACT.Sigmoid))
                    SC(lambda e: sSc.inc(e.activation(przC[:], pzq[:], ACT.Sigmoid)))
                    v_pzr = sSc.n
                    Y(lambda e, v=v_pzr: e.wait_ge(sSc.h, v))
                    Y(lambda e, h=h, q0=q0: sD.dma(e.dma_start(
                        c.sgp_s[h, q0:q0 + 512], przB[:])))
                    Y(lambda e, h=h, q0=q0: sD.dma(e.dma_start(
                        c.sgz_s[h, q0:q0 + 512], przC[:])))
                    dma_c[("pz", rnd, hh)] = sD.n

        SC(lambda e: sSc.inc(e.activation(kmaxr[:], r.kmax2[:], ACT.Sqrt)))
        s_kx = sSc.n
        V(lambda e, v=s_kx: e.wait_ge(sSc.h, v))
        fW = 1024
        dprev = 0
        for h in range(H):
            for fh in range(2):
                f0 = fh * fW
                if h + fh >= 1:
                    Y(lambda e, v=dprev: e.wait_ge(sV.h, v))
                Y(lambda e, h=h, f0=f0: sD.dma(e.dma_start(
                    featq[0:1, 0:fW], c.sgp_s[h, f0:f0 + fW].unsqueeze(0))))
                Y(lambda e, h=h, f0=f0: sD.dma(e.dma_start(
                    featq[0:1, fW:2 * fW], c.sgz_s[h, f0:f0 + fW].unsqueeze(0))))
                Y(lambda e, h=h, f0=f0: sD.dma(e.dma_start(
                    featq[0:1, 2 * fW:3 * fW], c.q2n_s[h, f0:f0 + fW].unsqueeze(0))))
                dln = sD.n
                V(lambda e, v=dln: e.wait_ge(sD.h, v))
                V(lambda e: e.reciprocal(featw[0:1, 0:fW], featq[0:1, fW:2 * fW]))
                V(lambda e: e.tensor_tensor(featw[0:1, 0:fW], featw[0:1, 0:fW],
                                            featw[0:1, 0:fW], ALU.mult))
                V(lambda e: e.tensor_tensor(featw[0:1, fW:2 * fW],
                                            featq[0:1, 0:fW],
                                            featw[0:1, 0:fW], ALU.mult))
                V(lambda e: e.tensor_scalar(featw[0:1, 2 * fW:3 * fW],
                                            featw[0:1, fW:2 * fW],
                                            float(np.float32(4.0 / 8192.0)),
                                            None, ALU.mult))
                V(lambda e: e.tensor_scalar(featw[0:1, 3 * fW:4 * fW],
                                            featw[0:1, 0:fW],
                                            float(np.float32(-2.0 / (8192.0 * 8192.0))),
                                            None, ALU.mult))
                V(lambda e: e.tensor_tensor(featw[0:1, 4 * fW:5 * fW],
                                            featq[0:1, 0:fW],
                                            featw[0:1, fW:2 * fW], ALU.mult))
                V(lambda e: e.tensor_scalar(featw[0:1, 4 * fW:5 * fW],
                                            featw[0:1, 4 * fW:5 * fW], -2.0,
                                            None, ALU.mult))
                sq_in = featw[0:1, 5 * fW:6 * fW]
                V(lambda e: sV.inc(e.tensor_copy(sq_in, featq[0:1, 2 * fW:3 * fW])))
                vq2 = sV.n
                SC(lambda e, v=vq2: e.wait_ge(sV.h, v))
                SC(lambda e, sq_in=sq_in: sSc.inc(e.activation(sq_in, sq_in, ACT.Sqrt)))
                sq2 = sSc.n
                V(lambda e, v=sq2: e.wait_ge(sSc.h, v))
                V(lambda e, h=h: e.tensor_scalar(sq_in, sq_in,
                                                 kmaxr[0:1, h:h + 1], None, ALU.mult))
                V(lambda e: e.tensor_tensor(featw[0:1, 4 * fW:5 * fW],
                                            featw[0:1, 4 * fW:5 * fW], sq_in,
                                            ALU.subtract))
                for i_, src0 in ((0, 2 * fW), (1, 3 * fW), (2, 4 * fW)):
                    V(lambda e, i_=i_, s0=src0: e.tensor_copy(
                        featb[0:1, (2 * i_) * fW:(2 * i_) * fW + fW],
                        featw[0:1, s0:s0 + fW]))
                    V(lambda e, i_=i_, s0=src0: e.tensor_tensor(
                        featb[0:1, (2 * i_ + 1) * fW:(2 * i_ + 1) * fW + fW],
                        featw[0:1, s0:s0 + fW],
                        featb[0:1, (2 * i_) * fW:(2 * i_) * fW + fW], ALU.subtract))
                V(lambda e: sV.inc(e.engine_nop()))
                vfeat = sV.n
                dprev = vfeat
                Y(lambda e, v=vfeat: e.wait_ge(sV.h, v))
                rowmap = [0, 0, 1, 2, 2, 3, 4, 5]
                for ri, si in enumerate(rowmap):
                    Y(lambda e, h=h, ri=ri, si=si, f0=f0: sD.dma(e.dma_start(
                        c.qarr2_s[h, ri, f0:f0 + fW],
                        featb[0:1, si * fW:(si + 1) * fW])))
        dnf = sD.n
        Y(lambda e, v=dnf: e.wait_ge(sD.h, v))
        sch.run(nc)
def _emit_b3(nc, c, r, sems, wq_factory=None):
    sD, sG, sT, sV, sSc, sQ, sO = sems
    ph = contextlib.ExitStack()
    with ph:
        sch = Sched()
        Y = mk_adder(sch.q["y"])
        V = mk_adder(sch.q["v"], track=True)
        T = mk_adder(sch.q["t"])
        SC = mk_adder(sch.q["s"])

        def sbP(name, shape, dtyp=F32):
            return ph.enter_context(nc.sbuf_tensor(name, shape, dtyp))

        qR = [sbP(f"qR{i}", [128, TOWN], BF16) for i in range(2)]
        q2R = [sbP(f"q2R{i}", [72, TOWN], BF16) for i in range(2)]
        wo1R = [sbP(f"wo1R{i}", [128, DM], BF16) for i in range(2)]
        wo2R = [sbP(f"wo2R{i}", [128, DM], BF16) for i in range(2)]
        kR = [sbP(f"kR{i}", [128, 1024], BF16) for i in range(2)]
        k2R = [sbP(f"k2R{i}", [72, 1024], BF16) for i in range(2)]
        vR = [sbP(f"vR{i}", [128, 1024], BF16) for i in range(2)]
        ptmp = [sbP(f"ptmp{i}", [128, 1024]) for i in range(2)]
        pH = [sbP(f"pH{i}", [128, 1024], BF16) for i in range(2)]
        pL = [sbP(f"pL{i}", [128, 1024], BF16) for i in range(2)]
        sacc = sbP("sacc", [128, 16 * DM])
        S1 = sbP("S1", [128, TOWN], BF16)
        S2 = sbP("S2", [128, TOWN], BF16)
        dsb = sbP("dsb", [128, 512])
        drow = sbP("drow", [1, 512])
        rrh = sbP("rrh", [1, 512], BF16)
        rrl = sbP("rrl", [1, 512], BF16)
        rbsb = sbP("rbsb", [128, 512])
        aft = sbP("aft", [128, 512])
        a1t = sbP("a1t", [128, 512], BF16)
        a2t = sbP("a2t", [128, 512], BF16)
        pS = [ph.enter_context(nc.psum_tensor(f"pS{i}", [128, 1024], F32))
              for i in range(2)]
        pctx = [ph.enter_context(nc.psum_tensor(f"pctx{i}", [128, 512], F32))
                for i in range(2)]
        prb = ph.enter_context(nc.psum_tensor("prb", [128, 512], F32))
        pso = ph.enter_context(nc.psum_tensor("pso", [128, 512], F32))
        psoB = [pso, prb]

        KGRP = 8
        NG = 64 // KGRP

        q_load = {}
        ring_load = {}
        qk_mark = {}
        ex_mark = {}
        sp_mark = {}
        pv_mark = {}
        v_sacc = 0
        d_sacc_spill = 0

        def load_head(h, par):
            ch, rr = h // 2, (h % 2) * 64
            Y(lambda e, ch=ch, rr=rr, par=par: sD.dma(e.dma_start(
                qR[par][0:64, :], c.mqh_s[ch, rr:rr + 64, 0:TOWN])))
            Y(lambda e, ch=ch, rr=rr, par=par: sD.dma(e.dma_start(
                qR[par][64:128, :], c.mqh_s[ch, rr:rr + 64, 0:TOWN])))
            Y(lambda e, ch=ch, rr=rr, par=par: sD.dma(e.dma_start(
                q2R[par][0:64, :], c.mql_s[ch, rr:rr + 64, 0:TOWN])))
            Y(lambda e, h=h, par=par: sD.dma(e.dma_start(
                q2R[par][64:72, :], c.qarr2_s[h])))
            Y(lambda e, h=h, par=par: sD.dma(e.dma_start(
                wo1R[par][:], c.woar1_s[h])))
            Y(lambda e, h=h, par=par: sD.dma(e.dma_start(
                wo2R[par][:], c.woar2_s[h])))
            q_load[h] = sD.n

        def load_group(h, g, par, u=0):
            k0 = g * KGRP * 128
            Y(lambda e, h=h, k0=k0, par=par: sD.dma(e.dma_start(
                kR[par][:], c.karr_s[h, :, k0:k0 + 1024])))
            Y(lambda e, h=h, k0=k0, par=par: sD.dma(e.dma_start(
                k2R[par][0:64, :], c.karr_s[h, 0:64, k0:k0 + 1024])))
            Y(lambda e, k0=k0, par=par: sD.dma(e.dma_start(
                k2R[par][64:72, :], c.jarr_s[:, k0:k0 + 1024])))
            for kbl in range(KGRP):
                Y(lambda e, h=h, k0=k0, kbl=kbl, par=par: sD.dma(e.dma_start(
                    vR[par][:, kbl * 128:(kbl + 1) * 128],
                    c.varr_s[h, k0 + kbl * 128:k0 + kbl * 128 + 128, :])))
            ring_load[(h, g, u)] = sD.n

        def entry(j):
            hh = j // (2 * NG)
            uu = (j // NG) % 2
            gg = j % NG
            return hh, uu, gg

        NENT = H * 2 * NG
        load_head(0, 0)
        for j0 in range(2):
            hh, uu, gg = entry(j0)
            load_group(hh, gg, j0 % 2, uu)

        it = 0
        pend_pv = None
        oproj_mark = {}
        wq_gen = wq_factory(V, Y) if wq_factory is not None else iter(())
        for h in range(H):
            hpar = h % 2
            if h + 1 < H:
                if h >= 1:
                    Y(lambda e, v=oproj_mark[h - 1]: e.wait_ge(sT.h, v))
                load_head(h + 1, (h + 1) % 2)
            for u in range(2):
                def emit_pv(pit, pb, pkb, pgpar, pj, pkbl):
                    T(lambda e, v=sp_mark[pit]: e.wait_ge(sV.h, v))
                    for qc in range(2):
                        T(lambda e, b=pb, qc=qc, kbl=pkbl, gpar=pgpar,
                          st=(pkb == 0): e.matmul(
                            pctx[qc][:], vR[gpar][:, kbl * 128:kbl * 128 + 128],
                            pH[b][:, qc * 512:qc * 512 + 512],
                            start=st, stop=False))
                        mm = lambda e, b=pb, qc=qc, kbl=pkbl, gpar=pgpar, \
                            sp=(pkb == 63): e.matmul(
                            pctx[qc][:], vR[gpar][:, kbl * 128:kbl * 128 + 128],
                            pL[b][:, qc * 512:qc * 512 + 512],
                            start=False, stop=sp)
                        if qc == 1:
                            T(lambda e, mm=mm: sT.inc(mm(e)))
                        else:
                            T(mm)
                    pv_mark[pit] = sT.n
                    if pkbl == KGRP - 1 and pj + 2 < NENT:
                        nh, nu, ng = entry(pj + 2)
                        Y(lambda e, v=pv_mark[pit]: e.wait_ge(sT.h, v))
                        load_group(nh, ng, pgpar, nu)

                for kb in range(64):
                    g, kbl = kb // KGRP, kb % KGRP
                    j = (h * 2 + u) * NG + g
                    gpar = j % 2
                    b = it % 2
                    if kbl == 0:
                        T(lambda e, v=ring_load[(h, g, u)]: e.wait_ge(sD.h, v))
                    if it >= 2:
                        T(lambda e, v=ex_mark[it - 2]: e.wait_ge(sSc.h, v))
                    for qc in range(2):
                        q0 = u * 1024 + qc * 512
                        T(lambda e, b=b, qc=qc, kbl=kbl, q0=q0, hpar=hpar,
                          gpar=gpar: e.matmul(
                              pS[b][:, qc * 512:qc * 512 + 512],
                              kR[gpar][:, kbl * 128:kbl * 128 + 128],
                              qR[hpar][:, q0:q0 + 512], start=True, stop=False))
                        mm = lambda e, b=b, qc=qc, kbl=kbl, q0=q0, hpar=hpar, \
                            gpar=gpar: e.matmul(
                                pS[b][:, qc * 512:qc * 512 + 512],
                                k2R[gpar][0:72, kbl * 128:kbl * 128 + 128],
                                q2R[hpar][0:72, q0:q0 + 512],
                                start=False, stop=True)
                        if qc == 1:
                            T(lambda e, mm=mm: sT.inc(mm(e)))
                        else:
                            T(mm)
                    qk_mark[it] = sT.n
                    SC(lambda e, v=qk_mark[it]: e.wait_ge(sT.h, v))
                    if it >= 2:
                        SC(lambda e, v=sp_mark[it - 2]: e.wait_ge(sV.h, v))
                    SC(lambda e, b=b: sSc.inc(e.activation(
                        ptmp[b][:], pS[b][:], ACT.Exp)))
                    ex_mark[it] = sSc.n
                    V(lambda e, v=ex_mark[it]: e.wait_ge(sSc.h, v))
                    if it >= 2:
                        V(lambda e, v=pv_mark[it - 2]: e.wait_ge(sT.h, v))
                    V(lambda e, b=b: e.tensor_copy(pH[b][:], ptmp[b][:]))
                    V(lambda e, b=b: sV.inc(e.tensor_tensor(
                        pL[b][:], ptmp[b][:], pH[b][:], ALU.subtract)))
                    sp_mark[it] = sV.n
                    if pend_pv is not None:
                        emit_pv(*pend_pv)
                    pend_pv = (it, b, kb, gpar, j, kbl)
                    next(wq_gen, None)
                    it += 1
                if pend_pv is not None:
                    emit_pv(*pend_pv)
                    pend_pv = None
                for qc in range(2):
                    V(lambda e, v=pv_mark[it - 1]: e.wait_ge(sT.h, v))
                    V(lambda e, qc=qc: sV.inc(e.tensor_copy(
                        dsb[64:65, :], pctx[qc][64:65, :])))
                    vden = sV.n
                    Y(lambda e, v=vden: e.wait_ge(sV.h, v))
                    Y(lambda e: sD.dma(e.dma_start(c.den_s[0], dsb[64:65, :])))
                    dn = sD.n
                    Y(lambda e, v=dn: e.wait_ge(sD.h, v))
                    Y(lambda e: sD.dma(e.dma_start(drow[:], c.den_s[0].unsqueeze(0))))
                    dn = sD.n
                    V(lambda e, v=dn: e.wait_ge(sD.h, v))
                    V(lambda e: e.reciprocal(drow[:], drow[:]))
                    V(lambda e: e.tensor_copy(rrh[:], drow[:]))
                    V(lambda e: sV.inc(e.tensor_tensor(rrl[:], drow[:], rrh[:],
                                                       ALU.subtract)))
                    vr = sV.n
                    T(lambda e, v=vr: e.wait_ge(sV.h, v))
                    T(lambda e: e.matmul(prb[:], r.ones128r[0:1, 0:128], rrh[:],
                                         start=True, stop=False))
                    T(lambda e: sT.inc(e.matmul(prb[:], r.ones128r[0:1, 0:128],
                                                rrl[:], start=False, stop=True)))
                    trb = sT.n
                    V(lambda e, v=trb: e.wait_ge(sT.h, v))
                    V(lambda e: e.tensor_copy(rbsb[:], prb[:]))
                    V(lambda e, qc=qc: e.tensor_tensor(
                        aft[0:64, :], pctx[qc][0:64, :], rbsb[0:64, :], ALU.mult))
                    V(lambda e: e.tensor_copy(a1t[0:64, :], aft[0:64, :]))
                    V(lambda e: e.tensor_tensor(a2t[0:64, :], aft[0:64, :],
                                                a1t[0:64, :], ALU.subtract))
                    V(lambda e, qc=qc, u=u: sV.inc(e.tensor_tensor(
                        S2[64:128, u * 1024 + qc * 512:u * 1024 + qc * 512 + 512],
                        pctx[qc][64:128, :], rbsb[64:128, :], ALU.mult)))
                    vev = sV.n
                    col = u * 1024 + qc * 512
                    V(lambda e, col=col: e.tensor_copy(
                        S1[0:64, col:col + 512], a1t[0:64, :]))
                    V(lambda e, col=col: sV.inc(e.tensor_copy(
                        S2[0:64, col:col + 512], a1t[0:64, :])))
                    vev = sV.n
                    Y(lambda e, v=vev: e.wait_ge(sV.h, v))
                    Y(lambda e, col=col: sD.dma(e.dma_start(
                        S1[64:128, col:col + 512], a2t[0:64, :])))
                    dS1 = sD.n
                    T(lambda e, v=vev: e.wait_ge(sV.h, v))
                    T(lambda e, v=dS1: e.wait_ge(sD.h, v))
            T(lambda e, v=q_load[h]: e.wait_ge(sD.h, v))
            v_sacc_hist = {}
            for tb in range(16):
                for ot in range(2):
                    rop = tb * 2 + ot
                    opar = rop % 2
                    if rop >= 2:
                        T(lambda e, v=v_sacc_hist[rop - 2]: e.wait_ge(sV.h, v))
                    else:
                        T(lambda e, v=v_sacc: e.wait_ge(sV.h, v))
                    T(lambda e, tb=tb, ot=ot, hpar=hpar, opar=opar: e.matmul(
                        psoB[opar][:], S1[:, tb * 128:tb * 128 + 128],
                        wo1R[hpar][:, ot * 512:ot * 512 + 512],
                        start=True, stop=False))
                    T(lambda e, tb=tb, ot=ot, hpar=hpar, opar=opar: sT.inc(e.matmul(
                        psoB[opar][:], S2[:, tb * 128:tb * 128 + 128],
                        wo2R[hpar][:, ot * 512:ot * 512 + 512],
                        start=False, stop=True)))
                    tso = sT.n
                    V(lambda e, v=tso: e.wait_ge(sT.h, v))
                    col = tb * DM + ot * 512
                    if h == 0:
                        V(lambda e, col=col, opar=opar: sV.inc(e.tensor_copy(
                            sacc[:, col:col + 512], psoB[opar][:])))
                    else:
                        V(lambda e, col=col, opar=opar: sV.inc(e.tensor_tensor(
                            sacc[:, col:col + 512], sacc[:, col:col + 512],
                            psoB[opar][:], ALU.add)))
                    v_sacc_hist[rop] = sV.n
                    v_sacc = sV.n
            oproj_mark[h] = sT.n
        for _ in wq_gen:
            pass
        Y(lambda e, v=v_sacc: e.wait_ge(sV.h, v))
        for tb in range(16):
            Y(lambda e, tb=tb: sO.dma(e.dma_start(
                c.sacc_s[tb * 128:(tb + 1) * 128, :],
                sacc[:, tb * DM:(tb + 1) * DM])))
        Y(lambda e, v=sO.n: e.wait_ge(sO.h, v))
        sch.run(nc)
def _emit_b4(nc, c, r, sems, gq_done_val):
    sD, sG, sT, sV, sSc, sQ, sO = sems
    ph = contextlib.ExitStack()
    with ph:
        sch = Sched()
        Y = mk_adder(sch.q["y"])
        V = mk_adder(sch.q["v"], track=True)
        T = mk_adder(sch.q["t"])
        SC = mk_adder(sch.q["s"])

        def sbP(name, shape, dtyp=F32):
            return ph.enter_context(nc.sbuf_tensor(name, shape, dtyp))

        bobc = sbP("b4_bobc", [128, DM])
        wtin = [sbP(f"wtin{i}", [128, FI], BF16) for i in range(2)]
        wtst = [sbP(f"wtst{i}", [128, 128], BF16) for i in range(2)]
        xqR = sbP("xqR", [128, NCH * 512], BF16)
        wgR = [sbP(f"wgR{i}", [128, NCH * 512], BF16) for i in range(2)]
        wuR = [sbP(f"wuR{i}", [128, NCH * 512], BF16) for i in range(2)]
        wdH = [sbP(f"wdH{i}", [128, 16 * 512], BF16) for i in range(2)]
        sil = sbP("sil", [128, 512])
        hidt = sbP("hidt", [128, 512])
        hq = sbP("hq", [128, 512], BF16)
        hqf = sbP("hqf", [128, 32 * 512], BF16)
        qsA = sbP("b4qsA", [128, 32])
        qsB = sbP("b4qsB", [128, 32])
        qsC = sbP("b4qsC", [128, 32])
        qsD = sbP("b4qsD", [128, 32])
        qtA = sbP("b4qtA", [128, 512])
        qtB = sbP("b4qtB", [128, 512])
        gt4 = sbP("gt4", [128, 4 * DM])
        xt = [sbP(f"b4xt{i}", [128, DM]) for i in range(2)]
        st_ = [sbP(f"b4st{i}", [128, DM]) for i in range(2)]
        ot_ = [sbP(f"b4ot{i}", [128, DM]) for i in range(2)]
        ptrB = ph.enter_context(nc.psum_tensor("ptrB", [128, 128], BF16))
        pga = ph.enter_context(nc.psum_tensor("pga", [128, 512], F32))
        pgb = ph.enter_context(nc.psum_tensor("pgb", [128, 512], F32))
        pd4 = [ph.enter_context(nc.psum_tensor(f"pd{i}", [128, 512], F32))
               for i in range(4)]

        Y(lambda e: sD.dma(e.dma_start(bobc[:], c.bo_bc_in)))
        d_bo = sD.n
        V(lambda e, v=gq_done_val: e.wait_ge(sQ.h, v))
        V(lambda e, v=d_bo: e.wait_ge(sD.h, v))
        V(lambda e: e.tensor_tensor(r.scbc[:, 7:8], r.scbc[:, 0:1],
                                    r.scbc[:, 2:3], ALU.mult))
        V(lambda e: e.tensor_tensor(r.scbc[:, 8:9], r.scbc[:, 0:1],
                                    r.scbc[:, 3:4], ALU.mult))
        V(lambda e: sV.inc(e.tensor_tensor(r.scbc[:, 9:10], r.scbc[:, 1:2],
                                           r.scbc[:, 4:5], ALU.mult)))
        v_al = sV.n
        SC(lambda e, v=v_al: e.wait_ge(sV.h, v))

        tr_jobs = [(c.wgq_s, c.wgT_s, FI, DM), (c.wuq_s, c.wuT_s, FI, DM),
                   (c.wdq_s, c.wdT_s, DM, FI)]
        v_trev = 0
        d_trst = {}
        t_trin = {}
        n_tr = 0
        n_rb = 0
        for win, wout, nr, ncl in tr_jobs:
            for rb in range(nr // 128):
                n_rb += 1
                par = n_rb % 2
                if n_rb >= 3:
                    Y(lambda e, v=t_trin.get(n_rb - 2, 0): e.wait_ge(sT.h, v))
                Y(lambda e, win=win, rb=rb, par=par, ncl=ncl: sD.dma(e.dma_start(
                    wtin[par][:, 0:ncl], win[rb * 128:(rb + 1) * 128, :])))
                dwin = sD.n
                for cb in range(ncl // 128):
                    n_tr += 1
                    spar = n_tr % 2
                    T(lambda e, v=dwin: e.wait_ge(sD.h, v))
                    T(lambda e, v=v_trev: e.wait_ge(sV.h, v))
                    T(lambda e, cb=cb, par=par: sT.inc(e.transpose(
                        ptrB[:], wtin[par][:, cb * 128:cb * 128 + 128],
                        r.identb[:])))
                    tt = sT.n
                    V(lambda e, v=tt: e.wait_ge(sT.h, v))
                    if n_tr >= 3:
                        V(lambda e, v=d_trst.get(n_tr - 2, 0): e.wait_ge(sD.h, v))
                    V(lambda e, spar=spar: sV.inc(e.tensor_copy(
                        wtst[spar][:], ptrB[:])))
                    v_trev = sV.n
                    Y(lambda e, v=v_trev: e.wait_ge(sV.h, v))
                    Y(lambda e, wout=wout, rb=rb, cb=cb, spar=spar: sD.dma(
                        e.dma_start(wout[cb * 128:(cb + 1) * 128,
                                         rb * 128:(rb + 1) * 128], wtst[spar][:])))
                    d_trst[n_tr] = sD.n
                t_trin[n_rb] = sT.n
        d_tr_all = sD.n

        d_wg = {}
        d_wd = {}
        v_hid = 0
        v_hqf = 0
        t_hqf_used = 0
        v_comb = 0
        d_ocp = {}
        s_gt = 0
        for t4 in range(4):
            tk0 = t4 * 512
            Y(lambda e, v=d_tr_all: e.wait_ge(sD.h, v))
            if t4 >= 1:
                Y(lambda e, v=t_hqf_used: e.wait_ge(sT.h, v))
            for ch in range(NCH):
                Y(lambda e, ch=ch, tk0=tk0: sD.dma(e.dma_start(
                    xqR[:, ch * 512:(ch + 1) * 512],
                    c.xqf_s[:, ch * TOWN + tk0:ch * TOWN + tk0 + 512])))
            d_xq = sD.n
            for ot in range(8):
                rnd = t4 * 8 + ot
                opar = rnd % 2
                if rnd >= 2:
                    Y(lambda e, v=d_wg.get(rnd - 2, 0): e.wait_ge(sT.h, v))
                for ch in range(NCH):
                    Y(lambda e, ch=ch, ot=ot, opar=opar: sD.dma(e.dma_start(
                        wgR[opar][:, ch * 512:(ch + 1) * 512],
                        c.wgT_s[ch * 128:(ch + 1) * 128, ot * 512:ot * 512 + 512])))
                    Y(lambda e, ch=ch, ot=ot, opar=opar: sD.dma(e.dma_start(
                        wuR[opar][:, ch * 512:(ch + 1) * 512],
                        c.wuT_s[ch * 128:(ch + 1) * 128, ot * 512:ot * 512 + 512])))
                d_w = sD.n
                for ts in range(4):
                    T(lambda e, v=d_w: e.wait_ge(sD.h, v))
                    T(lambda e, v=d_xq: e.wait_ge(sD.h, v))
                    T(lambda e, v=v_hid: e.wait_ge(sV.h, v))
                    for ch in range(NCH):
                        x0 = ch * 512 + ts * 128
                        mm = lambda e, x0=x0, ch=ch, opar=opar: e.matmul(
                            pga[:], xqR[:, x0:x0 + 128],
                            wgR[opar][:, ch * 512:(ch + 1) * 512],
                            start=(ch == 0), stop=(ch == NCH - 1))
                        if ch == NCH - 1:
                            T(lambda e, mm=mm: sT.inc(mm(e)))
                        else:
                            T(mm)
                    tga = sT.n
                    for ch in range(NCH):
                        x0 = ch * 512 + ts * 128
                        mm = lambda e, x0=x0, ch=ch, opar=opar: e.matmul(
                            pgb[:], xqR[:, x0:x0 + 128],
                            wuR[opar][:, ch * 512:(ch + 1) * 512],
                            start=(ch == 0), stop=(ch == NCH - 1))
                        if ch == NCH - 1:
                            T(lambda e, mm=mm: sT.inc(mm(e)))
                        else:
                            T(mm)
                    tgb = sT.n
                    SC(lambda e, v=tga: e.wait_ge(sT.h, v))
                    SC(lambda e, v=v_hid: e.wait_ge(sV.h, v))
                    SC(lambda e: sSc.inc(e.activation(
                        sil[:], pga[:], ACT.Silu, bias=0.0,
                        scale=r.scbc[:, 7:8])))
                    sgl = sSc.n
                    V(lambda e, v=sgl: e.wait_ge(sSc.h, v))
                    V(lambda e, v=tgb: e.wait_ge(sT.h, v))
                    V(lambda e: e.tensor_scalar(hidt[:], pgb[:], r.scbc[:, 8:9],
                                                None, ALU.mult))
                    V(lambda e: sV.inc(e.tensor_tensor(hidt[:], hidt[:], sil[:],
                                                       ALU.mult)))
                    v_hid = sV.n
                    last = emit_nvfp4(V, hidt[:], hq[:], qsA[:], qsB[:],
                                      qsC[:], qsD[:], qtA[:], qtB[:],
                                      r.scbc[:, 6:7])
                    V(lambda e, last=last: sV.inc(last(e)))
                    vhq = sV.n
                    for cb in range(4):
                        T(lambda e, v=vhq: e.wait_ge(sV.h, v))
                        T(lambda e, v=v_hqf: e.wait_ge(sV.h, v))
                        T(lambda e, cb=cb: sT.inc(e.transpose(
                            ptrB[:], hq[:, cb * 128:cb * 128 + 128], r.identb[:])))
                        tt = sT.n
                        ic = ot * 4 + cb
                        V(lambda e, v=tt: e.wait_ge(sT.h, v))
                        V(lambda e, ic=ic, ts=ts: sV.inc(e.tensor_copy(
                            hqf[:, ic * 512 + ts * 128:ic * 512 + ts * 128 + 128],
                            ptrB[:])))
                        v_hqf = sV.n
                d_wg[rnd] = sT.n
            for oc in range(2):
                for icg in range(2):
                    wpar = (t4 * 4 + oc * 2 + icg) % 2
                    if t4 * 4 + oc * 2 + icg >= 2:
                        Y(lambda e, v=d_wd.get(t4 * 4 + oc * 2 + icg - 2, 0):
                          e.wait_ge(sT.h, v))
                    for i16 in range(16):
                        icabs = icg * 16 + i16
                        Y(lambda e, icabs=icabs, i16=i16, oc=oc, wpar=wpar:
                          sD.dma(e.dma_start(
                              wdH[wpar][:, i16 * 512:(i16 + 1) * 512],
                              c.wdT_s[icabs * 128:(icabs + 1) * 128,
                                      oc * 512:oc * 512 + 512])))
                    d_wdn = sD.n
                    for ts in range(4):
                        T(lambda e, v=d_wdn: e.wait_ge(sD.h, v))
                        if icg == 0:
                            T(lambda e, v=v_hqf: e.wait_ge(sV.h, v))
                            if t4 > 0 or oc > 0:
                                T(lambda e, v=s_gt: e.wait_ge(sSc.h, v))
                        for i16 in range(16):
                            icabs = icg * 16 + i16
                            mm = lambda e, icabs=icabs, i16=i16, ts=ts, \
                                wpar=wpar: e.matmul(
                                    pd4[ts][:],
                                    hqf[:, icabs * 512 + ts * 128:
                                         icabs * 512 + ts * 128 + 128],
                                    wdH[wpar][:, i16 * 512:(i16 + 1) * 512],
                                    start=(icabs == 0), stop=(icabs == 31))
                            if i16 == 15:
                                T(lambda e, mm=mm: sT.inc(mm(e)))
                            else:
                                T(mm)
                    d_wd[t4 * 4 + oc * 2 + icg] = sT.n
                t_dn = sT.n
                for ts in range(4):
                    SC(lambda e, v=t_dn: e.wait_ge(sT.h, v))
                    if t4 > 0:
                        SC(lambda e, v=v_comb: e.wait_ge(sV.h, v))
                    SC(lambda e, ts=ts, oc=oc: sSc.inc(e.activation(
                        gt4[:, ts * DM + oc * 512:ts * DM + oc * 512 + 512],
                        pd4[ts][:], ACT.Silu, bias=0.0, scale=r.scbc[:, 9:10])))
                s_gt = sSc.n
            t_hqf_used = sT.n
            for ts in range(4):
                tok0 = tk0 + ts * 128
                cpar = ts % 2
                if t4 * 4 + ts >= 2:
                    Y(lambda e, v=d_ocp.get(t4 * 4 + ts - 2, 0): e.wait_ge(sO.h, v))
                Y(lambda e, tok0=tok0, cpar=cpar: sD.dma(e.dma_start(
                    xt[cpar][:], c.x_in[tok0:tok0 + 128, :])))
                Y(lambda e, tok0=tok0, cpar=cpar: sD.dma(e.dma_start(
                    st_[cpar][:], c.sacc_s[tok0:tok0 + 128, :])))
                d_xs = sD.n
                V(lambda e, v=d_xs: e.wait_ge(sD.h, v))
                V(lambda e, v=s_gt: e.wait_ge(sSc.h, v))
                V(lambda e, cpar=cpar: e.tensor_tensor(
                    st_[cpar][:], st_[cpar][:], bobc[:], ALU.add))
                V(lambda e, cpar=cpar, ts=ts: e.tensor_tensor(
                    ot_[cpar][:], st_[cpar][:], gt4[:, ts * DM:(ts + 1) * DM],
                    ALU.mult))
                V(lambda e, cpar=cpar: sV.inc(e.tensor_tensor(
                    ot_[cpar][:], ot_[cpar][:], xt[cpar][:], ALU.add)))
                v_comb = sV.n
                Y(lambda e, v=v_comb: e.wait_ge(sV.h, v))
                Y(lambda e, tok0=tok0, cpar=cpar: sO.dma(e.dma_start(
                    c.out_hbm[tok0:tok0 + 128, :], ot_[cpar][:])))
                d_ocp[t4 * 4 + ts] = sO.n
        Y(lambda e, v=sO.n: e.wait_ge(sO.h, v))
        sch.run(nc)


def kernel(**inputs):
    bf16 = ml_dtypes.bfloat16
    x = np.asarray(inputs["L0_attn_hidden_states"], np.float32)
    wq = np.ascontiguousarray(np.asarray(inputs["L0_attn_wq"], np.float32))
    wk = np.ascontiguousarray(np.asarray(inputs["L0_attn_wk"], np.float32))
    wv = np.ascontiguousarray(np.asarray(inputs["L0_attn_wv"], np.float32))
    wo = np.ascontiguousarray(np.asarray(inputs["L0_attn_wo"], np.float32))
    wfq = np.ascontiguousarray(np.asarray(inputs["L0_attn_wfq"], np.float32))
    wfg = np.ascontiguousarray(np.asarray(inputs["L0_attn_wfg"], np.float32))

    def bias_fold(b):
        return np.ascontiguousarray(np.asarray(b, np.float32).reshape(NCH, 128).T)

    bq = bias_fold(inputs["L0_attn_bq"])
    bk = bias_fold(inputs["L0_attn_bk"])
    bfq = bias_fold(inputs["L0_attn_bfq"])
    bfg = bias_fold(inputs["L0_attn_bfg"])
    bv_bc = np.ascontiguousarray(np.broadcast_to(
        np.asarray(inputs["L0_attn_bv"], np.float32)[None, :], (128, DM)))
    bo_bc = np.ascontiguousarray(np.broadcast_to(
        np.asarray(inputs["L0_attn_bo"], np.float32)[None, :], (128, DM)))
    up2 = np.asarray(inputs["L0_attn_up"], np.float32).reshape(H, D).T
    uz2 = np.asarray(inputs["L0_attn_uz"], np.float32).reshape(H, D).T
    upz_half = np.concatenate([up2, uz2], axis=1)
    upz = np.ascontiguousarray(np.concatenate([upz_half, upz_half], axis=0))
    gw = np.ascontiguousarray(np.asarray(inputs["L0_ffn_gate_w"], np.float32))
    uw = np.ascontiguousarray(np.asarray(inputs["L0_ffn_up_w"], np.float32))
    dw = np.ascontiguousarray(np.asarray(inputs["L0_ffn_down_w"], np.float32))
    s_in = np.asarray(inputs["L0_ffn_s_in"], np.float32).reshape(1, 1)
    s_in_down = np.asarray(inputs["L0_ffn_s_in_down"], np.float32).reshape(1, 1)
    ident = np.eye(128, dtype=np.float32)

    nc = build_program(4)
    in_maps = []
    for core in range(8):
        b, qq = core // 4, core % 4
        xr = np.ascontiguousarray(np.roll(x[b], -TOWN * qq, axis=0))
        pos = ((np.arange(S) + TOWN * qq) % S).astype(np.float64)
        j_hi = pos.astype(bf16).astype(np.float64)
        j_lo = (pos - j_hi).astype(bf16).astype(np.float64)
        jj = pos * pos
        jj_hi = jj.astype(bf16).astype(np.float64)
        jj_lo = (jj - jj_hi).astype(bf16).astype(np.float64)
        jconst = np.ascontiguousarray(
            np.stack([j_hi, j_lo, jj_hi, jj_lo]).astype(np.float32))
        in_maps.append({
            "x_roll": xr, "wq": wq, "wk": wk, "wv": wv, "wo": wo,
            "wfq": wfq, "wfg": wfg, "bq": bq, "bk": bk, "bfq": bfq,
            "bfg": bfg, "bv_bc": bv_bc, "bo_bc": bo_bc, "upz": upz,
            "jconst": jconst, "ident_in": ident, "gate_w": gw, "up_w": uw,
            "down_w": dw, "s_in": s_in, "s_in_down": s_in_down,
        })
    res = run_bass_kernel_spmd(nc, in_maps, core_ids=list(range(8)))
    out = np.empty((2, S, DM), np.float32)
    for core in range(8):
        b, qq = core // 4, core % 4
        out[b, TOWN * qq:TOWN * (qq + 1)] = np.asarray(res.results[core]["out"])
    return out
